# version 04
# baseline (speedup 1.0000x reference)
import numpy as np
import concourse.bass as bass
import concourse.mybir as mybir
from concourse.bass_utils import run_bass_kernel_spmd

F32 = mybir.dt.float32
AF = mybir.ActivationFunctionType
ALU = mybir.AluOpType
AX = mybir.AxisListType

T = 4096
D = 1024
NS = 64
HID0 = 2048
ME = 8
MM = 1408
MMT = 11
NSUB = T // 128
NCH = T // 512

ENGINES = ("sync", "tensor", "vector", "scalar", "gpsimd")
AUTO_FENCE = True


class Sched:
    def __init__(self):
        self.q = {e: [] for e in ENGINES}
        self.gseq = 0
        self.seqs = {e: [] for e in ENGINES}
        self.marks = []

    def emit(self, eng, fn, deps=(), dma=False):
        lst = self.q[eng]
        tok = (eng, len(lst))
        deps = tuple(d for d in deps if d is not None)
        lst.append([fn, deps, dma, False])
        self.seqs[eng].append(self.gseq)
        self.gseq += 1
        return tok

    def build(self, nc, maxseq=None):
        if maxseq is not None:
            for e in ENGINES:
                keep = sum(1 for s in self.seqs[e] if s < maxseq)
                self.q[e] = self.q[e][:keep]
        for e in ENGINES:
            for rec in self.q[e]:
                for (de, di) in rec[1]:
                    self.q[de][di][3] = True
        vals = {}
        for e in ENGINES:
            c = 0
            for i, rec in enumerate(self.q[e]):
                if rec[3] and rec[0] is not None:
                    c += 16 if rec[2] else 1
                vals[(e, i)] = c
        import contextlib
        stack = contextlib.ExitStack()
        sems = {e: stack.enter_context(nc.semaphore(f"sem_{e}")) for e in ENGINES}
        with nc.Block() as block:
            for e in ENGINES:
                def body(eng, _e=e):
                    hw = {o: 0 for o in ENGINES}
                    for i, (fn, deps, dma, needed) in enumerate(self.q[_e]):
                        for (de, di) in deps:
                            v = vals[(de, di)]
                            if v > hw[de]:
                                eng.wait_ge(sems[de], v)
                                hw[de] = v
                        if fn is not None:
                            inst = fn(eng)
                            if needed:
                                inst.then_inc(sems[_e], 16 if dma else 1)
                getattr(block, e)(body)
        stack.close()


def build_program(maxseq=None, debug=False):
    nc = bass.Bass("TRN2", target_bir_lowering=False)
    S = Sched()

    def din(name, shape):
        return nc.dram_tensor(name, shape, F32, kind="ExternalInput").ap()

    def dscr(name, shape):
        return nc.dram_tensor(name, shape, F32, kind="Internal").ap()

    x_d = din("x", [T, D])
    n1w_d = din("n1w", [1, D]); n2w_d = din("n2w", [1, D])
    wq_d = din("wq", [D, D]); wk_d = din("wk", [D, D]); wv_d = din("wv", [D, D]); wo_d = din("wo", [D, D])
    bw_d = din("bw", [1, D]); bb_d = din("bb", [1, 1]); alog_d = din("alog", [1, D])
    wg0_d = din("wg0", [D, HID0]); wu0_d = din("wu0", [D, HID0]); wd0_d = din("wd0", [HID0, D])
    cw1_d = din("cw1", [D, D]); cw2_d = din("cw2", [D, 512]); cw3_d = din("cw3", [512, 256]); cw4_d = din("cw4", [256, 1])
    cb1_d = din("cb1", [1, D]); cb2_d = din("cb2", [1, 512]); cb3_d = din("cb3", [1, 256]); cb4_d = din("cb4", [1, 1])
    gw1_d = din("gw1", [D, 512]); gw2_d = din("gw2", [512, 256]); gw3_d = din("gw3", [256, 1])
    gb1_d = din("gb1", [1, 512]); gb2_d = din("gb2", [1, 256]); gb3_d = din("gb3", [1, 1])
    ar_d = din("ar", [1, 1])
    logA_d = din("logA", [D, NS]); Bp_d = din("Bp", [D, NS]); Cp_d = din("Cp", [D, NS]); Dv_d = din("Dv", [1, D])
    wgate_d = din("wgate", [D, ME])
    wgE_d = din("wgE", [ME, MMT, 8, 128, 128])
    wuE_d = din("wuE", [ME, MMT, 8, 128, 128])
    wdE_d = din("wdE", [ME, MM, D])
    id_d = din("idn", [128, 128]); upinc_d = din("upinc", [128, 128]); upneg_d = din("upneg", [128, 128])
    iotaf_d = din("iotaf", [128, 128]); onesr_d = din("onesr", [1, 128]); onesc_d = din("onesc", [128, 1])
    niop_d = din("niop", [128, 1]); iop_d = din("iop", [128, 1])

    out_d = nc.dram_tensor("out", [T, D], F32, kind="ExternalOutput").ap()

    XN = dscr("XN", [D, T]); XF = dscr("XF", [D, T]); HFd = dscr("HFd", [D, T])
    BETA = dscr("BETAd", [T, 1])
    QFd = dscr("QFd", [D, T]); KFd = dscr("KFd", [D, T]); KTMd = dscr("KTMd", [T, D]); VEd = dscr("VEd", [T, D])
    Y1d = dscr("Y1d", [D, T]); H2d = dscr("H2d", [D, T]); HIDd = dscr("HIDd", [HID0, T])
    SSd = dscr("SSd", [D, T]); X1d = dscr("X1d", [D, T]); X1Nd = dscr("X1Nd", [D, T]); X1NTd = dscr("X1NTd", [T, D])
    YSSMd = dscr("YSSMd", [T, D]); YMOEd = dscr("YMOEd", [D, T]); GATEd = dscr("GATEd", [8, T])

    def sb(name, shape):
        return nc.alloc_sbuf_tensor(name, shape, F32).ap()

    IDN = sb("IDN", [128, 128]); UPI = sb("UPI", [128, 128]); UPN = sb("UPN", [128, 128])
    IOF = sb("IOF", [128, 128]); ONR = sb("ONR", [1, 128]); ONC = sb("ONC", [128, 1])
    NIOP = sb("NIOP", [128, 1]); IOP = sb("IOP", [128, 1])
    W1B = sb("W1B", [128, D]); BWB = sb("BWB", [128, D]); BB128 = sb("BB128", [128, 1])
    PINV_E = sb("PINV_E", [128, D]); PFWD_E = sb("PFWD_E", [128, D])
    OMAR = sb("OMAR", [1, D]); AROWV = sb("AROWV", [1, D])
    OMA127 = AROWV
    GS = sb("GS", [128, 8]); N2W = sb("N2W", [128, 8]); DPP = sb("DPP", [128, 8])
    CB1P = sb("CB1P", [128, 8]); CB2P = sb("CB2P", [128, 4]); CB3P = sb("CB3P", [128, 2])
    GB1P = sb("GB1P", [128, 4]); GB2P = sb("GB2P", [128, 2])
    CB4S = sb("CB4S", [1, 1]); GB3S = sb("GB3S", [1, 1]); AR2 = sb("AR2", [1, 1]); GF07 = sb("GF07", [1, 1])
    LNAB = sb("LNAB", [64, D]); CBND = sb("CBND", [64, D]); CPND = sb("CPND", [64, D])
    SMALL = sb("SMALLb", [128, 2048]); ROWS = sb("ROWSb", [1, 2048]); STAT = sb("STATb", [128, 16])
    PA = sb("PAb", [128, 39100])

    PS = [nc.alloc_psum_tensor(f"pb{i}", [128, 512], F32).ap() for i in range(8)]
    ps_writer = [None] * 8
    ps_reader = [None] * 8

    cur = [0]
    def arena_reset():
        cur[0] = 0
    def alloc(n):
        off = cur[0]
        assert off + n <= 39100, f"arena overflow {off+n}"
        cur[0] += n
        return PA[:, off:off + n]

    tok_last = {e: None for e in ENGINES}

    def cfence():
        return tuple(t for t in tok_last.values() if t is not None)

    def E(eng, fn, deps=(), dma=False, fence=AUTO_FENCE):
        d = tuple(deps) + (cfence() if fence else ())
        t = S.emit(eng, fn, d, dma)
        tok_last[eng] = t
        return t

    def barrier():
        toks = cfence()
        for e in ENGINES:
            tok_last[e] = S.emit(e, None, toks)
        S.marks.append(S.gseq)

    def mm_group(bank, ncols, pairs, deps=(), col0=0, rows=128):
        dl = list(deps)
        if ps_reader[bank] is not None:
            dl.append(ps_reader[bank])
        t = None
        n = len(pairs)
        for i, (lhsT, rhs) in enumerate(pairs):
            def fn(eng, _l=lhsT, _r=rhs, _s=(i == 0), _e=(i == n - 1)):
                return eng.matmul(PS[bank][0:rows, col0:col0 + ncols], _l, _r,
                                  start=_s, stop=_e, skip_group_check=True)
            t = E("tensor", fn, tuple(dl) if i == 0 else (), fence=(AUTO_FENCE and i == 0))
        ps_writer[bank] = t
        return t

    def mm_acc(bank, ncols, pairs, deps=(), col0=0, rows=128):
        t = None
        for i, (lhsT, rhs) in enumerate(pairs):
            def fn(eng, _l=lhsT, _r=rhs):
                return eng.matmul(PS[bank][0:rows, col0:col0 + ncols], _l, _r,
                                  start=False, stop=True, skip_group_check=True)
            t = E("tensor", fn, tuple(deps) if i == 0 else (), fence=(AUTO_FENCE and i == 0))
        ps_writer[bank] = t
        return t

    def ps_read(bank, eng, fn, deps=()):
        t = E(eng, fn, tuple(deps) + ((ps_writer[bank],) if ps_writer[bank] else ()))
        ps_reader[bank] = t
        return t

    def transpose(bank, src_ap, dst_ap, p, c, deps=()):
        dl = list(deps)
        if ps_reader[bank] is not None:
            dl.append(ps_reader[bank])
        tt = E("tensor", lambda eng, _s=src_ap, _p=p, _c=c: eng.transpose(
            PS[bank][0:_c, 0:_p], _s, IDN[0:_p, 0:_p]), tuple(dl))
        ps_writer[bank] = tt
        return ps_read(bank, "scalar", lambda eng, _d=dst_ap, _p=p, _c=c: eng.copy(_d, PS[bank][0:_c, 0:_p]))

    def rank1(bank, row_ap, ncols, deps=()):
        dl = list(deps)
        if ps_reader[bank] is not None:
            dl.append(ps_reader[bank])
        t = E("tensor", lambda eng, _r=row_ap, _n=ncols: eng.matmul(
            PS[bank][:, 0:_n], ONR[:], _r, start=True, stop=True, skip_group_check=True), tuple(dl))
        ps_writer[bank] = t
        return t

    def ld(dst, src, deps=(), eng="sync"):
        return E(eng, lambda e, _d=dst, _s=src: e.dma_start(_d, _s), deps, dma=True)

    def st_(dst, src, deps=(), eng="gpsimd"):
        return E(eng, lambda e, _d=dst, _s=src: e.dma_start(_d, _s), deps, dma=True)

    store_toks = []

    ld(IDN[:], id_d[:]); ld(UPI[:], upinc_d[:]); ld(UPN[:], upneg_d[:])
    ld(IOF[:], iotaf_d[:]); ld(ONR[:], onesr_d[:]); ld(ONC[:], onesc_d[:])
    ld(NIOP[:], niop_d[:]); ld(IOP[:], iop_d[:])

    def row_bcast(dst, src_row_ap, width):
        for h in range((width + 511) // 512):
            w = min(512, width - h * 512)
            rank1(0, src_row_ap[:, h * 512:h * 512 + w], w)
            ps_read(0, "scalar", lambda eng, _d=dst, _h=h, _w=w: eng.copy(_d[:, _h * 512:_h * 512 + _w], PS[0][:, 0:_w]))

    t = ld(ROWS[0:1, 0:D], n1w_d[:])
    row_bcast(W1B, ROWS[0:1, 0:D], D)
    ld(ROWS[0:1, 0:D], bw_d[:])
    row_bcast(BWB, ROWS[0:1, 0:D], D)
    ld(ROWS[0:1, 1024:1025], bb_d[:])
    rank1(0, ROWS[0:1, 1024:1025], 1)
    ps_read(0, "scalar", lambda eng: eng.copy(BB128[:], PS[0][:, 0:1]))

    ld(ROWS[0:1, 0:D], alog_d[:])
    E("scalar", lambda eng: eng.activation(AROWV[:], ROWS[0:1, 0:D], AF.Sigmoid))
    E("vector", lambda eng: eng.tensor_scalar(OMAR[:], AROWV[:], -1.0, 1.0, ALU.mult, ALU.add))
    lnoma = ROWS[0:1, 1024:1024 + D]
    E("scalar", lambda eng: eng.activation(lnoma, OMAR[:], AF.Ln))
    A_PP = SMALL[:, 0:8]; LOM_PP = SMALL[:, 8:16]; NLOM_PP = SMALL[:, 16:24]
    STG = SMALL[:, 24:152]
    for dt in range(8):
        transpose(1, AROWV[:, dt * 128:(dt + 1) * 128], A_PP[:, dt:dt + 1], 1, 128)
    for dt in range(8):
        transpose(1, lnoma[:, dt * 128:(dt + 1) * 128], LOM_PP[:, dt:dt + 1], 1, 128)
    E("vector", lambda eng: eng.tensor_scalar_mul(NLOM_PP[:], LOM_PP[:], -1.0))
    E("vector", lambda eng: eng.tensor_scalar_mul(OMA127[:], lnoma, 127.0))
    E("scalar", lambda eng: eng.activation(OMA127[:], OMA127[:], AF.Exp))
    for dt in range(8):
        E("scalar", lambda eng, _dt=dt: eng.activation(STG, IOF[:], AF.Exp, scale=NLOM_PP[:, _dt:_dt + 1]))
        E("vector", lambda eng, _dt=dt: eng.tensor_scalar(STG, STG, A_PP[:, _dt:_dt + 1], None, ALU.mult))
        transpose(1, STG, PINV_E[:, dt * 128:(dt + 1) * 128], 128, 128)
        E("scalar", lambda eng, _dt=dt: eng.activation(STG, IOF[:], AF.Exp, scale=LOM_PP[:, _dt:_dt + 1]))
        transpose(1, STG, PFWD_E[:, dt * 128:(dt + 1) * 128], 128, 128)

    def row_to_pp(dst, src_d, width):
        ld(ROWS[0:1, 0:width], src_d[:])
        for dt in range(width // 128):
            transpose(1, ROWS[0:1, dt * 128:(dt + 1) * 128], dst[:, dt:dt + 1], 1, 128)

    row_to_pp(N2W, n2w_d, D)
    row_to_pp(DPP, Dv_d, D)
    row_to_pp(CB1P, cb1_d, D)
    row_to_pp(CB2P, cb2_d, 512)
    row_to_pp(CB3P, cb3_d, 256)
    row_to_pp(GB1P, gb1_d, 512)
    row_to_pp(GB2P, gb2_d, 256)
    ld(CB4S[:], cb4_d[:]); ld(GB3S[:], gb3_d[:])
    ld(ROWS[0:1, 1024:1025], ar_d[:])
    E("vector", lambda eng: eng.tensor_scalar_mul(AR2[:], ROWS[0:1, 1024:1025], 2.0))

    LA_T = SMALL[:, 152:216]
    for (src_d, dst) in ((logA_d, LNAB), (Bp_d, CBND), (Cp_d, CPND)):
        for dt in range(8):
            ld(LA_T, src_d[dt * 128:(dt + 1) * 128, :])
            transpose(1, LA_T, dst[0:64, dt * 128:(dt + 1) * 128], 128, 64)
    arena_reset()
    TMPA = alloc(1024)[0:64, :]
    RECA = alloc(1024)[0:64, :]
    E("scalar", lambda eng: eng.activation(TMPA, LNAB[0:64, :], AF.Exp))
    E("vector", lambda eng: eng.tensor_scalar_mul(TMPA, TMPA, -1.0))
    E("vector", lambda eng: eng.reciprocal(RECA, TMPA))
    E("vector", lambda eng: eng.tensor_scalar_mul(LNAB[0:64, :], TMPA, 0.1))
    E("scalar", lambda eng: eng.activation(TMPA, LNAB[0:64, :], AF.Exp))
    E("vector", lambda eng: eng.tensor_scalar(TMPA, TMPA, 1.0, None, ALU.subtract))
    E("vector", lambda eng: eng.tensor_mul(TMPA, TMPA, RECA))
    E("vector", lambda eng: eng.tensor_mul(CBND[0:64, :], CBND[0:64, :], TMPA))
    E("vector", lambda eng: eng.tensor_mul(CBND[0:64, :], CBND[0:64, :], CPND[0:64, :]))
    LNABd = dscr("LNABd", [64, D]); CBNDd = dscr("CBNDd", [64, D])
    store_toks.append(st_(LNABd[:], LNAB[0:64, :]))
    store_toks.append(st_(CBNDd[:], CBND[0:64, :]))
    E("vector", lambda eng: eng.memset(GS[:], 0.0))
    barrier()

    arena_reset()
    XT = alloc(1024); XSQ = alloc(1024); XNT = alloc(1024); HT = alloc(1024)
    FMB = alloc(1024); FMB2 = alloc(1024)
    s1 = STAT[:, 0:1]; s2 = STAT[:, 1:2]; s3 = STAT[:, 2:3]; s4 = STAT[:, 3:4]
    for u in range(NSUB):
        ld(XT, x_d[u * 128:(u + 1) * 128, :])
        E("scalar", lambda eng: eng.activation(XSQ, XT, AF.Square))
        E("vector", lambda eng: eng.reduce_sum(s1, XT, AX.X))
        E("vector", lambda eng: eng.reduce_sum(s2, XSQ, AX.X))
        E("vector", lambda eng: eng.tensor_scalar_mul(s1, s1, 1.0 / D))
        E("vector", lambda eng: eng.tensor_mul(s3, s1, s1))
        E("vector", lambda eng: eng.scalar_tensor_tensor(s2, s2, 1.0 / D, s3, ALU.mult, ALU.subtract))
        E("vector", lambda eng: eng.tensor_scalar_add(s2, s2, 1e-5))
        E("scalar", lambda eng: eng.activation(s2, s2, AF.Sqrt))
        E("vector", lambda eng: eng.reciprocal(s2, s2))
        E("vector", lambda eng: eng.tensor_scalar(XNT, XT, s1, s2, ALU.subtract, ALU.mult))
        E("scalar", lambda eng: eng.activation(XSQ, XNT, AF.Square))
        E("vector", lambda eng: eng.reduce_sum(s3, XSQ, AX.X))
        E("vector", lambda eng: eng.tensor_scalar(s3, s3, 1.0 / D, 1e-6, ALU.mult, ALU.add))
        E("scalar", lambda eng: eng.activation(s3, s3, AF.Sqrt))
        E("vector", lambda eng: eng.reciprocal(s3, s3))
        E("vector", lambda eng: eng.tensor_scalar(HT, XNT, s3, None, ALU.mult))
        E("vector", lambda eng: eng.tensor_mul(HT, HT, W1B[:]))
        E("vector", lambda eng: eng.tensor_mul(XSQ, HT, BWB[:]))
        E("vector", lambda eng: eng.reduce_sum(s4, XSQ, AX.X))
        E("scalar", lambda eng: eng.activation(s4, s4, AF.Sigmoid, bias=BB128[:]))
        store_toks.append(st_(BETA[u * 128:(u + 1) * 128, :], s4))
        for dt in range(8):
            transpose(2, XNT[:, dt * 128:(dt + 1) * 128], FMB[:, dt * 128:(dt + 1) * 128], 128, 128)
            transpose(3, HT[:, dt * 128:(dt + 1) * 128], FMB2[:, dt * 128:(dt + 1) * 128], 128, 128)
            store_toks.append(st_(XN[dt * 128:(dt + 1) * 128, u * 128:(u + 1) * 128], FMB[:, dt * 128:(dt + 1) * 128]))
            store_toks.append(st_(HFd[dt * 128:(dt + 1) * 128, u * 128:(u + 1) * 128], FMB2[:, dt * 128:(dt + 1) * 128]))
            E("vector", lambda eng, _dt=dt: eng.reduce_sum(STAT[:, 8 + _dt:9 + _dt], FMB[:, _dt * 128:(_dt + 1) * 128], AX.X))
            E("vector", lambda eng, _dt=dt: eng.tensor_add(GS[:, _dt:_dt + 1], GS[:, _dt:_dt + 1], STAT[:, 8 + _dt:9 + _dt]))
        for dt in range(8):
            transpose(2, XT[:, dt * 128:(dt + 1) * 128], FMB[:, dt * 128:(dt + 1) * 128], 128, 128)
            store_toks.append(st_(XF[dt * 128:(dt + 1) * 128, u * 128:(u + 1) * 128], FMB[:, dt * 128:(dt + 1) * 128]))
    barrier()

    arena_reset()
    WQ = [alloc(1024) for _ in range(8)]
    WK = [alloc(1024) for _ in range(8)]
    WV = [alloc(1024) for _ in range(8)]
    HFB = alloc(4096); KFB = alloc(4096); VFB = alloc(4096)
    KSQ = VFB
    QM = alloc(512); VTM = alloc(1024)
    OMAC = HFB[:, 0:1024]
    KTMS = VTM
    ZB = KFB[:, 0:1024]
    INVB = QM
    CARRY = ROWS[0:1, 0:D]
    KROW = ROWS[0:1, 1024:1536]
    CTMP = ROWS[0:1, 1536:1536 + 512]

    for kp in range(8):
        ld(WQ[kp], wq_d[kp * 128:(kp + 1) * 128, :])
        ld(WK[kp], wk_d[kp * 128:(kp + 1) * 128, :])
        ld(WV[kp], wv_d[kp * 128:(kp + 1) * 128, :])
    E("vector", lambda eng: eng.memset(CARRY, 0.0))

    for c in range(NCH):
        for kp in range(8):
            ld(HFB[:, kp * 512:(kp + 1) * 512], HFd[kp * 128:(kp + 1) * 128, c * 512:(c + 1) * 512])
        for m in range(8):
            mm_group(0, 512, [(WQ[kp][:, m * 128:(m + 1) * 128], HFB[:, kp * 512:(kp + 1) * 512]) for kp in range(8)])
            ps_read(0, "scalar", lambda eng: eng.copy(QM, PS[0][:, 0:512]))
            store_toks.append(st_(QFd[m * 128:(m + 1) * 128, c * 512:(c + 1) * 512], QM))
        for m in range(8):
            mm_group(1, 512, [(WK[kp][:, m * 128:(m + 1) * 128], HFB[:, kp * 512:(kp + 1) * 512]) for kp in range(8)])
            ps_read(1, "scalar", lambda eng, _m=m: eng.copy(KFB[:, _m * 512:(_m + 1) * 512], PS[1][:, 0:512]))
        for m in range(8):
            E("scalar", lambda eng, _m=m: eng.activation(KSQ[:, _m * 512:(_m + 1) * 512], KFB[:, _m * 512:(_m + 1) * 512], AF.Square))
        mm_group(2, 512, [(ONC[:], KSQ[:, m * 512:(m + 1) * 512]) for m in range(8)], rows=1)
        ps_read(2, "scalar", lambda eng: eng.activation(KROW, PS[2][0:1, 0:512], AF.Sqrt))
        E("vector", lambda eng: eng.tensor_scalar_max(KROW, KROW, 1e-12))
        E("vector", lambda eng: eng.reciprocal(KROW, KROW))
        rank1(2, KROW, 512)
        ps_read(2, "scalar", lambda eng: eng.copy(INVB, PS[2][:, 0:512]))
        for m in range(8):
            E("vector", lambda eng, _m=m: eng.tensor_mul(KFB[:, _m * 512:(_m + 1) * 512], KFB[:, _m * 512:(_m + 1) * 512], INVB))
            store_toks.append(st_(KFd[m * 128:(m + 1) * 128, c * 512:(c + 1) * 512], KFB[:, m * 512:(m + 1) * 512]))
        for s in range(4):
            u = c * 4 + s
            for dt in range(8):
                transpose(3, KFB[:, dt * 512 + s * 128: dt * 512 + (s + 1) * 128], KTMS[:, dt * 128:(dt + 1) * 128], 128, 128)
            store_toks.append(st_(KTMd[u * 128:(u + 1) * 128, :], KTMS))
        for m in range(8):
            mm_group(4, 512, [(WV[kp][:, m * 128:(m + 1) * 128], HFB[:, kp * 512:(kp + 1) * 512]) for kp in range(8)])
            ps_read(4, "scalar", lambda eng, _m=m: eng.copy(VFB[:, _m * 512:(_m + 1) * 512], PS[4][:, 0:512]))
        for s in range(4):
            u = c * 4 + s
            for dt in range(8):
                transpose(5, VFB[:, dt * 512 + s * 128: dt * 512 + (s + 1) * 128], VTM[:, dt * 128:(dt + 1) * 128], 128, 128)
            E("vector", lambda eng: eng.tensor_mul(ZB, PINV_E[:], VTM))
            E("vector", lambda eng: eng.tensor_mul(OMAC[0:1, :], CARRY, OMAR[:]))
            E("vector", lambda eng: eng.tensor_add(ZB[0:1, :], ZB[0:1, :], OMAC[0:1, :]))
            mm_group(6, 512, [(UPI[:], ZB[:, 0:512])])
            mm_group(7, 512, [(UPI[:], ZB[:, 512:1024])])
            ps_read(6, "vector", lambda eng: eng.tensor_mul(VTM[:, 0:512], PFWD_E[:, 0:512], PS[6][:, 0:512]))
            ps_read(7, "vector", lambda eng: eng.tensor_mul(VTM[:, 512:1024], PFWD_E[:, 512:1024], PS[7][:, 0:512]))
            mm_group(2, 512, [(ONC[:], ZB[:, 0:512])], rows=1)
            ps_read(2, "vector", lambda eng: eng.tensor_mul(CARRY[:, 0:512], PS[2][0:1, 0:512], OMA127[:, 0:512]))
            mm_group(3, 512, [(ONC[:], ZB[:, 512:1024])], rows=1)
            ps_read(3, "vector", lambda eng: eng.tensor_mul(CARRY[:, 512:1024], PS[3][0:1, 0:512], OMA127[:, 512:1024]))
            store_toks.append(st_(VEd[u * 128:(u + 1) * 128, :], VTM))
    barrier()

    arena_reset()
    WO = [alloc(1024) for _ in range(8)]
    STT = [alloc(1024) for _ in range(8)]
    QFB = alloc(4096); KFB3 = alloc(4096)
    KTMS2 = alloc(1024); VES = alloc(1024)
    ATN = alloc(1024); AFB = alloc(4096)
    UB = alloc(1024); BB_ = alloc(1024)
    NT = alloc(128); RT_A = alloc(128); RT_B = alloc(128)
    MP0 = alloc(128); MP1 = alloc(128); MTP0 = alloc(128); MTP1 = alloc(128)
    MQK = alloc(128); BBC = alloc(128); BETR = alloc(128)
    BETP = alloc(4)
    Y1M = alloc(512); XNM = alloc(512); H2M = alloc(512); Y1SQ = alloc(512); RSTB = alloc(512)
    Y1ROW = ROWS[0:1, 1024:1536]

    for kp in range(8):
        ld(WO[kp], wo_d[kp * 128:(kp + 1) * 128, :])
        E("gpsimd", lambda eng, _k=kp: eng.memset(STT[_k], 0.0))

    for c in range(NCH):
        for kp in range(8):
            ld(QFB[:, kp * 512:(kp + 1) * 512], QFd[kp * 128:(kp + 1) * 128, c * 512:(c + 1) * 512])
            ld(KFB3[:, kp * 512:(kp + 1) * 512], KFd[kp * 128:(kp + 1) * 128, c * 512:(c + 1) * 512])
        for s in range(4):
            u = c * 4 + s
            ld(BETP[:, s:s + 1], BETA[u * 128:(u + 1) * 128, :])
            ld(KTMS2, KTMd[u * 128:(u + 1) * 128, :])
            ld(VES, VEd[u * 128:(u + 1) * 128, :])
            ksl = [KFB3[:, kp * 512 + s * 128: kp * 512 + (s + 1) * 128] for kp in range(8)]
            qsl = [QFB[:, kp * 512 + s * 128: kp * 512 + (s + 1) * 128] for kp in range(8)]
            bpp = BETP[:, s:s + 1]
            transpose(0, bpp, BETR[0:1, 0:128], 128, 1)
            rank1(0, BETR[0:1, 0:128], 128)
            ps_read(0, "scalar", lambda eng: eng.copy(BBC, PS[0][:, 0:128]))
            mm_group(1, 128, [(ksl[kp], ksl[kp]) for kp in range(8)])
            ps_read(1, "vector", lambda eng: eng.tensor_mul(NT, PS[1][:, 0:128], UPN[:]))
            E("vector", lambda eng: eng.tensor_mul(NT, NT, BBC))
            E("vector", lambda eng: eng.tensor_add(RT_A, NT, IDN[:]))
            transpose(2, NT, MP0, 128, 128)
            E("scalar", lambda eng: eng.copy(MTP0, NT))
            rt_cur, rt_oth = RT_A, RT_B
            m_cur, mt_cur = MP0, MTP0
            m_nxt, mt_nxt = MP1, MTP1
            for lvl in range(6):
                mm_group(3, 128, [(mt_cur, m_cur)])
                ps_read(3, "scalar", lambda eng, _m=m_nxt: eng.copy(_m, PS[3][:, 0:128]))
                mm_group(2, 128, [(m_nxt, rt_cur)])
                ps_read(2, "vector", lambda eng, _o=rt_oth, _c=rt_cur: eng.tensor_add(_o, _c, PS[2][:, 0:128]))
                rt_cur, rt_oth = rt_oth, rt_cur
                if lvl < 5:
                    transpose(3, m_nxt, mt_nxt, 128, 128)
                    m_cur, mt_cur = m_nxt, mt_nxt
                    m_nxt, mt_nxt = (MP0, MTP0) if m_cur is MP1 else (MP1, MTP1)
            mm_group(4, 512, [(ksl[kp], STT[kp][:, 0:512]) for kp in range(8)])
            mm_group(5, 512, [(ksl[kp], STT[kp][:, 512:1024]) for kp in range(8)])
            ps_read(4, "vector", lambda eng: eng.scalar_tensor_tensor(BB_[:, 0:512], PS[4][:, 0:512], -1.0, VES[:, 0:512], ALU.mult, ALU.add))
            ps_read(5, "vector", lambda eng: eng.scalar_tensor_tensor(BB_[:, 512:1024], PS[5][:, 0:512], -1.0, VES[:, 512:1024], ALU.mult, ALU.add))
            E("vector", lambda eng, _b=bpp: eng.tensor_scalar(BB_, BB_, _b, None, ALU.mult))
            mm_group(4, 512, [(rt_cur, BB_[:, 0:512])])
            ps_read(4, "scalar", lambda eng: eng.copy(UB[:, 0:512], PS[4][:, 0:512]))
            mm_group(5, 512, [(rt_cur, BB_[:, 512:1024])])
            ps_read(5, "scalar", lambda eng: eng.copy(UB[:, 512:1024], PS[5][:, 0:512]))
            mm_group(6, 512, [(qsl[kp], STT[kp][:, 0:512]) for kp in range(8)])
            mm_group(7, 512, [(qsl[kp], STT[kp][:, 512:1024]) for kp in range(8)])
            mm_group(1, 128, [(ksl[kp], qsl[kp]) for kp in range(8)])
            ps_read(1, "vector", lambda eng: eng.tensor_mul(MQK, PS[1][:, 0:128], UPI[:]))
            mm_acc(6, 512, [(MQK, UB[:, 0:512])])
            mm_acc(7, 512, [(MQK, UB[:, 512:1024])])
            ps_read(6, "scalar", lambda eng: eng.copy(ATN[:, 0:512], PS[6][:, 0:512]))
            ps_read(7, "scalar", lambda eng: eng.copy(ATN[:, 512:1024], PS[7][:, 0:512]))
            for m in range(8):
                mm_group(4, 512, [(KTMS2[:, m * 128:(m + 1) * 128], UB[:, 0:512])])
                ps_read(4, "vector", lambda eng, _m=m: eng.tensor_add(STT[_m][:, 0:512], STT[_m][:, 0:512], PS[4][:, 0:512]))
                mm_group(5, 512, [(KTMS2[:, m * 128:(m + 1) * 128], UB[:, 512:1024])])
                ps_read(5, "vector", lambda eng, _m=m: eng.tensor_add(STT[_m][:, 512:1024], STT[_m][:, 512:1024], PS[5][:, 0:512]))
            for dt in range(8):
                transpose(0, ATN[:, dt * 128:(dt + 1) * 128], AFB[:, dt * 512 + s * 128: dt * 512 + (s + 1) * 128], 128, 128)
        for m in range(8):
            mm_group(0, 512, [(WO[kp][:, m * 128:(m + 1) * 128], AFB[:, kp * 512:(kp + 1) * 512]) for kp in range(8)])
            ld(XNM, XN[m * 128:(m + 1) * 128, c * 512:(c + 1) * 512])
            ps_read(0, "vector", lambda eng: eng.tensor_add(Y1M, PS[0][:, 0:512], XNM))
            store_toks.append(st_(Y1d[m * 128:(m + 1) * 128, c * 512:(c + 1) * 512], Y1M))
            E("scalar", lambda eng: eng.activation(Y1SQ, Y1M, AF.Square))
            if m == 0:
                mm_group(1, 512, [(ONC[:], Y1SQ)], rows=1)
            else:
                mm_acc(1, 512, [(ONC[:], Y1SQ)], rows=1)
        ps_read(1, "vector", lambda eng: eng.tensor_scalar(Y1ROW, PS[1][0:1, 0:512], 1.0 / D, 1e-6, ALU.mult, ALU.add))
        E("scalar", lambda eng: eng.activation(Y1ROW, Y1ROW, AF.Sqrt))
        E("vector", lambda eng: eng.reciprocal(Y1ROW, Y1ROW))
        rank1(1, Y1ROW, 512)
        ps_read(1, "scalar", lambda eng: eng.copy(RSTB, PS[1][:, 0:512]))
        for m in range(8):
            ld(Y1M, Y1d[m * 128:(m + 1) * 128, c * 512:(c + 1) * 512])
            E("vector", lambda eng: eng.tensor_mul(H2M, Y1M, RSTB))
            E("vector", lambda eng, _m=m: eng.tensor_scalar(H2M, H2M, N2W[:, _m:_m + 1], None, ALU.mult))
            store_toks.append(st_(H2d[m * 128:(m + 1) * 128, c * 512:(c + 1) * 512], H2M))
    barrier()

    arena_reset()
    WGB = [alloc(2048) for _ in range(8)]
    WUB = [alloc(2048) for _ in range(8)]
    H2F = alloc(4096); HIDM = alloc(512); HIDT = alloc(512)
    for kp in range(8):
        ld(WGB[kp], wg0_d[kp * 128:(kp + 1) * 128, :])
        ld(WUB[kp], wu0_d[kp * 128:(kp + 1) * 128, :])
    for c in range(NCH):
        for kp in range(8):
            ld(H2F[:, kp * 512:(kp + 1) * 512], H2d[kp * 128:(kp + 1) * 128, c * 512:(c + 1) * 512])
        for m in range(16):
            mm_group(0, 512, [(WGB[kp][:, m * 128:(m + 1) * 128], H2F[:, kp * 512:(kp + 1) * 512]) for kp in range(8)])
            mm_group(1, 512, [(WUB[kp][:, m * 128:(m + 1) * 128], H2F[:, kp * 512:(kp + 1) * 512]) for kp in range(8)])
            ps_read(0, "scalar", lambda eng: eng.activation(HIDT, PS[0][:, 0:512], AF.Silu))
            ps_read(1, "vector", lambda eng: eng.tensor_mul(HIDM, HIDT, PS[1][:, 0:512]))
            store_toks.append(st_(HIDd[m * 128:(m + 1) * 128, c * 512:(c + 1) * 512], HIDM))
    barrier()

    arena_reset()
    WDB = [alloc(1024) for _ in range(16)]
    HIDF = alloc(8192); Y1M2 = alloc(512); SM_ = alloc(512)
    for kp in range(16):
        ld(WDB[kp], wd0_d[kp * 128:(kp + 1) * 128, :])
    for c in range(NCH):
        for kp in range(16):
            ld(HIDF[:, kp * 512:(kp + 1) * 512], HIDd[kp * 128:(kp + 1) * 128, c * 512:(c + 1) * 512])
        for m in range(8):
            mm_group(0, 512, [(WDB[kp][:, m * 128:(m + 1) * 128], HIDF[:, kp * 512:(kp + 1) * 512]) for kp in range(16)])
            ld(Y1M2, Y1d[m * 128:(m + 1) * 128, c * 512:(c + 1) * 512])
            ps_read(0, "vector", lambda eng: eng.tensor_add(SM_, PS[0][:, 0:512], Y1M2))
            store_toks.append(st_(SSd[m * 128:(m + 1) * 128, c * 512:(c + 1) * 512], SM_))
    barrier()

    arena_reset()
    CW1 = [alloc(1024) for _ in range(8)]
    CW2 = [alloc(512) for _ in range(8)]
    CW3 = [alloc(256) for _ in range(4)]
    CW4 = [alloc(1) for _ in range(2)]
    GW1B = alloc(4096)
    GW1 = [GW1B[:, kp * 512:(kp + 1) * 512] for kp in range(8)]
    GW2 = [alloc(256) for _ in range(4)]
    GW3 = [alloc(1) for _ in range(2)]
    OUTA = alloc(4096)
    OUTB = GW1B
    C1B = alloc(4096); C2B = alloc(2048); C3B = alloc(1024)
    X1FB = alloc(4096)
    X1NB = OUTA
    XM_ = C2B[:, 0:512]; SM2 = C2B[:, 512:1024]; X1SQ = C2B[:, 1024:1536]
    MB_ = C3B[:, 0:512]; RB_ = C3B[:, 512:1024]
    CPB = alloc(512); TMS_ = alloc(1024)
    GH1 = STAT[:, 8:12]; GH2 = STAT[:, 12:14]
    TFR = ROWS[0:1, 1024:1536]
    MROW = ROWS[0:1, 0:512]; VROW = ROWS[0:1, 512:1024]

    for kp in range(8):
        ld(CW1[kp], cw1_d[kp * 128:(kp + 1) * 128, :])
        ld(CW2[kp], cw2_d[kp * 128:(kp + 1) * 128, :])
        ld(GW1[kp], gw1_d[kp * 128:(kp + 1) * 128, :])
    for kp in range(4):
        ld(CW3[kp], cw3_d[kp * 128:(kp + 1) * 128, :])
        ld(GW2[kp], gw2_d[kp * 128:(kp + 1) * 128, :])
    for kp in range(2):
        ld(CW4[kp], cw4_d[kp * 128:(kp + 1) * 128, :])
        ld(GW3[kp], gw3_d[kp * 128:(kp + 1) * 128, :])

    E("vector", lambda eng: eng.tensor_scalar_mul(GS[:], GS[:], 1.0 / T))
    for mt in range(4):
        mm_group(0, 1, [(GW1[kp][:, mt * 128:(mt + 1) * 128], GS[:, kp:kp + 1]) for kp in range(8)], col0=mt)
    for mt in range(4):
        ps_read(0, "scalar", lambda eng, _m=mt: eng.activation(GH1[:, _m:_m + 1], PS[0][:, _m:_m + 1], AF.Gelu, bias=GB1P[:, _m:_m + 1]))
    for mt in range(2):
        mm_group(1, 1, [(GW2[kp][:, mt * 128:(mt + 1) * 128], GH1[:, kp:kp + 1]) for kp in range(4)], col0=mt)
    for mt in range(2):
        ps_read(1, "scalar", lambda eng, _m=mt: eng.activation(GH2[:, _m:_m + 1], PS[1][:, _m:_m + 1], AF.Gelu, bias=GB2P[:, _m:_m + 1]))
    mm_group(2, 1, [(GW3[kp], GH2[:, kp:kp + 1]) for kp in range(2)], rows=1)
    ps_read(2, "scalar", lambda eng: eng.activation(GF07[:], PS[2][0:1, 0:1], AF.Sigmoid, bias=GB3S[:]))
    E("vector", lambda eng: eng.tensor_scalar_mul(GF07[:], GF07[:], 0.7))

    twist_map = [(4, 1.0), (5, 1.0), (6, -1.0), (7, -1.0), (0, -1.0), (1, -1.0), (2, 1.0), (3, 1.0)]
    for c in range(NCH):
        for dt in range(8):
            ld(OUTA[:, dt * 512:(dt + 1) * 512], XN[dt * 128:(dt + 1) * 128, c * 512:(c + 1) * 512])
        cur_out, oth_out = OUTA, OUTB
        for cyc in range(3):
            for m in range(8):
                mm_group(0, 512, [(CW1[kp][:, m * 128:(m + 1) * 128], cur_out[:, kp * 512:(kp + 1) * 512]) for kp in range(8)])
                ps_read(0, "scalar", lambda eng, _m=m: eng.activation(C1B[:, _m * 512:(_m + 1) * 512], PS[0][:, 0:512], AF.Gelu, bias=CB1P[:, _m:_m + 1]))
            for m in range(4):
                mm_group(1, 512, [(CW2[kp][:, m * 128:(m + 1) * 128], C1B[:, kp * 512:(kp + 1) * 512]) for kp in range(8)])
                ps_read(1, "scalar", lambda eng, _m=m: eng.activation(C2B[:, _m * 512:(_m + 1) * 512], PS[1][:, 0:512], AF.Gelu, bias=CB2P[:, _m:_m + 1]))
            for m in range(2):
                mm_group(2, 512, [(CW3[kp][:, m * 128:(m + 1) * 128], C2B[:, kp * 512:(kp + 1) * 512]) for kp in range(4)])
                ps_read(2, "scalar", lambda eng, _m=m: eng.activation(C3B[:, _m * 512:(_m + 1) * 512], PS[2][:, 0:512], AF.Gelu, bias=CB3P[:, _m:_m + 1]))
            mm_group(3, 512, [(CW4[kp], C3B[:, kp * 512:(kp + 1) * 512]) for kp in range(2)], rows=1)
            ps_read(3, "scalar", lambda eng: eng.activation(TFR, PS[3][0:1, 0:512], AF.Sigmoid, bias=CB4S[:]))
            E("vector", lambda eng: eng.tensor_scalar(TFR, TFR, 0.3, GF07[:], ALU.mult, ALU.add))
            E("vector", lambda eng: eng.tensor_scalar(TFR, TFR, 0.5, AR2[:], ALU.subtract, ALU.mult))
            E("vector", lambda eng: eng.tensor_scalar_add(TFR, TFR, 0.1))
            rank1(3, TFR, 512)
            ps_read(3, "scalar", lambda eng: eng.copy(CPB, PS[3][:, 0:512]))
            for dt in range(8):
                src, sign = twist_map[dt]
                E("vector", lambda eng, _s=src, _g=sign, _c=cur_out: eng.scalar_tensor_tensor(TMS_[:, 0:512], _c[:, _s * 512:(_s + 1) * 512], _g, CPB, ALU.mult, ALU.mult))
                E("vector", lambda eng, _d=dt, _c=cur_out, _o=oth_out: eng.tensor_add(_o[:, _d * 512:(_d + 1) * 512], _c[:, _d * 512:(_d + 1) * 512], TMS_[:, 0:512]))
            cur_out, oth_out = oth_out, cur_out
        for dt in range(8):
            E("scalar", lambda eng, _d=dt, _c=cur_out: eng.activation(_c[:, _d * 512:(_d + 1) * 512], _c[:, _d * 512:(_d + 1) * 512], AF.Silu))
            ld(XM_, XF[dt * 128:(dt + 1) * 128, c * 512:(c + 1) * 512])
            ld(SM2, SSd[dt * 128:(dt + 1) * 128, c * 512:(c + 1) * 512], eng="gpsimd")
            E("vector", lambda eng, _d=dt, _c=cur_out: eng.tensor_mul(TMS_[:, 0:512], _c[:, _d * 512:(_d + 1) * 512], SM2))
            E("vector", lambda eng, _d=dt: eng.tensor_add(X1FB[:, _d * 512:(_d + 1) * 512], XM_, TMS_[:, 0:512]))
            store_toks.append(st_(X1d[dt * 128:(dt + 1) * 128, c * 512:(c + 1) * 512], X1FB[:, dt * 512:(dt + 1) * 512]))
            if dt == 0:
                mm_group(4, 512, [(ONC[:], X1FB[:, dt * 512:(dt + 1) * 512])], rows=1)
            else:
                mm_acc(4, 512, [(ONC[:], X1FB[:, dt * 512:(dt + 1) * 512])], rows=1)
            E("scalar", lambda eng, _d=dt: eng.activation(X1SQ, X1FB[:, _d * 512:(_d + 1) * 512], AF.Square))
            if dt == 0:
                mm_group(5, 512, [(ONC[:], X1SQ)], rows=1)
            else:
                mm_acc(5, 512, [(ONC[:], X1SQ)], rows=1)
        ps_read(4, "vector", lambda eng: eng.tensor_scalar_mul(MROW, PS[4][0:1, 0:512], 1.0 / D))
        E("vector", lambda eng: eng.tensor_mul(VROW, MROW, MROW))
        ps_read(5, "vector", lambda eng: eng.scalar_tensor_tensor(VROW, PS[5][0:1, 0:512], 1.0 / D, VROW, ALU.mult, ALU.subtract))
        E("vector", lambda eng: eng.tensor_scalar_add(VROW, VROW, 1e-5))
        E("scalar", lambda eng: eng.activation(VROW, VROW, AF.Sqrt))
        E("vector", lambda eng: eng.reciprocal(VROW, VROW))
        rank1(4, MROW, 512)
        ps_read(4, "scalar", lambda eng: eng.copy(MB_, PS[4][:, 0:512]))
        rank1(5, VROW, 512)
        ps_read(5, "scalar", lambda eng: eng.copy(RB_, PS[5][:, 0:512]))
        for dt in range(8):
            E("vector", lambda eng, _d=dt: eng.tensor_sub(X1NB[:, _d * 512:(_d + 1) * 512], X1FB[:, _d * 512:(_d + 1) * 512], MB_))
            E("vector", lambda eng, _d=dt: eng.tensor_mul(X1NB[:, _d * 512:(_d + 1) * 512], X1NB[:, _d * 512:(_d + 1) * 512], RB_))
            store_toks.append(st_(X1Nd[dt * 128:(dt + 1) * 128, c * 512:(c + 1) * 512], X1NB[:, dt * 512:(dt + 1) * 512]))
        for s in range(4):
            for dt in range(8):
                transpose(6, X1NB[:, dt * 512 + s * 128: dt * 512 + (s + 1) * 128], TMS_[:, dt * 128:(dt + 1) * 128], 128, 128)
            store_toks.append(st_(X1NTd[(c * 4 + s) * 128:(c * 4 + s + 1) * 128, :], TMS_[:, 0:1024]))
    barrier()

    arena_reset()
    XH = alloc(16384); YH = alloc(16384)
    PIN2 = alloc(1024); PPO2 = alloc(1024); ZW = alloc(1024); TW0 = alloc(512); TW1 = alloc(512)
    LNROW = alloc(512); CBROW = alloc(512)
    CRY = ROWS[0:1, 0:512]; ABR = ROWS[0:1, 512:1024]; ABC = ROWS[0:1, 1024:1536]; TRW = ROWS[0:1, 1536:2048]
    for hf in range(2):
        for u in range(NSUB):
            ld(XH[:, u * 512:(u + 1) * 512], X1NTd[u * 128:(u + 1) * 128, hf * 512:(hf + 1) * 512])
        for i in range(4):
            E("gpsimd", lambda eng, _i=i: eng.memset(YH[:, _i * 4096:(_i + 1) * 4096], 0.0))
        for n in range(NS):
            lnrow = LNROW[0:1, 0:512]
            cbrow = CBROW[0:1, 0:512]
            ld(lnrow, LNABd[n:n + 1, hf * 512:(hf + 1) * 512])
            ld(cbrow, CBNDd[n:n + 1, hf * 512:(hf + 1) * 512])
            rank1(6, lnrow, 512)
            ps_read(6, "scalar", lambda eng: eng.activation(PIN2[:, 0:512], PS[6][:, 0:512], AF.Exp, scale=NIOP[:]))
            E("scalar", lambda eng: eng.copy(PIN2[:, 512:1024], PIN2[:, 0:512]))
            ps_read(6, "scalar", lambda eng: eng.activation(PPO2[:, 0:512], PS[6][:, 0:512], AF.Exp, scale=IOP[:]))
            rank1(7, cbrow, 512)
            ps_read(7, "vector", lambda eng: eng.tensor_mul(PPO2[:, 0:512], PPO2[:, 0:512], PS[7][:, 0:512]))
            E("vector", lambda eng: eng.tensor_copy(PPO2[:, 512:1024], PPO2[:, 0:512]))
            E("scalar", lambda eng: eng.activation(ABR, lnrow, AF.Exp))
            E("vector", lambda eng: eng.tensor_scalar_mul(ABC, lnrow, 127.0))
            E("scalar", lambda eng: eng.activation(ABC, ABC, AF.Exp))
            E("vector", lambda eng: eng.memset(CRY, 0.0))
            for p in range(16):
                c0, c1 = 2 * p, 2 * p + 1
                E("vector", lambda eng, _p=p: eng.tensor_mul(ZW, PIN2, XH[:, _p * 1024:(_p + 1) * 1024]))
                E("vector", lambda eng: eng.tensor_mul(TRW, CRY, ABR))
                E("vector", lambda eng: eng.tensor_add(ZW[0:1, 0:512], ZW[0:1, 0:512], TRW))
                b0 = 0 if p % 2 == 0 else 2
                b1 = 1 if p % 2 == 0 else 3
                mm_group(b0, 512, [(UPI[:], ZW[:, 0:512])])
                mm_group(4, 512, [(ONC[:], ZW[:, 0:512])], rows=1)
                ps_read(4, "vector", lambda eng: eng.tensor_mul(CRY, PS[4][0:1, 0:512], ABC))
                E("vector", lambda eng: eng.tensor_mul(TRW, CRY, ABR))
                E("vector", lambda eng: eng.tensor_add(ZW[0:1, 512:1024], ZW[0:1, 512:1024], TRW))
                mm_group(b1, 512, [(UPI[:], ZW[:, 512:1024])])
                mm_group(5, 512, [(ONC[:], ZW[:, 512:1024])], rows=1)
                ps_read(5, "vector", lambda eng: eng.tensor_mul(CRY, PS[5][0:1, 0:512], ABC))
                ps_read(b0, "vector", lambda eng, _b=b0: eng.tensor_mul(TW0, PS[_b][:, 0:512], PPO2[:, 0:512]))
                E("vector", lambda eng, _c=c0: eng.tensor_add(YH[:, _c * 512:(_c + 1) * 512], YH[:, _c * 512:(_c + 1) * 512], TW0))
                ps_read(b1, "vector", lambda eng, _b=b1: eng.tensor_mul(TW1, PS[_b][:, 0:512], PPO2[:, 512:1024]))
                E("vector", lambda eng, _c=c1: eng.tensor_add(YH[:, _c * 512:(_c + 1) * 512], YH[:, _c * 512:(_c + 1) * 512], TW1))
        for u in range(NSUB):
            store_toks.append(st_(YSSMd[u * 128:(u + 1) * 128, hf * 512:(hf + 1) * 512], YH[:, u * 512:(u + 1) * 512]))
    barrier()

    arena_reset()
    XSC = alloc(8192); YSC = alloc(8192); HIDH = alloc(4096)
    WDH = [alloc(1024) for _ in range(4)]
    WGT = alloc(1024); WUT = alloc(1024)
    GTS = alloc(8); MK1 = alloc(8); MK2 = alloc(8); SMB = alloc(8)
    GBC = alloc(512); TW0b = alloc(512); TW1b = alloc(512); GROW = alloc(512)
    GATR = SMALL[0:8, 0:1024]; SCF = SMALL[0:8, 1024:1536]
    WGA = [SMALL[:, 1544 + 8 * kp: 1552 + 8 * kp] for kp in range(8)]
    M1 = STAT[:, 4:5]; M2 = STAT[:, 5:6]; P1_ = STAT[:, 6:7]; P2_ = STAT[:, 7:8]
    for kp in range(8):
        ld(WGA[kp], wgate_d[kp * 128:(kp + 1) * 128, :])
    mgroups = [(0, 4), (4, 4), (8, 3)]
    for sc in range(4):
        for kp in range(8):
            ld(XSC[:, kp * 1024:(kp + 1) * 1024], X1Nd[kp * 128:(kp + 1) * 128, sc * 1024:(sc + 1) * 1024])
        E("gpsimd", lambda eng: eng.memset(YSC, 0.0))
        for tc in range(2):
            mm_group(0, 512, [(WGA[kp], XSC[:, kp * 1024 + tc * 512: kp * 1024 + (tc + 1) * 512]) for kp in range(8)], rows=8)
            ps_read(0, "scalar", lambda eng: eng.copy(SCF, PS[0][0:8, 0:512]))
            for g in range(4):
                transpose(1, SCF[:, g * 128:(g + 1) * 128], GTS, 8, 128)
                E("vector", lambda eng: eng.reduce_max(M1, GTS, AX.X))
                E("vector", lambda eng: eng.tensor_single_scalar(MK1, GTS, M1, ALU.is_equal))
                E("vector", lambda eng: eng.scalar_tensor_tensor(SMB, MK1, -1e9, GTS, ALU.mult, ALU.add))
                E("vector", lambda eng: eng.reduce_max(M2, SMB, AX.X))
                E("vector", lambda eng: eng.tensor_single_scalar(MK2, SMB, M2, ALU.is_equal))
                E("vector", lambda eng: eng.tensor_sub(P1_, M1, M2))
                E("scalar", lambda eng: eng.activation(P1_, P1_, AF.Sigmoid))
                E("vector", lambda eng: eng.tensor_scalar(P2_, P1_, -1.0, 1.0, ALU.mult, ALU.add))
                E("vector", lambda eng: eng.tensor_scalar(MK1, MK1, P1_, None, ALU.mult))
                E("vector", lambda eng: eng.tensor_scalar(MK2, MK2, P2_, None, ALU.mult))
                E("vector", lambda eng: eng.tensor_add(GTS, MK1, MK2))
                transpose(2, GTS, GATR[:, tc * 512 + g * 128: tc * 512 + (g + 1) * 128], 128, 8)
        store_toks.append(st_(GATEd[:, sc * 1024:(sc + 1) * 1024], GATR[:, 0:1024]))
        for e in range(ME):
            for (m0, nmt) in mgroups:
                for mi in range(nmt):
                    mt = m0 + mi
                    for kp in range(8):
                        ld(WGT[:, kp * 128:(kp + 1) * 128], wgE_d[e, mt, kp, :, :])
                        ld(WUT[:, kp * 128:(kp + 1) * 128], wuE_d[e, mt, kp, :, :], eng="gpsimd")
                    for tc in range(2):
                        mm_group(3, 512, [(WGT[:, kp * 128:(kp + 1) * 128], XSC[:, kp * 1024 + tc * 512: kp * 1024 + (tc + 1) * 512]) for kp in range(8)])
                        mm_group(4, 512, [(WUT[:, kp * 128:(kp + 1) * 128], XSC[:, kp * 1024 + tc * 512: kp * 1024 + (tc + 1) * 512]) for kp in range(8)])
                        ps_read(3, "scalar", lambda eng: eng.activation(TW0b, PS[3][:, 0:512], AF.Silu))
                        ps_read(4, "vector", lambda eng, _mi=mi, _tc=tc: eng.tensor_mul(HIDH[:, _mi * 1024 + _tc * 512: _mi * 1024 + (_tc + 1) * 512], TW0b, PS[4][:, 0:512]))
                for mi in range(nmt):
                    ld(WDH[mi], wdE_d[e, (m0 + mi) * 128:(m0 + mi + 1) * 128, :])
                for tc in range(2):
                    ld(GROW[0:1, 0:512], GATEd[e:e + 1, sc * 1024 + tc * 512: sc * 1024 + (tc + 1) * 512])
                    rank1(5, GROW[0:1, 0:512], 512)
                    ps_read(5, "scalar", lambda eng: eng.copy(GBC, PS[5][:, 0:512]))
                    for m in range(8):
                        bnk = 6 if m % 2 == 0 else 7
                        mm_group(bnk, 512, [(WDH[mi][:, m * 128:(m + 1) * 128], HIDH[:, mi * 1024 + tc * 512: mi * 1024 + (tc + 1) * 512]) for mi in range(nmt)])
                        ps_read(bnk, "vector", lambda eng, _b=bnk: eng.tensor_mul(TW1b, PS[_b][:, 0:512], GBC))
                        E("vector", lambda eng, _m=m, _tc=tc: eng.tensor_add(YSC[:, _m * 1024 + _tc * 512: _m * 1024 + (_tc + 1) * 512],
                                                                             YSC[:, _m * 1024 + _tc * 512: _m * 1024 + (_tc + 1) * 512], TW1b))
        for m in range(8):
            store_toks.append(st_(YMOEd[m * 128:(m + 1) * 128, sc * 1024:(sc + 1) * 1024], YSC[:, m * 1024:(m + 1) * 1024]))
    barrier()

    arena_reset()
    YST = alloc(1024); YSF = alloc(4096); XOB = alloc(4096); OTM = alloc(1024)
    X1M_ = alloc(512); X1NM_ = alloc(512); YMM = alloc(512); G2M = alloc(512); S2M = alloc(512)
    for c in range(NCH):
        for s in range(4):
            ld(YST, YSSMd[(c * 4 + s) * 128:(c * 4 + s + 1) * 128, :])
            for dt in range(8):
                transpose(0, YST[:, dt * 128:(dt + 1) * 128], YSF[:, dt * 512 + s * 128: dt * 512 + (s + 1) * 128], 128, 128)
        for dt in range(8):
            ld(X1M_, X1d[dt * 128:(dt + 1) * 128, c * 512:(c + 1) * 512])
            ld(X1NM_, X1Nd[dt * 128:(dt + 1) * 128, c * 512:(c + 1) * 512])
            ld(YMM, YMOEd[dt * 128:(dt + 1) * 128, c * 512:(c + 1) * 512], eng="gpsimd")
            E("scalar", lambda eng: eng.activation(G2M, YMM, AF.Silu))
            E("vector", lambda eng, _d=dt: eng.tensor_scalar(S2M, X1NM_, DPP[:, _d:_d + 1], None, ALU.mult))
            E("vector", lambda eng, _d=dt: eng.tensor_add(S2M, S2M, YSF[:, _d * 512:(_d + 1) * 512]))
            E("vector", lambda eng: eng.tensor_mul(S2M, S2M, G2M))
            E("vector", lambda eng, _d=dt: eng.tensor_add(XOB[:, _d * 512:(_d + 1) * 512], X1M_, S2M))
        for s in range(4):
            for dt in range(8):
                transpose(1, XOB[:, dt * 512 + s * 128: dt * 512 + (s + 1) * 128], OTM[:, dt * 128:(dt + 1) * 128], 128, 128)
            store_toks.append(st_(out_d[(c * 4 + s) * 128:(c * 4 + s + 1) * 128, :], OTM))

    if debug:
        dbg = {"XN": XN, "HF": HFd, "QF": QFd, "KF": KFd, "KTM": KTMd, "VE": VEd,
               "Y1": Y1d, "H2": H2d, "SS": SSd, "X1": X1d, "X1N": X1Nd,
               "YSSM": YSSMd, "YMOE": YMOEd, "BETAo": BETA, "GATEo": GATEd}
        for nm, t in dbg.items():
            o = nc.dram_tensor("dbg_" + nm, list(t.tensor.shape), F32, kind="ExternalOutput").ap()
            store_toks.append(st_(o[:], t[:]))
    S.emit("sync", None, tuple(store_toks))
    S.build(nc, maxseq=maxseq)
    return nc, S


_nc_cache = [None]


def kernel(**inputs):
    def f(name):
        return np.ascontiguousarray(np.asarray(inputs[name], dtype=np.float32))

    x_full = f("L0_attn_x")
    wgE = f("L1_ffn_w_g").reshape(ME, 8, 128, MMT, 128).transpose(0, 3, 1, 2, 4).copy()
    wuE = f("L1_ffn_w_u").reshape(ME, 8, 128, MMT, 128).transpose(0, 3, 1, 2, 4).copy()

    iotaf = np.broadcast_to(np.arange(128, dtype=np.float32)[None, :], (128, 128)).copy()
    jj, tt = np.meshgrid(np.arange(128), np.arange(128), indexing="ij")
    upinc = (jj <= tt).astype(np.float32)
    upneg = -(jj < tt).astype(np.float32)

    common = {
        "n1w": f("L0_attn_norm1_w").reshape(1, D),
        "n2w": f("L0_attn_norm2_w").reshape(1, D),
        "wq": f("L0_attn_Wq"), "wk": f("L0_attn_Wk"), "wv": f("L0_attn_Wv"), "wo": f("L0_attn_Wo"),
        "bw": f("L0_attn_beta_w").reshape(1, D),
        "bb": f("L0_attn_beta_b").reshape(1, 1),
        "alog": f("L0_attn_alpha_logit").reshape(1, D),
        "wg0": f("L0_attn_w_gate"), "wu0": f("L0_attn_w_up"), "wd0": f("L0_attn_w_down"),
        "cw1": f("L0_ffn_cn_w1"), "cw2": f("L0_ffn_cn_w2"), "cw3": f("L0_ffn_cn_w3"), "cw4": f("L0_ffn_cn_w4"),
        "cb1": f("L0_ffn_cn_b1").reshape(1, D), "cb2": f("L0_ffn_cn_b2").reshape(1, 512),
        "cb3": f("L0_ffn_cn_b3").reshape(1, 256), "cb4": f("L0_ffn_cn_b4").reshape(1, 1),
        "gw1": f("L0_ffn_gc_w1"), "gw2": f("L0_ffn_gc_w2"), "gw3": f("L0_ffn_gc_w3"),
        "gb1": f("L0_ffn_gc_b1").reshape(1, 512), "gb2": f("L0_ffn_gc_b2").reshape(1, 256),
        "gb3": f("L0_ffn_gc_b3").reshape(1, 1),
        "ar": f("L0_ffn_adaptive_range").reshape(1, 1),
        "logA": f("L1_attn_log_A"), "Bp": f("L1_attn_B_param"), "Cp": f("L1_attn_C"),
        "Dv": f("L1_attn_D").reshape(1, D),
        "wgate": f("L1_ffn_w_gate"),
        "wgE": wgE, "wuE": wuE, "wdE": f("L1_ffn_w_d"),
        "idn": np.eye(128, dtype=np.float32),
        "upinc": upinc, "upneg": upneg, "iotaf": iotaf,
        "onesr": np.ones((1, 128), np.float32), "onesc": np.ones((128, 1), np.float32),
        "niop": -np.arange(128, dtype=np.float32).reshape(128, 1),
        "iop": np.arange(128, dtype=np.float32).reshape(128, 1),
    }
    in_maps = [dict(common, x=np.ascontiguousarray(x_full[b])) for b in range(8)]
    if _nc_cache[0] is None:
        _nc_cache[0] = build_program()[0]
    nc = _nc_cache[0]
    res = run_bass_kernel_spmd(nc, in_maps, core_ids=list(range(8)))
    return np.stack([r["out"] for r in res.results], axis=0)


# version 05
# speedup vs baseline: 1.0643x; 1.0643x over previous
import numpy as np
import concourse.bass as bass
import concourse.mybir as mybir
from concourse.bass_utils import run_bass_kernel_spmd

F32 = mybir.dt.float32
AF = mybir.ActivationFunctionType
ALU = mybir.AluOpType
AX = mybir.AxisListType

T = 4096
D = 1024
NS = 64
HID0 = 2048
ME = 8
MM = 1408
MMT = 11
NSUB = T // 128
NCH = T // 512

ENGINES = ("sync", "tensor", "vector", "scalar", "gpsimd")
AUTO_FENCE = True


class Sched:
    def __init__(self):
        self.q = {e: [] for e in ENGINES}
        self.gseq = 0
        self.seqs = {e: [] for e in ENGINES}
        self.marks = []

    def emit(self, eng, fn, deps=(), dma=False):
        lst = self.q[eng]
        tok = (eng, len(lst))
        deps = tuple(d for d in deps if d is not None)
        lst.append([fn, deps, dma, False])
        self.seqs[eng].append(self.gseq)
        self.gseq += 1
        return tok

    def build(self, nc, maxseq=None):
        if maxseq is not None:
            for e in ENGINES:
                keep = sum(1 for s in self.seqs[e] if s < maxseq)
                self.q[e] = self.q[e][:keep]
        for e in ENGINES:
            for rec in self.q[e]:
                for (de, di) in rec[1]:
                    self.q[de][di][3] = True
        vals = {}
        for e in ENGINES:
            c = 0
            for i, rec in enumerate(self.q[e]):
                if rec[3] and rec[0] is not None:
                    c += 16 if rec[2] else 1
                vals[(e, i)] = c
        import contextlib
        stack = contextlib.ExitStack()
        sems = {e: stack.enter_context(nc.semaphore(f"sem_{e}")) for e in ENGINES}
        with nc.Block() as block:
            for e in ENGINES:
                def body(eng, _e=e):
                    hw = {o: 0 for o in ENGINES}
                    for i, (fn, deps, dma, needed) in enumerate(self.q[_e]):
                        for (de, di) in deps:
                            v = vals[(de, di)]
                            if v > hw[de]:
                                eng.wait_ge(sems[de], v)
                                hw[de] = v
                        if fn is not None:
                            inst = fn(eng)
                            if needed:
                                inst.then_inc(sems[_e], 16 if dma else 1)
                getattr(block, e)(body)
        stack.close()


def build_program(maxseq=None, debug=False):
    nc = bass.Bass("TRN2", target_bir_lowering=False)
    S = Sched()

    def din(name, shape):
        return nc.dram_tensor(name, shape, F32, kind="ExternalInput").ap()

    def dscr(name, shape):
        return nc.dram_tensor(name, shape, F32, kind="Internal").ap()

    x_d = din("x", [T, D])
    n1w_d = din("n1w", [1, D]); n2w_d = din("n2w", [1, D])
    wq_d = din("wq", [D, D]); wk_d = din("wk", [D, D]); wv_d = din("wv", [D, D]); wo_d = din("wo", [D, D])
    bw_d = din("bw", [1, D]); bb_d = din("bb", [1, 1]); alog_d = din("alog", [1, D])
    wg0_d = din("wg0", [D, HID0]); wu0_d = din("wu0", [D, HID0]); wd0_d = din("wd0", [HID0, D])
    cw1_d = din("cw1", [D, D]); cw2_d = din("cw2", [D, 512]); cw3_d = din("cw3", [512, 256]); cw4_d = din("cw4", [256, 1])
    cb1_d = din("cb1", [1, D]); cb2_d = din("cb2", [1, 512]); cb3_d = din("cb3", [1, 256]); cb4_d = din("cb4", [1, 1])
    gw1_d = din("gw1", [D, 512]); gw2_d = din("gw2", [512, 256]); gw3_d = din("gw3", [256, 1])
    gb1_d = din("gb1", [1, 512]); gb2_d = din("gb2", [1, 256]); gb3_d = din("gb3", [1, 1])
    ar_d = din("ar", [1, 1])
    logA_d = din("logA", [D, NS]); Bp_d = din("Bp", [D, NS]); Cp_d = din("Cp", [D, NS]); Dv_d = din("Dv", [1, D])
    wgate_d = din("wgate", [D, ME])
    wgE_d = din("wgE", [ME, MMT, 8, 128, 128])
    wuE_d = din("wuE", [ME, MMT, 8, 128, 128])
    wdE_d = din("wdE", [ME, MM, D])
    id_d = din("idn", [128, 128]); upinc_d = din("upinc", [128, 128]); upneg_d = din("upneg", [128, 128])
    iotaf_d = din("iotaf", [128, 128]); onesr_d = din("onesr", [1, 128]); onesc_d = din("onesc", [128, 1])
    niop_d = din("niop", [128, 1]); iop_d = din("iop", [128, 1])

    out_d = nc.dram_tensor("out", [T, D], F32, kind="ExternalOutput").ap()

    XN = dscr("XN", [D, T]); XF = dscr("XF", [D, T]); HFd = dscr("HFd", [D, T])
    BETA = dscr("BETAd", [T, 1])
    QFd = dscr("QFd", [D, T]); KFd = dscr("KFd", [D, T]); KTMd = dscr("KTMd", [T, D]); VEd = dscr("VEd", [T, D])
    Y1d = dscr("Y1d", [D, T]); H2d = dscr("H2d", [D, T]); HIDd = dscr("HIDd", [HID0, T])
    SSd = dscr("SSd", [D, T]); X1d = dscr("X1d", [D, T]); X1Nd = dscr("X1Nd", [D, T]); X1NTd = dscr("X1NTd", [T, D])
    YSSMd = dscr("YSSMd", [T, D]); YMOEd = dscr("YMOEd", [D, T]); GATEd = dscr("GATEd", [8, T])

    def sb(name, shape):
        return nc.alloc_sbuf_tensor(name, shape, F32).ap()

    IDN = sb("IDN", [128, 128]); UPI = sb("UPI", [128, 128]); UPN = sb("UPN", [128, 128])
    IOF = sb("IOF", [128, 128]); ONR = sb("ONR", [1, 128]); ONC = sb("ONC", [128, 1])
    NIOP = sb("NIOP", [128, 1]); IOP = sb("IOP", [128, 1])
    W1B = sb("W1B", [128, D]); BWB = sb("BWB", [128, D]); BB128 = sb("BB128", [128, 1])
    PINV_E = sb("PINV_E", [128, D]); PFWD_E = sb("PFWD_E", [128, D])
    OMAR = sb("OMAR", [1, D]); AROWV = sb("AROWV", [1, D])
    OMA127 = AROWV
    GS = sb("GS", [128, 8]); N2W = sb("N2W", [128, 8]); DPP = sb("DPP", [128, 8])
    CB1P = sb("CB1P", [128, 8]); CB2P = sb("CB2P", [128, 4]); CB3P = sb("CB3P", [128, 2])
    GB1P = sb("GB1P", [128, 4]); GB2P = sb("GB2P", [128, 2])
    CB4S = sb("CB4S", [1, 1]); GB3S = sb("GB3S", [1, 1]); AR2 = sb("AR2", [1, 1]); GF07 = sb("GF07", [1, 1])
    LNAB = sb("LNAB", [64, D]); CBND = sb("CBND", [64, D]); CPND = sb("CPND", [64, D])
    SMALL = sb("SMALLb", [128, 2048]); ROWS = sb("ROWSb", [1, 2048]); STAT = sb("STATb", [128, 16])
    PA = sb("PAb", [128, 39100])

    PS = [nc.alloc_psum_tensor(f"pb{i}", [128, 512], F32).ap() for i in range(8)]
    ps_writer = [None] * 8
    ps_reader = [None] * 8

    cur = [0]
    def arena_reset():
        cur[0] = 0
    def alloc(n):
        off = cur[0]
        assert off + n <= 39100, f"arena overflow {off+n}"
        cur[0] += n
        return PA[:, off:off + n]

    tok_last = {e: None for e in ENGINES}
    RELAX = [False]
    fence_base = [()]

    def cfence():
        return tuple(t for t in tok_last.values() if t is not None)

    def soft_fence():
        fence_base[0] = cfence()

    def E(eng, fn, deps=(), dma=False, fence=None):
        if fence is None:
            fence = not RELAX[0]
        if fence:
            d = tuple(deps) + cfence()
        else:
            d = tuple(deps) + fence_base[0]
            if tok_last[eng] is not None:
                d = d + (tok_last[eng],)
        t = S.emit(eng, fn, d, dma)
        tok_last[eng] = t
        return t

    def barrier():
        toks = cfence()
        for e in ENGINES:
            tok_last[e] = S.emit(e, None, toks)
        S.marks.append(S.gseq)

    def mm_group(bank, ncols, pairs, deps=(), col0=0, rows=128):
        dl = list(deps)
        if ps_reader[bank] is not None:
            dl.append(ps_reader[bank])
        t = None
        n = len(pairs)
        for i, (lhsT, rhs) in enumerate(pairs):
            def fn(eng, _l=lhsT, _r=rhs, _s=(i == 0), _e=(i == n - 1)):
                return eng.matmul(PS[bank][0:rows, col0:col0 + ncols], _l, _r,
                                  start=_s, stop=_e, skip_group_check=True)
            t = E("tensor", fn, tuple(dl) if i == 0 else (), fence=(None if i == 0 else False))
        ps_writer[bank] = t
        return t

    def mm_acc(bank, ncols, pairs, deps=(), col0=0, rows=128):
        t = None
        for i, (lhsT, rhs) in enumerate(pairs):
            def fn(eng, _l=lhsT, _r=rhs):
                return eng.matmul(PS[bank][0:rows, col0:col0 + ncols], _l, _r,
                                  start=False, stop=True, skip_group_check=True)
            t = E("tensor", fn, tuple(deps) if i == 0 else (), fence=(None if i == 0 else False))
        ps_writer[bank] = t
        return t

    def ps_read(bank, eng, fn, deps=()):
        t = E(eng, fn, tuple(deps) + ((ps_writer[bank],) if ps_writer[bank] else ()))
        ps_reader[bank] = t
        return t

    def transpose(bank, src_ap, dst_ap, p, c, deps=()):
        dl = list(deps)
        if ps_reader[bank] is not None:
            dl.append(ps_reader[bank])
        tt = E("tensor", lambda eng, _s=src_ap, _p=p, _c=c: eng.transpose(
            PS[bank][0:_c, 0:_p], _s, IDN[0:_p, 0:_p]), tuple(dl))
        ps_writer[bank] = tt
        return ps_read(bank, "scalar", lambda eng, _d=dst_ap, _p=p, _c=c: eng.copy(_d, PS[bank][0:_c, 0:_p]))

    def rank1(bank, row_ap, ncols, deps=()):
        dl = list(deps)
        if ps_reader[bank] is not None:
            dl.append(ps_reader[bank])
        t = E("tensor", lambda eng, _r=row_ap, _n=ncols: eng.matmul(
            PS[bank][:, 0:_n], ONR[:], _r, start=True, stop=True, skip_group_check=True), tuple(dl))
        ps_writer[bank] = t
        return t

    def ld(dst, src, deps=(), eng="sync"):
        return E(eng, lambda e, _d=dst, _s=src: e.dma_start(_d, _s), deps, dma=True)

    def st_(dst, src, deps=(), eng="gpsimd"):
        return E(eng, lambda e, _d=dst, _s=src: e.dma_start(_d, _s), deps, dma=True)

    store_toks = []

    ld(IDN[:], id_d[:]); ld(UPI[:], upinc_d[:]); ld(UPN[:], upneg_d[:])
    ld(IOF[:], iotaf_d[:]); ld(ONR[:], onesr_d[:]); ld(ONC[:], onesc_d[:])
    ld(NIOP[:], niop_d[:]); ld(IOP[:], iop_d[:])

    def row_bcast(dst, src_row_ap, width):
        for h in range((width + 511) // 512):
            w = min(512, width - h * 512)
            rank1(0, src_row_ap[:, h * 512:h * 512 + w], w)
            ps_read(0, "scalar", lambda eng, _d=dst, _h=h, _w=w: eng.copy(_d[:, _h * 512:_h * 512 + _w], PS[0][:, 0:_w]))

    t = ld(ROWS[0:1, 0:D], n1w_d[:])
    row_bcast(W1B, ROWS[0:1, 0:D], D)
    ld(ROWS[0:1, 0:D], bw_d[:])
    row_bcast(BWB, ROWS[0:1, 0:D], D)
    ld(ROWS[0:1, 1024:1025], bb_d[:])
    rank1(0, ROWS[0:1, 1024:1025], 1)
    ps_read(0, "scalar", lambda eng: eng.copy(BB128[:], PS[0][:, 0:1]))

    ld(ROWS[0:1, 0:D], alog_d[:])
    E("scalar", lambda eng: eng.activation(AROWV[:], ROWS[0:1, 0:D], AF.Sigmoid))
    E("vector", lambda eng: eng.tensor_scalar(OMAR[:], AROWV[:], -1.0, 1.0, ALU.mult, ALU.add))
    lnoma = ROWS[0:1, 1024:1024 + D]
    E("scalar", lambda eng: eng.activation(lnoma, OMAR[:], AF.Ln))
    A_PP = SMALL[:, 0:8]; LOM_PP = SMALL[:, 8:16]; NLOM_PP = SMALL[:, 16:24]
    STG = SMALL[:, 24:152]
    for dt in range(8):
        transpose(1, AROWV[:, dt * 128:(dt + 1) * 128], A_PP[:, dt:dt + 1], 1, 128)
    for dt in range(8):
        transpose(1, lnoma[:, dt * 128:(dt + 1) * 128], LOM_PP[:, dt:dt + 1], 1, 128)
    E("vector", lambda eng: eng.tensor_scalar_mul(NLOM_PP[:], LOM_PP[:], -1.0))
    E("vector", lambda eng: eng.tensor_scalar_mul(OMA127[:], lnoma, 127.0))
    E("scalar", lambda eng: eng.activation(OMA127[:], OMA127[:], AF.Exp))
    for dt in range(8):
        E("scalar", lambda eng, _dt=dt: eng.activation(STG, IOF[:], AF.Exp, scale=NLOM_PP[:, _dt:_dt + 1]))
        E("vector", lambda eng, _dt=dt: eng.tensor_scalar(STG, STG, A_PP[:, _dt:_dt + 1], None, ALU.mult))
        transpose(1, STG, PINV_E[:, dt * 128:(dt + 1) * 128], 128, 128)
        E("scalar", lambda eng, _dt=dt: eng.activation(STG, IOF[:], AF.Exp, scale=LOM_PP[:, _dt:_dt + 1]))
        transpose(1, STG, PFWD_E[:, dt * 128:(dt + 1) * 128], 128, 128)

    def row_to_pp(dst, src_d, width):
        ld(ROWS[0:1, 0:width], src_d[:])
        for dt in range(width // 128):
            transpose(1, ROWS[0:1, dt * 128:(dt + 1) * 128], dst[:, dt:dt + 1], 1, 128)

    row_to_pp(N2W, n2w_d, D)
    row_to_pp(DPP, Dv_d, D)
    row_to_pp(CB1P, cb1_d, D)
    row_to_pp(CB2P, cb2_d, 512)
    row_to_pp(CB3P, cb3_d, 256)
    row_to_pp(GB1P, gb1_d, 512)
    row_to_pp(GB2P, gb2_d, 256)
    ld(CB4S[:], cb4_d[:]); ld(GB3S[:], gb3_d[:])
    ld(ROWS[0:1, 1024:1025], ar_d[:])
    E("vector", lambda eng: eng.tensor_scalar_mul(AR2[:], ROWS[0:1, 1024:1025], 2.0))

    LA_T = SMALL[:, 152:216]
    for (src_d, dst) in ((logA_d, LNAB), (Bp_d, CBND), (Cp_d, CPND)):
        for dt in range(8):
            ld(LA_T, src_d[dt * 128:(dt + 1) * 128, :])
            transpose(1, LA_T, dst[0:64, dt * 128:(dt + 1) * 128], 128, 64)
    arena_reset()
    TMPA = alloc(1024)[0:64, :]
    RECA = alloc(1024)[0:64, :]
    E("scalar", lambda eng: eng.activation(TMPA, LNAB[0:64, :], AF.Exp))
    E("vector", lambda eng: eng.tensor_scalar_mul(TMPA, TMPA, -1.0))
    E("vector", lambda eng: eng.reciprocal(RECA, TMPA))
    E("vector", lambda eng: eng.tensor_scalar_mul(LNAB[0:64, :], TMPA, 0.1))
    E("scalar", lambda eng: eng.activation(TMPA, LNAB[0:64, :], AF.Exp))
    E("vector", lambda eng: eng.tensor_scalar(TMPA, TMPA, 1.0, None, ALU.subtract))
    E("vector", lambda eng: eng.tensor_mul(TMPA, TMPA, RECA))
    E("vector", lambda eng: eng.tensor_mul(CBND[0:64, :], CBND[0:64, :], TMPA))
    E("vector", lambda eng: eng.tensor_mul(CBND[0:64, :], CBND[0:64, :], CPND[0:64, :]))
    LNABd = dscr("LNABd", [64, D]); CBNDd = dscr("CBNDd", [64, D])
    store_toks.append(st_(LNABd[:], LNAB[0:64, :]))
    store_toks.append(st_(CBNDd[:], CBND[0:64, :]))
    E("vector", lambda eng: eng.memset(GS[:], 0.0))
    barrier()

    arena_reset()
    XT = alloc(1024); XSQ = alloc(1024); XNT = alloc(1024); HT = alloc(1024)
    FMB = alloc(1024); FMB2 = alloc(1024)
    s1 = STAT[:, 0:1]; s2 = STAT[:, 1:2]; s3 = STAT[:, 2:3]; s4 = STAT[:, 3:4]
    for u in range(NSUB):
        ld(XT, x_d[u * 128:(u + 1) * 128, :])
        E("scalar", lambda eng: eng.activation(XSQ, XT, AF.Square))
        E("vector", lambda eng: eng.reduce_sum(s1, XT, AX.X))
        E("vector", lambda eng: eng.reduce_sum(s2, XSQ, AX.X))
        E("vector", lambda eng: eng.tensor_scalar_mul(s1, s1, 1.0 / D))
        E("vector", lambda eng: eng.tensor_mul(s3, s1, s1))
        E("vector", lambda eng: eng.scalar_tensor_tensor(s2, s2, 1.0 / D, s3, ALU.mult, ALU.subtract))
        E("vector", lambda eng: eng.tensor_scalar_add(s2, s2, 1e-5))
        E("scalar", lambda eng: eng.activation(s2, s2, AF.Sqrt))
        E("vector", lambda eng: eng.reciprocal(s2, s2))
        E("vector", lambda eng: eng.tensor_scalar(XNT, XT, s1, s2, ALU.subtract, ALU.mult))
        E("scalar", lambda eng: eng.activation(XSQ, XNT, AF.Square))
        E("vector", lambda eng: eng.reduce_sum(s3, XSQ, AX.X))
        E("vector", lambda eng: eng.tensor_scalar(s3, s3, 1.0 / D, 1e-6, ALU.mult, ALU.add))
        E("scalar", lambda eng: eng.activation(s3, s3, AF.Sqrt))
        E("vector", lambda eng: eng.reciprocal(s3, s3))
        E("vector", lambda eng: eng.tensor_scalar(HT, XNT, s3, None, ALU.mult))
        E("vector", lambda eng: eng.tensor_mul(HT, HT, W1B[:]))
        E("vector", lambda eng: eng.tensor_mul(XSQ, HT, BWB[:]))
        E("vector", lambda eng: eng.reduce_sum(s4, XSQ, AX.X))
        E("scalar", lambda eng: eng.activation(s4, s4, AF.Sigmoid, bias=BB128[:]))
        store_toks.append(st_(BETA[u * 128:(u + 1) * 128, :], s4))
        for dt in range(8):
            transpose(2, XNT[:, dt * 128:(dt + 1) * 128], FMB[:, dt * 128:(dt + 1) * 128], 128, 128)
            transpose(3, HT[:, dt * 128:(dt + 1) * 128], FMB2[:, dt * 128:(dt + 1) * 128], 128, 128)
            store_toks.append(st_(XN[dt * 128:(dt + 1) * 128, u * 128:(u + 1) * 128], FMB[:, dt * 128:(dt + 1) * 128]))
            store_toks.append(st_(HFd[dt * 128:(dt + 1) * 128, u * 128:(u + 1) * 128], FMB2[:, dt * 128:(dt + 1) * 128]))
            E("vector", lambda eng, _dt=dt: eng.reduce_sum(STAT[:, 8 + _dt:9 + _dt], FMB[:, _dt * 128:(_dt + 1) * 128], AX.X))
            E("vector", lambda eng, _dt=dt: eng.tensor_add(GS[:, _dt:_dt + 1], GS[:, _dt:_dt + 1], STAT[:, 8 + _dt:9 + _dt]))
        for dt in range(8):
            transpose(2, XT[:, dt * 128:(dt + 1) * 128], FMB[:, dt * 128:(dt + 1) * 128], 128, 128)
            store_toks.append(st_(XF[dt * 128:(dt + 1) * 128, u * 128:(u + 1) * 128], FMB[:, dt * 128:(dt + 1) * 128]))
    barrier()

    arena_reset()
    WQ = [alloc(1024) for _ in range(8)]
    WK = [alloc(1024) for _ in range(8)]
    WV = [alloc(1024) for _ in range(8)]
    HFB = alloc(4096); KFB = alloc(4096); VFB = alloc(4096)
    KSQ = VFB
    QM = alloc(512); VTM = alloc(1024)
    OMAC = HFB[:, 0:1024]
    KTMS = VTM
    ZB = KFB[:, 0:1024]
    INVB = QM
    CARRY = ROWS[0:1, 0:D]
    KROW = ROWS[0:1, 1024:1536]
    CTMP = ROWS[0:1, 1536:1536 + 512]

    for kp in range(8):
        ld(WQ[kp], wq_d[kp * 128:(kp + 1) * 128, :])
        ld(WK[kp], wk_d[kp * 128:(kp + 1) * 128, :])
        ld(WV[kp], wv_d[kp * 128:(kp + 1) * 128, :])
    E("vector", lambda eng: eng.memset(CARRY, 0.0))

    for c in range(NCH):
        for kp in range(8):
            ld(HFB[:, kp * 512:(kp + 1) * 512], HFd[kp * 128:(kp + 1) * 128, c * 512:(c + 1) * 512])
        for m in range(8):
            mm_group(0, 512, [(WQ[kp][:, m * 128:(m + 1) * 128], HFB[:, kp * 512:(kp + 1) * 512]) for kp in range(8)])
            ps_read(0, "scalar", lambda eng: eng.copy(QM, PS[0][:, 0:512]))
            store_toks.append(st_(QFd[m * 128:(m + 1) * 128, c * 512:(c + 1) * 512], QM))
        for m in range(8):
            mm_group(1, 512, [(WK[kp][:, m * 128:(m + 1) * 128], HFB[:, kp * 512:(kp + 1) * 512]) for kp in range(8)])
            ps_read(1, "scalar", lambda eng, _m=m: eng.copy(KFB[:, _m * 512:(_m + 1) * 512], PS[1][:, 0:512]))
        for m in range(8):
            E("scalar", lambda eng, _m=m: eng.activation(KSQ[:, _m * 512:(_m + 1) * 512], KFB[:, _m * 512:(_m + 1) * 512], AF.Square))
        mm_group(2, 512, [(ONC[:], KSQ[:, m * 512:(m + 1) * 512]) for m in range(8)], rows=1)
        ps_read(2, "scalar", lambda eng: eng.activation(KROW, PS[2][0:1, 0:512], AF.Sqrt))
        E("vector", lambda eng: eng.tensor_scalar_max(KROW, KROW, 1e-12))
        E("vector", lambda eng: eng.reciprocal(KROW, KROW))
        rank1(2, KROW, 512)
        ps_read(2, "scalar", lambda eng: eng.copy(INVB, PS[2][:, 0:512]))
        for m in range(8):
            E("vector", lambda eng, _m=m: eng.tensor_mul(KFB[:, _m * 512:(_m + 1) * 512], KFB[:, _m * 512:(_m + 1) * 512], INVB))
            store_toks.append(st_(KFd[m * 128:(m + 1) * 128, c * 512:(c + 1) * 512], KFB[:, m * 512:(m + 1) * 512]))
        for s in range(4):
            u = c * 4 + s
            for dt in range(8):
                transpose(3, KFB[:, dt * 512 + s * 128: dt * 512 + (s + 1) * 128], KTMS[:, dt * 128:(dt + 1) * 128], 128, 128)
            store_toks.append(st_(KTMd[u * 128:(u + 1) * 128, :], KTMS))
        for m in range(8):
            mm_group(4, 512, [(WV[kp][:, m * 128:(m + 1) * 128], HFB[:, kp * 512:(kp + 1) * 512]) for kp in range(8)])
            ps_read(4, "scalar", lambda eng, _m=m: eng.copy(VFB[:, _m * 512:(_m + 1) * 512], PS[4][:, 0:512]))
        for s in range(4):
            u = c * 4 + s
            for dt in range(8):
                transpose(5, VFB[:, dt * 512 + s * 128: dt * 512 + (s + 1) * 128], VTM[:, dt * 128:(dt + 1) * 128], 128, 128)
            E("vector", lambda eng: eng.tensor_mul(ZB, PINV_E[:], VTM))
            E("vector", lambda eng: eng.tensor_mul(OMAC[0:1, :], CARRY, OMAR[:]))
            E("vector", lambda eng: eng.tensor_add(ZB[0:1, :], ZB[0:1, :], OMAC[0:1, :]))
            mm_group(6, 512, [(UPI[:], ZB[:, 0:512])])
            mm_group(7, 512, [(UPI[:], ZB[:, 512:1024])])
            ps_read(6, "vector", lambda eng: eng.tensor_mul(VTM[:, 0:512], PFWD_E[:, 0:512], PS[6][:, 0:512]))
            ps_read(7, "vector", lambda eng: eng.tensor_mul(VTM[:, 512:1024], PFWD_E[:, 512:1024], PS[7][:, 0:512]))
            mm_group(2, 512, [(ONC[:], ZB[:, 0:512])], rows=1)
            ps_read(2, "vector", lambda eng: eng.tensor_mul(CARRY[:, 0:512], PS[2][0:1, 0:512], OMA127[:, 0:512]))
            mm_group(3, 512, [(ONC[:], ZB[:, 512:1024])], rows=1)
            ps_read(3, "vector", lambda eng: eng.tensor_mul(CARRY[:, 512:1024], PS[3][0:1, 0:512], OMA127[:, 512:1024]))
            store_toks.append(st_(VEd[u * 128:(u + 1) * 128, :], VTM))
    barrier()

    arena_reset()
    WO = [alloc(1024) for _ in range(8)]
    STT = [alloc(1024) for _ in range(8)]
    QFB = alloc(4096); KFB3 = alloc(4096)
    KTMS2 = alloc(1024); VES = alloc(1024)
    ATN = alloc(1024); AFB = alloc(4096)
    UB = alloc(1024); BB_ = alloc(1024)
    NT = alloc(128); RT_A = alloc(128); RT_B = alloc(128)
    MP0 = alloc(128); MP1 = alloc(128); MTP0 = alloc(128); MTP1 = alloc(128)
    MQK = alloc(128); BBC = alloc(128); BETR = alloc(128)
    BETP = alloc(4)
    Y1M = alloc(512); XNM = alloc(512); H2M = alloc(512); Y1SQ = alloc(512); RSTB = alloc(512)
    Y1ROW = ROWS[0:1, 1024:1536]

    for kp in range(8):
        ld(WO[kp], wo_d[kp * 128:(kp + 1) * 128, :])
        E("gpsimd", lambda eng, _k=kp: eng.memset(STT[_k], 0.0))

    for c in range(NCH):
        for kp in range(8):
            ld(QFB[:, kp * 512:(kp + 1) * 512], QFd[kp * 128:(kp + 1) * 128, c * 512:(c + 1) * 512])
            ld(KFB3[:, kp * 512:(kp + 1) * 512], KFd[kp * 128:(kp + 1) * 128, c * 512:(c + 1) * 512])
        for s in range(4):
            u = c * 4 + s
            ld(BETP[:, s:s + 1], BETA[u * 128:(u + 1) * 128, :])
            ld(KTMS2, KTMd[u * 128:(u + 1) * 128, :])
            ld(VES, VEd[u * 128:(u + 1) * 128, :])
            ksl = [KFB3[:, kp * 512 + s * 128: kp * 512 + (s + 1) * 128] for kp in range(8)]
            qsl = [QFB[:, kp * 512 + s * 128: kp * 512 + (s + 1) * 128] for kp in range(8)]
            bpp = BETP[:, s:s + 1]
            transpose(0, bpp, BETR[0:1, 0:128], 128, 1)
            rank1(0, BETR[0:1, 0:128], 128)
            ps_read(0, "scalar", lambda eng: eng.copy(BBC, PS[0][:, 0:128]))
            mm_group(1, 128, [(ksl[kp], ksl[kp]) for kp in range(8)])
            ps_read(1, "vector", lambda eng: eng.tensor_mul(NT, PS[1][:, 0:128], UPN[:]))
            E("vector", lambda eng: eng.tensor_mul(NT, NT, BBC))
            E("vector", lambda eng: eng.tensor_add(RT_A, NT, IDN[:]))
            transpose(2, NT, MP0, 128, 128)
            E("scalar", lambda eng: eng.copy(MTP0, NT))
            rt_cur, rt_oth = RT_A, RT_B
            m_cur, mt_cur = MP0, MTP0
            m_nxt, mt_nxt = MP1, MTP1
            for lvl in range(6):
                mm_group(3, 128, [(mt_cur, m_cur)])
                ps_read(3, "scalar", lambda eng, _m=m_nxt: eng.copy(_m, PS[3][:, 0:128]))
                mm_group(2, 128, [(m_nxt, rt_cur)])
                ps_read(2, "vector", lambda eng, _o=rt_oth, _c=rt_cur: eng.tensor_add(_o, _c, PS[2][:, 0:128]))
                rt_cur, rt_oth = rt_oth, rt_cur
                if lvl < 5:
                    transpose(3, m_nxt, mt_nxt, 128, 128)
                    m_cur, mt_cur = m_nxt, mt_nxt
                    m_nxt, mt_nxt = (MP0, MTP0) if m_cur is MP1 else (MP1, MTP1)
            mm_group(4, 512, [(ksl[kp], STT[kp][:, 0:512]) for kp in range(8)])
            mm_group(5, 512, [(ksl[kp], STT[kp][:, 512:1024]) for kp in range(8)])
            ps_read(4, "vector", lambda eng: eng.scalar_tensor_tensor(BB_[:, 0:512], PS[4][:, 0:512], -1.0, VES[:, 0:512], ALU.mult, ALU.add))
            ps_read(5, "vector", lambda eng: eng.scalar_tensor_tensor(BB_[:, 512:1024], PS[5][:, 0:512], -1.0, VES[:, 512:1024], ALU.mult, ALU.add))
            E("vector", lambda eng, _b=bpp: eng.tensor_scalar(BB_, BB_, _b, None, ALU.mult))
            mm_group(4, 512, [(rt_cur, BB_[:, 0:512])])
            ps_read(4, "scalar", lambda eng: eng.copy(UB[:, 0:512], PS[4][:, 0:512]))
            mm_group(5, 512, [(rt_cur, BB_[:, 512:1024])])
            ps_read(5, "scalar", lambda eng: eng.copy(UB[:, 512:1024], PS[5][:, 0:512]))
            mm_group(6, 512, [(qsl[kp], STT[kp][:, 0:512]) for kp in range(8)])
            mm_group(7, 512, [(qsl[kp], STT[kp][:, 512:1024]) for kp in range(8)])
            mm_group(1, 128, [(ksl[kp], qsl[kp]) for kp in range(8)])
            ps_read(1, "vector", lambda eng: eng.tensor_mul(MQK, PS[1][:, 0:128], UPI[:]))
            mm_acc(6, 512, [(MQK, UB[:, 0:512])])
            mm_acc(7, 512, [(MQK, UB[:, 512:1024])])
            ps_read(6, "scalar", lambda eng: eng.copy(ATN[:, 0:512], PS[6][:, 0:512]))
            ps_read(7, "scalar", lambda eng: eng.copy(ATN[:, 512:1024], PS[7][:, 0:512]))
            for m in range(8):
                mm_group(4, 512, [(KTMS2[:, m * 128:(m + 1) * 128], UB[:, 0:512])])
                ps_read(4, "vector", lambda eng, _m=m: eng.tensor_add(STT[_m][:, 0:512], STT[_m][:, 0:512], PS[4][:, 0:512]))
                mm_group(5, 512, [(KTMS2[:, m * 128:(m + 1) * 128], UB[:, 512:1024])])
                ps_read(5, "vector", lambda eng, _m=m: eng.tensor_add(STT[_m][:, 512:1024], STT[_m][:, 512:1024], PS[5][:, 0:512]))
            for dt in range(8):
                transpose(0, ATN[:, dt * 128:(dt + 1) * 128], AFB[:, dt * 512 + s * 128: dt * 512 + (s + 1) * 128], 128, 128)
        for m in range(8):
            mm_group(0, 512, [(WO[kp][:, m * 128:(m + 1) * 128], AFB[:, kp * 512:(kp + 1) * 512]) for kp in range(8)])
            ld(XNM, XN[m * 128:(m + 1) * 128, c * 512:(c + 1) * 512])
            ps_read(0, "vector", lambda eng: eng.tensor_add(Y1M, PS[0][:, 0:512], XNM))
            store_toks.append(st_(Y1d[m * 128:(m + 1) * 128, c * 512:(c + 1) * 512], Y1M))
            E("scalar", lambda eng: eng.activation(Y1SQ, Y1M, AF.Square))
            if m == 0:
                mm_group(1, 512, [(ONC[:], Y1SQ)], rows=1)
            else:
                mm_acc(1, 512, [(ONC[:], Y1SQ)], rows=1)
        ps_read(1, "vector", lambda eng: eng.tensor_scalar(Y1ROW, PS[1][0:1, 0:512], 1.0 / D, 1e-6, ALU.mult, ALU.add))
        E("scalar", lambda eng: eng.activation(Y1ROW, Y1ROW, AF.Sqrt))
        E("vector", lambda eng: eng.reciprocal(Y1ROW, Y1ROW))
        rank1(1, Y1ROW, 512)
        ps_read(1, "scalar", lambda eng: eng.copy(RSTB, PS[1][:, 0:512]))
        for m in range(8):
            ld(Y1M, Y1d[m * 128:(m + 1) * 128, c * 512:(c + 1) * 512])
            E("vector", lambda eng: eng.tensor_mul(H2M, Y1M, RSTB))
            E("vector", lambda eng, _m=m: eng.tensor_scalar(H2M, H2M, N2W[:, _m:_m + 1], None, ALU.mult))
            store_toks.append(st_(H2d[m * 128:(m + 1) * 128, c * 512:(c + 1) * 512], H2M))
    barrier()

    arena_reset()
    WGB = [alloc(2048) for _ in range(8)]
    WUB = [alloc(2048) for _ in range(8)]
    H2F = alloc(4096); HIDM = alloc(512); HIDT = alloc(512)
    for kp in range(8):
        ld(WGB[kp], wg0_d[kp * 128:(kp + 1) * 128, :])
        ld(WUB[kp], wu0_d[kp * 128:(kp + 1) * 128, :])
    for c in range(NCH):
        for kp in range(8):
            ld(H2F[:, kp * 512:(kp + 1) * 512], H2d[kp * 128:(kp + 1) * 128, c * 512:(c + 1) * 512])
        for m in range(16):
            mm_group(0, 512, [(WGB[kp][:, m * 128:(m + 1) * 128], H2F[:, kp * 512:(kp + 1) * 512]) for kp in range(8)])
            mm_group(1, 512, [(WUB[kp][:, m * 128:(m + 1) * 128], H2F[:, kp * 512:(kp + 1) * 512]) for kp in range(8)])
            ps_read(0, "scalar", lambda eng: eng.activation(HIDT, PS[0][:, 0:512], AF.Silu))
            ps_read(1, "vector", lambda eng: eng.tensor_mul(HIDM, HIDT, PS[1][:, 0:512]))
            store_toks.append(st_(HIDd[m * 128:(m + 1) * 128, c * 512:(c + 1) * 512], HIDM))
    barrier()

    arena_reset()
    WDB = [alloc(1024) for _ in range(16)]
    HIDF = alloc(8192); Y1M2 = alloc(512); SM_ = alloc(512)
    for kp in range(16):
        ld(WDB[kp], wd0_d[kp * 128:(kp + 1) * 128, :])
    for c in range(NCH):
        for kp in range(16):
            ld(HIDF[:, kp * 512:(kp + 1) * 512], HIDd[kp * 128:(kp + 1) * 128, c * 512:(c + 1) * 512])
        for m in range(8):
            mm_group(0, 512, [(WDB[kp][:, m * 128:(m + 1) * 128], HIDF[:, kp * 512:(kp + 1) * 512]) for kp in range(16)])
            ld(Y1M2, Y1d[m * 128:(m + 1) * 128, c * 512:(c + 1) * 512])
            ps_read(0, "vector", lambda eng: eng.tensor_add(SM_, PS[0][:, 0:512], Y1M2))
            store_toks.append(st_(SSd[m * 128:(m + 1) * 128, c * 512:(c + 1) * 512], SM_))
    barrier()

    arena_reset()
    CW1 = [alloc(1024) for _ in range(8)]
    CW2 = [alloc(512) for _ in range(8)]
    CW3 = [alloc(256) for _ in range(4)]
    CW4 = [alloc(1) for _ in range(2)]
    GW1B = alloc(4096)
    GW1 = [GW1B[:, kp * 512:(kp + 1) * 512] for kp in range(8)]
    GW2 = [alloc(256) for _ in range(4)]
    GW3 = [alloc(1) for _ in range(2)]
    OUTA = alloc(4096)
    OUTB = GW1B
    C1B = alloc(4096); C2B = alloc(2048); C3B = alloc(1024)
    X1FB = alloc(4096)
    X1NB = OUTA
    XM_ = C2B[:, 0:512]; SM2 = C2B[:, 512:1024]; X1SQ = C2B[:, 1024:1536]
    MB_ = C3B[:, 0:512]; RB_ = C3B[:, 512:1024]
    CPB = alloc(512); TMS_ = alloc(1024)
    GH1 = STAT[:, 8:12]; GH2 = STAT[:, 12:14]
    TFR = ROWS[0:1, 1024:1536]
    MROW = ROWS[0:1, 0:512]; VROW = ROWS[0:1, 512:1024]

    for kp in range(8):
        ld(CW1[kp], cw1_d[kp * 128:(kp + 1) * 128, :])
        ld(CW2[kp], cw2_d[kp * 128:(kp + 1) * 128, :])
        ld(GW1[kp], gw1_d[kp * 128:(kp + 1) * 128, :])
    for kp in range(4):
        ld(CW3[kp], cw3_d[kp * 128:(kp + 1) * 128, :])
        ld(GW2[kp], gw2_d[kp * 128:(kp + 1) * 128, :])
    for kp in range(2):
        ld(CW4[kp], cw4_d[kp * 128:(kp + 1) * 128, :])
        ld(GW3[kp], gw3_d[kp * 128:(kp + 1) * 128, :])

    E("vector", lambda eng: eng.tensor_scalar_mul(GS[:], GS[:], 1.0 / T))
    for mt in range(4):
        mm_group(0, 1, [(GW1[kp][:, mt * 128:(mt + 1) * 128], GS[:, kp:kp + 1]) for kp in range(8)], col0=mt)
    for mt in range(4):
        ps_read(0, "scalar", lambda eng, _m=mt: eng.activation(GH1[:, _m:_m + 1], PS[0][:, _m:_m + 1], AF.Gelu, bias=GB1P[:, _m:_m + 1]))
    for mt in range(2):
        mm_group(1, 1, [(GW2[kp][:, mt * 128:(mt + 1) * 128], GH1[:, kp:kp + 1]) for kp in range(4)], col0=mt)
    for mt in range(2):
        ps_read(1, "scalar", lambda eng, _m=mt: eng.activation(GH2[:, _m:_m + 1], PS[1][:, _m:_m + 1], AF.Gelu, bias=GB2P[:, _m:_m + 1]))
    mm_group(2, 1, [(GW3[kp], GH2[:, kp:kp + 1]) for kp in range(2)], rows=1)
    ps_read(2, "scalar", lambda eng: eng.activation(GF07[:], PS[2][0:1, 0:1], AF.Sigmoid, bias=GB3S[:]))
    E("vector", lambda eng: eng.tensor_scalar_mul(GF07[:], GF07[:], 0.7))

    twist_map = [(4, 1.0), (5, 1.0), (6, -1.0), (7, -1.0), (0, -1.0), (1, -1.0), (2, 1.0), (3, 1.0)]
    for c in range(NCH):
        for dt in range(8):
            ld(OUTA[:, dt * 512:(dt + 1) * 512], XN[dt * 128:(dt + 1) * 128, c * 512:(c + 1) * 512])
        cur_out, oth_out = OUTA, OUTB
        for cyc in range(3):
            for m in range(8):
                mm_group(0, 512, [(CW1[kp][:, m * 128:(m + 1) * 128], cur_out[:, kp * 512:(kp + 1) * 512]) for kp in range(8)])
                ps_read(0, "scalar", lambda eng, _m=m: eng.activation(C1B[:, _m * 512:(_m + 1) * 512], PS[0][:, 0:512], AF.Gelu, bias=CB1P[:, _m:_m + 1]))
            for m in range(4):
                mm_group(1, 512, [(CW2[kp][:, m * 128:(m + 1) * 128], C1B[:, kp * 512:(kp + 1) * 512]) for kp in range(8)])
                ps_read(1, "scalar", lambda eng, _m=m: eng.activation(C2B[:, _m * 512:(_m + 1) * 512], PS[1][:, 0:512], AF.Gelu, bias=CB2P[:, _m:_m + 1]))
            for m in range(2):
                mm_group(2, 512, [(CW3[kp][:, m * 128:(m + 1) * 128], C2B[:, kp * 512:(kp + 1) * 512]) for kp in range(4)])
                ps_read(2, "scalar", lambda eng, _m=m: eng.activation(C3B[:, _m * 512:(_m + 1) * 512], PS[2][:, 0:512], AF.Gelu, bias=CB3P[:, _m:_m + 1]))
            mm_group(3, 512, [(CW4[kp], C3B[:, kp * 512:(kp + 1) * 512]) for kp in range(2)], rows=1)
            ps_read(3, "scalar", lambda eng: eng.activation(TFR, PS[3][0:1, 0:512], AF.Sigmoid, bias=CB4S[:]))
            E("vector", lambda eng: eng.tensor_scalar(TFR, TFR, 0.3, GF07[:], ALU.mult, ALU.add))
            E("vector", lambda eng: eng.tensor_scalar(TFR, TFR, 0.5, AR2[:], ALU.subtract, ALU.mult))
            E("vector", lambda eng: eng.tensor_scalar_add(TFR, TFR, 0.1))
            rank1(3, TFR, 512)
            ps_read(3, "scalar", lambda eng: eng.copy(CPB, PS[3][:, 0:512]))
            for dt in range(8):
                src, sign = twist_map[dt]
                E("vector", lambda eng, _s=src, _g=sign, _c=cur_out: eng.scalar_tensor_tensor(TMS_[:, 0:512], _c[:, _s * 512:(_s + 1) * 512], _g, CPB, ALU.mult, ALU.mult))
                E("vector", lambda eng, _d=dt, _c=cur_out, _o=oth_out: eng.tensor_add(_o[:, _d * 512:(_d + 1) * 512], _c[:, _d * 512:(_d + 1) * 512], TMS_[:, 0:512]))
            cur_out, oth_out = oth_out, cur_out
        for dt in range(8):
            E("scalar", lambda eng, _d=dt, _c=cur_out: eng.activation(_c[:, _d * 512:(_d + 1) * 512], _c[:, _d * 512:(_d + 1) * 512], AF.Silu))
            ld(XM_, XF[dt * 128:(dt + 1) * 128, c * 512:(c + 1) * 512])
            ld(SM2, SSd[dt * 128:(dt + 1) * 128, c * 512:(c + 1) * 512], eng="gpsimd")
            E("vector", lambda eng, _d=dt, _c=cur_out: eng.tensor_mul(TMS_[:, 0:512], _c[:, _d * 512:(_d + 1) * 512], SM2))
            E("vector", lambda eng, _d=dt: eng.tensor_add(X1FB[:, _d * 512:(_d + 1) * 512], XM_, TMS_[:, 0:512]))
            store_toks.append(st_(X1d[dt * 128:(dt + 1) * 128, c * 512:(c + 1) * 512], X1FB[:, dt * 512:(dt + 1) * 512]))
            if dt == 0:
                mm_group(4, 512, [(ONC[:], X1FB[:, dt * 512:(dt + 1) * 512])], rows=1)
            else:
                mm_acc(4, 512, [(ONC[:], X1FB[:, dt * 512:(dt + 1) * 512])], rows=1)
            E("scalar", lambda eng, _d=dt: eng.activation(X1SQ, X1FB[:, _d * 512:(_d + 1) * 512], AF.Square))
            if dt == 0:
                mm_group(5, 512, [(ONC[:], X1SQ)], rows=1)
            else:
                mm_acc(5, 512, [(ONC[:], X1SQ)], rows=1)
        ps_read(4, "vector", lambda eng: eng.tensor_scalar_mul(MROW, PS[4][0:1, 0:512], 1.0 / D))
        E("vector", lambda eng: eng.tensor_mul(VROW, MROW, MROW))
        ps_read(5, "vector", lambda eng: eng.scalar_tensor_tensor(VROW, PS[5][0:1, 0:512], 1.0 / D, VROW, ALU.mult, ALU.subtract))
        E("vector", lambda eng: eng.tensor_scalar_add(VROW, VROW, 1e-5))
        E("scalar", lambda eng: eng.activation(VROW, VROW, AF.Sqrt))
        E("vector", lambda eng: eng.reciprocal(VROW, VROW))
        rank1(4, MROW, 512)
        ps_read(4, "scalar", lambda eng: eng.copy(MB_, PS[4][:, 0:512]))
        rank1(5, VROW, 512)
        ps_read(5, "scalar", lambda eng: eng.copy(RB_, PS[5][:, 0:512]))
        for dt in range(8):
            E("vector", lambda eng, _d=dt: eng.tensor_sub(X1NB[:, _d * 512:(_d + 1) * 512], X1FB[:, _d * 512:(_d + 1) * 512], MB_))
            E("vector", lambda eng, _d=dt: eng.tensor_mul(X1NB[:, _d * 512:(_d + 1) * 512], X1NB[:, _d * 512:(_d + 1) * 512], RB_))
            store_toks.append(st_(X1Nd[dt * 128:(dt + 1) * 128, c * 512:(c + 1) * 512], X1NB[:, dt * 512:(dt + 1) * 512]))
        for s in range(4):
            for dt in range(8):
                transpose(6, X1NB[:, dt * 512 + s * 128: dt * 512 + (s + 1) * 128], TMS_[:, dt * 128:(dt + 1) * 128], 128, 128)
            store_toks.append(st_(X1NTd[(c * 4 + s) * 128:(c * 4 + s + 1) * 128, :], TMS_[:, 0:1024]))
    barrier()

    arena_reset()
    XH = alloc(16384); YH = alloc(16384)
    PIN2 = alloc(1024); PPO2 = alloc(1024); ZW = alloc(1024); TW0 = alloc(512); TW1 = alloc(512)
    LNROW = alloc(512); CBROW = alloc(512)
    CRY = ROWS[0:1, 0:512]; ABR = ROWS[0:1, 512:1024]; ABC = ROWS[0:1, 1024:1536]; TRW = ROWS[0:1, 1536:2048]
    RELAX[0] = True
    for hf in range(2):
        soft_fence()
        for u in range(NSUB):
            ld(XH[:, u * 512:(u + 1) * 512], X1NTd[u * 128:(u + 1) * 128, hf * 512:(hf + 1) * 512])
        for i in range(4):
            E("gpsimd", lambda eng, _i=i: eng.memset(YH[:, _i * 4096:(_i + 1) * 4096], 0.0))
        soft_fence()
        for n in range(NS):
            soft_fence()
            lnrow = LNROW[0:1, 0:512]
            cbrow = CBROW[0:1, 0:512]
            t_ln = ld(lnrow, LNABd[n:n + 1, hf * 512:(hf + 1) * 512])
            t_cb = ld(cbrow, CBNDd[n:n + 1, hf * 512:(hf + 1) * 512])
            rank1(6, lnrow, 512, deps=(t_ln,))
            tp1 = ps_read(6, "scalar", lambda eng: eng.activation(PIN2[:, 0:512], PS[6][:, 0:512], AF.Exp, scale=NIOP[:]))
            tp1b = E("scalar", lambda eng: eng.copy(PIN2[:, 512:1024], PIN2[:, 0:512]))
            tp2 = ps_read(6, "scalar", lambda eng: eng.activation(PPO2[:, 0:512], PS[6][:, 0:512], AF.Exp, scale=IOP[:]))
            rank1(7, cbrow, 512, deps=(t_cb,))
            tp3 = ps_read(7, "vector", lambda eng: eng.tensor_mul(PPO2[:, 0:512], PPO2[:, 0:512], PS[7][:, 0:512]), deps=(tp2,))
            tp3b = E("vector", lambda eng: eng.tensor_copy(PPO2[:, 512:1024], PPO2[:, 0:512]))
            tab = E("scalar", lambda eng: eng.activation(ABR, lnrow, AF.Exp), (t_ln,))
            tabc1 = E("vector", lambda eng: eng.tensor_scalar_mul(ABC, lnrow, 127.0), (t_ln,))
            tabc = E("scalar", lambda eng: eng.activation(ABC, ABC, AF.Exp), (tabc1,))
            E("vector", lambda eng: eng.memset(CRY, 0.0))
            zw_guard = ()
            for p in range(16):
                c0, c1 = 2 * p, 2 * p + 1
                tz = E("vector", lambda eng, _p=p: eng.tensor_mul(ZW, PIN2, XH[:, _p * 1024:(_p + 1) * 1024]), zw_guard + (tp1b,))
                E("vector", lambda eng: eng.tensor_mul(TRW, CRY, ABR), (tab,))
                ti0b = E("vector", lambda eng: eng.tensor_add(ZW[0:1, 0:512], ZW[0:1, 0:512], TRW))
                b0 = 0 if p % 2 == 0 else 2
                b1 = 1 if p % 2 == 0 else 3
                m_b0 = mm_group(b0, 512, [(UPI[:], ZW[:, 0:512])], deps=(ti0b,))
                m_4 = mm_group(4, 512, [(ONC[:], ZW[:, 0:512])], rows=1, deps=(ti0b,))
                ps_read(4, "vector", lambda eng: eng.tensor_mul(CRY, PS[4][0:1, 0:512], ABC), deps=(tabc,))
                E("vector", lambda eng: eng.tensor_mul(TRW, CRY, ABR))
                ti1b = E("vector", lambda eng: eng.tensor_add(ZW[0:1, 512:1024], ZW[0:1, 512:1024], TRW))
                m_b1 = mm_group(b1, 512, [(UPI[:], ZW[:, 512:1024])], deps=(ti1b,))
                m_5 = mm_group(5, 512, [(ONC[:], ZW[:, 512:1024])], rows=1, deps=(ti1b,))
                ps_read(5, "vector", lambda eng: eng.tensor_mul(CRY, PS[5][0:1, 0:512], ABC))
                ps_read(b0, "vector", lambda eng, _b=b0: eng.tensor_mul(TW0, PS[_b][:, 0:512], PPO2[:, 0:512]), deps=(tp3,))
                E("vector", lambda eng, _c=c0: eng.tensor_add(YH[:, _c * 512:(_c + 1) * 512], YH[:, _c * 512:(_c + 1) * 512], TW0))
                ps_read(b1, "vector", lambda eng, _b=b1: eng.tensor_mul(TW1, PS[_b][:, 0:512], PPO2[:, 512:1024]), deps=(tp3b,))
                E("vector", lambda eng, _c=c1: eng.tensor_add(YH[:, _c * 512:(_c + 1) * 512], YH[:, _c * 512:(_c + 1) * 512], TW1))
                zw_guard = (m_b0, m_4, m_b1, m_5)
        soft_fence()
        for u in range(NSUB):
            store_toks.append(st_(YSSMd[u * 128:(u + 1) * 128, hf * 512:(hf + 1) * 512], YH[:, u * 512:(u + 1) * 512]))
    RELAX[0] = False
    barrier()

    arena_reset()
    XSC = alloc(8192); YSC = alloc(8192); HIDH = alloc(4096)
    WDH = [alloc(1024) for _ in range(4)]
    WGT = alloc(1024); WUT = alloc(1024)
    GTS = alloc(8); MK1 = alloc(8); MK2 = alloc(8); SMB = alloc(8)
    GBC = alloc(512); TW0b = alloc(512); TW1b = alloc(512); GROW = alloc(512)
    GATR = SMALL[0:8, 0:1024]; SCF = SMALL[0:8, 1024:1536]
    WGA = [SMALL[:, 1544 + 8 * kp: 1552 + 8 * kp] for kp in range(8)]
    M1 = STAT[:, 4:5]; M2 = STAT[:, 5:6]; P1_ = STAT[:, 6:7]; P2_ = STAT[:, 7:8]
    for kp in range(8):
        ld(WGA[kp], wgate_d[kp * 128:(kp + 1) * 128, :])
    mgroups = [(0, 4), (4, 4), (8, 3)]
    for sc in range(4):
        RELAX[0] = True
        soft_fence()
        for kp in range(8):
            ld(XSC[:, kp * 1024:(kp + 1) * 1024], X1Nd[kp * 128:(kp + 1) * 128, sc * 1024:(sc + 1) * 1024])
        tyz = E("gpsimd", lambda eng: eng.memset(YSC, 0.0))
        RELAX[0] = False
        for tc in range(2):
            mm_group(0, 512, [(WGA[kp], XSC[:, kp * 1024 + tc * 512: kp * 1024 + (tc + 1) * 512]) for kp in range(8)], rows=8)
            ps_read(0, "scalar", lambda eng: eng.copy(SCF, PS[0][0:8, 0:512]))
            for g in range(4):
                transpose(1, SCF[:, g * 128:(g + 1) * 128], GTS, 8, 128)
                E("vector", lambda eng: eng.reduce_max(M1, GTS, AX.X))
                E("vector", lambda eng: eng.tensor_single_scalar(MK1, GTS, M1, ALU.is_equal))
                E("vector", lambda eng: eng.scalar_tensor_tensor(SMB, MK1, -1e9, GTS, ALU.mult, ALU.add))
                E("vector", lambda eng: eng.reduce_max(M2, SMB, AX.X))
                E("vector", lambda eng: eng.tensor_single_scalar(MK2, SMB, M2, ALU.is_equal))
                E("vector", lambda eng: eng.tensor_sub(P1_, M1, M2))
                E("scalar", lambda eng: eng.activation(P1_, P1_, AF.Sigmoid))
                E("vector", lambda eng: eng.tensor_scalar(P2_, P1_, -1.0, 1.0, ALU.mult, ALU.add))
                E("vector", lambda eng: eng.tensor_scalar(MK1, MK1, P1_, None, ALU.mult))
                E("vector", lambda eng: eng.tensor_scalar(MK2, MK2, P2_, None, ALU.mult))
                E("vector", lambda eng: eng.tensor_add(GTS, MK1, MK2))
                transpose(2, GTS, GATR[:, tc * 512 + g * 128: tc * 512 + (g + 1) * 128], 128, 8)
        store_toks.append(st_(GATEd[:, sc * 1024:(sc + 1) * 1024], GATR[:, 0:1024]))
        RELAX[0] = True
        soft_fence()
        thd_prev = None
        tyy_prev = None
        grow_guard = None
        for e in range(ME):
            for (m0, nmt) in mgroups:
                soft_fence()
                hid_done = []
                gu_guard = ()
                for mi in range(nmt):
                    mt = m0 + mi
                    wg_tk = [ld(WGT[:, kp * 128:(kp + 1) * 128], wgE_d[e, mt, kp, :, :], deps=gu_guard) for kp in range(8)]
                    wu_tk = [ld(WUT[:, kp * 128:(kp + 1) * 128], wuE_d[e, mt, kp, :, :], deps=gu_guard, eng="gpsimd") for kp in range(8)]
                    gu_guard = ()
                    for tc in range(2):
                        gmm = mm_group(3, 512, [(WGT[:, kp * 128:(kp + 1) * 128], XSC[:, kp * 1024 + tc * 512: kp * 1024 + (tc + 1) * 512]) for kp in range(8)], deps=tuple(wg_tk))
                        umm = mm_group(4, 512, [(WUT[:, kp * 128:(kp + 1) * 128], XSC[:, kp * 1024 + tc * 512: kp * 1024 + (tc + 1) * 512]) for kp in range(8)], deps=tuple(wu_tk))
                        tsl = ps_read(3, "scalar", lambda eng: eng.activation(TW0b, PS[3][:, 0:512], AF.Silu), deps=(thd_prev,) if thd_prev else ())
                        thd = ps_read(4, "vector", lambda eng, _mi=mi, _tc=tc: eng.tensor_mul(HIDH[:, _mi * 1024 + _tc * 512: _mi * 1024 + (_tc + 1) * 512], TW0b, PS[4][:, 0:512]), deps=(tsl,))
                        hid_done.append(thd)
                        thd_prev = thd
                        gu_guard = gu_guard + (gmm, umm)
                wd_tk = [ld(WDH[mi], wdE_d[e, (m0 + mi) * 128:(m0 + mi + 1) * 128, :]) for mi in range(nmt)]
                for tc in range(2):
                    t_gr = ld(GROW[0:1, 0:512], GATEd[e:e + 1, sc * 1024 + tc * 512: sc * 1024 + (tc + 1) * 512],
                              deps=(grow_guard,) if grow_guard else ())
                    grow_guard = rank1(5, GROW[0:1, 0:512], 512, deps=(t_gr,))
                    tgb = ps_read(5, "scalar", lambda eng: eng.copy(GBC, PS[5][:, 0:512]), deps=(tyy_prev,) if tyy_prev else ())
                    for m in range(8):
                        bnk = 6 if m % 2 == 0 else 7
                        mm_group(bnk, 512, [(WDH[mi][:, m * 128:(m + 1) * 128], HIDH[:, mi * 1024 + tc * 512: mi * 1024 + (tc + 1) * 512]) for mi in range(nmt)],
                                 deps=tuple(wd_tk) + tuple(hid_done))
                        tyy = ps_read(bnk, "vector", lambda eng, _b=bnk: eng.tensor_mul(TW1b, PS[_b][:, 0:512], GBC), deps=(tgb,))
                        tyy_prev = tyy
                        E("vector", lambda eng, _m=m, _tc=tc: eng.tensor_add(YSC[:, _m * 1024 + _tc * 512: _m * 1024 + (_tc + 1) * 512],
                                                                             YSC[:, _m * 1024 + _tc * 512: _m * 1024 + (_tc + 1) * 512], TW1b),
                          deps=(tyz,))
        soft_fence()
        for m in range(8):
            store_toks.append(st_(YMOEd[m * 128:(m + 1) * 128, sc * 1024:(sc + 1) * 1024], YSC[:, m * 1024:(m + 1) * 1024]))
        RELAX[0] = False
    barrier()

    arena_reset()
    YST = alloc(1024); YSF = alloc(4096); XOB = alloc(4096); OTM = alloc(1024)
    X1M_ = alloc(512); X1NM_ = alloc(512); YMM = alloc(512); G2M = alloc(512); S2M = alloc(512)
    for c in range(NCH):
        for s in range(4):
            ld(YST, YSSMd[(c * 4 + s) * 128:(c * 4 + s + 1) * 128, :])
            for dt in range(8):
                transpose(0, YST[:, dt * 128:(dt + 1) * 128], YSF[:, dt * 512 + s * 128: dt * 512 + (s + 1) * 128], 128, 128)
        for dt in range(8):
            ld(X1M_, X1d[dt * 128:(dt + 1) * 128, c * 512:(c + 1) * 512])
            ld(X1NM_, X1Nd[dt * 128:(dt + 1) * 128, c * 512:(c + 1) * 512])
            ld(YMM, YMOEd[dt * 128:(dt + 1) * 128, c * 512:(c + 1) * 512], eng="gpsimd")
            E("scalar", lambda eng: eng.activation(G2M, YMM, AF.Silu))
            E("vector", lambda eng, _d=dt: eng.tensor_scalar(S2M, X1NM_, DPP[:, _d:_d + 1], None, ALU.mult))
            E("vector", lambda eng, _d=dt: eng.tensor_add(S2M, S2M, YSF[:, _d * 512:(_d + 1) * 512]))
            E("vector", lambda eng: eng.tensor_mul(S2M, S2M, G2M))
            E("vector", lambda eng, _d=dt: eng.tensor_add(XOB[:, _d * 512:(_d + 1) * 512], X1M_, S2M))
        for s in range(4):
            for dt in range(8):
                transpose(1, XOB[:, dt * 512 + s * 128: dt * 512 + (s + 1) * 128], OTM[:, dt * 128:(dt + 1) * 128], 128, 128)
            store_toks.append(st_(out_d[(c * 4 + s) * 128:(c * 4 + s + 1) * 128, :], OTM))

    if debug:
        dbg = {"XN": XN, "HF": HFd, "QF": QFd, "KF": KFd, "KTM": KTMd, "VE": VEd,
               "Y1": Y1d, "H2": H2d, "SS": SSd, "X1": X1d, "X1N": X1Nd,
               "YSSM": YSSMd, "YMOE": YMOEd, "BETAo": BETA, "GATEo": GATEd}
        for nm, t in dbg.items():
            o = nc.dram_tensor("dbg_" + nm, list(t.tensor.shape), F32, kind="ExternalOutput").ap()
            store_toks.append(st_(o[:], t[:]))
    S.emit("sync", None, tuple(store_toks))
    S.build(nc, maxseq=maxseq)
    return nc, S


_nc_cache = [None]


def kernel(**inputs):
    def f(name):
        return np.ascontiguousarray(np.asarray(inputs[name], dtype=np.float32))

    x_full = f("L0_attn_x")
    wgE = f("L1_ffn_w_g").reshape(ME, 8, 128, MMT, 128).transpose(0, 3, 1, 2, 4).copy()
    wuE = f("L1_ffn_w_u").reshape(ME, 8, 128, MMT, 128).transpose(0, 3, 1, 2, 4).copy()

    iotaf = np.broadcast_to(np.arange(128, dtype=np.float32)[None, :], (128, 128)).copy()
    jj, tt = np.meshgrid(np.arange(128), np.arange(128), indexing="ij")
    upinc = (jj <= tt).astype(np.float32)
    upneg = -(jj < tt).astype(np.float32)

    common = {
        "n1w": f("L0_attn_norm1_w").reshape(1, D),
        "n2w": f("L0_attn_norm2_w").reshape(1, D),
        "wq": f("L0_attn_Wq"), "wk": f("L0_attn_Wk"), "wv": f("L0_attn_Wv"), "wo": f("L0_attn_Wo"),
        "bw": f("L0_attn_beta_w").reshape(1, D),
        "bb": f("L0_attn_beta_b").reshape(1, 1),
        "alog": f("L0_attn_alpha_logit").reshape(1, D),
        "wg0": f("L0_attn_w_gate"), "wu0": f("L0_attn_w_up"), "wd0": f("L0_attn_w_down"),
        "cw1": f("L0_ffn_cn_w1"), "cw2": f("L0_ffn_cn_w2"), "cw3": f("L0_ffn_cn_w3"), "cw4": f("L0_ffn_cn_w4"),
        "cb1": f("L0_ffn_cn_b1").reshape(1, D), "cb2": f("L0_ffn_cn_b2").reshape(1, 512),
        "cb3": f("L0_ffn_cn_b3").reshape(1, 256), "cb4": f("L0_ffn_cn_b4").reshape(1, 1),
        "gw1": f("L0_ffn_gc_w1"), "gw2": f("L0_ffn_gc_w2"), "gw3": f("L0_ffn_gc_w3"),
        "gb1": f("L0_ffn_gc_b1").reshape(1, 512), "gb2": f("L0_ffn_gc_b2").reshape(1, 256),
        "gb3": f("L0_ffn_gc_b3").reshape(1, 1),
        "ar": f("L0_ffn_adaptive_range").reshape(1, 1),
        "logA": f("L1_attn_log_A"), "Bp": f("L1_attn_B_param"), "Cp": f("L1_attn_C"),
        "Dv": f("L1_attn_D").reshape(1, D),
        "wgate": f("L1_ffn_w_gate"),
        "wgE": wgE, "wuE": wuE, "wdE": f("L1_ffn_w_d"),
        "idn": np.eye(128, dtype=np.float32),
        "upinc": upinc, "upneg": upneg, "iotaf": iotaf,
        "onesr": np.ones((1, 128), np.float32), "onesc": np.ones((128, 1), np.float32),
        "niop": -np.arange(128, dtype=np.float32).reshape(128, 1),
        "iop": np.arange(128, dtype=np.float32).reshape(128, 1),
    }
    in_maps = [dict(common, x=np.ascontiguousarray(x_full[b])) for b in range(8)]
    if _nc_cache[0] is None:
        _nc_cache[0] = build_program()[0]
    nc = _nc_cache[0]
    res = run_bass_kernel_spmd(nc, in_maps, core_ids=list(range(8)))
    return np.stack([r["out"] for r in res.results], axis=0)


# version 07
# speedup vs baseline: 1.3119x; 1.2327x over previous
import numpy as np
import concourse.bass as bass
import concourse.mybir as mybir
from concourse.bass_utils import run_bass_kernel_spmd

F32 = mybir.dt.float32
AF = mybir.ActivationFunctionType
ALU = mybir.AluOpType
AX = mybir.AxisListType

T = 4096
D = 1024
NS = 64
HID0 = 2048
ME = 8
MM = 1408
MMT = 11
NSUB = T // 128
NCH = T // 512

ENGINES = ("sync", "tensor", "vector", "scalar", "gpsimd")
AUTO_FENCE = True


class Sched:
    def __init__(self):
        self.q = {e: [] for e in ENGINES}
        self.gseq = 0
        self.seqs = {e: [] for e in ENGINES}
        self.marks = []

    def emit(self, eng, fn, deps=(), dma=False):
        lst = self.q[eng]
        tok = (eng, len(lst))
        deps = tuple(d for d in deps if d is not None)
        lst.append([fn, deps, dma, False])
        self.seqs[eng].append(self.gseq)
        self.gseq += 1
        return tok

    def build(self, nc, maxseq=None):
        if maxseq is not None:
            for e in ENGINES:
                keep = sum(1 for s in self.seqs[e] if s < maxseq)
                self.q[e] = self.q[e][:keep]
        for e in ENGINES:
            for rec in self.q[e]:
                if rec[2]:
                    rec[3] = True
                for (de, di) in rec[1]:
                    self.q[de][di][3] = True
        vals = {}
        for e in ENGINES:
            c = 0
            for i, rec in enumerate(self.q[e]):
                if rec[3] and rec[0] is not None:
                    c += 16 if rec[2] else 1
                vals[(e, i)] = c
        import contextlib
        stack = contextlib.ExitStack()
        sems = {e: stack.enter_context(nc.semaphore(f"sem_{e}")) for e in ENGINES}
        with nc.Block() as block:
            for e in ENGINES:
                def body(eng, _e=e):
                    hw = {o: 0 for o in ENGINES}
                    for i, (fn, deps, dma, needed) in enumerate(self.q[_e]):
                        for (de, di) in deps:
                            v = vals[(de, di)]
                            if v > hw[de]:
                                eng.wait_ge(sems[de], v)
                                hw[de] = v
                        if fn is not None:
                            inst = fn(eng)
                            if needed:
                                inst.then_inc(sems[_e], 16 if dma else 1)
                getattr(block, e)(body)
        stack.close()


def build_program(maxseq=None, debug=False):
    nc = bass.Bass("TRN2", target_bir_lowering=False)
    S = Sched()

    def din(name, shape):
        return nc.dram_tensor(name, shape, F32, kind="ExternalInput").ap()

    def dscr(name, shape):
        return nc.dram_tensor(name, shape, F32, kind="Internal").ap()

    x_d = din("x", [T, D])
    n1w_d = din("n1w", [1, D]); n2w_d = din("n2w", [1, D])
    wq_d = din("wq", [D, D]); wk_d = din("wk", [D, D]); wv_d = din("wv", [D, D]); wo_d = din("wo", [D, D])
    bw_d = din("bw", [1, D]); bb_d = din("bb", [1, 1]); alog_d = din("alog", [1, D])
    wg0_d = din("wg0", [D, HID0]); wu0_d = din("wu0", [D, HID0]); wd0_d = din("wd0", [HID0, D])
    cw1_d = din("cw1", [D, D]); cw2_d = din("cw2", [D, 512]); cw3_d = din("cw3", [512, 256]); cw4_d = din("cw4", [256, 1])
    cb1_d = din("cb1", [1, D]); cb2_d = din("cb2", [1, 512]); cb3_d = din("cb3", [1, 256]); cb4_d = din("cb4", [1, 1])
    gw1_d = din("gw1", [D, 512]); gw2_d = din("gw2", [512, 256]); gw3_d = din("gw3", [256, 1])
    gb1_d = din("gb1", [1, 512]); gb2_d = din("gb2", [1, 256]); gb3_d = din("gb3", [1, 1])
    ar_d = din("ar", [1, 1])
    logA_d = din("logA", [D, NS]); Bp_d = din("Bp", [D, NS]); Cp_d = din("Cp", [D, NS]); Dv_d = din("Dv", [1, D])
    wgate_d = din("wgate", [D, ME])
    wgE_d = din("wgE", [ME, MMT, 8, 128, 128])
    wuE_d = din("wuE", [ME, MMT, 8, 128, 128])
    wdE_d = din("wdE", [ME, MM, D])
    id_d = din("idn", [128, 128]); upinc_d = din("upinc", [128, 128]); upneg_d = din("upneg", [128, 128])
    iotaf_d = din("iotaf", [128, 128]); onesr_d = din("onesr", [1, 128]); onesc_d = din("onesc", [128, 1])
    niop_d = din("niop", [128, 1]); iop_d = din("iop", [128, 1])

    out_d = nc.dram_tensor("out", [T, D], F32, kind="ExternalOutput").ap()

    XN = dscr("XN", [D, T]); XF = dscr("XF", [D, T]); HFd = dscr("HFd", [D, T])
    BETA = dscr("BETAd", [T, 1])
    QFd = dscr("QFd", [D, T]); KFd = dscr("KFd", [D, T]); KTMd = dscr("KTMd", [T, D]); VEd = dscr("VEd", [T, D])
    Y1d = dscr("Y1d", [D, T]); H2d = dscr("H2d", [D, T]); HIDd = dscr("HIDd", [HID0, T])
    SSd = dscr("SSd", [D, T]); X1d = dscr("X1d", [D, T]); X1Nd = dscr("X1Nd", [D, T]); X1NTd = dscr("X1NTd", [T, D])
    YSSMd = dscr("YSSMd", [T, D]); YMOEd = dscr("YMOEd", [D, T]); GATEd = dscr("GATEd", [8, T])

    def sb(name, shape):
        return nc.alloc_sbuf_tensor(name, shape, F32).ap()

    IDN = sb("IDN", [128, 128]); UPI = sb("UPI", [128, 128]); UPN = sb("UPN", [128, 128])
    IOF = sb("IOF", [128, 128]); ONR = sb("ONR", [1, 128]); ONC = sb("ONC", [128, 1])
    NIOP = sb("NIOP", [128, 1]); IOP = sb("IOP", [128, 1])
    W1B = sb("W1B", [128, D]); BWB = sb("BWB", [128, D]); BB128 = sb("BB128", [128, 1])
    PINV_E = sb("PINV_E", [128, D]); PFWD_E = sb("PFWD_E", [128, D])
    OMAR = sb("OMAR", [1, D]); AROWV = sb("AROWV", [1, D])
    OMA127 = AROWV
    GS = sb("GS", [128, 8]); N2W = sb("N2W", [128, 8]); DPP = sb("DPP", [128, 8])
    CB1P = sb("CB1P", [128, 8]); CB2P = sb("CB2P", [128, 4]); CB3P = sb("CB3P", [128, 2])
    GB1P = sb("GB1P", [128, 4]); GB2P = sb("GB2P", [128, 2])
    CB4S = sb("CB4S", [1, 1]); GB3S = sb("GB3S", [1, 1]); AR2 = sb("AR2", [1, 1]); GF07 = sb("GF07", [1, 1])
    LNAB = sb("LNAB", [64, D]); CBND = sb("CBND", [64, D]); CPND = sb("CPND", [64, D])
    SMALL = sb("SMALLb", [128, 2048]); ROWS = sb("ROWSb", [1, 2048]); STAT = sb("STATb", [128, 16])
    PA = sb("PAb", [128, 39100])

    PS = [nc.alloc_psum_tensor(f"pb{i}", [128, 512], F32).ap() for i in range(8)]
    ps_writer = [None] * 8
    ps_reader = [None] * 8

    cur = [0]
    def arena_reset():
        cur[0] = 0
    def alloc(n):
        off = cur[0]
        assert off + n <= 39100, f"arena overflow {off+n}"
        cur[0] += n
        return PA[:, off:off + n]

    tok_last = {e: None for e in ENGINES}
    RELAX = [False]
    fence_base = [()]

    def cfence():
        return tuple(t for t in tok_last.values() if t is not None)

    def soft_fence():
        fence_base[0] = cfence()

    def E(eng, fn, deps=(), dma=False, fence=None):
        if fence is None:
            fence = not RELAX[0]
        if fence:
            d = tuple(deps) + cfence()
        else:
            d = tuple(deps) + fence_base[0]
        t = S.emit(eng, fn, d, dma)
        tok_last[eng] = t
        return t

    def barrier():
        toks = cfence()
        for e in ENGINES:
            tok_last[e] = S.emit(e, None, toks)
        S.marks.append(S.gseq)

    def mm_group(bank, ncols, pairs, deps=(), col0=0, rows=128):
        dl = list(deps)
        if ps_reader[bank] is not None:
            dl.append(ps_reader[bank])
        t = None
        n = len(pairs)
        for i, (lhsT, rhs) in enumerate(pairs):
            def fn(eng, _l=lhsT, _r=rhs, _s=(i == 0), _e=(i == n - 1)):
                return eng.matmul(PS[bank][0:rows, col0:col0 + ncols], _l, _r,
                                  start=_s, stop=_e, skip_group_check=True)
            t = E("tensor", fn, tuple(dl) if i == 0 else (), fence=(None if i == 0 else False))
        ps_writer[bank] = t
        return t

    def mm_acc(bank, ncols, pairs, deps=(), col0=0, rows=128):
        t = None
        for i, (lhsT, rhs) in enumerate(pairs):
            def fn(eng, _l=lhsT, _r=rhs):
                return eng.matmul(PS[bank][0:rows, col0:col0 + ncols], _l, _r,
                                  start=False, stop=True, skip_group_check=True)
            t = E("tensor", fn, tuple(deps) if i == 0 else (), fence=(None if i == 0 else False))
        ps_writer[bank] = t
        return t

    def ps_read(bank, eng, fn, deps=()):
        t = E(eng, fn, tuple(deps) + ((ps_writer[bank],) if ps_writer[bank] else ()))
        ps_reader[bank] = t
        return t

    def transpose(bank, src_ap, dst_ap, p, c, deps=()):
        dl = list(deps)
        if ps_reader[bank] is not None:
            dl.append(ps_reader[bank])
        tt = E("tensor", lambda eng, _s=src_ap, _p=p, _c=c: eng.transpose(
            PS[bank][0:_c, 0:_p], _s, IDN[0:_p, 0:_p]), tuple(dl))
        ps_writer[bank] = tt
        return ps_read(bank, "scalar", lambda eng, _d=dst_ap, _p=p, _c=c: eng.copy(_d, PS[bank][0:_c, 0:_p]))

    def rank1(bank, row_ap, ncols, deps=()):
        dl = list(deps)
        if ps_reader[bank] is not None:
            dl.append(ps_reader[bank])
        t = E("tensor", lambda eng, _r=row_ap, _n=ncols: eng.matmul(
            PS[bank][:, 0:_n], ONR[:], _r, start=True, stop=True, skip_group_check=True), tuple(dl))
        ps_writer[bank] = t
        return t

    def ld(dst, src, deps=(), eng="sync"):
        return E(eng, lambda e, _d=dst, _s=src: e.dma_start(_d, _s), deps, dma=True)

    def st_(dst, src, deps=(), eng="gpsimd"):
        return E(eng, lambda e, _d=dst, _s=src: e.dma_start(_d, _s), deps, dma=True)

    store_toks = []

    ld(IDN[:], id_d[:]); ld(UPI[:], upinc_d[:]); ld(UPN[:], upneg_d[:])
    ld(IOF[:], iotaf_d[:]); ld(ONR[:], onesr_d[:]); ld(ONC[:], onesc_d[:])
    ld(NIOP[:], niop_d[:]); ld(IOP[:], iop_d[:])

    def row_bcast(dst, src_row_ap, width):
        for h in range((width + 511) // 512):
            w = min(512, width - h * 512)
            rank1(0, src_row_ap[:, h * 512:h * 512 + w], w)
            ps_read(0, "scalar", lambda eng, _d=dst, _h=h, _w=w: eng.copy(_d[:, _h * 512:_h * 512 + _w], PS[0][:, 0:_w]))

    t = ld(ROWS[0:1, 0:D], n1w_d[:])
    row_bcast(W1B, ROWS[0:1, 0:D], D)
    ld(ROWS[0:1, 0:D], bw_d[:])
    row_bcast(BWB, ROWS[0:1, 0:D], D)
    ld(ROWS[0:1, 1024:1025], bb_d[:])
    rank1(0, ROWS[0:1, 1024:1025], 1)
    ps_read(0, "scalar", lambda eng: eng.copy(BB128[:], PS[0][:, 0:1]))

    ld(ROWS[0:1, 0:D], alog_d[:])
    E("scalar", lambda eng: eng.activation(AROWV[:], ROWS[0:1, 0:D], AF.Sigmoid))
    E("vector", lambda eng: eng.tensor_scalar(OMAR[:], AROWV[:], -1.0, 1.0, ALU.mult, ALU.add))
    lnoma = ROWS[0:1, 1024:1024 + D]
    E("scalar", lambda eng: eng.activation(lnoma, OMAR[:], AF.Ln))
    A_PP = SMALL[:, 0:8]; LOM_PP = SMALL[:, 8:16]; NLOM_PP = SMALL[:, 16:24]
    STG = SMALL[:, 24:152]
    for dt in range(8):
        transpose(1, AROWV[:, dt * 128:(dt + 1) * 128], A_PP[:, dt:dt + 1], 1, 128)
    for dt in range(8):
        transpose(1, lnoma[:, dt * 128:(dt + 1) * 128], LOM_PP[:, dt:dt + 1], 1, 128)
    E("vector", lambda eng: eng.tensor_scalar_mul(NLOM_PP[:], LOM_PP[:], -1.0))
    E("vector", lambda eng: eng.tensor_scalar_mul(OMA127[:], lnoma, 127.0))
    E("scalar", lambda eng: eng.activation(OMA127[:], OMA127[:], AF.Exp))
    for dt in range(8):
        E("scalar", lambda eng, _dt=dt: eng.activation(STG, IOF[:], AF.Exp, scale=NLOM_PP[:, _dt:_dt + 1]))
        E("vector", lambda eng, _dt=dt: eng.tensor_scalar(STG, STG, A_PP[:, _dt:_dt + 1], None, ALU.mult))
        transpose(1, STG, PINV_E[:, dt * 128:(dt + 1) * 128], 128, 128)
        E("scalar", lambda eng, _dt=dt: eng.activation(STG, IOF[:], AF.Exp, scale=LOM_PP[:, _dt:_dt + 1]))
        transpose(1, STG, PFWD_E[:, dt * 128:(dt + 1) * 128], 128, 128)

    def row_to_pp(dst, src_d, width):
        ld(ROWS[0:1, 0:width], src_d[:])
        for dt in range(width // 128):
            transpose(1, ROWS[0:1, dt * 128:(dt + 1) * 128], dst[:, dt:dt + 1], 1, 128)

    row_to_pp(N2W, n2w_d, D)
    row_to_pp(DPP, Dv_d, D)
    row_to_pp(CB1P, cb1_d, D)
    row_to_pp(CB2P, cb2_d, 512)
    row_to_pp(CB3P, cb3_d, 256)
    row_to_pp(GB1P, gb1_d, 512)
    row_to_pp(GB2P, gb2_d, 256)
    ld(CB4S[:], cb4_d[:]); ld(GB3S[:], gb3_d[:])
    ld(ROWS[0:1, 1024:1025], ar_d[:])
    E("vector", lambda eng: eng.tensor_scalar_mul(AR2[:], ROWS[0:1, 1024:1025], 2.0))

    LA_T = SMALL[:, 152:216]
    for (src_d, dst) in ((logA_d, LNAB), (Bp_d, CBND), (Cp_d, CPND)):
        for dt in range(8):
            ld(LA_T, src_d[dt * 128:(dt + 1) * 128, :])
            transpose(1, LA_T, dst[0:64, dt * 128:(dt + 1) * 128], 128, 64)
    arena_reset()
    TMPA = alloc(1024)[0:64, :]
    RECA = alloc(1024)[0:64, :]
    E("scalar", lambda eng: eng.activation(TMPA, LNAB[0:64, :], AF.Exp))
    E("vector", lambda eng: eng.tensor_scalar_mul(TMPA, TMPA, -1.0))
    E("vector", lambda eng: eng.reciprocal(RECA, TMPA))
    E("vector", lambda eng: eng.tensor_scalar_mul(LNAB[0:64, :], TMPA, 0.1))
    E("scalar", lambda eng: eng.activation(TMPA, LNAB[0:64, :], AF.Exp))
    E("vector", lambda eng: eng.tensor_scalar(TMPA, TMPA, 1.0, None, ALU.subtract))
    E("vector", lambda eng: eng.tensor_mul(TMPA, TMPA, RECA))
    E("vector", lambda eng: eng.tensor_mul(CBND[0:64, :], CBND[0:64, :], TMPA))
    E("vector", lambda eng: eng.tensor_mul(CBND[0:64, :], CBND[0:64, :], CPND[0:64, :]))
    LNABd = dscr("LNABd", [64, D]); CBNDd = dscr("CBNDd", [64, D])
    store_toks.append(st_(LNABd[:], LNAB[0:64, :]))
    store_toks.append(st_(CBNDd[:], CBND[0:64, :]))
    E("vector", lambda eng: eng.memset(GS[:], 0.0))
    barrier()

    arena_reset()
    XT = alloc(1024); XSQ = alloc(1024); XNT = alloc(1024); HT = alloc(1024)
    FMB = alloc(1024); FMB2 = alloc(1024)
    s1 = STAT[:, 0:1]; s2 = STAT[:, 1:2]; s3 = STAT[:, 2:3]; s4 = STAT[:, 3:4]
    for u in range(NSUB):
        ld(XT, x_d[u * 128:(u + 1) * 128, :])
        E("scalar", lambda eng: eng.activation(XSQ, XT, AF.Square))
        E("vector", lambda eng: eng.reduce_sum(s1, XT, AX.X))
        E("vector", lambda eng: eng.reduce_sum(s2, XSQ, AX.X))
        E("vector", lambda eng: eng.tensor_scalar_mul(s1, s1, 1.0 / D))
        E("vector", lambda eng: eng.tensor_mul(s3, s1, s1))
        E("vector", lambda eng: eng.scalar_tensor_tensor(s2, s2, 1.0 / D, s3, ALU.mult, ALU.subtract))
        E("vector", lambda eng: eng.tensor_scalar_add(s2, s2, 1e-5))
        E("scalar", lambda eng: eng.activation(s2, s2, AF.Sqrt))
        E("vector", lambda eng: eng.reciprocal(s2, s2))
        E("vector", lambda eng: eng.tensor_scalar(XNT, XT, s1, s2, ALU.subtract, ALU.mult))
        E("scalar", lambda eng: eng.activation(XSQ, XNT, AF.Square))
        E("vector", lambda eng: eng.reduce_sum(s3, XSQ, AX.X))
        E("vector", lambda eng: eng.tensor_scalar(s3, s3, 1.0 / D, 1e-6, ALU.mult, ALU.add))
        E("scalar", lambda eng: eng.activation(s3, s3, AF.Sqrt))
        E("vector", lambda eng: eng.reciprocal(s3, s3))
        E("vector", lambda eng: eng.tensor_scalar(HT, XNT, s3, None, ALU.mult))
        E("vector", lambda eng: eng.tensor_mul(HT, HT, W1B[:]))
        E("vector", lambda eng: eng.tensor_mul(XSQ, HT, BWB[:]))
        E("vector", lambda eng: eng.reduce_sum(s4, XSQ, AX.X))
        E("scalar", lambda eng: eng.activation(s4, s4, AF.Sigmoid, bias=BB128[:]))
        store_toks.append(st_(BETA[u * 128:(u + 1) * 128, :], s4))
        for dt in range(8):
            transpose(2, XNT[:, dt * 128:(dt + 1) * 128], FMB[:, dt * 128:(dt + 1) * 128], 128, 128)
            transpose(3, HT[:, dt * 128:(dt + 1) * 128], FMB2[:, dt * 128:(dt + 1) * 128], 128, 128)
            store_toks.append(st_(XN[dt * 128:(dt + 1) * 128, u * 128:(u + 1) * 128], FMB[:, dt * 128:(dt + 1) * 128]))
            store_toks.append(st_(HFd[dt * 128:(dt + 1) * 128, u * 128:(u + 1) * 128], FMB2[:, dt * 128:(dt + 1) * 128]))
            E("vector", lambda eng, _dt=dt: eng.reduce_sum(STAT[:, 8 + _dt:9 + _dt], FMB[:, _dt * 128:(_dt + 1) * 128], AX.X))
            E("vector", lambda eng, _dt=dt: eng.tensor_add(GS[:, _dt:_dt + 1], GS[:, _dt:_dt + 1], STAT[:, 8 + _dt:9 + _dt]))
        for dt in range(8):
            transpose(2, XT[:, dt * 128:(dt + 1) * 128], FMB[:, dt * 128:(dt + 1) * 128], 128, 128)
            store_toks.append(st_(XF[dt * 128:(dt + 1) * 128, u * 128:(u + 1) * 128], FMB[:, dt * 128:(dt + 1) * 128]))
    barrier()

    arena_reset()
    WQ = [alloc(1024) for _ in range(8)]
    WK = [alloc(1024) for _ in range(8)]
    WV = [alloc(1024) for _ in range(8)]
    HFB = alloc(4096); KFB = alloc(4096); VFB = alloc(4096)
    KSQ = VFB
    QM = alloc(512); VTM = alloc(1024)
    OMAC = HFB[:, 0:1024]
    KTMS = VTM
    ZB = KFB[:, 0:1024]
    INVB = QM
    CARRY = ROWS[0:1, 0:D]
    KROW = ROWS[0:1, 1024:1536]
    CTMP = ROWS[0:1, 1536:1536 + 512]

    for kp in range(8):
        ld(WQ[kp], wq_d[kp * 128:(kp + 1) * 128, :])
        ld(WK[kp], wk_d[kp * 128:(kp + 1) * 128, :])
        ld(WV[kp], wv_d[kp * 128:(kp + 1) * 128, :])
    E("vector", lambda eng: eng.memset(CARRY, 0.0))

    for c in range(NCH):
        for kp in range(8):
            ld(HFB[:, kp * 512:(kp + 1) * 512], HFd[kp * 128:(kp + 1) * 128, c * 512:(c + 1) * 512])
        for m in range(8):
            mm_group(0, 512, [(WQ[kp][:, m * 128:(m + 1) * 128], HFB[:, kp * 512:(kp + 1) * 512]) for kp in range(8)])
            ps_read(0, "scalar", lambda eng: eng.copy(QM, PS[0][:, 0:512]))
            store_toks.append(st_(QFd[m * 128:(m + 1) * 128, c * 512:(c + 1) * 512], QM))
        for m in range(8):
            mm_group(1, 512, [(WK[kp][:, m * 128:(m + 1) * 128], HFB[:, kp * 512:(kp + 1) * 512]) for kp in range(8)])
            ps_read(1, "scalar", lambda eng, _m=m: eng.copy(KFB[:, _m * 512:(_m + 1) * 512], PS[1][:, 0:512]))
        for m in range(8):
            E("scalar", lambda eng, _m=m: eng.activation(KSQ[:, _m * 512:(_m + 1) * 512], KFB[:, _m * 512:(_m + 1) * 512], AF.Square))
        mm_group(2, 512, [(ONC[:], KSQ[:, m * 512:(m + 1) * 512]) for m in range(8)], rows=1)
        ps_read(2, "scalar", lambda eng: eng.activation(KROW, PS[2][0:1, 0:512], AF.Sqrt))
        E("vector", lambda eng: eng.tensor_scalar_max(KROW, KROW, 1e-12))
        E("vector", lambda eng: eng.reciprocal(KROW, KROW))
        rank1(2, KROW, 512)
        ps_read(2, "scalar", lambda eng: eng.copy(INVB, PS[2][:, 0:512]))
        for m in range(8):
            E("vector", lambda eng, _m=m: eng.tensor_mul(KFB[:, _m * 512:(_m + 1) * 512], KFB[:, _m * 512:(_m + 1) * 512], INVB))
            store_toks.append(st_(KFd[m * 128:(m + 1) * 128, c * 512:(c + 1) * 512], KFB[:, m * 512:(m + 1) * 512]))
        for s in range(4):
            u = c * 4 + s
            for dt in range(8):
                transpose(3, KFB[:, dt * 512 + s * 128: dt * 512 + (s + 1) * 128], KTMS[:, dt * 128:(dt + 1) * 128], 128, 128)
            store_toks.append(st_(KTMd[u * 128:(u + 1) * 128, :], KTMS))
        for m in range(8):
            mm_group(4, 512, [(WV[kp][:, m * 128:(m + 1) * 128], HFB[:, kp * 512:(kp + 1) * 512]) for kp in range(8)])
            ps_read(4, "scalar", lambda eng, _m=m: eng.copy(VFB[:, _m * 512:(_m + 1) * 512], PS[4][:, 0:512]))
        for s in range(4):
            u = c * 4 + s
            for dt in range(8):
                transpose(5, VFB[:, dt * 512 + s * 128: dt * 512 + (s + 1) * 128], VTM[:, dt * 128:(dt + 1) * 128], 128, 128)
            E("vector", lambda eng: eng.tensor_mul(ZB, PINV_E[:], VTM))
            E("vector", lambda eng: eng.tensor_mul(OMAC[0:1, :], CARRY, OMAR[:]))
            E("vector", lambda eng: eng.tensor_add(ZB[0:1, :], ZB[0:1, :], OMAC[0:1, :]))
            mm_group(6, 512, [(UPI[:], ZB[:, 0:512])])
            mm_group(7, 512, [(UPI[:], ZB[:, 512:1024])])
            ps_read(6, "vector", lambda eng: eng.tensor_mul(VTM[:, 0:512], PFWD_E[:, 0:512], PS[6][:, 0:512]))
            ps_read(7, "vector", lambda eng: eng.tensor_mul(VTM[:, 512:1024], PFWD_E[:, 512:1024], PS[7][:, 0:512]))
            mm_group(2, 512, [(ONC[:], ZB[:, 0:512])], rows=1)
            ps_read(2, "vector", lambda eng: eng.tensor_mul(CARRY[:, 0:512], PS[2][0:1, 0:512], OMA127[:, 0:512]))
            mm_group(3, 512, [(ONC[:], ZB[:, 512:1024])], rows=1)
            ps_read(3, "vector", lambda eng: eng.tensor_mul(CARRY[:, 512:1024], PS[3][0:1, 0:512], OMA127[:, 512:1024]))
            store_toks.append(st_(VEd[u * 128:(u + 1) * 128, :], VTM))
    barrier()

    arena_reset()
    WO = [alloc(1024) for _ in range(8)]
    STT = [alloc(1024) for _ in range(8)]
    QFB = alloc(4096); KFB3 = alloc(4096)
    KTMS2 = alloc(1024); VES = alloc(1024)
    ATN = alloc(1024); AFB = alloc(4096)
    UB = alloc(1024); BB_ = alloc(1024)
    NT = alloc(128); RT_A = alloc(128); RT_B = alloc(128)
    MP0 = alloc(128); MP1 = alloc(128); MTP0 = alloc(128); MTP1 = alloc(128)
    MQK = alloc(128); BBC = alloc(128); BETR = alloc(128)
    BETP = alloc(4)
    Y1M = alloc(512); XNM = alloc(512); H2M = alloc(512); Y1SQ = alloc(512); RSTB = alloc(512)
    Y1ROW = ROWS[0:1, 1024:1536]

    for kp in range(8):
        ld(WO[kp], wo_d[kp * 128:(kp + 1) * 128, :])
        E("gpsimd", lambda eng, _k=kp: eng.memset(STT[_k], 0.0))

    for c in range(NCH):
        for kp in range(8):
            ld(QFB[:, kp * 512:(kp + 1) * 512], QFd[kp * 128:(kp + 1) * 128, c * 512:(c + 1) * 512])
            ld(KFB3[:, kp * 512:(kp + 1) * 512], KFd[kp * 128:(kp + 1) * 128, c * 512:(c + 1) * 512])
        for s in range(4):
            u = c * 4 + s
            ld(BETP[:, s:s + 1], BETA[u * 128:(u + 1) * 128, :])
            ld(KTMS2, KTMd[u * 128:(u + 1) * 128, :])
            ld(VES, VEd[u * 128:(u + 1) * 128, :])
            ksl = [KFB3[:, kp * 512 + s * 128: kp * 512 + (s + 1) * 128] for kp in range(8)]
            qsl = [QFB[:, kp * 512 + s * 128: kp * 512 + (s + 1) * 128] for kp in range(8)]
            bpp = BETP[:, s:s + 1]
            transpose(0, bpp, BETR[0:1, 0:128], 128, 1)
            rank1(0, BETR[0:1, 0:128], 128)
            ps_read(0, "scalar", lambda eng: eng.copy(BBC, PS[0][:, 0:128]))
            mm_group(1, 128, [(ksl[kp], ksl[kp]) for kp in range(8)])
            ps_read(1, "vector", lambda eng: eng.tensor_mul(NT, PS[1][:, 0:128], UPN[:]))
            E("vector", lambda eng: eng.tensor_mul(NT, NT, BBC))
            E("vector", lambda eng: eng.tensor_add(RT_A, NT, IDN[:]))
            transpose(2, NT, MP0, 128, 128)
            E("scalar", lambda eng: eng.copy(MTP0, NT))
            rt_cur, rt_oth = RT_A, RT_B
            m_cur, mt_cur = MP0, MTP0
            m_nxt, mt_nxt = MP1, MTP1
            for lvl in range(6):
                mm_group(3, 128, [(mt_cur, m_cur)])
                ps_read(3, "scalar", lambda eng, _m=m_nxt: eng.copy(_m, PS[3][:, 0:128]))
                mm_group(2, 128, [(m_nxt, rt_cur)])
                ps_read(2, "vector", lambda eng, _o=rt_oth, _c=rt_cur: eng.tensor_add(_o, _c, PS[2][:, 0:128]))
                rt_cur, rt_oth = rt_oth, rt_cur
                if lvl < 5:
                    transpose(3, m_nxt, mt_nxt, 128, 128)
                    m_cur, mt_cur = m_nxt, mt_nxt
                    m_nxt, mt_nxt = (MP0, MTP0) if m_cur is MP1 else (MP1, MTP1)
            mm_group(4, 512, [(ksl[kp], STT[kp][:, 0:512]) for kp in range(8)])
            mm_group(5, 512, [(ksl[kp], STT[kp][:, 512:1024]) for kp in range(8)])
            ps_read(4, "vector", lambda eng: eng.scalar_tensor_tensor(BB_[:, 0:512], PS[4][:, 0:512], -1.0, VES[:, 0:512], ALU.mult, ALU.add))
            ps_read(5, "vector", lambda eng: eng.scalar_tensor_tensor(BB_[:, 512:1024], PS[5][:, 0:512], -1.0, VES[:, 512:1024], ALU.mult, ALU.add))
            E("vector", lambda eng, _b=bpp: eng.tensor_scalar(BB_, BB_, _b, None, ALU.mult))
            mm_group(4, 512, [(rt_cur, BB_[:, 0:512])])
            ps_read(4, "scalar", lambda eng: eng.copy(UB[:, 0:512], PS[4][:, 0:512]))
            mm_group(5, 512, [(rt_cur, BB_[:, 512:1024])])
            ps_read(5, "scalar", lambda eng: eng.copy(UB[:, 512:1024], PS[5][:, 0:512]))
            mm_group(6, 512, [(qsl[kp], STT[kp][:, 0:512]) for kp in range(8)])
            mm_group(7, 512, [(qsl[kp], STT[kp][:, 512:1024]) for kp in range(8)])
            mm_group(1, 128, [(ksl[kp], qsl[kp]) for kp in range(8)])
            ps_read(1, "vector", lambda eng: eng.tensor_mul(MQK, PS[1][:, 0:128], UPI[:]))
            mm_acc(6, 512, [(MQK, UB[:, 0:512])])
            mm_acc(7, 512, [(MQK, UB[:, 512:1024])])
            ps_read(6, "scalar", lambda eng: eng.copy(ATN[:, 0:512], PS[6][:, 0:512]))
            ps_read(7, "scalar", lambda eng: eng.copy(ATN[:, 512:1024], PS[7][:, 0:512]))
            for m in range(8):
                mm_group(4, 512, [(KTMS2[:, m * 128:(m + 1) * 128], UB[:, 0:512])])
                ps_read(4, "vector", lambda eng, _m=m: eng.tensor_add(STT[_m][:, 0:512], STT[_m][:, 0:512], PS[4][:, 0:512]))
                mm_group(5, 512, [(KTMS2[:, m * 128:(m + 1) * 128], UB[:, 512:1024])])
                ps_read(5, "vector", lambda eng, _m=m: eng.tensor_add(STT[_m][:, 512:1024], STT[_m][:, 512:1024], PS[5][:, 0:512]))
            for dt in range(8):
                transpose(0, ATN[:, dt * 128:(dt + 1) * 128], AFB[:, dt * 512 + s * 128: dt * 512 + (s + 1) * 128], 128, 128)
        for m in range(8):
            mm_group(0, 512, [(WO[kp][:, m * 128:(m + 1) * 128], AFB[:, kp * 512:(kp + 1) * 512]) for kp in range(8)])
            ld(XNM, XN[m * 128:(m + 1) * 128, c * 512:(c + 1) * 512])
            ps_read(0, "vector", lambda eng: eng.tensor_add(Y1M, PS[0][:, 0:512], XNM))
            store_toks.append(st_(Y1d[m * 128:(m + 1) * 128, c * 512:(c + 1) * 512], Y1M))
            E("scalar", lambda eng: eng.activation(Y1SQ, Y1M, AF.Square))
            if m == 0:
                mm_group(1, 512, [(ONC[:], Y1SQ)], rows=1)
            else:
                mm_acc(1, 512, [(ONC[:], Y1SQ)], rows=1)
        ps_read(1, "vector", lambda eng: eng.tensor_scalar(Y1ROW, PS[1][0:1, 0:512], 1.0 / D, 1e-6, ALU.mult, ALU.add))
        E("scalar", lambda eng: eng.activation(Y1ROW, Y1ROW, AF.Sqrt))
        E("vector", lambda eng: eng.reciprocal(Y1ROW, Y1ROW))
        rank1(1, Y1ROW, 512)
        ps_read(1, "scalar", lambda eng: eng.copy(RSTB, PS[1][:, 0:512]))
        for m in range(8):
            ld(Y1M, Y1d[m * 128:(m + 1) * 128, c * 512:(c + 1) * 512])
            E("vector", lambda eng: eng.tensor_mul(H2M, Y1M, RSTB))
            E("vector", lambda eng, _m=m: eng.tensor_scalar(H2M, H2M, N2W[:, _m:_m + 1], None, ALU.mult))
            store_toks.append(st_(H2d[m * 128:(m + 1) * 128, c * 512:(c + 1) * 512], H2M))
    barrier()

    arena_reset()
    WGB = [alloc(2048) for _ in range(8)]
    WUB = [alloc(2048) for _ in range(8)]
    H2F = alloc(4096); HIDM = alloc(512); HIDT = alloc(512)
    for kp in range(8):
        ld(WGB[kp], wg0_d[kp * 128:(kp + 1) * 128, :])
        ld(WUB[kp], wu0_d[kp * 128:(kp + 1) * 128, :])
    for c in range(NCH):
        for kp in range(8):
            ld(H2F[:, kp * 512:(kp + 1) * 512], H2d[kp * 128:(kp + 1) * 128, c * 512:(c + 1) * 512])
        for m in range(16):
            mm_group(0, 512, [(WGB[kp][:, m * 128:(m + 1) * 128], H2F[:, kp * 512:(kp + 1) * 512]) for kp in range(8)])
            mm_group(1, 512, [(WUB[kp][:, m * 128:(m + 1) * 128], H2F[:, kp * 512:(kp + 1) * 512]) for kp in range(8)])
            ps_read(0, "scalar", lambda eng: eng.activation(HIDT, PS[0][:, 0:512], AF.Silu))
            ps_read(1, "vector", lambda eng: eng.tensor_mul(HIDM, HIDT, PS[1][:, 0:512]))
            store_toks.append(st_(HIDd[m * 128:(m + 1) * 128, c * 512:(c + 1) * 512], HIDM))
    barrier()

    arena_reset()
    WDB = [alloc(1024) for _ in range(16)]
    HIDF = alloc(8192); Y1M2 = alloc(512); SM_ = alloc(512)
    for kp in range(16):
        ld(WDB[kp], wd0_d[kp * 128:(kp + 1) * 128, :])
    for c in range(NCH):
        for kp in range(16):
            ld(HIDF[:, kp * 512:(kp + 1) * 512], HIDd[kp * 128:(kp + 1) * 128, c * 512:(c + 1) * 512])
        for m in range(8):
            mm_group(0, 512, [(WDB[kp][:, m * 128:(m + 1) * 128], HIDF[:, kp * 512:(kp + 1) * 512]) for kp in range(16)])
            ld(Y1M2, Y1d[m * 128:(m + 1) * 128, c * 512:(c + 1) * 512])
            ps_read(0, "vector", lambda eng: eng.tensor_add(SM_, PS[0][:, 0:512], Y1M2))
            store_toks.append(st_(SSd[m * 128:(m + 1) * 128, c * 512:(c + 1) * 512], SM_))
    barrier()

    arena_reset()
    CW1 = [alloc(1024) for _ in range(8)]
    CW2 = [alloc(512) for _ in range(8)]
    CW3 = [alloc(256) for _ in range(4)]
    CW4 = [alloc(1) for _ in range(2)]
    GW1B = alloc(4096)
    GW1 = [GW1B[:, kp * 512:(kp + 1) * 512] for kp in range(8)]
    GW2 = [alloc(256) for _ in range(4)]
    GW3 = [alloc(1) for _ in range(2)]
    OUTA = alloc(4096)
    OUTB = GW1B
    C1B = alloc(4096); C2B = alloc(2048); C3B = alloc(1024)
    X1FB = alloc(4096)
    X1NB = OUTA
    XM_ = C2B[:, 0:512]; SM2 = C2B[:, 512:1024]; X1SQ = C2B[:, 1024:1536]
    MB_ = C3B[:, 0:512]; RB_ = C3B[:, 512:1024]
    CPB = alloc(512); TMS_ = alloc(1024)
    GH1 = STAT[:, 8:12]; GH2 = STAT[:, 12:14]
    TFR = ROWS[0:1, 1024:1536]
    MROW = ROWS[0:1, 0:512]; VROW = ROWS[0:1, 512:1024]

    for kp in range(8):
        ld(CW1[kp], cw1_d[kp * 128:(kp + 1) * 128, :])
        ld(CW2[kp], cw2_d[kp * 128:(kp + 1) * 128, :])
        ld(GW1[kp], gw1_d[kp * 128:(kp + 1) * 128, :])
    for kp in range(4):
        ld(CW3[kp], cw3_d[kp * 128:(kp + 1) * 128, :])
        ld(GW2[kp], gw2_d[kp * 128:(kp + 1) * 128, :])
    for kp in range(2):
        ld(CW4[kp], cw4_d[kp * 128:(kp + 1) * 128, :])
        ld(GW3[kp], gw3_d[kp * 128:(kp + 1) * 128, :])

    E("vector", lambda eng: eng.tensor_scalar_mul(GS[:], GS[:], 1.0 / T))
    for mt in range(4):
        mm_group(0, 1, [(GW1[kp][:, mt * 128:(mt + 1) * 128], GS[:, kp:kp + 1]) for kp in range(8)], col0=mt)
    for mt in range(4):
        ps_read(0, "scalar", lambda eng, _m=mt: eng.activation(GH1[:, _m:_m + 1], PS[0][:, _m:_m + 1], AF.Gelu, bias=GB1P[:, _m:_m + 1]))
    for mt in range(2):
        mm_group(1, 1, [(GW2[kp][:, mt * 128:(mt + 1) * 128], GH1[:, kp:kp + 1]) for kp in range(4)], col0=mt)
    for mt in range(2):
        ps_read(1, "scalar", lambda eng, _m=mt: eng.activation(GH2[:, _m:_m + 1], PS[1][:, _m:_m + 1], AF.Gelu, bias=GB2P[:, _m:_m + 1]))
    mm_group(2, 1, [(GW3[kp], GH2[:, kp:kp + 1]) for kp in range(2)], rows=1)
    ps_read(2, "scalar", lambda eng: eng.activation(GF07[:], PS[2][0:1, 0:1], AF.Sigmoid, bias=GB3S[:]))
    E("vector", lambda eng: eng.tensor_scalar_mul(GF07[:], GF07[:], 0.7))

    twist_map = [(4, 1.0), (5, 1.0), (6, -1.0), (7, -1.0), (0, -1.0), (1, -1.0), (2, 1.0), (3, 1.0)]
    for c in range(NCH):
        for dt in range(8):
            ld(OUTA[:, dt * 512:(dt + 1) * 512], XN[dt * 128:(dt + 1) * 128, c * 512:(c + 1) * 512])
        cur_out, oth_out = OUTA, OUTB
        for cyc in range(3):
            for m in range(8):
                mm_group(0, 512, [(CW1[kp][:, m * 128:(m + 1) * 128], cur_out[:, kp * 512:(kp + 1) * 512]) for kp in range(8)])
                ps_read(0, "scalar", lambda eng, _m=m: eng.activation(C1B[:, _m * 512:(_m + 1) * 512], PS[0][:, 0:512], AF.Gelu, bias=CB1P[:, _m:_m + 1]))
            for m in range(4):
                mm_group(1, 512, [(CW2[kp][:, m * 128:(m + 1) * 128], C1B[:, kp * 512:(kp + 1) * 512]) for kp in range(8)])
                ps_read(1, "scalar", lambda eng, _m=m: eng.activation(C2B[:, _m * 512:(_m + 1) * 512], PS[1][:, 0:512], AF.Gelu, bias=CB2P[:, _m:_m + 1]))
            for m in range(2):
                mm_group(2, 512, [(CW3[kp][:, m * 128:(m + 1) * 128], C2B[:, kp * 512:(kp + 1) * 512]) for kp in range(4)])
                ps_read(2, "scalar", lambda eng, _m=m: eng.activation(C3B[:, _m * 512:(_m + 1) * 512], PS[2][:, 0:512], AF.Gelu, bias=CB3P[:, _m:_m + 1]))
            mm_group(3, 512, [(CW4[kp], C3B[:, kp * 512:(kp + 1) * 512]) for kp in range(2)], rows=1)
            ps_read(3, "scalar", lambda eng: eng.activation(TFR, PS[3][0:1, 0:512], AF.Sigmoid, bias=CB4S[:]))
            E("vector", lambda eng: eng.tensor_scalar(TFR, TFR, 0.3, GF07[:], ALU.mult, ALU.add))
            E("vector", lambda eng: eng.tensor_scalar(TFR, TFR, 0.5, AR2[:], ALU.subtract, ALU.mult))
            E("vector", lambda eng: eng.tensor_scalar_add(TFR, TFR, 0.1))
            rank1(3, TFR, 512)
            ps_read(3, "scalar", lambda eng: eng.copy(CPB, PS[3][:, 0:512]))
            for dt in range(8):
                src, sign = twist_map[dt]
                E("vector", lambda eng, _s=src, _g=sign, _c=cur_out: eng.scalar_tensor_tensor(TMS_[:, 0:512], _c[:, _s * 512:(_s + 1) * 512], _g, CPB, ALU.mult, ALU.mult))
                E("vector", lambda eng, _d=dt, _c=cur_out, _o=oth_out: eng.tensor_add(_o[:, _d * 512:(_d + 1) * 512], _c[:, _d * 512:(_d + 1) * 512], TMS_[:, 0:512]))
            cur_out, oth_out = oth_out, cur_out
        for dt in range(8):
            E("scalar", lambda eng, _d=dt, _c=cur_out: eng.activation(_c[:, _d * 512:(_d + 1) * 512], _c[:, _d * 512:(_d + 1) * 512], AF.Silu))
            ld(XM_, XF[dt * 128:(dt + 1) * 128, c * 512:(c + 1) * 512])
            ld(SM2, SSd[dt * 128:(dt + 1) * 128, c * 512:(c + 1) * 512], eng="gpsimd")
            E("vector", lambda eng, _d=dt, _c=cur_out: eng.tensor_mul(TMS_[:, 0:512], _c[:, _d * 512:(_d + 1) * 512], SM2))
            E("vector", lambda eng, _d=dt: eng.tensor_add(X1FB[:, _d * 512:(_d + 1) * 512], XM_, TMS_[:, 0:512]))
            store_toks.append(st_(X1d[dt * 128:(dt + 1) * 128, c * 512:(c + 1) * 512], X1FB[:, dt * 512:(dt + 1) * 512]))
            if dt == 0:
                mm_group(4, 512, [(ONC[:], X1FB[:, dt * 512:(dt + 1) * 512])], rows=1)
            else:
                mm_acc(4, 512, [(ONC[:], X1FB[:, dt * 512:(dt + 1) * 512])], rows=1)
            E("scalar", lambda eng, _d=dt: eng.activation(X1SQ, X1FB[:, _d * 512:(_d + 1) * 512], AF.Square))
            if dt == 0:
                mm_group(5, 512, [(ONC[:], X1SQ)], rows=1)
            else:
                mm_acc(5, 512, [(ONC[:], X1SQ)], rows=1)
        ps_read(4, "vector", lambda eng: eng.tensor_scalar_mul(MROW, PS[4][0:1, 0:512], 1.0 / D))
        E("vector", lambda eng: eng.tensor_mul(VROW, MROW, MROW))
        ps_read(5, "vector", lambda eng: eng.scalar_tensor_tensor(VROW, PS[5][0:1, 0:512], 1.0 / D, VROW, ALU.mult, ALU.subtract))
        E("vector", lambda eng: eng.tensor_scalar_add(VROW, VROW, 1e-5))
        E("scalar", lambda eng: eng.activation(VROW, VROW, AF.Sqrt))
        E("vector", lambda eng: eng.reciprocal(VROW, VROW))
        rank1(4, MROW, 512)
        ps_read(4, "scalar", lambda eng: eng.copy(MB_, PS[4][:, 0:512]))
        rank1(5, VROW, 512)
        ps_read(5, "scalar", lambda eng: eng.copy(RB_, PS[5][:, 0:512]))
        for dt in range(8):
            E("vector", lambda eng, _d=dt: eng.tensor_sub(X1NB[:, _d * 512:(_d + 1) * 512], X1FB[:, _d * 512:(_d + 1) * 512], MB_))
            E("vector", lambda eng, _d=dt: eng.tensor_mul(X1NB[:, _d * 512:(_d + 1) * 512], X1NB[:, _d * 512:(_d + 1) * 512], RB_))
            store_toks.append(st_(X1Nd[dt * 128:(dt + 1) * 128, c * 512:(c + 1) * 512], X1NB[:, dt * 512:(dt + 1) * 512]))
        for s in range(4):
            for dt in range(8):
                transpose(6, X1NB[:, dt * 512 + s * 128: dt * 512 + (s + 1) * 128], TMS_[:, dt * 128:(dt + 1) * 128], 128, 128)
            store_toks.append(st_(X1NTd[(c * 4 + s) * 128:(c * 4 + s + 1) * 128, :], TMS_[:, 0:1024]))
    barrier()

    arena_reset()
    XH = alloc(16384); YH = alloc(16384)
    PIN2 = alloc(1024); PPO2 = alloc(1024); ZW = alloc(1024); TW0 = alloc(512); TW1 = alloc(512)
    LNROW = alloc(512); CBROW = alloc(512)
    CRY = ROWS[0:1, 0:512]; ABR = ROWS[0:1, 512:1024]; ABC = ROWS[0:1, 1024:1536]; TRW = ROWS[0:1, 1536:2048]
    RELAX[0] = True
    for hf in range(2):
        soft_fence()
        for u in range(NSUB):
            ld(XH[:, u * 512:(u + 1) * 512], X1NTd[u * 128:(u + 1) * 128, hf * 512:(hf + 1) * 512])
        for i in range(4):
            E("gpsimd", lambda eng, _i=i: eng.memset(YH[:, _i * 4096:(_i + 1) * 4096], 0.0))
        soft_fence()
        for n in range(NS):
            soft_fence()
            lnrow = LNROW[0:1, 0:512]
            cbrow = CBROW[0:1, 0:512]
            t_ln = ld(lnrow, LNABd[n:n + 1, hf * 512:(hf + 1) * 512])
            t_cb = ld(cbrow, CBNDd[n:n + 1, hf * 512:(hf + 1) * 512])
            rank1(6, lnrow, 512, deps=(t_ln,))
            tp1 = ps_read(6, "scalar", lambda eng: eng.activation(PIN2[:, 0:512], PS[6][:, 0:512], AF.Exp, scale=NIOP[:]))
            tp1b = E("scalar", lambda eng: eng.copy(PIN2[:, 512:1024], PIN2[:, 0:512]))
            tp2 = ps_read(6, "scalar", lambda eng: eng.activation(PPO2[:, 0:512], PS[6][:, 0:512], AF.Exp, scale=IOP[:]))
            rank1(7, cbrow, 512, deps=(t_cb,))
            tp3 = ps_read(7, "vector", lambda eng: eng.tensor_mul(PPO2[:, 0:512], PPO2[:, 0:512], PS[7][:, 0:512]), deps=(tp2,))
            tp3b = E("vector", lambda eng: eng.tensor_copy(PPO2[:, 512:1024], PPO2[:, 0:512]))
            tab = E("scalar", lambda eng: eng.activation(ABR, lnrow, AF.Exp), (t_ln,))
            tabc1 = E("vector", lambda eng: eng.tensor_scalar_mul(ABC, lnrow, 127.0), (t_ln,))
            tabc = E("scalar", lambda eng: eng.activation(ABC, ABC, AF.Exp), (tabc1,))
            E("vector", lambda eng: eng.memset(CRY, 0.0))
            zw_guard = ()
            for p in range(16):
                c0, c1 = 2 * p, 2 * p + 1
                tz = E("vector", lambda eng, _p=p: eng.tensor_mul(ZW, PIN2, XH[:, _p * 1024:(_p + 1) * 1024]), zw_guard + (tp1b,))
                E("vector", lambda eng: eng.tensor_mul(TRW, CRY, ABR), (tab,))
                ti0b = E("vector", lambda eng: eng.tensor_add(ZW[0:1, 0:512], ZW[0:1, 0:512], TRW))
                b0 = 0 if p % 2 == 0 else 2
                b1 = 1 if p % 2 == 0 else 3
                m_b0 = mm_group(b0, 512, [(UPI[:], ZW[:, 0:512])], deps=(ti0b,))
                m_4 = mm_group(4, 512, [(ONC[:], ZW[:, 0:512])], rows=1, deps=(ti0b,))
                ps_read(4, "vector", lambda eng: eng.tensor_mul(CRY, PS[4][0:1, 0:512], ABC), deps=(tabc,))
                E("vector", lambda eng: eng.tensor_mul(TRW, CRY, ABR))
                ti1b = E("vector", lambda eng: eng.tensor_add(ZW[0:1, 512:1024], ZW[0:1, 512:1024], TRW))
                m_b1 = mm_group(b1, 512, [(UPI[:], ZW[:, 512:1024])], deps=(ti1b,))
                m_5 = mm_group(5, 512, [(ONC[:], ZW[:, 512:1024])], rows=1, deps=(ti1b,))
                ps_read(5, "vector", lambda eng: eng.tensor_mul(CRY, PS[5][0:1, 0:512], ABC))
                ps_read(b0, "vector", lambda eng, _b=b0: eng.tensor_mul(TW0, PS[_b][:, 0:512], PPO2[:, 0:512]), deps=(tp3,))
                E("vector", lambda eng, _c=c0: eng.tensor_add(YH[:, _c * 512:(_c + 1) * 512], YH[:, _c * 512:(_c + 1) * 512], TW0))
                ps_read(b1, "vector", lambda eng, _b=b1: eng.tensor_mul(TW1, PS[_b][:, 0:512], PPO2[:, 512:1024]), deps=(tp3b,))
                E("vector", lambda eng, _c=c1: eng.tensor_add(YH[:, _c * 512:(_c + 1) * 512], YH[:, _c * 512:(_c + 1) * 512], TW1))
                zw_guard = (m_b0, m_4, m_b1, m_5)
        soft_fence()
        for u in range(NSUB):
            store_toks.append(st_(YSSMd[u * 128:(u + 1) * 128, hf * 512:(hf + 1) * 512], YH[:, u * 512:(u + 1) * 512]))
    RELAX[0] = False
    barrier()

    arena_reset()
    XSC = alloc(8192); YSC = alloc(8192); HIDH = alloc(4096)
    WDH = [alloc(1024) for _ in range(4)]
    WGT = alloc(1024); WUT = alloc(1024)
    GTS = alloc(8); MK1 = alloc(8); MK2 = alloc(8); SMB = alloc(8)
    GBC = alloc(512); TW0b = alloc(512); TW1b = alloc(512); GROW = alloc(512)
    GATR = SMALL[0:8, 0:1024]; SCF = SMALL[0:8, 1024:1536]
    WGA = [SMALL[:, 1544 + 8 * kp: 1552 + 8 * kp] for kp in range(8)]
    M1 = STAT[:, 4:5]; M2 = STAT[:, 5:6]; P1_ = STAT[:, 6:7]; P2_ = STAT[:, 7:8]
    for kp in range(8):
        ld(WGA[kp], wgate_d[kp * 128:(kp + 1) * 128, :])
    mgroups = [(0, 4), (4, 4), (8, 3)]
    for sc in range(4):
        RELAX[0] = True
        soft_fence()
        for kp in range(8):
            ld(XSC[:, kp * 1024:(kp + 1) * 1024], X1Nd[kp * 128:(kp + 1) * 128, sc * 1024:(sc + 1) * 1024])
        tyz = E("gpsimd", lambda eng: eng.memset(YSC, 0.0))
        RELAX[0] = False
        for tc in range(2):
            mm_group(0, 512, [(WGA[kp], XSC[:, kp * 1024 + tc * 512: kp * 1024 + (tc + 1) * 512]) for kp in range(8)], rows=8)
            ps_read(0, "scalar", lambda eng: eng.copy(SCF, PS[0][0:8, 0:512]))
            for g in range(4):
                transpose(1, SCF[:, g * 128:(g + 1) * 128], GTS, 8, 128)
                E("vector", lambda eng: eng.reduce_max(M1, GTS, AX.X))
                E("vector", lambda eng: eng.tensor_single_scalar(MK1, GTS, M1, ALU.is_equal))
                E("vector", lambda eng: eng.scalar_tensor_tensor(SMB, MK1, -1e9, GTS, ALU.mult, ALU.add))
                E("vector", lambda eng: eng.reduce_max(M2, SMB, AX.X))
                E("vector", lambda eng: eng.tensor_single_scalar(MK2, SMB, M2, ALU.is_equal))
                E("vector", lambda eng: eng.tensor_sub(P1_, M1, M2))
                E("scalar", lambda eng: eng.activation(P1_, P1_, AF.Sigmoid))
                E("vector", lambda eng: eng.tensor_scalar(P2_, P1_, -1.0, 1.0, ALU.mult, ALU.add))
                E("vector", lambda eng: eng.tensor_scalar(MK1, MK1, P1_, None, ALU.mult))
                E("vector", lambda eng: eng.tensor_scalar(MK2, MK2, P2_, None, ALU.mult))
                E("vector", lambda eng: eng.tensor_add(GTS, MK1, MK2))
                transpose(2, GTS, GATR[:, tc * 512 + g * 128: tc * 512 + (g + 1) * 128], 128, 8)
        store_toks.append(st_(GATEd[:, sc * 1024:(sc + 1) * 1024], GATR[:, 0:1024]))
        RELAX[0] = True
        soft_fence()
        thd_prev = None
        tyy_prev = None
        grow_guard = None
        for e in range(ME):
            for (m0, nmt) in mgroups:
                soft_fence()
                hid_done = []
                gu_guard = ()
                for mi in range(nmt):
                    mt = m0 + mi
                    wg_tk = [ld(WGT[:, kp * 128:(kp + 1) * 128], wgE_d[e, mt, kp, :, :], deps=gu_guard) for kp in range(8)]
                    wu_tk = [ld(WUT[:, kp * 128:(kp + 1) * 128], wuE_d[e, mt, kp, :, :], deps=gu_guard, eng="gpsimd") for kp in range(8)]
                    gu_guard = ()
                    for tc in range(2):
                        gmm = mm_group(3, 512, [(WGT[:, kp * 128:(kp + 1) * 128], XSC[:, kp * 1024 + tc * 512: kp * 1024 + (tc + 1) * 512]) for kp in range(8)], deps=tuple(wg_tk))
                        umm = mm_group(4, 512, [(WUT[:, kp * 128:(kp + 1) * 128], XSC[:, kp * 1024 + tc * 512: kp * 1024 + (tc + 1) * 512]) for kp in range(8)], deps=tuple(wu_tk))
                        tsl = ps_read(3, "scalar", lambda eng: eng.activation(TW0b, PS[3][:, 0:512], AF.Silu), deps=(thd_prev,) if thd_prev else ())
                        thd = ps_read(4, "vector", lambda eng, _mi=mi, _tc=tc: eng.tensor_mul(HIDH[:, _mi * 1024 + _tc * 512: _mi * 1024 + (_tc + 1) * 512], TW0b, PS[4][:, 0:512]), deps=(tsl,))
                        hid_done.append(thd)
                        thd_prev = thd
                        gu_guard = gu_guard + (gmm, umm)
                wd_tk = [ld(WDH[mi], wdE_d[e, (m0 + mi) * 128:(m0 + mi + 1) * 128, :]) for mi in range(nmt)]
                for tc in range(2):
                    t_gr = ld(GROW[0:1, 0:512], GATEd[e:e + 1, sc * 1024 + tc * 512: sc * 1024 + (tc + 1) * 512],
                              deps=(grow_guard,) if grow_guard else ())
                    grow_guard = rank1(5, GROW[0:1, 0:512], 512, deps=(t_gr,))
                    tgb = ps_read(5, "scalar", lambda eng: eng.copy(GBC, PS[5][:, 0:512]), deps=(tyy_prev,) if tyy_prev else ())
                    for m in range(8):
                        bnk = 6 if m % 2 == 0 else 7
                        mm_group(bnk, 512, [(WDH[mi][:, m * 128:(m + 1) * 128], HIDH[:, mi * 1024 + tc * 512: mi * 1024 + (tc + 1) * 512]) for mi in range(nmt)],
                                 deps=tuple(wd_tk) + tuple(hid_done))
                        tyy = ps_read(bnk, "vector", lambda eng, _b=bnk: eng.tensor_mul(TW1b, PS[_b][:, 0:512], GBC), deps=(tgb,))
                        tyy_prev = tyy
                        E("vector", lambda eng, _m=m, _tc=tc: eng.tensor_add(YSC[:, _m * 1024 + _tc * 512: _m * 1024 + (_tc + 1) * 512],
                                                                             YSC[:, _m * 1024 + _tc * 512: _m * 1024 + (_tc + 1) * 512], TW1b),
                          deps=(tyz,))
        soft_fence()
        for m in range(8):
            store_toks.append(st_(YMOEd[m * 128:(m + 1) * 128, sc * 1024:(sc + 1) * 1024], YSC[:, m * 1024:(m + 1) * 1024]))
        RELAX[0] = False
    barrier()

    arena_reset()
    YST = alloc(1024); YSF = alloc(4096); XOB = alloc(4096); OTM = alloc(1024)
    X1M_ = alloc(512); X1NM_ = alloc(512); YMM = alloc(512); G2M = alloc(512); S2M = alloc(512)
    for c in range(NCH):
        for s in range(4):
            ld(YST, YSSMd[(c * 4 + s) * 128:(c * 4 + s + 1) * 128, :])
            for dt in range(8):
                transpose(0, YST[:, dt * 128:(dt + 1) * 128], YSF[:, dt * 512 + s * 128: dt * 512 + (s + 1) * 128], 128, 128)
        for dt in range(8):
            ld(X1M_, X1d[dt * 128:(dt + 1) * 128, c * 512:(c + 1) * 512])
            ld(X1NM_, X1Nd[dt * 128:(dt + 1) * 128, c * 512:(c + 1) * 512])
            ld(YMM, YMOEd[dt * 128:(dt + 1) * 128, c * 512:(c + 1) * 512], eng="gpsimd")
            E("scalar", lambda eng: eng.activation(G2M, YMM, AF.Silu))
            E("vector", lambda eng, _d=dt: eng.tensor_scalar(S2M, X1NM_, DPP[:, _d:_d + 1], None, ALU.mult))
            E("vector", lambda eng, _d=dt: eng.tensor_add(S2M, S2M, YSF[:, _d * 512:(_d + 1) * 512]))
            E("vector", lambda eng: eng.tensor_mul(S2M, S2M, G2M))
            E("vector", lambda eng, _d=dt: eng.tensor_add(XOB[:, _d * 512:(_d + 1) * 512], X1M_, S2M))
        for s in range(4):
            for dt in range(8):
                transpose(1, XOB[:, dt * 512 + s * 128: dt * 512 + (s + 1) * 128], OTM[:, dt * 128:(dt + 1) * 128], 128, 128)
            store_toks.append(st_(out_d[(c * 4 + s) * 128:(c * 4 + s + 1) * 128, :], OTM))

    if debug:
        dbg = {"XN": XN, "HF": HFd, "QF": QFd, "KF": KFd, "KTM": KTMd, "VE": VEd,
               "Y1": Y1d, "H2": H2d, "SS": SSd, "X1": X1d, "X1N": X1Nd,
               "YSSM": YSSMd, "YMOE": YMOEd, "BETAo": BETA, "GATEo": GATEd}
        for nm, t in dbg.items():
            o = nc.dram_tensor("dbg_" + nm, list(t.tensor.shape), F32, kind="ExternalOutput").ap()
            store_toks.append(st_(o[:], t[:]))
    S.emit("sync", None, tuple(store_toks))
    S.build(nc, maxseq=maxseq)
    return nc, S


_nc_cache = [None]


def kernel(**inputs):
    def f(name):
        return np.ascontiguousarray(np.asarray(inputs[name], dtype=np.float32))

    x_full = f("L0_attn_x")
    wgE = f("L1_ffn_w_g").reshape(ME, 8, 128, MMT, 128).transpose(0, 3, 1, 2, 4).copy()
    wuE = f("L1_ffn_w_u").reshape(ME, 8, 128, MMT, 128).transpose(0, 3, 1, 2, 4).copy()

    iotaf = np.broadcast_to(np.arange(128, dtype=np.float32)[None, :], (128, 128)).copy()
    jj, tt = np.meshgrid(np.arange(128), np.arange(128), indexing="ij")
    upinc = (jj <= tt).astype(np.float32)
    upneg = -(jj < tt).astype(np.float32)

    common = {
        "n1w": f("L0_attn_norm1_w").reshape(1, D),
        "n2w": f("L0_attn_norm2_w").reshape(1, D),
        "wq": f("L0_attn_Wq"), "wk": f("L0_attn_Wk"), "wv": f("L0_attn_Wv"), "wo": f("L0_attn_Wo"),
        "bw": f("L0_attn_beta_w").reshape(1, D),
        "bb": f("L0_attn_beta_b").reshape(1, 1),
        "alog": f("L0_attn_alpha_logit").reshape(1, D),
        "wg0": f("L0_attn_w_gate"), "wu0": f("L0_attn_w_up"), "wd0": f("L0_attn_w_down"),
        "cw1": f("L0_ffn_cn_w1"), "cw2": f("L0_ffn_cn_w2"), "cw3": f("L0_ffn_cn_w3"), "cw4": f("L0_ffn_cn_w4"),
        "cb1": f("L0_ffn_cn_b1").reshape(1, D), "cb2": f("L0_ffn_cn_b2").reshape(1, 512),
        "cb3": f("L0_ffn_cn_b3").reshape(1, 256), "cb4": f("L0_ffn_cn_b4").reshape(1, 1),
        "gw1": f("L0_ffn_gc_w1"), "gw2": f("L0_ffn_gc_w2"), "gw3": f("L0_ffn_gc_w3"),
        "gb1": f("L0_ffn_gc_b1").reshape(1, 512), "gb2": f("L0_ffn_gc_b2").reshape(1, 256),
        "gb3": f("L0_ffn_gc_b3").reshape(1, 1),
        "ar": f("L0_ffn_adaptive_range").reshape(1, 1),
        "logA": f("L1_attn_log_A"), "Bp": f("L1_attn_B_param"), "Cp": f("L1_attn_C"),
        "Dv": f("L1_attn_D").reshape(1, D),
        "wgate": f("L1_ffn_w_gate"),
        "wgE": wgE, "wuE": wuE, "wdE": f("L1_ffn_w_d"),
        "idn": np.eye(128, dtype=np.float32),
        "upinc": upinc, "upneg": upneg, "iotaf": iotaf,
        "onesr": np.ones((1, 128), np.float32), "onesc": np.ones((128, 1), np.float32),
        "niop": -np.arange(128, dtype=np.float32).reshape(128, 1),
        "iop": np.arange(128, dtype=np.float32).reshape(128, 1),
    }
    in_maps = [dict(common, x=np.ascontiguousarray(x_full[b])) for b in range(8)]
    if _nc_cache[0] is None:
        _nc_cache[0] = build_program()[0]
    nc = _nc_cache[0]
    res = run_bass_kernel_spmd(nc, in_maps, core_ids=list(range(8)))
    return np.stack([r["out"] for r in res.results], axis=0)


# version 08
# speedup vs baseline: 1.4489x; 1.1044x over previous
import numpy as np
import concourse.bass as bass
import concourse.mybir as mybir
from concourse.bass_utils import run_bass_kernel_spmd

F32 = mybir.dt.float32
AF = mybir.ActivationFunctionType
ALU = mybir.AluOpType
AX = mybir.AxisListType

T = 4096
D = 1024
NS = 64
HID0 = 2048
ME = 8
MM = 1408
MMT = 11
NSUB = T // 128
NCH = T // 512

ENGINES = ("sync", "tensor", "vector", "scalar", "gpsimd")
AUTO_FENCE = True


class Sched:
    def __init__(self):
        self.q = {e: [] for e in ENGINES}
        self.gseq = 0
        self.seqs = {e: [] for e in ENGINES}
        self.marks = []

    def emit(self, eng, fn, deps=(), dma=False):
        lst = self.q[eng]
        tok = (eng, len(lst))
        deps = tuple(d for d in deps if d is not None)
        lst.append([fn, deps, dma, False])
        self.seqs[eng].append(self.gseq)
        self.gseq += 1
        return tok

    def build(self, nc, maxseq=None):
        if maxseq is not None:
            for e in ENGINES:
                keep = sum(1 for s in self.seqs[e] if s < maxseq)
                self.q[e] = self.q[e][:keep]
        for e in ENGINES:
            for rec in self.q[e]:
                if rec[2]:
                    rec[3] = True
                for (de, di) in rec[1]:
                    self.q[de][di][3] = True
        vals = {}
        for e in ENGINES:
            c = 0
            for i, rec in enumerate(self.q[e]):
                if rec[3] and rec[0] is not None:
                    c += 16 if rec[2] else 1
                vals[(e, i)] = c
        import contextlib
        stack = contextlib.ExitStack()
        sems = {e: stack.enter_context(nc.semaphore(f"sem_{e}")) for e in ENGINES}
        with nc.Block() as block:
            for e in ENGINES:
                def body(eng, _e=e):
                    hw = {o: 0 for o in ENGINES}
                    for i, (fn, deps, dma, needed) in enumerate(self.q[_e]):
                        for (de, di) in deps:
                            v = vals[(de, di)]
                            if v > hw[de]:
                                eng.wait_ge(sems[de], v)
                                hw[de] = v
                        if fn is not None:
                            inst = fn(eng)
                            if needed:
                                inst.then_inc(sems[_e], 16 if dma else 1)
                getattr(block, e)(body)
        stack.close()


def build_program(maxseq=None, debug=False):
    nc = bass.Bass("TRN2", target_bir_lowering=False)
    S = Sched()

    def din(name, shape):
        return nc.dram_tensor(name, shape, F32, kind="ExternalInput").ap()

    def dscr(name, shape):
        return nc.dram_tensor(name, shape, F32, kind="Internal").ap()

    x_d = din("x", [T, D])
    n1w_d = din("n1w", [1, D]); n2w_d = din("n2w", [1, D])
    wq_d = din("wq", [D, D]); wk_d = din("wk", [D, D]); wv_d = din("wv", [D, D]); wo_d = din("wo", [D, D])
    bw_d = din("bw", [1, D]); bb_d = din("bb", [1, 1]); alog_d = din("alog", [1, D])
    wg0_d = din("wg0", [D, HID0]); wu0_d = din("wu0", [D, HID0]); wd0_d = din("wd0", [HID0, D])
    cw1_d = din("cw1", [D, D]); cw2_d = din("cw2", [D, 512]); cw3_d = din("cw3", [512, 256]); cw4_d = din("cw4", [256, 1])
    cb1_d = din("cb1", [1, D]); cb2_d = din("cb2", [1, 512]); cb3_d = din("cb3", [1, 256]); cb4_d = din("cb4", [1, 1])
    gw1_d = din("gw1", [D, 512]); gw2_d = din("gw2", [512, 256]); gw3_d = din("gw3", [256, 1])
    gb1_d = din("gb1", [1, 512]); gb2_d = din("gb2", [1, 256]); gb3_d = din("gb3", [1, 1])
    ar_d = din("ar", [1, 1])
    logA_d = din("logA", [D, NS]); Bp_d = din("Bp", [D, NS]); Cp_d = din("Cp", [D, NS]); Dv_d = din("Dv", [1, D])
    wgate_d = din("wgate", [D, ME])
    wgE_d = din("wgE", [ME, MMT, 8, 128, 128])
    wuE_d = din("wuE", [ME, MMT, 8, 128, 128])
    wdE_d = din("wdE", [ME, MM, D])
    id_d = din("idn", [128, 128]); upinc_d = din("upinc", [128, 128]); upneg_d = din("upneg", [128, 128])
    iotaf_d = din("iotaf", [128, 128]); onesr_d = din("onesr", [1, 128]); onesc_d = din("onesc", [128, 1])
    niop_d = din("niop", [128, 1]); iop_d = din("iop", [128, 1])

    out_d = nc.dram_tensor("out", [T, D], F32, kind="ExternalOutput").ap()

    XN = dscr("XN", [D, T]); XF = dscr("XF", [D, T]); HFd = dscr("HFd", [D, T])
    BETA = dscr("BETAd", [T, 1])
    QFd = dscr("QFd", [D, T]); KFd = dscr("KFd", [D, T]); KTMd = dscr("KTMd", [T, D]); VEd = dscr("VEd", [T, D])
    Y1d = dscr("Y1d", [D, T]); H2d = dscr("H2d", [D, T]); HIDd = dscr("HIDd", [HID0, T])
    SSd = dscr("SSd", [D, T]); X1d = dscr("X1d", [D, T]); X1Nd = dscr("X1Nd", [D, T]); X1NTd = dscr("X1NTd", [T, D])
    YSSMd = dscr("YSSMd", [T, D]); YMOEd = dscr("YMOEd", [D, T]); GATEd = dscr("GATEd", [8, T])

    def sb(name, shape):
        return nc.alloc_sbuf_tensor(name, shape, F32).ap()

    IDN = sb("IDN", [128, 128]); UPI = sb("UPI", [128, 128]); UPN = sb("UPN", [128, 128])
    IOF = sb("IOF", [128, 128]); ONR = sb("ONR", [1, 128]); ONC = sb("ONC", [128, 1])
    NIOP = sb("NIOP", [128, 1]); IOP = sb("IOP", [128, 1])
    W1B = sb("W1B", [128, D]); BWB = sb("BWB", [128, D]); BB128 = sb("BB128", [128, 1])
    PINV_E = sb("PINV_E", [128, D]); PFWD_E = sb("PFWD_E", [128, D])
    OMAR = sb("OMAR", [1, D]); AROWV = sb("AROWV", [1, D])
    OMA127 = AROWV
    GS = sb("GS", [128, 8]); N2W = sb("N2W", [128, 8]); DPP = sb("DPP", [128, 8])
    CB1P = sb("CB1P", [128, 8]); CB2P = sb("CB2P", [128, 4]); CB3P = sb("CB3P", [128, 2])
    GB1P = sb("GB1P", [128, 4]); GB2P = sb("GB2P", [128, 2])
    CB4S = sb("CB4S", [1, 1]); GB3S = sb("GB3S", [1, 1]); AR2 = sb("AR2", [1, 1]); GF07 = sb("GF07", [1, 1])
    LNAB = sb("LNAB", [64, D]); CBND = sb("CBND", [64, D]); CPND = sb("CPND", [64, D])
    SMALL = sb("SMALLb", [128, 2048]); ROWS = sb("ROWSb", [1, 2048]); STAT = sb("STATb", [128, 16])
    PA = sb("PAb", [128, 39100])

    PS = [nc.alloc_psum_tensor(f"pb{i}", [128, 512], F32).ap() for i in range(8)]
    ps_writer = [None] * 8
    ps_reader = [None] * 8

    cur = [0]
    def arena_reset():
        cur[0] = 0
    def alloc(n):
        off = cur[0]
        assert off + n <= 39100, f"arena overflow {off+n}"
        cur[0] += n
        return PA[:, off:off + n]

    tok_last = {e: None for e in ENGINES}
    RELAX = [False]
    fence_base = [()]

    def cfence():
        return tuple(t for t in tok_last.values() if t is not None)

    def soft_fence():
        fence_base[0] = cfence()

    def E(eng, fn, deps=(), dma=False, fence=None):
        if fence is None:
            fence = not RELAX[0]
        if fence:
            d = tuple(deps) + cfence()
        else:
            d = tuple(deps) + fence_base[0]
        t = S.emit(eng, fn, d, dma)
        tok_last[eng] = t
        return t

    def barrier():
        toks = cfence()
        for e in ENGINES:
            tok_last[e] = S.emit(e, None, toks)
        S.marks.append(S.gseq)

    def mm_group(bank, ncols, pairs, deps=(), col0=0, rows=128):
        dl = list(deps)
        if ps_reader[bank] is not None:
            dl.append(ps_reader[bank])
        t = None
        n = len(pairs)
        for i, (lhsT, rhs) in enumerate(pairs):
            def fn(eng, _l=lhsT, _r=rhs, _s=(i == 0), _e=(i == n - 1)):
                return eng.matmul(PS[bank][0:rows, col0:col0 + ncols], _l, _r,
                                  start=_s, stop=_e, skip_group_check=True)
            t = E("tensor", fn, tuple(dl) if i == 0 else (), fence=(None if i == 0 else False))
        ps_writer[bank] = t
        return t

    def mm_acc(bank, ncols, pairs, deps=(), col0=0, rows=128):
        t = None
        for i, (lhsT, rhs) in enumerate(pairs):
            def fn(eng, _l=lhsT, _r=rhs):
                return eng.matmul(PS[bank][0:rows, col0:col0 + ncols], _l, _r,
                                  start=False, stop=True, skip_group_check=True)
            t = E("tensor", fn, tuple(deps) if i == 0 else (), fence=(None if i == 0 else False))
        ps_writer[bank] = t
        return t

    def ps_read(bank, eng, fn, deps=()):
        t = E(eng, fn, tuple(deps) + ((ps_writer[bank],) if ps_writer[bank] else ()))
        ps_reader[bank] = t
        return t

    def transpose(bank, src_ap, dst_ap, p, c, deps=()):
        dl = list(deps)
        if ps_reader[bank] is not None:
            dl.append(ps_reader[bank])
        tt = E("tensor", lambda eng, _s=src_ap, _p=p, _c=c: eng.transpose(
            PS[bank][0:_c, 0:_p], _s, IDN[0:_p, 0:_p]), tuple(dl))
        ps_writer[bank] = tt
        return ps_read(bank, "scalar", lambda eng, _d=dst_ap, _p=p, _c=c: eng.copy(_d, PS[bank][0:_c, 0:_p]))

    def rank1(bank, row_ap, ncols, deps=()):
        dl = list(deps)
        if ps_reader[bank] is not None:
            dl.append(ps_reader[bank])
        t = E("tensor", lambda eng, _r=row_ap, _n=ncols: eng.matmul(
            PS[bank][:, 0:_n], ONR[:], _r, start=True, stop=True, skip_group_check=True), tuple(dl))
        ps_writer[bank] = t
        return t

    def ld(dst, src, deps=(), eng="sync"):
        return E(eng, lambda e, _d=dst, _s=src: e.dma_start(_d, _s), deps, dma=True)

    def st_(dst, src, deps=(), eng="gpsimd"):
        return E(eng, lambda e, _d=dst, _s=src: e.dma_start(_d, _s), deps, dma=True)

    store_toks = []

    ld(IDN[:], id_d[:]); ld(UPI[:], upinc_d[:]); ld(UPN[:], upneg_d[:])
    ld(IOF[:], iotaf_d[:]); ld(ONR[:], onesr_d[:]); ld(ONC[:], onesc_d[:])
    ld(NIOP[:], niop_d[:]); ld(IOP[:], iop_d[:])

    def row_bcast(dst, src_row_ap, width):
        for h in range((width + 511) // 512):
            w = min(512, width - h * 512)
            rank1(0, src_row_ap[:, h * 512:h * 512 + w], w)
            ps_read(0, "scalar", lambda eng, _d=dst, _h=h, _w=w: eng.copy(_d[:, _h * 512:_h * 512 + _w], PS[0][:, 0:_w]))

    t = ld(ROWS[0:1, 0:D], n1w_d[:])
    row_bcast(W1B, ROWS[0:1, 0:D], D)
    ld(ROWS[0:1, 0:D], bw_d[:])
    row_bcast(BWB, ROWS[0:1, 0:D], D)
    ld(ROWS[0:1, 1024:1025], bb_d[:])
    rank1(0, ROWS[0:1, 1024:1025], 1)
    ps_read(0, "scalar", lambda eng: eng.copy(BB128[:], PS[0][:, 0:1]))

    ld(ROWS[0:1, 0:D], alog_d[:])
    E("scalar", lambda eng: eng.activation(AROWV[:], ROWS[0:1, 0:D], AF.Sigmoid))
    E("vector", lambda eng: eng.tensor_scalar(OMAR[:], AROWV[:], -1.0, 1.0, ALU.mult, ALU.add))
    lnoma = ROWS[0:1, 1024:1024 + D]
    E("scalar", lambda eng: eng.activation(lnoma, OMAR[:], AF.Ln))
    A_PP = SMALL[:, 0:8]; LOM_PP = SMALL[:, 8:16]; NLOM_PP = SMALL[:, 16:24]
    STG = SMALL[:, 24:152]
    for dt in range(8):
        transpose(1, AROWV[:, dt * 128:(dt + 1) * 128], A_PP[:, dt:dt + 1], 1, 128)
    for dt in range(8):
        transpose(1, lnoma[:, dt * 128:(dt + 1) * 128], LOM_PP[:, dt:dt + 1], 1, 128)
    E("vector", lambda eng: eng.tensor_scalar_mul(NLOM_PP[:], LOM_PP[:], -1.0))
    E("vector", lambda eng: eng.tensor_scalar_mul(OMA127[:], lnoma, 127.0))
    E("scalar", lambda eng: eng.activation(OMA127[:], OMA127[:], AF.Exp))
    for dt in range(8):
        E("scalar", lambda eng, _dt=dt: eng.activation(STG, IOF[:], AF.Exp, scale=NLOM_PP[:, _dt:_dt + 1]))
        E("vector", lambda eng, _dt=dt: eng.tensor_scalar(STG, STG, A_PP[:, _dt:_dt + 1], None, ALU.mult))
        transpose(1, STG, PINV_E[:, dt * 128:(dt + 1) * 128], 128, 128)
        E("scalar", lambda eng, _dt=dt: eng.activation(STG, IOF[:], AF.Exp, scale=LOM_PP[:, _dt:_dt + 1]))
        transpose(1, STG, PFWD_E[:, dt * 128:(dt + 1) * 128], 128, 128)

    def row_to_pp(dst, src_d, width):
        ld(ROWS[0:1, 0:width], src_d[:])
        for dt in range(width // 128):
            transpose(1, ROWS[0:1, dt * 128:(dt + 1) * 128], dst[:, dt:dt + 1], 1, 128)

    row_to_pp(N2W, n2w_d, D)
    row_to_pp(DPP, Dv_d, D)
    row_to_pp(CB1P, cb1_d, D)
    row_to_pp(CB2P, cb2_d, 512)
    row_to_pp(CB3P, cb3_d, 256)
    row_to_pp(GB1P, gb1_d, 512)
    row_to_pp(GB2P, gb2_d, 256)
    ld(CB4S[:], cb4_d[:]); ld(GB3S[:], gb3_d[:])
    ld(ROWS[0:1, 1024:1025], ar_d[:])
    E("vector", lambda eng: eng.tensor_scalar_mul(AR2[:], ROWS[0:1, 1024:1025], 2.0))

    LA_T = SMALL[:, 152:216]
    for (src_d, dst) in ((logA_d, LNAB), (Bp_d, CBND), (Cp_d, CPND)):
        for dt in range(8):
            ld(LA_T, src_d[dt * 128:(dt + 1) * 128, :])
            transpose(1, LA_T, dst[0:64, dt * 128:(dt + 1) * 128], 128, 64)
    arena_reset()
    TMPA = alloc(1024)[0:64, :]
    RECA = alloc(1024)[0:64, :]
    E("scalar", lambda eng: eng.activation(TMPA, LNAB[0:64, :], AF.Exp))
    E("vector", lambda eng: eng.tensor_scalar_mul(TMPA, TMPA, -1.0))
    E("vector", lambda eng: eng.reciprocal(RECA, TMPA))
    E("vector", lambda eng: eng.tensor_scalar_mul(LNAB[0:64, :], TMPA, 0.1))
    E("scalar", lambda eng: eng.activation(TMPA, LNAB[0:64, :], AF.Exp))
    E("vector", lambda eng: eng.tensor_scalar(TMPA, TMPA, 1.0, None, ALU.subtract))
    E("vector", lambda eng: eng.tensor_mul(TMPA, TMPA, RECA))
    E("vector", lambda eng: eng.tensor_mul(CBND[0:64, :], CBND[0:64, :], TMPA))
    E("vector", lambda eng: eng.tensor_mul(CBND[0:64, :], CBND[0:64, :], CPND[0:64, :]))
    LNABd = dscr("LNABd", [64, D]); CBNDd = dscr("CBNDd", [64, D])
    store_toks.append(st_(LNABd[:], LNAB[0:64, :]))
    store_toks.append(st_(CBNDd[:], CBND[0:64, :]))
    E("vector", lambda eng: eng.memset(GS[:], 0.0))
    barrier()

    arena_reset()
    XT = alloc(1024); XSQ = alloc(1024); XNT = alloc(1024); HT = alloc(1024)
    FMB = alloc(1024); FMB2 = alloc(1024)
    s1 = STAT[:, 0:1]; s2 = STAT[:, 1:2]; s3 = STAT[:, 2:3]; s4 = STAT[:, 3:4]
    for u in range(NSUB):
        ld(XT, x_d[u * 128:(u + 1) * 128, :])
        E("scalar", lambda eng: eng.activation(XSQ, XT, AF.Square))
        E("vector", lambda eng: eng.reduce_sum(s1, XT, AX.X))
        E("vector", lambda eng: eng.reduce_sum(s2, XSQ, AX.X))
        E("vector", lambda eng: eng.tensor_scalar_mul(s1, s1, 1.0 / D))
        E("vector", lambda eng: eng.tensor_mul(s3, s1, s1))
        E("vector", lambda eng: eng.scalar_tensor_tensor(s2, s2, 1.0 / D, s3, ALU.mult, ALU.subtract))
        E("vector", lambda eng: eng.tensor_scalar_add(s2, s2, 1e-5))
        E("scalar", lambda eng: eng.activation(s2, s2, AF.Sqrt))
        E("vector", lambda eng: eng.reciprocal(s2, s2))
        E("vector", lambda eng: eng.tensor_scalar(XNT, XT, s1, s2, ALU.subtract, ALU.mult))
        E("scalar", lambda eng: eng.activation(XSQ, XNT, AF.Square))
        E("vector", lambda eng: eng.reduce_sum(s3, XSQ, AX.X))
        E("vector", lambda eng: eng.tensor_scalar(s3, s3, 1.0 / D, 1e-6, ALU.mult, ALU.add))
        E("scalar", lambda eng: eng.activation(s3, s3, AF.Sqrt))
        E("vector", lambda eng: eng.reciprocal(s3, s3))
        E("vector", lambda eng: eng.tensor_scalar(HT, XNT, s3, None, ALU.mult))
        E("vector", lambda eng: eng.tensor_mul(HT, HT, W1B[:]))
        E("vector", lambda eng: eng.tensor_mul(XSQ, HT, BWB[:]))
        E("vector", lambda eng: eng.reduce_sum(s4, XSQ, AX.X))
        E("scalar", lambda eng: eng.activation(s4, s4, AF.Sigmoid, bias=BB128[:]))
        store_toks.append(st_(BETA[u * 128:(u + 1) * 128, :], s4))
        for dt in range(8):
            transpose(2, XNT[:, dt * 128:(dt + 1) * 128], FMB[:, dt * 128:(dt + 1) * 128], 128, 128)
            transpose(3, HT[:, dt * 128:(dt + 1) * 128], FMB2[:, dt * 128:(dt + 1) * 128], 128, 128)
            store_toks.append(st_(XN[dt * 128:(dt + 1) * 128, u * 128:(u + 1) * 128], FMB[:, dt * 128:(dt + 1) * 128]))
            store_toks.append(st_(HFd[dt * 128:(dt + 1) * 128, u * 128:(u + 1) * 128], FMB2[:, dt * 128:(dt + 1) * 128]))
            E("vector", lambda eng, _dt=dt: eng.reduce_sum(STAT[:, 8 + _dt:9 + _dt], FMB[:, _dt * 128:(_dt + 1) * 128], AX.X))
            E("vector", lambda eng, _dt=dt: eng.tensor_add(GS[:, _dt:_dt + 1], GS[:, _dt:_dt + 1], STAT[:, 8 + _dt:9 + _dt]))
        for dt in range(8):
            transpose(2, XT[:, dt * 128:(dt + 1) * 128], FMB[:, dt * 128:(dt + 1) * 128], 128, 128)
            store_toks.append(st_(XF[dt * 128:(dt + 1) * 128, u * 128:(u + 1) * 128], FMB[:, dt * 128:(dt + 1) * 128]))
    barrier()

    arena_reset()
    WQ = [alloc(1024) for _ in range(8)]
    WK = [alloc(1024) for _ in range(8)]
    WV = [alloc(1024) for _ in range(8)]
    HFB = alloc(4096); KFB = alloc(4096); VFB = alloc(4096)
    KSQ = VFB
    QM = alloc(512); VTM = alloc(1024)
    OMAC = HFB[:, 0:1024]
    KTMS = VTM
    ZB = KFB[:, 0:1024]
    INVB = QM
    CARRY = ROWS[0:1, 0:D]
    KROW = ROWS[0:1, 1024:1536]

    for kp in range(8):
        ld(WQ[kp], wq_d[kp * 128:(kp + 1) * 128, :])
        ld(WK[kp], wk_d[kp * 128:(kp + 1) * 128, :])
        ld(WV[kp], wv_d[kp * 128:(kp + 1) * 128, :])
    E("vector", lambda eng: eng.memset(CARRY, 0.0))
    RELAX[0] = True
    qst_p = None
    for c in range(NCH):
        soft_fence()
        hft = [ld(HFB[:, kp * 512:(kp + 1) * 512], HFd[kp * 128:(kp + 1) * 128, c * 512:(c + 1) * 512]) for kp in range(8)]
        for m in range(8):
            bk = m % 2
            mm_group(bk, 512, [(WQ[kp][:, m * 128:(m + 1) * 128], HFB[:, kp * 512:(kp + 1) * 512]) for kp in range(8)], deps=tuple(hft))
            te = ps_read(bk, "scalar", lambda eng, _b=bk: eng.copy(QM, PS[_b][:, 0:512]), deps=(qst_p,) if qst_p else ())
            qst_p = st_(QFd[m * 128:(m + 1) * 128, c * 512:(c + 1) * 512], QM, deps=(te,))
            store_toks.append(qst_p)
        kev = []
        for m in range(8):
            bk = 2 + m % 2
            mm_group(bk, 512, [(WK[kp][:, m * 128:(m + 1) * 128], HFB[:, kp * 512:(kp + 1) * 512]) for kp in range(8)], deps=tuple(hft))
            kev.append(ps_read(bk, "scalar", lambda eng, _b=bk, _m=m: eng.copy(KFB[:, _m * 512:(_m + 1) * 512], PS[_b][:, 0:512])))
        ksq = [E("scalar", lambda eng, _m=m: eng.activation(KSQ[:, _m * 512:(_m + 1) * 512], KFB[:, _m * 512:(_m + 1) * 512], AF.Square), (kev[m],)) for m in range(8)]
        tss = mm_group(4, 512, [(ONC[:], KSQ[:, m * 512:(m + 1) * 512]) for m in range(8)], rows=1, deps=tuple(ksq))
        tnm = ps_read(4, "scalar", lambda eng: eng.activation(KROW, PS[4][0:1, 0:512], AF.Sqrt))
        tmx = E("vector", lambda eng: eng.tensor_scalar_max(KROW, KROW, 1e-12), (tnm,))
        tin = E("vector", lambda eng: eng.reciprocal(KROW, KROW))
        rank1(4, KROW, 512, deps=(tin,))
        tib = ps_read(4, "scalar", lambda eng: eng.copy(INVB, PS[4][:, 0:512]), deps=(qst_p,))
        kn = [E("vector", lambda eng, _m=m: eng.tensor_mul(KFB[:, _m * 512:(_m + 1) * 512], KFB[:, _m * 512:(_m + 1) * 512], INVB), (tib, kev[m])) for m in range(8)]
        kst = []
        for m in range(8):
            kst.append(st_(KFd[m * 128:(m + 1) * 128, c * 512:(c + 1) * 512], KFB[:, m * 512:(m + 1) * 512], deps=(kn[m],)))
            store_toks.append(kst[-1])
        ktm_st = []
        tst_p = None
        for s in range(4):
            u = c * 4 + s
            trs = []
            for dt in range(8):
                trs.append(transpose(5, KFB[:, dt * 512 + s * 128: dt * 512 + (s + 1) * 128], KTMS[:, dt * 128:(dt + 1) * 128],
                                     128, 128, deps=(kn[dt],) + ((tst_p,) if tst_p else ())))
            tst_p = st_(KTMd[u * 128:(u + 1) * 128, :], KTMS, deps=tuple(trs))
            ktm_st.append(tst_p)
            store_toks.append(tst_p)
        vev = []
        vmm_last = None
        for m in range(8):
            bk = m % 2
            vmm_last = mm_group(bk, 512, [(WV[kp][:, m * 128:(m + 1) * 128], HFB[:, kp * 512:(kp + 1) * 512]) for kp in range(8)], deps=tuple(hft))
            vev.append(ps_read(bk, "scalar", lambda eng, _b=bk, _m=m: eng.copy(VFB[:, _m * 512:(_m + 1) * 512], PS[_b][:, 0:512]), deps=(tss,)))
        vst_p = None
        for s in range(4):
            u = c * 4 + s
            trs = []
            for dt in range(8):
                trs.append(transpose(5, VFB[:, dt * 512 + s * 128: dt * 512 + (s + 1) * 128], VTM[:, dt * 128:(dt + 1) * 128],
                                     128, 128, deps=(vev[dt],) + tuple(ktm_st) + ((vst_p,) if vst_p else ())))
            tz = E("vector", lambda eng: eng.tensor_mul(ZB, PINV_E[:], VTM), tuple(trs) + tuple(ktm_st) + (kst[0], kst[1]))
            tc0 = E("vector", lambda eng: eng.tensor_mul(OMAC[0:1, :], CARRY, OMAR[:]), (vmm_last,))
            tc1 = E("vector", lambda eng: eng.tensor_add(ZB[0:1, :], ZB[0:1, :], OMAC[0:1, :]))
            mm_group(6, 512, [(UPI[:], ZB[:, 0:512])], deps=(tc1,))
            mm_group(7, 512, [(UPI[:], ZB[:, 512:1024])], deps=(tc1,))
            e1 = ps_read(6, "vector", lambda eng: eng.tensor_mul(VTM[:, 0:512], PFWD_E[:, 0:512], PS[6][:, 0:512]))
            e2 = ps_read(7, "vector", lambda eng: eng.tensor_mul(VTM[:, 512:1024], PFWD_E[:, 512:1024], PS[7][:, 0:512]))
            mm_group(2, 512, [(ONC[:], ZB[:, 0:512])], rows=1, deps=(tc1,))
            ps_read(2, "vector", lambda eng: eng.tensor_mul(CARRY[:, 0:512], PS[2][0:1, 0:512], OMA127[:, 0:512]))
            mm_group(3, 512, [(ONC[:], ZB[:, 512:1024])], rows=1, deps=(tc1,))
            ps_read(3, "vector", lambda eng: eng.tensor_mul(CARRY[:, 512:1024], PS[3][0:1, 0:512], OMA127[:, 512:1024]))
            vst_p = st_(VEd[u * 128:(u + 1) * 128, :], VTM, deps=(e1, e2))
            store_toks.append(vst_p)
    RELAX[0] = False
    barrier()

    arena_reset()
    WO = [alloc(1024) for _ in range(8)]
    STT = [alloc(1024) for _ in range(8)]
    QFB = alloc(4096); KFB3 = alloc(4096)
    KTMS2 = alloc(1024); VES = alloc(1024)
    ATN = alloc(1024); AFB = alloc(4096)
    UB = alloc(1024); BB_ = alloc(1024)
    NT = alloc(128); RT_A = alloc(128); RT_B = alloc(128)
    MP0 = alloc(128); MP1 = alloc(128); MTP0 = alloc(128); MTP1 = alloc(128)
    MQK = alloc(128); BBC = alloc(128); BETR = alloc(128)
    BETP = alloc(4)
    Y1M = alloc(512); XNM = alloc(512); H2M = alloc(512); Y1SQ = alloc(512); RSTB = alloc(512)
    Y1ROW = ROWS[0:1, 1024:1536]

    for kp in range(8):
        ld(WO[kp], wo_d[kp * 128:(kp + 1) * 128, :])
        E("gpsimd", lambda eng, _k=kp: eng.memset(STT[_k], 0.0))

    for c in range(NCH):
        for kp in range(8):
            ld(QFB[:, kp * 512:(kp + 1) * 512], QFd[kp * 128:(kp + 1) * 128, c * 512:(c + 1) * 512])
            ld(KFB3[:, kp * 512:(kp + 1) * 512], KFd[kp * 128:(kp + 1) * 128, c * 512:(c + 1) * 512])
        for s in range(4):
            u = c * 4 + s
            ld(BETP[:, s:s + 1], BETA[u * 128:(u + 1) * 128, :])
            ld(KTMS2, KTMd[u * 128:(u + 1) * 128, :])
            ld(VES, VEd[u * 128:(u + 1) * 128, :])
            ksl = [KFB3[:, kp * 512 + s * 128: kp * 512 + (s + 1) * 128] for kp in range(8)]
            qsl = [QFB[:, kp * 512 + s * 128: kp * 512 + (s + 1) * 128] for kp in range(8)]
            bpp = BETP[:, s:s + 1]
            transpose(0, bpp, BETR[0:1, 0:128], 128, 1)
            rank1(0, BETR[0:1, 0:128], 128)
            ps_read(0, "scalar", lambda eng: eng.copy(BBC, PS[0][:, 0:128]))
            mm_group(1, 128, [(ksl[kp], ksl[kp]) for kp in range(8)])
            ps_read(1, "vector", lambda eng: eng.tensor_mul(NT, PS[1][:, 0:128], UPN[:]))
            E("vector", lambda eng: eng.tensor_mul(NT, NT, BBC))
            E("vector", lambda eng: eng.tensor_add(RT_A, NT, IDN[:]))
            transpose(2, NT, MP0, 128, 128)
            E("scalar", lambda eng: eng.copy(MTP0, NT))
            rt_cur, rt_oth = RT_A, RT_B
            m_cur, mt_cur = MP0, MTP0
            m_nxt, mt_nxt = MP1, MTP1
            for lvl in range(6):
                mm_group(3, 128, [(mt_cur, m_cur)])
                ps_read(3, "scalar", lambda eng, _m=m_nxt: eng.copy(_m, PS[3][:, 0:128]))
                mm_group(2, 128, [(m_nxt, rt_cur)])
                ps_read(2, "vector", lambda eng, _o=rt_oth, _c=rt_cur: eng.tensor_add(_o, _c, PS[2][:, 0:128]))
                rt_cur, rt_oth = rt_oth, rt_cur
                if lvl < 5:
                    transpose(3, m_nxt, mt_nxt, 128, 128)
                    m_cur, mt_cur = m_nxt, mt_nxt
                    m_nxt, mt_nxt = (MP0, MTP0) if m_cur is MP1 else (MP1, MTP1)
            mm_group(4, 512, [(ksl[kp], STT[kp][:, 0:512]) for kp in range(8)])
            mm_group(5, 512, [(ksl[kp], STT[kp][:, 512:1024]) for kp in range(8)])
            ps_read(4, "vector", lambda eng: eng.scalar_tensor_tensor(BB_[:, 0:512], PS[4][:, 0:512], -1.0, VES[:, 0:512], ALU.mult, ALU.add))
            ps_read(5, "vector", lambda eng: eng.scalar_tensor_tensor(BB_[:, 512:1024], PS[5][:, 0:512], -1.0, VES[:, 512:1024], ALU.mult, ALU.add))
            E("vector", lambda eng, _b=bpp: eng.tensor_scalar(BB_, BB_, _b, None, ALU.mult))
            mm_group(4, 512, [(rt_cur, BB_[:, 0:512])])
            ps_read(4, "scalar", lambda eng: eng.copy(UB[:, 0:512], PS[4][:, 0:512]))
            mm_group(5, 512, [(rt_cur, BB_[:, 512:1024])])
            ps_read(5, "scalar", lambda eng: eng.copy(UB[:, 512:1024], PS[5][:, 0:512]))
            mm_group(6, 512, [(qsl[kp], STT[kp][:, 0:512]) for kp in range(8)])
            mm_group(7, 512, [(qsl[kp], STT[kp][:, 512:1024]) for kp in range(8)])
            mm_group(1, 128, [(ksl[kp], qsl[kp]) for kp in range(8)])
            ps_read(1, "vector", lambda eng: eng.tensor_mul(MQK, PS[1][:, 0:128], UPI[:]))
            mm_acc(6, 512, [(MQK, UB[:, 0:512])])
            mm_acc(7, 512, [(MQK, UB[:, 512:1024])])
            ps_read(6, "scalar", lambda eng: eng.copy(ATN[:, 0:512], PS[6][:, 0:512]))
            ps_read(7, "scalar", lambda eng: eng.copy(ATN[:, 512:1024], PS[7][:, 0:512]))
            for m in range(8):
                mm_group(4, 512, [(KTMS2[:, m * 128:(m + 1) * 128], UB[:, 0:512])])
                ps_read(4, "vector", lambda eng, _m=m: eng.tensor_add(STT[_m][:, 0:512], STT[_m][:, 0:512], PS[4][:, 0:512]))
                mm_group(5, 512, [(KTMS2[:, m * 128:(m + 1) * 128], UB[:, 512:1024])])
                ps_read(5, "vector", lambda eng, _m=m: eng.tensor_add(STT[_m][:, 512:1024], STT[_m][:, 512:1024], PS[5][:, 0:512]))
            for dt in range(8):
                transpose(0, ATN[:, dt * 128:(dt + 1) * 128], AFB[:, dt * 512 + s * 128: dt * 512 + (s + 1) * 128], 128, 128)
        for m in range(8):
            mm_group(0, 512, [(WO[kp][:, m * 128:(m + 1) * 128], AFB[:, kp * 512:(kp + 1) * 512]) for kp in range(8)])
            ld(XNM, XN[m * 128:(m + 1) * 128, c * 512:(c + 1) * 512])
            ps_read(0, "vector", lambda eng: eng.tensor_add(Y1M, PS[0][:, 0:512], XNM))
            store_toks.append(st_(Y1d[m * 128:(m + 1) * 128, c * 512:(c + 1) * 512], Y1M))
            E("scalar", lambda eng: eng.activation(Y1SQ, Y1M, AF.Square))
            if m == 0:
                mm_group(1, 512, [(ONC[:], Y1SQ)], rows=1)
            else:
                mm_acc(1, 512, [(ONC[:], Y1SQ)], rows=1)
        ps_read(1, "vector", lambda eng: eng.tensor_scalar(Y1ROW, PS[1][0:1, 0:512], 1.0 / D, 1e-6, ALU.mult, ALU.add))
        E("scalar", lambda eng: eng.activation(Y1ROW, Y1ROW, AF.Sqrt))
        E("vector", lambda eng: eng.reciprocal(Y1ROW, Y1ROW))
        rank1(1, Y1ROW, 512)
        ps_read(1, "scalar", lambda eng: eng.copy(RSTB, PS[1][:, 0:512]))
        for m in range(8):
            ld(Y1M, Y1d[m * 128:(m + 1) * 128, c * 512:(c + 1) * 512])
            E("vector", lambda eng: eng.tensor_mul(H2M, Y1M, RSTB))
            E("vector", lambda eng, _m=m: eng.tensor_scalar(H2M, H2M, N2W[:, _m:_m + 1], None, ALU.mult))
            store_toks.append(st_(H2d[m * 128:(m + 1) * 128, c * 512:(c + 1) * 512], H2M))
    barrier()

    arena_reset()
    WGB = [alloc(2048) for _ in range(8)]
    WUB = [alloc(2048) for _ in range(8)]
    H2F = alloc(4096); HIDM = alloc(512); HIDT = alloc(512)
    for kp in range(8):
        ld(WGB[kp], wg0_d[kp * 128:(kp + 1) * 128, :])
        ld(WUB[kp], wu0_d[kp * 128:(kp + 1) * 128, :])
    RELAX[0] = True
    thd_p = None
    sst_p = None
    for c in range(NCH):
        soft_fence()
        h2t = [ld(H2F[:, kp * 512:(kp + 1) * 512], H2d[kp * 128:(kp + 1) * 128, c * 512:(c + 1) * 512]) for kp in range(8)]
        for m in range(16):
            b0, b1 = (0, 1) if m % 2 == 0 else (2, 3)
            mm_group(b0, 512, [(WGB[kp][:, m * 128:(m + 1) * 128], H2F[:, kp * 512:(kp + 1) * 512]) for kp in range(8)], deps=tuple(h2t))
            mm_group(b1, 512, [(WUB[kp][:, m * 128:(m + 1) * 128], H2F[:, kp * 512:(kp + 1) * 512]) for kp in range(8)], deps=tuple(h2t))
            tsl = ps_read(b0, "scalar", lambda eng, _b=b0: eng.activation(HIDT, PS[_b][:, 0:512], AF.Silu), deps=(thd_p,) if thd_p else ())
            thd = ps_read(b1, "vector", lambda eng, _b=b1: eng.tensor_mul(HIDM, HIDT, PS[_b][:, 0:512]), deps=(tsl,) + ((sst_p,) if sst_p else ()))
            sst = st_(HIDd[m * 128:(m + 1) * 128, c * 512:(c + 1) * 512], HIDM, deps=(thd,))
            store_toks.append(sst)
            thd_p = thd; sst_p = sst
    RELAX[0] = False
    barrier()

    arena_reset()
    WDB = [alloc(1024) for _ in range(16)]
    HIDF = alloc(8192); Y1M2 = alloc(512); SM_ = alloc(512)
    for kp in range(16):
        ld(WDB[kp], wd0_d[kp * 128:(kp + 1) * 128, :])
    RELAX[0] = True
    ts_p = None
    sst_p = None
    for c in range(NCH):
        soft_fence()
        hidt = [ld(HIDF[:, kp * 512:(kp + 1) * 512], HIDd[kp * 128:(kp + 1) * 128, c * 512:(c + 1) * 512]) for kp in range(16)]
        for m in range(8):
            bk = m % 2
            mm_group(bk, 512, [(WDB[kp][:, m * 128:(m + 1) * 128], HIDF[:, kp * 512:(kp + 1) * 512]) for kp in range(16)], deps=tuple(hidt))
            ty = ld(Y1M2, Y1d[m * 128:(m + 1) * 128, c * 512:(c + 1) * 512], deps=(ts_p,) if ts_p else ())
            ts = ps_read(bk, "vector", lambda eng, _b=bk: eng.tensor_add(SM_, PS[_b][:, 0:512], Y1M2), deps=(ty,) + ((sst_p,) if sst_p else ()))
            sst = st_(SSd[m * 128:(m + 1) * 128, c * 512:(c + 1) * 512], SM_, deps=(ts,))
            store_toks.append(sst)
            ts_p = ts; sst_p = sst
    RELAX[0] = False
    barrier()

    arena_reset()
    CW1 = [alloc(1024) for _ in range(8)]
    CW2 = [alloc(512) for _ in range(8)]
    CW3 = [alloc(256) for _ in range(4)]
    CW4 = [alloc(1) for _ in range(2)]
    GW1B = alloc(4096)
    GW1 = [GW1B[:, kp * 512:(kp + 1) * 512] for kp in range(8)]
    GW2 = [alloc(256) for _ in range(4)]
    GW3 = [alloc(1) for _ in range(2)]
    OUTA = alloc(4096)
    OUTB = GW1B
    C1B = alloc(4096); C2B = alloc(2048); C3B = alloc(1024)
    X1FB = alloc(4096)
    X1NB = OUTA
    XM_ = C2B[:, 0:512]; SM2 = C2B[:, 512:1024]; X1SQ = C2B[:, 1024:1536]
    MB_ = C3B[:, 0:512]; RB_ = C3B[:, 512:1024]
    CPB = alloc(512); TMS_ = alloc(1024)
    GH1 = STAT[:, 8:12]; GH2 = STAT[:, 12:14]
    TFR = ROWS[0:1, 1024:1536]
    MROW = ROWS[0:1, 0:512]; VROW = ROWS[0:1, 512:1024]

    for kp in range(8):
        ld(CW1[kp], cw1_d[kp * 128:(kp + 1) * 128, :])
        ld(CW2[kp], cw2_d[kp * 128:(kp + 1) * 128, :])
        ld(GW1[kp], gw1_d[kp * 128:(kp + 1) * 128, :])
    for kp in range(4):
        ld(CW3[kp], cw3_d[kp * 128:(kp + 1) * 128, :])
        ld(GW2[kp], gw2_d[kp * 128:(kp + 1) * 128, :])
    for kp in range(2):
        ld(CW4[kp], cw4_d[kp * 128:(kp + 1) * 128, :])
        ld(GW3[kp], gw3_d[kp * 128:(kp + 1) * 128, :])

    E("vector", lambda eng: eng.tensor_scalar_mul(GS[:], GS[:], 1.0 / T))
    for mt in range(4):
        mm_group(0, 1, [(GW1[kp][:, mt * 128:(mt + 1) * 128], GS[:, kp:kp + 1]) for kp in range(8)], col0=mt)
    for mt in range(4):
        ps_read(0, "scalar", lambda eng, _m=mt: eng.activation(GH1[:, _m:_m + 1], PS[0][:, _m:_m + 1], AF.Gelu, bias=GB1P[:, _m:_m + 1]))
    for mt in range(2):
        mm_group(1, 1, [(GW2[kp][:, mt * 128:(mt + 1) * 128], GH1[:, kp:kp + 1]) for kp in range(4)], col0=mt)
    for mt in range(2):
        ps_read(1, "scalar", lambda eng, _m=mt: eng.activation(GH2[:, _m:_m + 1], PS[1][:, _m:_m + 1], AF.Gelu, bias=GB2P[:, _m:_m + 1]))
    mm_group(2, 1, [(GW3[kp], GH2[:, kp:kp + 1]) for kp in range(2)], rows=1)
    ps_read(2, "scalar", lambda eng: eng.activation(GF07[:], PS[2][0:1, 0:1], AF.Sigmoid, bias=GB3S[:]))
    E("vector", lambda eng: eng.tensor_scalar_mul(GF07[:], GF07[:], 0.7))

    twist_map = [(4, 1.0), (5, 1.0), (6, -1.0), (7, -1.0), (0, -1.0), (1, -1.0), (2, 1.0), (3, 1.0)]
    RELAX[0] = True
    for c in range(NCH):
        soft_fence()
        xnt_ = [ld(OUTA[:, dt * 512:(dt + 1) * 512], XN[dt * 128:(dt + 1) * 128, c * 512:(c + 1) * 512]) for dt in range(8)]
        cur_out, oth_out = OUTA, OUTB
        for cyc in range(3):
            soft_fence()
            c1t = []
            for m in range(8):
                bk = m % 2
                mm_group(bk, 512, [(CW1[kp][:, m * 128:(m + 1) * 128], cur_out[:, kp * 512:(kp + 1) * 512]) for kp in range(8)],
                         deps=tuple(xnt_))
                c1t.append(ps_read(bk, "scalar", lambda eng, _b=bk, _m=m: eng.activation(C1B[:, _m * 512:(_m + 1) * 512], PS[_b][:, 0:512], AF.Gelu, bias=CB1P[:, _m:_m + 1])))
            c2t = []
            for m in range(4):
                bk = 2 + m % 2
                mm_group(bk, 512, [(CW2[kp][:, m * 128:(m + 1) * 128], C1B[:, kp * 512:(kp + 1) * 512]) for kp in range(8)],
                         deps=tuple(c1t))
                c2t.append(ps_read(bk, "scalar", lambda eng, _b=bk, _m=m: eng.activation(C2B[:, _m * 512:(_m + 1) * 512], PS[_b][:, 0:512], AF.Gelu, bias=CB2P[:, _m:_m + 1])))
            c3t = []
            for m in range(2):
                bk = m % 2
                mm_group(bk, 512, [(CW3[kp][:, m * 128:(m + 1) * 128], C2B[:, kp * 512:(kp + 1) * 512]) for kp in range(4)],
                         deps=tuple(c2t))
                c3t.append(ps_read(bk, "scalar", lambda eng, _b=bk, _m=m: eng.activation(C3B[:, _m * 512:(_m + 1) * 512], PS[_b][:, 0:512], AF.Gelu, bias=CB3P[:, _m:_m + 1])))
            mm_group(2, 512, [(CW4[kp], C3B[:, kp * 512:(kp + 1) * 512]) for kp in range(2)], rows=1, deps=tuple(c3t))
            ttf = ps_read(2, "scalar", lambda eng: eng.activation(TFR, PS[2][0:1, 0:512], AF.Sigmoid, bias=CB4S[:]))
            tcm = E("vector", lambda eng: eng.tensor_scalar(TFR, TFR, 0.3, GF07[:], ALU.mult, ALU.add), (ttf,))
            tcp = E("vector", lambda eng: eng.tensor_scalar(TFR, TFR, 0.5, AR2[:], ALU.subtract, ALU.mult))
            tcp2 = E("vector", lambda eng: eng.tensor_scalar_add(TFR, TFR, 0.1))
            rank1(3, TFR, 512, deps=(tcp2,))
            tcb = ps_read(3, "scalar", lambda eng: eng.copy(CPB, PS[3][:, 0:512]))
            for dt in range(8):
                src2, sign = twist_map[dt]
                E("vector", lambda eng, _s=src2, _g=sign, _c=cur_out: eng.scalar_tensor_tensor(TMS_[:, 0:512], _c[:, _s * 512:(_s + 1) * 512], _g, CPB, ALU.mult, ALU.mult), (tcb,))
                E("vector", lambda eng, _d=dt, _c=cur_out, _o=oth_out: eng.tensor_add(_o[:, _d * 512:(_d + 1) * 512], _c[:, _d * 512:(_d + 1) * 512], TMS_[:, 0:512]))
            cur_out, oth_out = oth_out, cur_out
        soft_fence()
        t1_p = None; t2_p = None
        mm4_t = []; mm5_t = []
        for dt in range(8):
            tg = E("scalar", lambda eng, _d=dt, _c=cur_out: eng.activation(_c[:, _d * 512:(_d + 1) * 512], _c[:, _d * 512:(_d + 1) * 512], AF.Silu))
            tx = ld(XM_, XF[dt * 128:(dt + 1) * 128, c * 512:(c + 1) * 512], deps=(t2_p,) if t2_p else ())
            ts = ld(SM2, SSd[dt * 128:(dt + 1) * 128, c * 512:(c + 1) * 512], deps=(t1_p,) if t1_p else (), eng="gpsimd")
            t1 = E("vector", lambda eng, _d=dt, _c=cur_out: eng.tensor_mul(TMS_[:, 0:512], _c[:, _d * 512:(_d + 1) * 512], SM2), (tg, ts))
            t2 = E("vector", lambda eng, _d=dt: eng.tensor_add(X1FB[:, _d * 512:(_d + 1) * 512], XM_, TMS_[:, 0:512]), (t1, tx))
            store_toks.append(st_(X1d[dt * 128:(dt + 1) * 128, c * 512:(c + 1) * 512], X1FB[:, dt * 512:(dt + 1) * 512], deps=(t2,)))
            if dt == 0:
                mm4_t.append(mm_group(4, 512, [(ONC[:], X1FB[:, dt * 512:(dt + 1) * 512])], rows=1, deps=(t2,)))
            else:
                mm4_t.append(mm_acc(4, 512, [(ONC[:], X1FB[:, dt * 512:(dt + 1) * 512])], rows=1, deps=(t2,)))
            tq = E("scalar", lambda eng, _d=dt: eng.activation(X1SQ, X1FB[:, _d * 512:(_d + 1) * 512], AF.Square), (t2,) + ((mm5_t[-1],) if mm5_t else ()))
            if dt == 0:
                mm5_t.append(mm_group(5, 512, [(ONC[:], X1SQ)], rows=1, deps=(tq,)))
            else:
                mm5_t.append(mm_acc(5, 512, [(ONC[:], X1SQ)], rows=1, deps=(tq,)))
            t1_p, t2_p = t1, t2
        tmean = ps_read(4, "vector", lambda eng: eng.tensor_scalar_mul(MROW, PS[4][0:1, 0:512], 1.0 / D), deps=(mm4_t[-1],))
        tms2 = E("vector", lambda eng: eng.tensor_mul(VROW, MROW, MROW))
        tvar = ps_read(5, "vector", lambda eng: eng.scalar_tensor_tensor(VROW, PS[5][0:1, 0:512], 1.0 / D, VROW, ALU.mult, ALU.subtract), deps=(mm5_t[-1],))
        tve = E("vector", lambda eng: eng.tensor_scalar_add(VROW, VROW, 1e-5))
        tsq = E("scalar", lambda eng: eng.activation(VROW, VROW, AF.Sqrt), (tve,))
        trc = E("vector", lambda eng: eng.reciprocal(VROW, VROW), (tsq,))
        rank1(4, MROW, 512, deps=(tmean, tms2))
        tmb = ps_read(4, "scalar", lambda eng: eng.copy(MB_, PS[4][:, 0:512]))
        rank1(5, VROW, 512, deps=(trc,))
        trb2 = ps_read(5, "scalar", lambda eng: eng.copy(RB_, PS[5][:, 0:512]))
        xn_new = []
        tst_p = None
        for dt in range(8):
            ta = E("vector", lambda eng, _d=dt: eng.tensor_sub(X1NB[:, _d * 512:(_d + 1) * 512], X1FB[:, _d * 512:(_d + 1) * 512], MB_), (tmb,))
            tb = E("vector", lambda eng, _d=dt: eng.tensor_mul(X1NB[:, _d * 512:(_d + 1) * 512], X1NB[:, _d * 512:(_d + 1) * 512], RB_), (trb2,))
            store_toks.append(st_(X1Nd[dt * 128:(dt + 1) * 128, c * 512:(c + 1) * 512], X1NB[:, dt * 512:(dt + 1) * 512], deps=(tb,)))
            xn_new.append(tb)
        for s in range(4):
            trs = []
            for dt in range(8):
                trs.append(transpose(6, X1NB[:, dt * 512 + s * 128: dt * 512 + (s + 1) * 128], TMS_[:, dt * 128:(dt + 1) * 128],
                                     128, 128, deps=tuple(xn_new) + ((tst_p,) if tst_p else ())))
            tst_p = st_(X1NTd[(c * 4 + s) * 128:(c * 4 + s + 1) * 128, :], TMS_[:, 0:1024], deps=tuple(trs))
            store_toks.append(tst_p)
    RELAX[0] = False
    barrier()

    arena_reset()
    XH = alloc(16384); YH = alloc(16384)
    PIN2 = alloc(1024); PPO2 = alloc(1024); ZW = alloc(1024); TW0 = alloc(512); TW1 = alloc(512)
    LNROW = alloc(512); CBROW = alloc(512)
    CRY = ROWS[0:1, 0:512]; ABR = ROWS[0:1, 512:1024]; ABC = ROWS[0:1, 1024:1536]; TRW = ROWS[0:1, 1536:2048]
    RELAX[0] = True
    for hf in range(2):
        soft_fence()
        for u in range(NSUB):
            ld(XH[:, u * 512:(u + 1) * 512], X1NTd[u * 128:(u + 1) * 128, hf * 512:(hf + 1) * 512])
        for i in range(4):
            E("gpsimd", lambda eng, _i=i: eng.memset(YH[:, _i * 4096:(_i + 1) * 4096], 0.0))
        soft_fence()
        for n in range(NS):
            soft_fence()
            lnrow = LNROW[0:1, 0:512]
            cbrow = CBROW[0:1, 0:512]
            t_ln = ld(lnrow, LNABd[n:n + 1, hf * 512:(hf + 1) * 512])
            t_cb = ld(cbrow, CBNDd[n:n + 1, hf * 512:(hf + 1) * 512])
            rank1(6, lnrow, 512, deps=(t_ln,))
            tp1 = ps_read(6, "scalar", lambda eng: eng.activation(PIN2[:, 0:512], PS[6][:, 0:512], AF.Exp, scale=NIOP[:]))
            tp1b = E("scalar", lambda eng: eng.copy(PIN2[:, 512:1024], PIN2[:, 0:512]))
            tp2 = ps_read(6, "scalar", lambda eng: eng.activation(PPO2[:, 0:512], PS[6][:, 0:512], AF.Exp, scale=IOP[:]))
            rank1(7, cbrow, 512, deps=(t_cb,))
            tp3 = ps_read(7, "vector", lambda eng: eng.tensor_mul(PPO2[:, 0:512], PPO2[:, 0:512], PS[7][:, 0:512]), deps=(tp2,))
            tp3b = E("vector", lambda eng: eng.tensor_copy(PPO2[:, 512:1024], PPO2[:, 0:512]))
            tab = E("scalar", lambda eng: eng.activation(ABR, lnrow, AF.Exp), (t_ln,))
            tabc1 = E("vector", lambda eng: eng.tensor_scalar_mul(ABC, lnrow, 127.0), (t_ln,))
            tabc = E("scalar", lambda eng: eng.activation(ABC, ABC, AF.Exp), (tabc1,))
            E("vector", lambda eng: eng.memset(CRY, 0.0))
            zw_guard = ()
            for p in range(16):
                c0, c1 = 2 * p, 2 * p + 1
                tz = E("vector", lambda eng, _p=p: eng.tensor_mul(ZW, PIN2, XH[:, _p * 1024:(_p + 1) * 1024]), zw_guard + (tp1b,))
                E("vector", lambda eng: eng.tensor_mul(TRW, CRY, ABR), (tab,))
                ti0b = E("vector", lambda eng: eng.tensor_add(ZW[0:1, 0:512], ZW[0:1, 0:512], TRW))
                b0 = 0 if p % 2 == 0 else 2
                b1 = 1 if p % 2 == 0 else 3
                m_b0 = mm_group(b0, 512, [(UPI[:], ZW[:, 0:512])], deps=(ti0b,))
                m_4 = mm_group(4, 512, [(ONC[:], ZW[:, 0:512])], rows=1, deps=(ti0b,))
                ps_read(4, "vector", lambda eng: eng.tensor_mul(CRY, PS[4][0:1, 0:512], ABC), deps=(tabc,))
                E("vector", lambda eng: eng.tensor_mul(TRW, CRY, ABR))
                ti1b = E("vector", lambda eng: eng.tensor_add(ZW[0:1, 512:1024], ZW[0:1, 512:1024], TRW))
                m_b1 = mm_group(b1, 512, [(UPI[:], ZW[:, 512:1024])], deps=(ti1b,))
                m_5 = mm_group(5, 512, [(ONC[:], ZW[:, 512:1024])], rows=1, deps=(ti1b,))
                ps_read(5, "vector", lambda eng: eng.tensor_mul(CRY, PS[5][0:1, 0:512], ABC))
                ps_read(b0, "vector", lambda eng, _b=b0: eng.tensor_mul(TW0, PS[_b][:, 0:512], PPO2[:, 0:512]), deps=(tp3,))
                E("vector", lambda eng, _c=c0: eng.tensor_add(YH[:, _c * 512:(_c + 1) * 512], YH[:, _c * 512:(_c + 1) * 512], TW0))
                ps_read(b1, "vector", lambda eng, _b=b1: eng.tensor_mul(TW1, PS[_b][:, 0:512], PPO2[:, 512:1024]), deps=(tp3b,))
                E("vector", lambda eng, _c=c1: eng.tensor_add(YH[:, _c * 512:(_c + 1) * 512], YH[:, _c * 512:(_c + 1) * 512], TW1))
                zw_guard = (m_b0, m_4, m_b1, m_5)
        soft_fence()
        for u in range(NSUB):
            store_toks.append(st_(YSSMd[u * 128:(u + 1) * 128, hf * 512:(hf + 1) * 512], YH[:, u * 512:(u + 1) * 512]))
    RELAX[0] = False
    barrier()

    arena_reset()
    XSC = alloc(8192); YSC = alloc(8192); HIDH = alloc(4096)
    WDH = [alloc(1024) for _ in range(4)]
    WGT = alloc(1024); WUT = alloc(1024)
    GTS = alloc(8); MK1 = alloc(8); MK2 = alloc(8); SMB = alloc(8)
    GBC = alloc(512); TW0b = alloc(512); TW1b = alloc(512); GROW = alloc(512)
    GATR = SMALL[0:8, 0:1024]; SCF = SMALL[0:8, 1024:1536]
    WGA = [SMALL[:, 1544 + 8 * kp: 1552 + 8 * kp] for kp in range(8)]
    M1 = STAT[:, 4:5]; M2 = STAT[:, 5:6]; P1_ = STAT[:, 6:7]; P2_ = STAT[:, 7:8]
    for kp in range(8):
        ld(WGA[kp], wgate_d[kp * 128:(kp + 1) * 128, :])
    mgroups = [(0, 4), (4, 4), (8, 3)]
    for sc in range(4):
        RELAX[0] = True
        soft_fence()
        for kp in range(8):
            ld(XSC[:, kp * 1024:(kp + 1) * 1024], X1Nd[kp * 128:(kp + 1) * 128, sc * 1024:(sc + 1) * 1024])
        tyz = E("gpsimd", lambda eng: eng.memset(YSC, 0.0))
        RELAX[0] = False
        for tc in range(2):
            mm_group(0, 512, [(WGA[kp], XSC[:, kp * 1024 + tc * 512: kp * 1024 + (tc + 1) * 512]) for kp in range(8)], rows=8)
            ps_read(0, "scalar", lambda eng: eng.copy(SCF, PS[0][0:8, 0:512]))
            for g in range(4):
                transpose(1, SCF[:, g * 128:(g + 1) * 128], GTS, 8, 128)
                E("vector", lambda eng: eng.reduce_max(M1, GTS, AX.X))
                E("vector", lambda eng: eng.tensor_single_scalar(MK1, GTS, M1, ALU.is_equal))
                E("vector", lambda eng: eng.scalar_tensor_tensor(SMB, MK1, -1e9, GTS, ALU.mult, ALU.add))
                E("vector", lambda eng: eng.reduce_max(M2, SMB, AX.X))
                E("vector", lambda eng: eng.tensor_single_scalar(MK2, SMB, M2, ALU.is_equal))
                E("vector", lambda eng: eng.tensor_sub(P1_, M1, M2))
                E("scalar", lambda eng: eng.activation(P1_, P1_, AF.Sigmoid))
                E("vector", lambda eng: eng.tensor_scalar(P2_, P1_, -1.0, 1.0, ALU.mult, ALU.add))
                E("vector", lambda eng: eng.tensor_scalar(MK1, MK1, P1_, None, ALU.mult))
                E("vector", lambda eng: eng.tensor_scalar(MK2, MK2, P2_, None, ALU.mult))
                E("vector", lambda eng: eng.tensor_add(GTS, MK1, MK2))
                transpose(2, GTS, GATR[:, tc * 512 + g * 128: tc * 512 + (g + 1) * 128], 128, 8)
        store_toks.append(st_(GATEd[:, sc * 1024:(sc + 1) * 1024], GATR[:, 0:1024]))
        RELAX[0] = True
        soft_fence()
        thd_prev = None
        tyy_prev = None
        grow_guard = None
        for e in range(ME):
            for (m0, nmt) in mgroups:
                soft_fence()
                hid_done = []
                gu_guard = ()
                for mi in range(nmt):
                    mt = m0 + mi
                    wg_tk = [ld(WGT[:, kp * 128:(kp + 1) * 128], wgE_d[e, mt, kp, :, :], deps=gu_guard) for kp in range(8)]
                    wu_tk = [ld(WUT[:, kp * 128:(kp + 1) * 128], wuE_d[e, mt, kp, :, :], deps=gu_guard, eng="gpsimd") for kp in range(8)]
                    gu_guard = ()
                    for tc in range(2):
                        gmm = mm_group(3, 512, [(WGT[:, kp * 128:(kp + 1) * 128], XSC[:, kp * 1024 + tc * 512: kp * 1024 + (tc + 1) * 512]) for kp in range(8)], deps=tuple(wg_tk))
                        umm = mm_group(4, 512, [(WUT[:, kp * 128:(kp + 1) * 128], XSC[:, kp * 1024 + tc * 512: kp * 1024 + (tc + 1) * 512]) for kp in range(8)], deps=tuple(wu_tk))
                        tsl = ps_read(3, "scalar", lambda eng: eng.activation(TW0b, PS[3][:, 0:512], AF.Silu), deps=(thd_prev,) if thd_prev else ())
                        thd = ps_read(4, "vector", lambda eng, _mi=mi, _tc=tc: eng.tensor_mul(HIDH[:, _mi * 1024 + _tc * 512: _mi * 1024 + (_tc + 1) * 512], TW0b, PS[4][:, 0:512]), deps=(tsl,))
                        hid_done.append(thd)
                        thd_prev = thd
                        gu_guard = gu_guard + (gmm, umm)
                wd_tk = [ld(WDH[mi], wdE_d[e, (m0 + mi) * 128:(m0 + mi + 1) * 128, :]) for mi in range(nmt)]
                for tc in range(2):
                    t_gr = ld(GROW[0:1, 0:512], GATEd[e:e + 1, sc * 1024 + tc * 512: sc * 1024 + (tc + 1) * 512],
                              deps=(grow_guard,) if grow_guard else ())
                    grow_guard = rank1(5, GROW[0:1, 0:512], 512, deps=(t_gr,))
                    tgb = ps_read(5, "scalar", lambda eng: eng.copy(GBC, PS[5][:, 0:512]), deps=(tyy_prev,) if tyy_prev else ())
                    for m in range(8):
                        bnk = 6 if m % 2 == 0 else 7
                        mm_group(bnk, 512, [(WDH[mi][:, m * 128:(m + 1) * 128], HIDH[:, mi * 1024 + tc * 512: mi * 1024 + (tc + 1) * 512]) for mi in range(nmt)],
                                 deps=tuple(wd_tk) + tuple(hid_done))
                        tyy = ps_read(bnk, "vector", lambda eng, _b=bnk: eng.tensor_mul(TW1b, PS[_b][:, 0:512], GBC), deps=(tgb,))
                        tyy_prev = tyy
                        E("vector", lambda eng, _m=m, _tc=tc: eng.tensor_add(YSC[:, _m * 1024 + _tc * 512: _m * 1024 + (_tc + 1) * 512],
                                                                             YSC[:, _m * 1024 + _tc * 512: _m * 1024 + (_tc + 1) * 512], TW1b),
                          deps=(tyz,))
        soft_fence()
        for m in range(8):
            store_toks.append(st_(YMOEd[m * 128:(m + 1) * 128, sc * 1024:(sc + 1) * 1024], YSC[:, m * 1024:(m + 1) * 1024]))
        RELAX[0] = False
    barrier()

    arena_reset()
    YST = alloc(1024); YSF = alloc(4096); XOB = alloc(4096); OTM = alloc(1024)
    X1M_ = alloc(512); X1NM_ = alloc(512); YMM = alloc(512); G2M = alloc(512); S2M = alloc(512)
    for c in range(NCH):
        for s in range(4):
            ld(YST, YSSMd[(c * 4 + s) * 128:(c * 4 + s + 1) * 128, :])
            for dt in range(8):
                transpose(0, YST[:, dt * 128:(dt + 1) * 128], YSF[:, dt * 512 + s * 128: dt * 512 + (s + 1) * 128], 128, 128)
        for dt in range(8):
            ld(X1M_, X1d[dt * 128:(dt + 1) * 128, c * 512:(c + 1) * 512])
            ld(X1NM_, X1Nd[dt * 128:(dt + 1) * 128, c * 512:(c + 1) * 512])
            ld(YMM, YMOEd[dt * 128:(dt + 1) * 128, c * 512:(c + 1) * 512], eng="gpsimd")
            E("scalar", lambda eng: eng.activation(G2M, YMM, AF.Silu))
            E("vector", lambda eng, _d=dt: eng.tensor_scalar(S2M, X1NM_, DPP[:, _d:_d + 1], None, ALU.mult))
            E("vector", lambda eng, _d=dt: eng.tensor_add(S2M, S2M, YSF[:, _d * 512:(_d + 1) * 512]))
            E("vector", lambda eng: eng.tensor_mul(S2M, S2M, G2M))
            E("vector", lambda eng, _d=dt: eng.tensor_add(XOB[:, _d * 512:(_d + 1) * 512], X1M_, S2M))
        for s in range(4):
            for dt in range(8):
                transpose(1, XOB[:, dt * 512 + s * 128: dt * 512 + (s + 1) * 128], OTM[:, dt * 128:(dt + 1) * 128], 128, 128)
            store_toks.append(st_(out_d[(c * 4 + s) * 128:(c * 4 + s + 1) * 128, :], OTM))

    if debug:
        dbg = {"XN": XN, "HF": HFd, "QF": QFd, "KF": KFd, "KTM": KTMd, "VE": VEd,
               "Y1": Y1d, "H2": H2d, "SS": SSd, "X1": X1d, "X1N": X1Nd,
               "YSSM": YSSMd, "YMOE": YMOEd, "BETAo": BETA, "GATEo": GATEd}
        for nm, t in dbg.items():
            o = nc.dram_tensor("dbg_" + nm, list(t.tensor.shape), F32, kind="ExternalOutput").ap()
            store_toks.append(st_(o[:], t[:]))
    S.emit("sync", None, tuple(store_toks))
    S.build(nc, maxseq=maxseq)
    return nc, S


_nc_cache = [None]


def kernel(**inputs):
    def f(name):
        return np.ascontiguousarray(np.asarray(inputs[name], dtype=np.float32))

    x_full = f("L0_attn_x")
    wgE = f("L1_ffn_w_g").reshape(ME, 8, 128, MMT, 128).transpose(0, 3, 1, 2, 4).copy()
    wuE = f("L1_ffn_w_u").reshape(ME, 8, 128, MMT, 128).transpose(0, 3, 1, 2, 4).copy()

    iotaf = np.broadcast_to(np.arange(128, dtype=np.float32)[None, :], (128, 128)).copy()
    jj, tt = np.meshgrid(np.arange(128), np.arange(128), indexing="ij")
    upinc = (jj <= tt).astype(np.float32)
    upneg = -(jj < tt).astype(np.float32)

    common = {
        "n1w": f("L0_attn_norm1_w").reshape(1, D),
        "n2w": f("L0_attn_norm2_w").reshape(1, D),
        "wq": f("L0_attn_Wq"), "wk": f("L0_attn_Wk"), "wv": f("L0_attn_Wv"), "wo": f("L0_attn_Wo"),
        "bw": f("L0_attn_beta_w").reshape(1, D),
        "bb": f("L0_attn_beta_b").reshape(1, 1),
        "alog": f("L0_attn_alpha_logit").reshape(1, D),
        "wg0": f("L0_attn_w_gate"), "wu0": f("L0_attn_w_up"), "wd0": f("L0_attn_w_down"),
        "cw1": f("L0_ffn_cn_w1"), "cw2": f("L0_ffn_cn_w2"), "cw3": f("L0_ffn_cn_w3"), "cw4": f("L0_ffn_cn_w4"),
        "cb1": f("L0_ffn_cn_b1").reshape(1, D), "cb2": f("L0_ffn_cn_b2").reshape(1, 512),
        "cb3": f("L0_ffn_cn_b3").reshape(1, 256), "cb4": f("L0_ffn_cn_b4").reshape(1, 1),
        "gw1": f("L0_ffn_gc_w1"), "gw2": f("L0_ffn_gc_w2"), "gw3": f("L0_ffn_gc_w3"),
        "gb1": f("L0_ffn_gc_b1").reshape(1, 512), "gb2": f("L0_ffn_gc_b2").reshape(1, 256),
        "gb3": f("L0_ffn_gc_b3").reshape(1, 1),
        "ar": f("L0_ffn_adaptive_range").reshape(1, 1),
        "logA": f("L1_attn_log_A"), "Bp": f("L1_attn_B_param"), "Cp": f("L1_attn_C"),
        "Dv": f("L1_attn_D").reshape(1, D),
        "wgate": f("L1_ffn_w_gate"),
        "wgE": wgE, "wuE": wuE, "wdE": f("L1_ffn_w_d"),
        "idn": np.eye(128, dtype=np.float32),
        "upinc": upinc, "upneg": upneg, "iotaf": iotaf,
        "onesr": np.ones((1, 128), np.float32), "onesc": np.ones((128, 1), np.float32),
        "niop": -np.arange(128, dtype=np.float32).reshape(128, 1),
        "iop": np.arange(128, dtype=np.float32).reshape(128, 1),
    }
    in_maps = [dict(common, x=np.ascontiguousarray(x_full[b])) for b in range(8)]
    if _nc_cache[0] is None:
        _nc_cache[0] = build_program()[0]
    nc = _nc_cache[0]
    res = run_bass_kernel_spmd(nc, in_maps, core_ids=list(range(8)))
    return np.stack([r["out"] for r in res.results], axis=0)


# version 09
# speedup vs baseline: 1.4612x; 1.0085x over previous
import numpy as np
import concourse.bass as bass
import concourse.mybir as mybir
from concourse.bass_utils import run_bass_kernel_spmd

F32 = mybir.dt.float32
AF = mybir.ActivationFunctionType
ALU = mybir.AluOpType
AX = mybir.AxisListType

T = 4096
D = 1024
NS = 64
HID0 = 2048
ME = 8
MM = 1408
MMT = 11
NSUB = T // 128
NCH = T // 512

ENGINES = ("sync", "tensor", "vector", "scalar", "gpsimd")
AUTO_FENCE = True


class Sched:
    def __init__(self):
        self.q = {e: [] for e in ENGINES}
        self.gseq = 0
        self.seqs = {e: [] for e in ENGINES}
        self.marks = []

    def emit(self, eng, fn, deps=(), dma=False):
        lst = self.q[eng]
        tok = (eng, len(lst))
        deps = tuple(d for d in deps if d is not None)
        lst.append([fn, deps, dma, False])
        self.seqs[eng].append(self.gseq)
        self.gseq += 1
        return tok

    def build(self, nc, maxseq=None):
        if maxseq is not None:
            for e in ENGINES:
                keep = sum(1 for s in self.seqs[e] if s < maxseq)
                self.q[e] = self.q[e][:keep]
        for e in ENGINES:
            for rec in self.q[e]:
                if rec[2]:
                    rec[3] = True
                for (de, di) in rec[1]:
                    self.q[de][di][3] = True
        vals = {}
        for e in ENGINES:
            c = 0
            for i, rec in enumerate(self.q[e]):
                if rec[3] and rec[0] is not None:
                    c += 16 if rec[2] else 1
                vals[(e, i)] = c
        import contextlib
        stack = contextlib.ExitStack()
        sems = {e: stack.enter_context(nc.semaphore(f"sem_{e}")) for e in ENGINES}
        with nc.Block() as block:
            for e in ENGINES:
                def body(eng, _e=e):
                    hw = {o: 0 for o in ENGINES}
                    for i, (fn, deps, dma, needed) in enumerate(self.q[_e]):
                        for (de, di) in deps:
                            v = vals[(de, di)]
                            if v > hw[de]:
                                eng.wait_ge(sems[de], v)
                                hw[de] = v
                        if fn is not None:
                            inst = fn(eng)
                            if needed:
                                inst.then_inc(sems[_e], 16 if dma else 1)
                getattr(block, e)(body)
        stack.close()


def build_program(maxseq=None, debug=False):
    nc = bass.Bass("TRN2", target_bir_lowering=False)
    S = Sched()

    def din(name, shape):
        return nc.dram_tensor(name, shape, F32, kind="ExternalInput").ap()

    def dscr(name, shape):
        return nc.dram_tensor(name, shape, F32, kind="Internal").ap()

    x_d = din("x", [T, D])
    n1w_d = din("n1w", [1, D]); n2w_d = din("n2w", [1, D])
    wq_d = din("wq", [D, D]); wk_d = din("wk", [D, D]); wv_d = din("wv", [D, D]); wo_d = din("wo", [D, D])
    bw_d = din("bw", [1, D]); bb_d = din("bb", [1, 1]); alog_d = din("alog", [1, D])
    wg0_d = din("wg0", [D, HID0]); wu0_d = din("wu0", [D, HID0]); wd0_d = din("wd0", [HID0, D])
    cw1_d = din("cw1", [D, D]); cw2_d = din("cw2", [D, 512]); cw3_d = din("cw3", [512, 256]); cw4_d = din("cw4", [256, 1])
    cb1_d = din("cb1", [1, D]); cb2_d = din("cb2", [1, 512]); cb3_d = din("cb3", [1, 256]); cb4_d = din("cb4", [1, 1])
    gw1_d = din("gw1", [D, 512]); gw2_d = din("gw2", [512, 256]); gw3_d = din("gw3", [256, 1])
    gb1_d = din("gb1", [1, 512]); gb2_d = din("gb2", [1, 256]); gb3_d = din("gb3", [1, 1])
    ar_d = din("ar", [1, 1])
    logA_d = din("logA", [D, NS]); Bp_d = din("Bp", [D, NS]); Cp_d = din("Cp", [D, NS]); Dv_d = din("Dv", [1, D])
    wgate_d = din("wgate", [D, ME])
    wgE_d = din("wgE", [ME, MMT, 8, 128, 128])
    wuE_d = din("wuE", [ME, MMT, 8, 128, 128])
    wdE_d = din("wdE", [ME, MM, D])
    id_d = din("idn", [128, 128]); upinc_d = din("upinc", [128, 128]); upneg_d = din("upneg", [128, 128])
    iotaf_d = din("iotaf", [128, 128]); onesr_d = din("onesr", [1, 128]); onesc_d = din("onesc", [128, 1])
    niop_d = din("niop", [128, 1]); iop_d = din("iop", [128, 1])

    out_d = nc.dram_tensor("out", [T, D], F32, kind="ExternalOutput").ap()

    XN = dscr("XN", [D, T]); XF = dscr("XF", [D, T]); HFd = dscr("HFd", [D, T])
    BETA = dscr("BETAd", [T, 1])
    QFd = dscr("QFd", [D, T]); KFd = dscr("KFd", [D, T]); KTMd = dscr("KTMd", [T, D]); VEd = dscr("VEd", [T, D])
    Y1d = dscr("Y1d", [D, T]); H2d = dscr("H2d", [D, T]); HIDd = dscr("HIDd", [HID0, T])
    SSd = dscr("SSd", [D, T]); X1d = dscr("X1d", [D, T]); X1Nd = dscr("X1Nd", [D, T]); X1NTd = dscr("X1NTd", [T, D])
    YSSMd = dscr("YSSMd", [T, D]); YMOEd = dscr("YMOEd", [D, T]); GATEd = dscr("GATEd", [8, T])

    def sb(name, shape):
        return nc.alloc_sbuf_tensor(name, shape, F32).ap()

    IDN = sb("IDN", [128, 128]); UPI = sb("UPI", [128, 128]); UPN = sb("UPN", [128, 128])
    IOF = sb("IOF", [128, 128]); ONR = sb("ONR", [1, 128]); ONC = sb("ONC", [128, 1])
    NIOP = sb("NIOP", [128, 1]); IOP = sb("IOP", [128, 1])
    W1B = sb("W1B", [128, D]); BWB = sb("BWB", [128, D]); BB128 = sb("BB128", [128, 1])
    PINV_E = sb("PINV_E", [128, D]); PFWD_E = sb("PFWD_E", [128, D])
    OMAR = sb("OMAR", [1, D]); AROWV = sb("AROWV", [1, D])
    OMA127 = AROWV
    GS = sb("GS", [128, 8]); N2W = sb("N2W", [128, 8]); DPP = sb("DPP", [128, 8])
    CB1P = sb("CB1P", [128, 8]); CB2P = sb("CB2P", [128, 4]); CB3P = sb("CB3P", [128, 2])
    GB1P = sb("GB1P", [128, 4]); GB2P = sb("GB2P", [128, 2])
    CB4S = sb("CB4S", [1, 1]); GB3S = sb("GB3S", [1, 1]); AR2 = sb("AR2", [1, 1]); GF07 = sb("GF07", [1, 1])
    LNAB = sb("LNAB", [64, D]); CBND = sb("CBND", [64, D]); CPND = sb("CPND", [64, D])
    SMALL = sb("SMALLb", [128, 2048]); ROWS = sb("ROWSb", [1, 2048]); STAT = sb("STATb", [128, 16])
    PA = sb("PAb", [128, 39100])

    PS = [nc.alloc_psum_tensor(f"pb{i}", [128, 512], F32).ap() for i in range(8)]
    ps_writer = [None] * 8
    ps_reader = [None] * 8

    cur = [0]
    def arena_reset():
        cur[0] = 0
    def alloc(n):
        off = cur[0]
        assert off + n <= 39100, f"arena overflow {off+n}"
        cur[0] += n
        return PA[:, off:off + n]

    tok_last = {e: None for e in ENGINES}
    RELAX = [False]
    fence_base = [()]

    def cfence():
        return tuple(t for t in tok_last.values() if t is not None)

    def soft_fence():
        fence_base[0] = cfence()

    def E(eng, fn, deps=(), dma=False, fence=None):
        if fence is None:
            fence = not RELAX[0]
        if fence:
            d = tuple(deps) + cfence()
        else:
            d = tuple(deps) + fence_base[0]
        t = S.emit(eng, fn, d, dma)
        tok_last[eng] = t
        return t

    def barrier():
        toks = cfence()
        for e in ENGINES:
            tok_last[e] = S.emit(e, None, toks)
        S.marks.append(S.gseq)

    def mm_group(bank, ncols, pairs, deps=(), col0=0, rows=128):
        dl = list(deps)
        if ps_reader[bank] is not None:
            dl.append(ps_reader[bank])
        t = None
        n = len(pairs)
        for i, (lhsT, rhs) in enumerate(pairs):
            def fn(eng, _l=lhsT, _r=rhs, _s=(i == 0), _e=(i == n - 1)):
                return eng.matmul(PS[bank][0:rows, col0:col0 + ncols], _l, _r,
                                  start=_s, stop=_e, skip_group_check=True)
            t = E("tensor", fn, tuple(dl) if i == 0 else (), fence=(None if i == 0 else False))
        ps_writer[bank] = t
        return t

    def mm_acc(bank, ncols, pairs, deps=(), col0=0, rows=128):
        t = None
        for i, (lhsT, rhs) in enumerate(pairs):
            def fn(eng, _l=lhsT, _r=rhs):
                return eng.matmul(PS[bank][0:rows, col0:col0 + ncols], _l, _r,
                                  start=False, stop=True, skip_group_check=True)
            t = E("tensor", fn, tuple(deps) if i == 0 else (), fence=(None if i == 0 else False))
        ps_writer[bank] = t
        return t

    def ps_read(bank, eng, fn, deps=()):
        t = E(eng, fn, tuple(deps) + ((ps_writer[bank],) if ps_writer[bank] else ()))
        ps_reader[bank] = t
        return t

    def transpose(bank, src_ap, dst_ap, p, c, deps=()):
        dl = list(deps)
        if ps_reader[bank] is not None:
            dl.append(ps_reader[bank])
        tt = E("tensor", lambda eng, _s=src_ap, _p=p, _c=c: eng.transpose(
            PS[bank][0:_c, 0:_p], _s, IDN[0:_p, 0:_p]), tuple(dl))
        ps_writer[bank] = tt
        return ps_read(bank, "scalar", lambda eng, _d=dst_ap, _p=p, _c=c: eng.copy(_d, PS[bank][0:_c, 0:_p]))

    def rank1(bank, row_ap, ncols, deps=()):
        dl = list(deps)
        if ps_reader[bank] is not None:
            dl.append(ps_reader[bank])
        t = E("tensor", lambda eng, _r=row_ap, _n=ncols: eng.matmul(
            PS[bank][:, 0:_n], ONR[:], _r, start=True, stop=True, skip_group_check=True), tuple(dl))
        ps_writer[bank] = t
        return t

    def ld(dst, src, deps=(), eng="sync"):
        return E(eng, lambda e, _d=dst, _s=src: e.dma_start(_d, _s), deps, dma=True)

    def st_(dst, src, deps=(), eng="gpsimd"):
        return E(eng, lambda e, _d=dst, _s=src: e.dma_start(_d, _s), deps, dma=True)

    store_toks = []

    ld(IDN[:], id_d[:]); ld(UPI[:], upinc_d[:]); ld(UPN[:], upneg_d[:])
    ld(IOF[:], iotaf_d[:]); ld(ONR[:], onesr_d[:]); ld(ONC[:], onesc_d[:])
    ld(NIOP[:], niop_d[:]); ld(IOP[:], iop_d[:])

    def row_bcast(dst, src_row_ap, width):
        for h in range((width + 511) // 512):
            w = min(512, width - h * 512)
            rank1(0, src_row_ap[:, h * 512:h * 512 + w], w)
            ps_read(0, "scalar", lambda eng, _d=dst, _h=h, _w=w: eng.copy(_d[:, _h * 512:_h * 512 + _w], PS[0][:, 0:_w]))

    t = ld(ROWS[0:1, 0:D], n1w_d[:])
    row_bcast(W1B, ROWS[0:1, 0:D], D)
    ld(ROWS[0:1, 0:D], bw_d[:])
    row_bcast(BWB, ROWS[0:1, 0:D], D)
    ld(ROWS[0:1, 1024:1025], bb_d[:])
    rank1(0, ROWS[0:1, 1024:1025], 1)
    ps_read(0, "scalar", lambda eng: eng.copy(BB128[:], PS[0][:, 0:1]))

    ld(ROWS[0:1, 0:D], alog_d[:])
    E("scalar", lambda eng: eng.activation(AROWV[:], ROWS[0:1, 0:D], AF.Sigmoid))
    E("vector", lambda eng: eng.tensor_scalar(OMAR[:], AROWV[:], -1.0, 1.0, ALU.mult, ALU.add))
    lnoma = ROWS[0:1, 1024:1024 + D]
    E("scalar", lambda eng: eng.activation(lnoma, OMAR[:], AF.Ln))
    A_PP = SMALL[:, 0:8]; LOM_PP = SMALL[:, 8:16]; NLOM_PP = SMALL[:, 16:24]
    STG = SMALL[:, 24:152]
    for dt in range(8):
        transpose(1, AROWV[:, dt * 128:(dt + 1) * 128], A_PP[:, dt:dt + 1], 1, 128)
    for dt in range(8):
        transpose(1, lnoma[:, dt * 128:(dt + 1) * 128], LOM_PP[:, dt:dt + 1], 1, 128)
    E("vector", lambda eng: eng.tensor_scalar_mul(NLOM_PP[:], LOM_PP[:], -1.0))
    E("vector", lambda eng: eng.tensor_scalar_mul(OMA127[:], lnoma, 127.0))
    E("scalar", lambda eng: eng.activation(OMA127[:], OMA127[:], AF.Exp))
    for dt in range(8):
        E("scalar", lambda eng, _dt=dt: eng.activation(STG, IOF[:], AF.Exp, scale=NLOM_PP[:, _dt:_dt + 1]))
        E("vector", lambda eng, _dt=dt: eng.tensor_scalar(STG, STG, A_PP[:, _dt:_dt + 1], None, ALU.mult))
        transpose(1, STG, PINV_E[:, dt * 128:(dt + 1) * 128], 128, 128)
        E("scalar", lambda eng, _dt=dt: eng.activation(STG, IOF[:], AF.Exp, scale=LOM_PP[:, _dt:_dt + 1]))
        transpose(1, STG, PFWD_E[:, dt * 128:(dt + 1) * 128], 128, 128)

    def row_to_pp(dst, src_d, width):
        ld(ROWS[0:1, 0:width], src_d[:])
        for dt in range(width // 128):
            transpose(1, ROWS[0:1, dt * 128:(dt + 1) * 128], dst[:, dt:dt + 1], 1, 128)

    row_to_pp(N2W, n2w_d, D)
    row_to_pp(DPP, Dv_d, D)
    row_to_pp(CB1P, cb1_d, D)
    row_to_pp(CB2P, cb2_d, 512)
    row_to_pp(CB3P, cb3_d, 256)
    row_to_pp(GB1P, gb1_d, 512)
    row_to_pp(GB2P, gb2_d, 256)
    ld(CB4S[:], cb4_d[:]); ld(GB3S[:], gb3_d[:])
    ld(ROWS[0:1, 1024:1025], ar_d[:])
    E("vector", lambda eng: eng.tensor_scalar_mul(AR2[:], ROWS[0:1, 1024:1025], 2.0))

    LA_T = SMALL[:, 152:216]
    for (src_d, dst) in ((logA_d, LNAB), (Bp_d, CBND), (Cp_d, CPND)):
        for dt in range(8):
            ld(LA_T, src_d[dt * 128:(dt + 1) * 128, :])
            transpose(1, LA_T, dst[0:64, dt * 128:(dt + 1) * 128], 128, 64)
    arena_reset()
    TMPA = alloc(1024)[0:64, :]
    RECA = alloc(1024)[0:64, :]
    E("scalar", lambda eng: eng.activation(TMPA, LNAB[0:64, :], AF.Exp))
    E("vector", lambda eng: eng.tensor_scalar_mul(TMPA, TMPA, -1.0))
    E("vector", lambda eng: eng.reciprocal(RECA, TMPA))
    E("vector", lambda eng: eng.tensor_scalar_mul(LNAB[0:64, :], TMPA, 0.1))
    E("scalar", lambda eng: eng.activation(TMPA, LNAB[0:64, :], AF.Exp))
    E("vector", lambda eng: eng.tensor_scalar(TMPA, TMPA, 1.0, None, ALU.subtract))
    E("vector", lambda eng: eng.tensor_mul(TMPA, TMPA, RECA))
    E("vector", lambda eng: eng.tensor_mul(CBND[0:64, :], CBND[0:64, :], TMPA))
    E("vector", lambda eng: eng.tensor_mul(CBND[0:64, :], CBND[0:64, :], CPND[0:64, :]))
    LNABd = dscr("LNABd", [64, D]); CBNDd = dscr("CBNDd", [64, D])
    store_toks.append(st_(LNABd[:], LNAB[0:64, :]))
    store_toks.append(st_(CBNDd[:], CBND[0:64, :]))
    E("vector", lambda eng: eng.memset(GS[:], 0.0))
    barrier()

    arena_reset()
    XT = alloc(1024); XSQ = alloc(1024); XNT = alloc(1024); HT = alloc(1024)
    FMB = alloc(1024); FMB2 = alloc(1024)
    s1 = STAT[:, 0:1]; s2 = STAT[:, 1:2]; s3 = STAT[:, 2:3]; s4 = STAT[:, 3:4]
    for u in range(NSUB):
        ld(XT, x_d[u * 128:(u + 1) * 128, :])
        E("scalar", lambda eng: eng.activation(XSQ, XT, AF.Square))
        E("vector", lambda eng: eng.reduce_sum(s1, XT, AX.X))
        E("vector", lambda eng: eng.reduce_sum(s2, XSQ, AX.X))
        E("vector", lambda eng: eng.tensor_scalar_mul(s1, s1, 1.0 / D))
        E("vector", lambda eng: eng.tensor_mul(s3, s1, s1))
        E("vector", lambda eng: eng.scalar_tensor_tensor(s2, s2, 1.0 / D, s3, ALU.mult, ALU.subtract))
        E("vector", lambda eng: eng.tensor_scalar_add(s2, s2, 1e-5))
        E("scalar", lambda eng: eng.activation(s2, s2, AF.Sqrt))
        E("vector", lambda eng: eng.reciprocal(s2, s2))
        E("vector", lambda eng: eng.tensor_scalar(XNT, XT, s1, s2, ALU.subtract, ALU.mult))
        E("scalar", lambda eng: eng.activation(XSQ, XNT, AF.Square))
        E("vector", lambda eng: eng.reduce_sum(s3, XSQ, AX.X))
        E("vector", lambda eng: eng.tensor_scalar(s3, s3, 1.0 / D, 1e-6, ALU.mult, ALU.add))
        E("scalar", lambda eng: eng.activation(s3, s3, AF.Sqrt))
        E("vector", lambda eng: eng.reciprocal(s3, s3))
        E("vector", lambda eng: eng.tensor_scalar(HT, XNT, s3, None, ALU.mult))
        E("vector", lambda eng: eng.tensor_mul(HT, HT, W1B[:]))
        E("vector", lambda eng: eng.tensor_mul(XSQ, HT, BWB[:]))
        E("vector", lambda eng: eng.reduce_sum(s4, XSQ, AX.X))
        E("scalar", lambda eng: eng.activation(s4, s4, AF.Sigmoid, bias=BB128[:]))
        store_toks.append(st_(BETA[u * 128:(u + 1) * 128, :], s4))
        for dt in range(8):
            transpose(2, XNT[:, dt * 128:(dt + 1) * 128], FMB[:, dt * 128:(dt + 1) * 128], 128, 128)
            transpose(3, HT[:, dt * 128:(dt + 1) * 128], FMB2[:, dt * 128:(dt + 1) * 128], 128, 128)
            store_toks.append(st_(XN[dt * 128:(dt + 1) * 128, u * 128:(u + 1) * 128], FMB[:, dt * 128:(dt + 1) * 128]))
            store_toks.append(st_(HFd[dt * 128:(dt + 1) * 128, u * 128:(u + 1) * 128], FMB2[:, dt * 128:(dt + 1) * 128]))
            E("vector", lambda eng, _dt=dt: eng.reduce_sum(STAT[:, 8 + _dt:9 + _dt], FMB[:, _dt * 128:(_dt + 1) * 128], AX.X))
            E("vector", lambda eng, _dt=dt: eng.tensor_add(GS[:, _dt:_dt + 1], GS[:, _dt:_dt + 1], STAT[:, 8 + _dt:9 + _dt]))
        for dt in range(8):
            transpose(2, XT[:, dt * 128:(dt + 1) * 128], FMB[:, dt * 128:(dt + 1) * 128], 128, 128)
            store_toks.append(st_(XF[dt * 128:(dt + 1) * 128, u * 128:(u + 1) * 128], FMB[:, dt * 128:(dt + 1) * 128]))
    barrier()

    arena_reset()
    WQ = [alloc(1024) for _ in range(8)]
    WK = [alloc(1024) for _ in range(8)]
    WV = [alloc(1024) for _ in range(8)]
    HFB = alloc(4096); KFB = alloc(4096); VFB = alloc(4096)
    KSQ = VFB
    QM = alloc(512); VTM = alloc(1024)
    OMAC = HFB[:, 0:1024]
    KTMS = VTM
    ZB = KFB[:, 0:1024]
    INVB = QM
    CARRY = ROWS[0:1, 0:D]
    KROW = ROWS[0:1, 1024:1536]

    for kp in range(8):
        ld(WQ[kp], wq_d[kp * 128:(kp + 1) * 128, :])
        ld(WK[kp], wk_d[kp * 128:(kp + 1) * 128, :])
        ld(WV[kp], wv_d[kp * 128:(kp + 1) * 128, :])
    E("vector", lambda eng: eng.memset(CARRY, 0.0))
    RELAX[0] = True
    qst_p = None
    for c in range(NCH):
        soft_fence()
        hft = [ld(HFB[:, kp * 512:(kp + 1) * 512], HFd[kp * 128:(kp + 1) * 128, c * 512:(c + 1) * 512]) for kp in range(8)]
        for m in range(8):
            bk = m % 2
            mm_group(bk, 512, [(WQ[kp][:, m * 128:(m + 1) * 128], HFB[:, kp * 512:(kp + 1) * 512]) for kp in range(8)], deps=tuple(hft))
            te = ps_read(bk, "scalar", lambda eng, _b=bk: eng.copy(QM, PS[_b][:, 0:512]), deps=(qst_p,) if qst_p else ())
            qst_p = st_(QFd[m * 128:(m + 1) * 128, c * 512:(c + 1) * 512], QM, deps=(te,))
            store_toks.append(qst_p)
        kev = []
        for m in range(8):
            bk = 2 + m % 2
            mm_group(bk, 512, [(WK[kp][:, m * 128:(m + 1) * 128], HFB[:, kp * 512:(kp + 1) * 512]) for kp in range(8)], deps=tuple(hft))
            kev.append(ps_read(bk, "scalar", lambda eng, _b=bk, _m=m: eng.copy(KFB[:, _m * 512:(_m + 1) * 512], PS[_b][:, 0:512])))
        ksq = [E("scalar", lambda eng, _m=m: eng.activation(KSQ[:, _m * 512:(_m + 1) * 512], KFB[:, _m * 512:(_m + 1) * 512], AF.Square), (kev[m],)) for m in range(8)]
        tss = mm_group(4, 512, [(ONC[:], KSQ[:, m * 512:(m + 1) * 512]) for m in range(8)], rows=1, deps=tuple(ksq))
        tnm = ps_read(4, "scalar", lambda eng: eng.activation(KROW, PS[4][0:1, 0:512], AF.Sqrt))
        tmx = E("vector", lambda eng: eng.tensor_scalar_max(KROW, KROW, 1e-12), (tnm,))
        tin = E("vector", lambda eng: eng.reciprocal(KROW, KROW))
        rank1(4, KROW, 512, deps=(tin,))
        tib = ps_read(4, "scalar", lambda eng: eng.copy(INVB, PS[4][:, 0:512]), deps=(qst_p,))
        kn = [E("vector", lambda eng, _m=m: eng.tensor_mul(KFB[:, _m * 512:(_m + 1) * 512], KFB[:, _m * 512:(_m + 1) * 512], INVB), (tib, kev[m])) for m in range(8)]
        kst = []
        for m in range(8):
            kst.append(st_(KFd[m * 128:(m + 1) * 128, c * 512:(c + 1) * 512], KFB[:, m * 512:(m + 1) * 512], deps=(kn[m],)))
            store_toks.append(kst[-1])
        ktm_st = []
        tst_p = None
        for s in range(4):
            u = c * 4 + s
            trs = []
            for dt in range(8):
                trs.append(transpose(5, KFB[:, dt * 512 + s * 128: dt * 512 + (s + 1) * 128], KTMS[:, dt * 128:(dt + 1) * 128],
                                     128, 128, deps=(kn[dt],) + ((tst_p,) if tst_p else ())))
            tst_p = st_(KTMd[u * 128:(u + 1) * 128, :], KTMS, deps=tuple(trs))
            ktm_st.append(tst_p)
            store_toks.append(tst_p)
        vev = []
        vmm_last = None
        for m in range(8):
            bk = m % 2
            vmm_last = mm_group(bk, 512, [(WV[kp][:, m * 128:(m + 1) * 128], HFB[:, kp * 512:(kp + 1) * 512]) for kp in range(8)], deps=tuple(hft))
            vev.append(ps_read(bk, "scalar", lambda eng, _b=bk, _m=m: eng.copy(VFB[:, _m * 512:(_m + 1) * 512], PS[_b][:, 0:512]), deps=(tss,)))
        vst_p = None
        for s in range(4):
            u = c * 4 + s
            trs = []
            for dt in range(8):
                trs.append(transpose(5, VFB[:, dt * 512 + s * 128: dt * 512 + (s + 1) * 128], VTM[:, dt * 128:(dt + 1) * 128],
                                     128, 128, deps=(vev[dt],) + tuple(ktm_st) + ((vst_p,) if vst_p else ())))
            tz = E("vector", lambda eng: eng.tensor_mul(ZB, PINV_E[:], VTM), tuple(trs) + tuple(ktm_st) + (kst[0], kst[1]))
            tc0 = E("vector", lambda eng: eng.tensor_mul(OMAC[0:1, :], CARRY, OMAR[:]), (vmm_last,))
            tc1 = E("vector", lambda eng: eng.tensor_add(ZB[0:1, :], ZB[0:1, :], OMAC[0:1, :]))
            mm_group(6, 512, [(UPI[:], ZB[:, 0:512])], deps=(tc1,))
            mm_group(7, 512, [(UPI[:], ZB[:, 512:1024])], deps=(tc1,))
            e1 = ps_read(6, "vector", lambda eng: eng.tensor_mul(VTM[:, 0:512], PFWD_E[:, 0:512], PS[6][:, 0:512]))
            e2 = ps_read(7, "vector", lambda eng: eng.tensor_mul(VTM[:, 512:1024], PFWD_E[:, 512:1024], PS[7][:, 0:512]))
            mm_group(2, 512, [(ONC[:], ZB[:, 0:512])], rows=1, deps=(tc1,))
            ps_read(2, "vector", lambda eng: eng.tensor_mul(CARRY[:, 0:512], PS[2][0:1, 0:512], OMA127[:, 0:512]))
            mm_group(3, 512, [(ONC[:], ZB[:, 512:1024])], rows=1, deps=(tc1,))
            ps_read(3, "vector", lambda eng: eng.tensor_mul(CARRY[:, 512:1024], PS[3][0:1, 0:512], OMA127[:, 512:1024]))
            vst_p = st_(VEd[u * 128:(u + 1) * 128, :], VTM, deps=(e1, e2))
            store_toks.append(vst_p)
    RELAX[0] = False
    barrier()

    arena_reset()
    WO = [alloc(1024) for _ in range(8)]
    STT = [alloc(1024) for _ in range(8)]
    QFB = alloc(4096); KFB3 = alloc(4096)
    KTMS2 = alloc(1024); VES = alloc(1024)
    ATN = alloc(1024); AFB = alloc(4096)
    UB = alloc(1024); BB_ = alloc(1024)
    NT = alloc(128); RT_A = alloc(128); RT_B = alloc(128)
    MP0 = alloc(128); MP1 = alloc(128); MTP0 = alloc(128); MTP1 = alloc(128)
    MQK = alloc(128); BBC = alloc(128); BETR = alloc(128)
    BETP = alloc(4)
    Y1M = alloc(512); XNM = alloc(512); H2M = alloc(512); Y1SQ = alloc(512); RSTB = alloc(512)
    Y1ROW = ROWS[0:1, 1024:1536]

    for kp in range(8):
        ld(WO[kp], wo_d[kp * 128:(kp + 1) * 128, :])
        E("gpsimd", lambda eng, _k=kp: eng.memset(STT[_k], 0.0))

    RELAX[0] = True
    for c in range(NCH):
        soft_fence()
        qkt = [ld(QFB[:, kp * 512:(kp + 1) * 512], QFd[kp * 128:(kp + 1) * 128, c * 512:(c + 1) * 512]) for kp in range(8)]
        qkt += [ld(KFB3[:, kp * 512:(kp + 1) * 512], KFd[kp * 128:(kp + 1) * 128, c * 512:(c + 1) * 512]) for kp in range(8)]
        atn_tr = []
        for s in range(4):
            soft_fence()
            u = c * 4 + s
            t_bp = ld(BETP[:, s:s + 1], BETA[u * 128:(u + 1) * 128, :])
            t_kt = ld(KTMS2, KTMd[u * 128:(u + 1) * 128, :])
            t_ve = ld(VES, VEd[u * 128:(u + 1) * 128, :])
            ksl = [KFB3[:, kp * 512 + s * 128: kp * 512 + (s + 1) * 128] for kp in range(8)]
            qsl = [QFB[:, kp * 512 + s * 128: kp * 512 + (s + 1) * 128] for kp in range(8)]
            bpp = BETP[:, s:s + 1]
            tbr = transpose(0, bpp, BETR[0:1, 0:128], 128, 1, deps=(t_bp,))
            rank1(0, BETR[0:1, 0:128], 128, deps=(tbr,))
            tbb = ps_read(0, "scalar", lambda eng: eng.copy(BBC, PS[0][:, 0:128]))
            mm_group(1, 128, [(ksl[kp], ksl[kp]) for kp in range(8)], deps=tuple(qkt))
            tnt1 = ps_read(1, "vector", lambda eng: eng.tensor_mul(NT, PS[1][:, 0:128], UPN[:]))
            t_nt2 = E("vector", lambda eng: eng.tensor_mul(NT, NT, BBC), (tbb,))
            t_rt = E("vector", lambda eng: eng.tensor_add(RT_A, NT, IDN[:]))
            t_mc = transpose(2, NT, MP0, 128, 128, deps=(t_nt2,))
            t_mt = E("scalar", lambda eng: eng.copy(MTP0, NT), (t_nt2,))
            rt_cur, rt_oth = RT_A, RT_B
            m_cur, mt_cur = MP0, MTP0
            m_nxt, mt_nxt = MP1, MTP1
            t_rtc = t_rt
            for lvl in range(6):
                mm_group(3, 128, [(mt_cur, m_cur)], deps=(t_mt, t_mc))
                tcpm = ps_read(3, "scalar", lambda eng, _m=m_nxt: eng.copy(_m, PS[3][:, 0:128]))
                mm_group(2, 128, [(m_nxt, rt_cur)], deps=(tcpm, t_rtc))
                t_rtc = ps_read(2, "vector", lambda eng, _o=rt_oth, _c=rt_cur: eng.tensor_add(_o, _c, PS[2][:, 0:128]))
                rt_cur, rt_oth = rt_oth, rt_cur
                if lvl < 5:
                    t_mc = transpose(3, m_nxt, mt_nxt, 128, 128, deps=(tcpm,))
                    t_mt = tcpm
                    m_cur, mt_cur = m_nxt, mt_nxt
                    m_nxt, mt_nxt = (MP0, MTP0) if m_cur is MP1 else (MP1, MTP1)
            mm_group(4, 512, [(ksl[kp], STT[kp][:, 0:512]) for kp in range(8)], deps=tuple(qkt))
            mm_group(5, 512, [(ksl[kp], STT[kp][:, 512:1024]) for kp in range(8)], deps=tuple(qkt))
            tb1 = ps_read(4, "vector", lambda eng: eng.scalar_tensor_tensor(BB_[:, 0:512], PS[4][:, 0:512], -1.0, VES[:, 0:512], ALU.mult, ALU.add), deps=(t_ve,))
            tb2 = ps_read(5, "vector", lambda eng: eng.scalar_tensor_tensor(BB_[:, 512:1024], PS[5][:, 0:512], -1.0, VES[:, 512:1024], ALU.mult, ALU.add), deps=(t_ve,))
            tb3 = E("vector", lambda eng, _b=bpp: eng.tensor_scalar(BB_, BB_, _b, None, ALU.mult), (t_bp,))
            mm_group(4, 512, [(rt_cur, BB_[:, 0:512])], deps=(t_rtc, tb3))
            tu1 = ps_read(4, "scalar", lambda eng: eng.copy(UB[:, 0:512], PS[4][:, 0:512]))
            mm_group(5, 512, [(rt_cur, BB_[:, 512:1024])], deps=(t_rtc, tb3))
            tu2 = ps_read(5, "scalar", lambda eng: eng.copy(UB[:, 512:1024], PS[5][:, 0:512]))
            mm_group(6, 512, [(qsl[kp], STT[kp][:, 0:512]) for kp in range(8)], deps=tuple(qkt))
            mm_group(7, 512, [(qsl[kp], STT[kp][:, 512:1024]) for kp in range(8)], deps=tuple(qkt))
            mm_group(1, 128, [(ksl[kp], qsl[kp]) for kp in range(8)], deps=(t_nt2,))
            tmq = ps_read(1, "vector", lambda eng: eng.tensor_mul(MQK, PS[1][:, 0:128], UPI[:]))
            mm_acc(6, 512, [(MQK, UB[:, 0:512])], deps=(tmq, tu1))
            mm_acc(7, 512, [(MQK, UB[:, 512:1024])], deps=(tmq, tu2))
            to1 = ps_read(6, "scalar", lambda eng: eng.copy(ATN[:, 0:512], PS[6][:, 0:512]))
            to2 = ps_read(7, "scalar", lambda eng: eng.copy(ATN[:, 512:1024], PS[7][:, 0:512]))
            for m in range(8):
                mm_group(4, 512, [(KTMS2[:, m * 128:(m + 1) * 128], UB[:, 0:512])], deps=(tu1, t_kt))
                ps_read(4, "vector", lambda eng, _m=m: eng.tensor_add(STT[_m][:, 0:512], STT[_m][:, 0:512], PS[4][:, 0:512]))
                mm_group(5, 512, [(KTMS2[:, m * 128:(m + 1) * 128], UB[:, 512:1024])], deps=(tu2, t_kt))
                ps_read(5, "vector", lambda eng, _m=m: eng.tensor_add(STT[_m][:, 512:1024], STT[_m][:, 512:1024], PS[5][:, 0:512]))
            for dt in range(8):
                atn_tr.append(transpose(0, ATN[:, dt * 128:(dt + 1) * 128], AFB[:, dt * 512 + s * 128: dt * 512 + (s + 1) * 128], 128, 128,
                                        deps=(to1, to2)))
        soft_fence()
        ty_p = None; sy_p = None
        mm1_t = []
        for m in range(8):
            bk = m % 2
            mm_group(bk, 512, [(WO[kp][:, m * 128:(m + 1) * 128], AFB[:, kp * 512:(kp + 1) * 512]) for kp in range(8)], deps=tuple(atn_tr))
            txn = ld(XNM, XN[m * 128:(m + 1) * 128, c * 512:(c + 1) * 512], deps=(ty_p,) if ty_p else ())
            ty = ps_read(bk, "vector", lambda eng, _b=bk: eng.tensor_add(Y1M, PS[_b][:, 0:512], XNM), deps=(txn,) + ((sy_p,) if sy_p else ()))
            sy = st_(Y1d[m * 128:(m + 1) * 128, c * 512:(c + 1) * 512], Y1M, deps=(ty,))
            store_toks.append(sy)
            tq = E("scalar", lambda eng: eng.activation(Y1SQ, Y1M, AF.Square), (ty,) + ((mm1_t[-1],) if mm1_t else ()))
            if m == 0:
                mm1_t.append(mm_group(3, 512, [(ONC[:], Y1SQ)], rows=1, deps=(tq,)))
            else:
                mm1_t.append(mm_acc(3, 512, [(ONC[:], Y1SQ)], rows=1, deps=(tq,)))
            ty_p, sy_p = ty, sy
        trow = ps_read(3, "vector", lambda eng: eng.tensor_scalar(Y1ROW, PS[3][0:1, 0:512], 1.0 / D, 1e-6, ALU.mult, ALU.add), deps=(mm1_t[-1],))
        tsq2 = E("scalar", lambda eng: eng.activation(Y1ROW, Y1ROW, AF.Sqrt), (trow,))
        trc2 = E("vector", lambda eng: eng.reciprocal(Y1ROW, Y1ROW), (tsq2,))
        rank1(3, Y1ROW, 512, deps=(trc2,))
        trb = ps_read(3, "scalar", lambda eng: eng.copy(RSTB, PS[3][:, 0:512]))
        tl_p = None; sh_p = None
        for m in range(8):
            tl = ld(Y1M, Y1d[m * 128:(m + 1) * 128, c * 512:(c + 1) * 512], deps=(tl_p,) if tl_p else (sy_p, mm1_t[-1]))
            th = E("vector", lambda eng: eng.tensor_mul(H2M, Y1M, RSTB), (tl, trb) + ((sh_p,) if sh_p else ()))
            th2 = E("vector", lambda eng, _m=m: eng.tensor_scalar(H2M, H2M, N2W[:, _m:_m + 1], None, ALU.mult))
            sh = st_(H2d[m * 128:(m + 1) * 128, c * 512:(c + 1) * 512], H2M, deps=(th2,))
            store_toks.append(sh)
            tl_p = th; sh_p = sh
    RELAX[0] = False
    barrier()

    arena_reset()
    WGB = [alloc(2048) for _ in range(8)]
    WUB = [alloc(2048) for _ in range(8)]
    H2F = alloc(4096); HIDM = alloc(512); HIDT = alloc(512)
    for kp in range(8):
        ld(WGB[kp], wg0_d[kp * 128:(kp + 1) * 128, :])
        ld(WUB[kp], wu0_d[kp * 128:(kp + 1) * 128, :])
    RELAX[0] = True
    thd_p = None
    sst_p = None
    for c in range(NCH):
        soft_fence()
        h2t = [ld(H2F[:, kp * 512:(kp + 1) * 512], H2d[kp * 128:(kp + 1) * 128, c * 512:(c + 1) * 512]) for kp in range(8)]
        for m in range(16):
            b0, b1 = (0, 1) if m % 2 == 0 else (2, 3)
            mm_group(b0, 512, [(WGB[kp][:, m * 128:(m + 1) * 128], H2F[:, kp * 512:(kp + 1) * 512]) for kp in range(8)], deps=tuple(h2t))
            mm_group(b1, 512, [(WUB[kp][:, m * 128:(m + 1) * 128], H2F[:, kp * 512:(kp + 1) * 512]) for kp in range(8)], deps=tuple(h2t))
            tsl = ps_read(b0, "scalar", lambda eng, _b=b0: eng.activation(HIDT, PS[_b][:, 0:512], AF.Silu), deps=(thd_p,) if thd_p else ())
            thd = ps_read(b1, "vector", lambda eng, _b=b1: eng.tensor_mul(HIDM, HIDT, PS[_b][:, 0:512]), deps=(tsl,) + ((sst_p,) if sst_p else ()))
            sst = st_(HIDd[m * 128:(m + 1) * 128, c * 512:(c + 1) * 512], HIDM, deps=(thd,))
            store_toks.append(sst)
            thd_p = thd; sst_p = sst
    RELAX[0] = False
    barrier()

    arena_reset()
    WDB = [alloc(1024) for _ in range(16)]
    HIDF = alloc(8192); Y1M2 = alloc(512); SM_ = alloc(512)
    for kp in range(16):
        ld(WDB[kp], wd0_d[kp * 128:(kp + 1) * 128, :])
    RELAX[0] = True
    ts_p = None
    sst_p = None
    for c in range(NCH):
        soft_fence()
        hidt = [ld(HIDF[:, kp * 512:(kp + 1) * 512], HIDd[kp * 128:(kp + 1) * 128, c * 512:(c + 1) * 512]) for kp in range(16)]
        for m in range(8):
            bk = m % 2
            mm_group(bk, 512, [(WDB[kp][:, m * 128:(m + 1) * 128], HIDF[:, kp * 512:(kp + 1) * 512]) for kp in range(16)], deps=tuple(hidt))
            ty = ld(Y1M2, Y1d[m * 128:(m + 1) * 128, c * 512:(c + 1) * 512], deps=(ts_p,) if ts_p else ())
            ts = ps_read(bk, "vector", lambda eng, _b=bk: eng.tensor_add(SM_, PS[_b][:, 0:512], Y1M2), deps=(ty,) + ((sst_p,) if sst_p else ()))
            sst = st_(SSd[m * 128:(m + 1) * 128, c * 512:(c + 1) * 512], SM_, deps=(ts,))
            store_toks.append(sst)
            ts_p = ts; sst_p = sst
    RELAX[0] = False
    barrier()

    arena_reset()
    CW1 = [alloc(1024) for _ in range(8)]
    CW2 = [alloc(512) for _ in range(8)]
    CW3 = [alloc(256) for _ in range(4)]
    CW4 = [alloc(1) for _ in range(2)]
    GW1B = alloc(4096)
    GW1 = [GW1B[:, kp * 512:(kp + 1) * 512] for kp in range(8)]
    GW2 = [alloc(256) for _ in range(4)]
    GW3 = [alloc(1) for _ in range(2)]
    OUTA = alloc(4096)
    OUTB = GW1B
    C1B = alloc(4096); C2B = alloc(2048); C3B = alloc(1024)
    X1FB = alloc(4096)
    X1NB = OUTA
    XM_ = C2B[:, 0:512]; SM2 = C2B[:, 512:1024]; X1SQ = C2B[:, 1024:1536]
    MB_ = C3B[:, 0:512]; RB_ = C3B[:, 512:1024]
    CPB = alloc(512); TMS_ = alloc(1024)
    GH1 = STAT[:, 8:12]; GH2 = STAT[:, 12:14]
    TFR = ROWS[0:1, 1024:1536]
    MROW = ROWS[0:1, 0:512]; VROW = ROWS[0:1, 512:1024]

    for kp in range(8):
        ld(CW1[kp], cw1_d[kp * 128:(kp + 1) * 128, :])
        ld(CW2[kp], cw2_d[kp * 128:(kp + 1) * 128, :])
        ld(GW1[kp], gw1_d[kp * 128:(kp + 1) * 128, :])
    for kp in range(4):
        ld(CW3[kp], cw3_d[kp * 128:(kp + 1) * 128, :])
        ld(GW2[kp], gw2_d[kp * 128:(kp + 1) * 128, :])
    for kp in range(2):
        ld(CW4[kp], cw4_d[kp * 128:(kp + 1) * 128, :])
        ld(GW3[kp], gw3_d[kp * 128:(kp + 1) * 128, :])

    E("vector", lambda eng: eng.tensor_scalar_mul(GS[:], GS[:], 1.0 / T))
    for mt in range(4):
        mm_group(0, 1, [(GW1[kp][:, mt * 128:(mt + 1) * 128], GS[:, kp:kp + 1]) for kp in range(8)], col0=mt)
    for mt in range(4):
        ps_read(0, "scalar", lambda eng, _m=mt: eng.activation(GH1[:, _m:_m + 1], PS[0][:, _m:_m + 1], AF.Gelu, bias=GB1P[:, _m:_m + 1]))
    for mt in range(2):
        mm_group(1, 1, [(GW2[kp][:, mt * 128:(mt + 1) * 128], GH1[:, kp:kp + 1]) for kp in range(4)], col0=mt)
    for mt in range(2):
        ps_read(1, "scalar", lambda eng, _m=mt: eng.activation(GH2[:, _m:_m + 1], PS[1][:, _m:_m + 1], AF.Gelu, bias=GB2P[:, _m:_m + 1]))
    mm_group(2, 1, [(GW3[kp], GH2[:, kp:kp + 1]) for kp in range(2)], rows=1)
    ps_read(2, "scalar", lambda eng: eng.activation(GF07[:], PS[2][0:1, 0:1], AF.Sigmoid, bias=GB3S[:]))
    E("vector", lambda eng: eng.tensor_scalar_mul(GF07[:], GF07[:], 0.7))

    twist_map = [(4, 1.0), (5, 1.0), (6, -1.0), (7, -1.0), (0, -1.0), (1, -1.0), (2, 1.0), (3, 1.0)]
    RELAX[0] = True
    for c in range(NCH):
        soft_fence()
        xnt_ = [ld(OUTA[:, dt * 512:(dt + 1) * 512], XN[dt * 128:(dt + 1) * 128, c * 512:(c + 1) * 512]) for dt in range(8)]
        cur_out, oth_out = OUTA, OUTB
        for cyc in range(3):
            soft_fence()
            c1t = []
            for m in range(8):
                bk = m % 2
                mm_group(bk, 512, [(CW1[kp][:, m * 128:(m + 1) * 128], cur_out[:, kp * 512:(kp + 1) * 512]) for kp in range(8)],
                         deps=tuple(xnt_))
                c1t.append(ps_read(bk, "scalar", lambda eng, _b=bk, _m=m: eng.activation(C1B[:, _m * 512:(_m + 1) * 512], PS[_b][:, 0:512], AF.Gelu, bias=CB1P[:, _m:_m + 1])))
            c2t = []
            for m in range(4):
                bk = 2 + m % 2
                mm_group(bk, 512, [(CW2[kp][:, m * 128:(m + 1) * 128], C1B[:, kp * 512:(kp + 1) * 512]) for kp in range(8)],
                         deps=tuple(c1t))
                c2t.append(ps_read(bk, "scalar", lambda eng, _b=bk, _m=m: eng.activation(C2B[:, _m * 512:(_m + 1) * 512], PS[_b][:, 0:512], AF.Gelu, bias=CB2P[:, _m:_m + 1])))
            c3t = []
            for m in range(2):
                bk = m % 2
                mm_group(bk, 512, [(CW3[kp][:, m * 128:(m + 1) * 128], C2B[:, kp * 512:(kp + 1) * 512]) for kp in range(4)],
                         deps=tuple(c2t))
                c3t.append(ps_read(bk, "scalar", lambda eng, _b=bk, _m=m: eng.activation(C3B[:, _m * 512:(_m + 1) * 512], PS[_b][:, 0:512], AF.Gelu, bias=CB3P[:, _m:_m + 1])))
            mm_group(2, 512, [(CW4[kp], C3B[:, kp * 512:(kp + 1) * 512]) for kp in range(2)], rows=1, deps=tuple(c3t))
            ttf = ps_read(2, "scalar", lambda eng: eng.activation(TFR, PS[2][0:1, 0:512], AF.Sigmoid, bias=CB4S[:]))
            tcm = E("vector", lambda eng: eng.tensor_scalar(TFR, TFR, 0.3, GF07[:], ALU.mult, ALU.add), (ttf,))
            tcp = E("vector", lambda eng: eng.tensor_scalar(TFR, TFR, 0.5, AR2[:], ALU.subtract, ALU.mult))
            tcp2 = E("vector", lambda eng: eng.tensor_scalar_add(TFR, TFR, 0.1))
            rank1(3, TFR, 512, deps=(tcp2,))
            tcb = ps_read(3, "scalar", lambda eng: eng.copy(CPB, PS[3][:, 0:512]))
            for dt in range(8):
                src2, sign = twist_map[dt]
                E("vector", lambda eng, _s=src2, _g=sign, _c=cur_out: eng.scalar_tensor_tensor(TMS_[:, 0:512], _c[:, _s * 512:(_s + 1) * 512], _g, CPB, ALU.mult, ALU.mult), (tcb,))
                E("vector", lambda eng, _d=dt, _c=cur_out, _o=oth_out: eng.tensor_add(_o[:, _d * 512:(_d + 1) * 512], _c[:, _d * 512:(_d + 1) * 512], TMS_[:, 0:512]))
            cur_out, oth_out = oth_out, cur_out
        soft_fence()
        t1_p = None; t2_p = None
        mm4_t = []; mm5_t = []
        for dt in range(8):
            tg = E("scalar", lambda eng, _d=dt, _c=cur_out: eng.activation(_c[:, _d * 512:(_d + 1) * 512], _c[:, _d * 512:(_d + 1) * 512], AF.Silu))
            tx = ld(XM_, XF[dt * 128:(dt + 1) * 128, c * 512:(c + 1) * 512], deps=(t2_p,) if t2_p else ())
            ts = ld(SM2, SSd[dt * 128:(dt + 1) * 128, c * 512:(c + 1) * 512], deps=(t1_p,) if t1_p else (), eng="gpsimd")
            t1 = E("vector", lambda eng, _d=dt, _c=cur_out: eng.tensor_mul(TMS_[:, 0:512], _c[:, _d * 512:(_d + 1) * 512], SM2), (tg, ts))
            t2 = E("vector", lambda eng, _d=dt: eng.tensor_add(X1FB[:, _d * 512:(_d + 1) * 512], XM_, TMS_[:, 0:512]), (t1, tx))
            store_toks.append(st_(X1d[dt * 128:(dt + 1) * 128, c * 512:(c + 1) * 512], X1FB[:, dt * 512:(dt + 1) * 512], deps=(t2,)))
            if dt == 0:
                mm4_t.append(mm_group(4, 512, [(ONC[:], X1FB[:, dt * 512:(dt + 1) * 512])], rows=1, deps=(t2,)))
            else:
                mm4_t.append(mm_acc(4, 512, [(ONC[:], X1FB[:, dt * 512:(dt + 1) * 512])], rows=1, deps=(t2,)))
            tq = E("scalar", lambda eng, _d=dt: eng.activation(X1SQ, X1FB[:, _d * 512:(_d + 1) * 512], AF.Square), (t2,) + ((mm5_t[-1],) if mm5_t else ()))
            if dt == 0:
                mm5_t.append(mm_group(5, 512, [(ONC[:], X1SQ)], rows=1, deps=(tq,)))
            else:
                mm5_t.append(mm_acc(5, 512, [(ONC[:], X1SQ)], rows=1, deps=(tq,)))
            t1_p, t2_p = t1, t2
        tmean = ps_read(4, "vector", lambda eng: eng.tensor_scalar_mul(MROW, PS[4][0:1, 0:512], 1.0 / D), deps=(mm4_t[-1],))
        tms2 = E("vector", lambda eng: eng.tensor_mul(VROW, MROW, MROW))
        tvar = ps_read(5, "vector", lambda eng: eng.scalar_tensor_tensor(VROW, PS[5][0:1, 0:512], 1.0 / D, VROW, ALU.mult, ALU.subtract), deps=(mm5_t[-1],))
        tve = E("vector", lambda eng: eng.tensor_scalar_add(VROW, VROW, 1e-5))
        tsq = E("scalar", lambda eng: eng.activation(VROW, VROW, AF.Sqrt), (tve,))
        trc = E("vector", lambda eng: eng.reciprocal(VROW, VROW), (tsq,))
        rank1(4, MROW, 512, deps=(tmean, tms2))
        tmb = ps_read(4, "scalar", lambda eng: eng.copy(MB_, PS[4][:, 0:512]))
        rank1(5, VROW, 512, deps=(trc,))
        trb2 = ps_read(5, "scalar", lambda eng: eng.copy(RB_, PS[5][:, 0:512]))
        xn_new = []
        tst_p = None
        for dt in range(8):
            ta = E("vector", lambda eng, _d=dt: eng.tensor_sub(X1NB[:, _d * 512:(_d + 1) * 512], X1FB[:, _d * 512:(_d + 1) * 512], MB_), (tmb,))
            tb = E("vector", lambda eng, _d=dt: eng.tensor_mul(X1NB[:, _d * 512:(_d + 1) * 512], X1NB[:, _d * 512:(_d + 1) * 512], RB_), (trb2,))
            store_toks.append(st_(X1Nd[dt * 128:(dt + 1) * 128, c * 512:(c + 1) * 512], X1NB[:, dt * 512:(dt + 1) * 512], deps=(tb,)))
            xn_new.append(tb)
        for s in range(4):
            trs = []
            for dt in range(8):
                trs.append(transpose(6, X1NB[:, dt * 512 + s * 128: dt * 512 + (s + 1) * 128], TMS_[:, dt * 128:(dt + 1) * 128],
                                     128, 128, deps=tuple(xn_new) + ((tst_p,) if tst_p else ())))
            tst_p = st_(X1NTd[(c * 4 + s) * 128:(c * 4 + s + 1) * 128, :], TMS_[:, 0:1024], deps=tuple(trs))
            store_toks.append(tst_p)
    RELAX[0] = False
    barrier()

    arena_reset()
    XH = alloc(16384); YH = alloc(16384)
    PIN2 = alloc(1024); PPO2 = alloc(1024); ZW = alloc(1024); TW0 = alloc(512); TW1 = alloc(512)
    LNROW = alloc(512); CBROW = alloc(512)
    CRY = ROWS[0:1, 0:512]; ABR = ROWS[0:1, 512:1024]; ABC = ROWS[0:1, 1024:1536]; TRW = ROWS[0:1, 1536:2048]
    RELAX[0] = True
    for hf in range(2):
        soft_fence()
        for u in range(NSUB):
            ld(XH[:, u * 512:(u + 1) * 512], X1NTd[u * 128:(u + 1) * 128, hf * 512:(hf + 1) * 512])
        for i in range(4):
            E("gpsimd", lambda eng, _i=i: eng.memset(YH[:, _i * 4096:(_i + 1) * 4096], 0.0))
        soft_fence()
        for n in range(NS):
            soft_fence()
            lnrow = LNROW[0:1, 0:512]
            cbrow = CBROW[0:1, 0:512]
            t_ln = ld(lnrow, LNABd[n:n + 1, hf * 512:(hf + 1) * 512])
            t_cb = ld(cbrow, CBNDd[n:n + 1, hf * 512:(hf + 1) * 512])
            rank1(6, lnrow, 512, deps=(t_ln,))
            tp1 = ps_read(6, "scalar", lambda eng: eng.activation(PIN2[:, 0:512], PS[6][:, 0:512], AF.Exp, scale=NIOP[:]))
            tp1b = E("scalar", lambda eng: eng.copy(PIN2[:, 512:1024], PIN2[:, 0:512]))
            tp2 = ps_read(6, "scalar", lambda eng: eng.activation(PPO2[:, 0:512], PS[6][:, 0:512], AF.Exp, scale=IOP[:]))
            rank1(7, cbrow, 512, deps=(t_cb,))
            tp3 = ps_read(7, "vector", lambda eng: eng.tensor_mul(PPO2[:, 0:512], PPO2[:, 0:512], PS[7][:, 0:512]), deps=(tp2,))
            tp3b = E("vector", lambda eng: eng.tensor_copy(PPO2[:, 512:1024], PPO2[:, 0:512]))
            tab = E("scalar", lambda eng: eng.activation(ABR, lnrow, AF.Exp), (t_ln,))
            tabc1 = E("vector", lambda eng: eng.tensor_scalar_mul(ABC, lnrow, 127.0), (t_ln,))
            tabc = E("scalar", lambda eng: eng.activation(ABC, ABC, AF.Exp), (tabc1,))
            E("vector", lambda eng: eng.memset(CRY, 0.0))
            zw_guard = ()
            for p in range(16):
                c0, c1 = 2 * p, 2 * p + 1
                tz = E("vector", lambda eng, _p=p: eng.tensor_mul(ZW, PIN2, XH[:, _p * 1024:(_p + 1) * 1024]), zw_guard + (tp1b,))
                E("vector", lambda eng: eng.tensor_mul(TRW, CRY, ABR), (tab,))
                ti0b = E("vector", lambda eng: eng.tensor_add(ZW[0:1, 0:512], ZW[0:1, 0:512], TRW))
                b0 = 0 if p % 2 == 0 else 2
                b1 = 1 if p % 2 == 0 else 3
                m_b0 = mm_group(b0, 512, [(UPI[:], ZW[:, 0:512])], deps=(ti0b,))
                m_4 = mm_group(4, 512, [(ONC[:], ZW[:, 0:512])], rows=1, deps=(ti0b,))
                ps_read(4, "vector", lambda eng: eng.tensor_mul(CRY, PS[4][0:1, 0:512], ABC), deps=(tabc,))
                E("vector", lambda eng: eng.tensor_mul(TRW, CRY, ABR))
                ti1b = E("vector", lambda eng: eng.tensor_add(ZW[0:1, 512:1024], ZW[0:1, 512:1024], TRW))
                m_b1 = mm_group(b1, 512, [(UPI[:], ZW[:, 512:1024])], deps=(ti1b,))
                m_5 = mm_group(5, 512, [(ONC[:], ZW[:, 512:1024])], rows=1, deps=(ti1b,))
                ps_read(5, "vector", lambda eng: eng.tensor_mul(CRY, PS[5][0:1, 0:512], ABC))
                ps_read(b0, "vector", lambda eng, _b=b0: eng.tensor_mul(TW0, PS[_b][:, 0:512], PPO2[:, 0:512]), deps=(tp3,))
                E("vector", lambda eng, _c=c0: eng.tensor_add(YH[:, _c * 512:(_c + 1) * 512], YH[:, _c * 512:(_c + 1) * 512], TW0))
                ps_read(b1, "vector", lambda eng, _b=b1: eng.tensor_mul(TW1, PS[_b][:, 0:512], PPO2[:, 512:1024]), deps=(tp3b,))
                E("vector", lambda eng, _c=c1: eng.tensor_add(YH[:, _c * 512:(_c + 1) * 512], YH[:, _c * 512:(_c + 1) * 512], TW1))
                zw_guard = (m_b0, m_4, m_b1, m_5)
        soft_fence()
        for u in range(NSUB):
            store_toks.append(st_(YSSMd[u * 128:(u + 1) * 128, hf * 512:(hf + 1) * 512], YH[:, u * 512:(u + 1) * 512]))
    RELAX[0] = False
    barrier()

    arena_reset()
    XSC = alloc(8192); YSC = alloc(8192); HIDH = alloc(4096)
    WDH = [alloc(1024) for _ in range(4)]
    WGT = alloc(1024); WUT = alloc(1024)
    GTS = alloc(8); MK1 = alloc(8); MK2 = alloc(8); SMB = alloc(8)
    GBC = alloc(512); TW0b = alloc(512); TW1b = alloc(512); GROW = alloc(512)
    GATR = SMALL[0:8, 0:1024]; SCF = SMALL[0:8, 1024:1536]
    WGA = [SMALL[:, 1544 + 8 * kp: 1552 + 8 * kp] for kp in range(8)]
    M1 = STAT[:, 4:5]; M2 = STAT[:, 5:6]; P1_ = STAT[:, 6:7]; P2_ = STAT[:, 7:8]
    for kp in range(8):
        ld(WGA[kp], wgate_d[kp * 128:(kp + 1) * 128, :])
    mgroups = [(0, 4), (4, 4), (8, 3)]
    for sc in range(4):
        RELAX[0] = True
        soft_fence()
        for kp in range(8):
            ld(XSC[:, kp * 1024:(kp + 1) * 1024], X1Nd[kp * 128:(kp + 1) * 128, sc * 1024:(sc + 1) * 1024])
        tyz = E("gpsimd", lambda eng: eng.memset(YSC, 0.0))
        RELAX[0] = False
        for tc in range(2):
            mm_group(0, 512, [(WGA[kp], XSC[:, kp * 1024 + tc * 512: kp * 1024 + (tc + 1) * 512]) for kp in range(8)], rows=8)
            ps_read(0, "scalar", lambda eng: eng.copy(SCF, PS[0][0:8, 0:512]))
            for g in range(4):
                transpose(1, SCF[:, g * 128:(g + 1) * 128], GTS, 8, 128)
                E("vector", lambda eng: eng.reduce_max(M1, GTS, AX.X))
                E("vector", lambda eng: eng.tensor_single_scalar(MK1, GTS, M1, ALU.is_equal))
                E("vector", lambda eng: eng.scalar_tensor_tensor(SMB, MK1, -1e9, GTS, ALU.mult, ALU.add))
                E("vector", lambda eng: eng.reduce_max(M2, SMB, AX.X))
                E("vector", lambda eng: eng.tensor_single_scalar(MK2, SMB, M2, ALU.is_equal))
                E("vector", lambda eng: eng.tensor_sub(P1_, M1, M2))
                E("scalar", lambda eng: eng.activation(P1_, P1_, AF.Sigmoid))
                E("vector", lambda eng: eng.tensor_scalar(P2_, P1_, -1.0, 1.0, ALU.mult, ALU.add))
                E("vector", lambda eng: eng.tensor_scalar(MK1, MK1, P1_, None, ALU.mult))
                E("vector", lambda eng: eng.tensor_scalar(MK2, MK2, P2_, None, ALU.mult))
                E("vector", lambda eng: eng.tensor_add(GTS, MK1, MK2))
                transpose(2, GTS, GATR[:, tc * 512 + g * 128: tc * 512 + (g + 1) * 128], 128, 8)
        store_toks.append(st_(GATEd[:, sc * 1024:(sc + 1) * 1024], GATR[:, 0:1024]))
        RELAX[0] = True
        soft_fence()
        thd_prev = None
        tyy_prev = None
        grow_guard = None
        for e in range(ME):
            for (m0, nmt) in mgroups:
                soft_fence()
                hid_done = []
                gu_guard = ()
                for mi in range(nmt):
                    mt = m0 + mi
                    wg_tk = [ld(WGT[:, kp * 128:(kp + 1) * 128], wgE_d[e, mt, kp, :, :], deps=gu_guard) for kp in range(8)]
                    wu_tk = [ld(WUT[:, kp * 128:(kp + 1) * 128], wuE_d[e, mt, kp, :, :], deps=gu_guard, eng="gpsimd") for kp in range(8)]
                    gu_guard = ()
                    for tc in range(2):
                        gmm = mm_group(3, 512, [(WGT[:, kp * 128:(kp + 1) * 128], XSC[:, kp * 1024 + tc * 512: kp * 1024 + (tc + 1) * 512]) for kp in range(8)], deps=tuple(wg_tk))
                        umm = mm_group(4, 512, [(WUT[:, kp * 128:(kp + 1) * 128], XSC[:, kp * 1024 + tc * 512: kp * 1024 + (tc + 1) * 512]) for kp in range(8)], deps=tuple(wu_tk))
                        tsl = ps_read(3, "scalar", lambda eng: eng.activation(TW0b, PS[3][:, 0:512], AF.Silu), deps=(thd_prev,) if thd_prev else ())
                        thd = ps_read(4, "vector", lambda eng, _mi=mi, _tc=tc: eng.tensor_mul(HIDH[:, _mi * 1024 + _tc * 512: _mi * 1024 + (_tc + 1) * 512], TW0b, PS[4][:, 0:512]), deps=(tsl,))
                        hid_done.append(thd)
                        thd_prev = thd
                        gu_guard = gu_guard + (gmm, umm)
                wd_tk = [ld(WDH[mi], wdE_d[e, (m0 + mi) * 128:(m0 + mi + 1) * 128, :]) for mi in range(nmt)]
                for tc in range(2):
                    t_gr = ld(GROW[0:1, 0:512], GATEd[e:e + 1, sc * 1024 + tc * 512: sc * 1024 + (tc + 1) * 512],
                              deps=(grow_guard,) if grow_guard else ())
                    grow_guard = rank1(5, GROW[0:1, 0:512], 512, deps=(t_gr,))
                    tgb = ps_read(5, "scalar", lambda eng: eng.copy(GBC, PS[5][:, 0:512]), deps=(tyy_prev,) if tyy_prev else ())
                    for m in range(8):
                        bnk = 6 if m % 2 == 0 else 7
                        mm_group(bnk, 512, [(WDH[mi][:, m * 128:(m + 1) * 128], HIDH[:, mi * 1024 + tc * 512: mi * 1024 + (tc + 1) * 512]) for mi in range(nmt)],
                                 deps=tuple(wd_tk) + tuple(hid_done))
                        tyy = ps_read(bnk, "vector", lambda eng, _b=bnk: eng.tensor_mul(TW1b, PS[_b][:, 0:512], GBC), deps=(tgb,))
                        tyy_prev = tyy
                        E("vector", lambda eng, _m=m, _tc=tc: eng.tensor_add(YSC[:, _m * 1024 + _tc * 512: _m * 1024 + (_tc + 1) * 512],
                                                                             YSC[:, _m * 1024 + _tc * 512: _m * 1024 + (_tc + 1) * 512], TW1b),
                          deps=(tyz,))
        soft_fence()
        for m in range(8):
            store_toks.append(st_(YMOEd[m * 128:(m + 1) * 128, sc * 1024:(sc + 1) * 1024], YSC[:, m * 1024:(m + 1) * 1024]))
        RELAX[0] = False
    barrier()

    arena_reset()
    YST = alloc(1024); YSF = alloc(4096); XOB = alloc(4096); OTM = alloc(1024)
    X1M_ = alloc(512); X1NM_ = alloc(512); YMM = alloc(512); G2M = alloc(512); S2M = alloc(512)
    for c in range(NCH):
        for s in range(4):
            ld(YST, YSSMd[(c * 4 + s) * 128:(c * 4 + s + 1) * 128, :])
            for dt in range(8):
                transpose(0, YST[:, dt * 128:(dt + 1) * 128], YSF[:, dt * 512 + s * 128: dt * 512 + (s + 1) * 128], 128, 128)
        for dt in range(8):
            ld(X1M_, X1d[dt * 128:(dt + 1) * 128, c * 512:(c + 1) * 512])
            ld(X1NM_, X1Nd[dt * 128:(dt + 1) * 128, c * 512:(c + 1) * 512])
            ld(YMM, YMOEd[dt * 128:(dt + 1) * 128, c * 512:(c + 1) * 512], eng="gpsimd")
            E("scalar", lambda eng: eng.activation(G2M, YMM, AF.Silu))
            E("vector", lambda eng, _d=dt: eng.tensor_scalar(S2M, X1NM_, DPP[:, _d:_d + 1], None, ALU.mult))
            E("vector", lambda eng, _d=dt: eng.tensor_add(S2M, S2M, YSF[:, _d * 512:(_d + 1) * 512]))
            E("vector", lambda eng: eng.tensor_mul(S2M, S2M, G2M))
            E("vector", lambda eng, _d=dt: eng.tensor_add(XOB[:, _d * 512:(_d + 1) * 512], X1M_, S2M))
        for s in range(4):
            for dt in range(8):
                transpose(1, XOB[:, dt * 512 + s * 128: dt * 512 + (s + 1) * 128], OTM[:, dt * 128:(dt + 1) * 128], 128, 128)
            store_toks.append(st_(out_d[(c * 4 + s) * 128:(c * 4 + s + 1) * 128, :], OTM))

    if debug:
        dbg = {"XN": XN, "HF": HFd, "QF": QFd, "KF": KFd, "KTM": KTMd, "VE": VEd,
               "Y1": Y1d, "H2": H2d, "SS": SSd, "X1": X1d, "X1N": X1Nd,
               "YSSM": YSSMd, "YMOE": YMOEd, "BETAo": BETA, "GATEo": GATEd}
        for nm, t in dbg.items():
            o = nc.dram_tensor("dbg_" + nm, list(t.tensor.shape), F32, kind="ExternalOutput").ap()
            store_toks.append(st_(o[:], t[:]))
    S.emit("sync", None, tuple(store_toks))
    S.build(nc, maxseq=maxseq)
    return nc, S


_nc_cache = [None]


def kernel(**inputs):
    def f(name):
        return np.ascontiguousarray(np.asarray(inputs[name], dtype=np.float32))

    x_full = f("L0_attn_x")
    wgE = f("L1_ffn_w_g").reshape(ME, 8, 128, MMT, 128).transpose(0, 3, 1, 2, 4).copy()
    wuE = f("L1_ffn_w_u").reshape(ME, 8, 128, MMT, 128).transpose(0, 3, 1, 2, 4).copy()

    iotaf = np.broadcast_to(np.arange(128, dtype=np.float32)[None, :], (128, 128)).copy()
    jj, tt = np.meshgrid(np.arange(128), np.arange(128), indexing="ij")
    upinc = (jj <= tt).astype(np.float32)
    upneg = -(jj < tt).astype(np.float32)

    common = {
        "n1w": f("L0_attn_norm1_w").reshape(1, D),
        "n2w": f("L0_attn_norm2_w").reshape(1, D),
        "wq": f("L0_attn_Wq"), "wk": f("L0_attn_Wk"), "wv": f("L0_attn_Wv"), "wo": f("L0_attn_Wo"),
        "bw": f("L0_attn_beta_w").reshape(1, D),
        "bb": f("L0_attn_beta_b").reshape(1, 1),
        "alog": f("L0_attn_alpha_logit").reshape(1, D),
        "wg0": f("L0_attn_w_gate"), "wu0": f("L0_attn_w_up"), "wd0": f("L0_attn_w_down"),
        "cw1": f("L0_ffn_cn_w1"), "cw2": f("L0_ffn_cn_w2"), "cw3": f("L0_ffn_cn_w3"), "cw4": f("L0_ffn_cn_w4"),
        "cb1": f("L0_ffn_cn_b1").reshape(1, D), "cb2": f("L0_ffn_cn_b2").reshape(1, 512),
        "cb3": f("L0_ffn_cn_b3").reshape(1, 256), "cb4": f("L0_ffn_cn_b4").reshape(1, 1),
        "gw1": f("L0_ffn_gc_w1"), "gw2": f("L0_ffn_gc_w2"), "gw3": f("L0_ffn_gc_w3"),
        "gb1": f("L0_ffn_gc_b1").reshape(1, 512), "gb2": f("L0_ffn_gc_b2").reshape(1, 256),
        "gb3": f("L0_ffn_gc_b3").reshape(1, 1),
        "ar": f("L0_ffn_adaptive_range").reshape(1, 1),
        "logA": f("L1_attn_log_A"), "Bp": f("L1_attn_B_param"), "Cp": f("L1_attn_C"),
        "Dv": f("L1_attn_D").reshape(1, D),
        "wgate": f("L1_ffn_w_gate"),
        "wgE": wgE, "wuE": wuE, "wdE": f("L1_ffn_w_d"),
        "idn": np.eye(128, dtype=np.float32),
        "upinc": upinc, "upneg": upneg, "iotaf": iotaf,
        "onesr": np.ones((1, 128), np.float32), "onesc": np.ones((128, 1), np.float32),
        "niop": -np.arange(128, dtype=np.float32).reshape(128, 1),
        "iop": np.arange(128, dtype=np.float32).reshape(128, 1),
    }
    in_maps = [dict(common, x=np.ascontiguousarray(x_full[b])) for b in range(8)]
    if _nc_cache[0] is None:
        _nc_cache[0] = build_program()[0]
    nc = _nc_cache[0]
    res = run_bass_kernel_spmd(nc, in_maps, core_ids=list(range(8)))
    return np.stack([r["out"] for r in res.results], axis=0)


# version 10
# speedup vs baseline: 1.4617x; 1.0004x over previous
import numpy as np
import concourse.bass as bass
import concourse.mybir as mybir
from concourse.bass_utils import run_bass_kernel_spmd

F32 = mybir.dt.float32
AF = mybir.ActivationFunctionType
ALU = mybir.AluOpType
AX = mybir.AxisListType

T = 4096
D = 1024
NS = 64
HID0 = 2048
ME = 8
MM = 1408
MMT = 11
NSUB = T // 128
NCH = T // 512

ENGINES = ("sync", "tensor", "vector", "scalar", "gpsimd")
AUTO_FENCE = True


class Sched:
    def __init__(self):
        self.q = {e: [] for e in ENGINES}
        self.gseq = 0
        self.seqs = {e: [] for e in ENGINES}
        self.marks = []

    def emit(self, eng, fn, deps=(), dma=False):
        lst = self.q[eng]
        tok = (eng, len(lst))
        deps = tuple(d for d in deps if d is not None)
        lst.append([fn, deps, dma, False])
        self.seqs[eng].append(self.gseq)
        self.gseq += 1
        return tok

    def build(self, nc, maxseq=None):
        if maxseq is not None:
            for e in ENGINES:
                keep = sum(1 for s in self.seqs[e] if s < maxseq)
                self.q[e] = self.q[e][:keep]
        for e in ENGINES:
            for rec in self.q[e]:
                if rec[2]:
                    rec[3] = True
                for (de, di) in rec[1]:
                    self.q[de][di][3] = True
        vals = {}
        for e in ENGINES:
            c = 0
            for i, rec in enumerate(self.q[e]):
                if rec[3] and rec[0] is not None:
                    c += 16 if rec[2] else 1
                vals[(e, i)] = c
        import contextlib
        stack = contextlib.ExitStack()
        sems = {e: stack.enter_context(nc.semaphore(f"sem_{e}")) for e in ENGINES}
        with nc.Block() as block:
            for e in ENGINES:
                def body(eng, _e=e):
                    hw = {o: 0 for o in ENGINES}
                    for i, (fn, deps, dma, needed) in enumerate(self.q[_e]):
                        for (de, di) in deps:
                            v = vals[(de, di)]
                            if v > hw[de]:
                                eng.wait_ge(sems[de], v)
                                hw[de] = v
                        if fn is not None:
                            inst = fn(eng)
                            if needed:
                                inst.then_inc(sems[_e], 16 if dma else 1)
                getattr(block, e)(body)
        stack.close()


def build_program(maxseq=None, debug=False):
    nc = bass.Bass("TRN2", target_bir_lowering=False)
    S = Sched()

    def din(name, shape):
        return nc.dram_tensor(name, shape, F32, kind="ExternalInput").ap()

    def dscr(name, shape):
        return nc.dram_tensor(name, shape, F32, kind="Internal").ap()

    x_d = din("x", [T, D])
    n1w_d = din("n1w", [1, D]); n2w_d = din("n2w", [1, D])
    wq_d = din("wq", [D, D]); wk_d = din("wk", [D, D]); wv_d = din("wv", [D, D]); wo_d = din("wo", [D, D])
    bw_d = din("bw", [1, D]); bb_d = din("bb", [1, 1]); alog_d = din("alog", [1, D])
    wg0_d = din("wg0", [D, HID0]); wu0_d = din("wu0", [D, HID0]); wd0_d = din("wd0", [HID0, D])
    cw1_d = din("cw1", [D, D]); cw2_d = din("cw2", [D, 512]); cw3_d = din("cw3", [512, 256]); cw4_d = din("cw4", [256, 1])
    cb1_d = din("cb1", [1, D]); cb2_d = din("cb2", [1, 512]); cb3_d = din("cb3", [1, 256]); cb4_d = din("cb4", [1, 1])
    gw1_d = din("gw1", [D, 512]); gw2_d = din("gw2", [512, 256]); gw3_d = din("gw3", [256, 1])
    gb1_d = din("gb1", [1, 512]); gb2_d = din("gb2", [1, 256]); gb3_d = din("gb3", [1, 1])
    ar_d = din("ar", [1, 1])
    logA_d = din("logA", [D, NS]); Bp_d = din("Bp", [D, NS]); Cp_d = din("Cp", [D, NS]); Dv_d = din("Dv", [1, D])
    wgate_d = din("wgate", [D, ME])
    wgE_d = din("wgE", [ME, MMT, 8, 128, 128])
    wuE_d = din("wuE", [ME, MMT, 8, 128, 128])
    wdE_d = din("wdE", [ME, MM, D])
    id_d = din("idn", [128, 128]); upinc_d = din("upinc", [128, 128]); upneg_d = din("upneg", [128, 128])
    iotaf_d = din("iotaf", [128, 128]); onesr_d = din("onesr", [1, 128]); onesc_d = din("onesc", [128, 1])
    niop_d = din("niop", [128, 1]); iop_d = din("iop", [128, 1])

    out_d = nc.dram_tensor("out", [T, D], F32, kind="ExternalOutput").ap()

    XN = dscr("XN", [D, T]); XF = dscr("XF", [D, T]); HFd = dscr("HFd", [D, T])
    BETA = dscr("BETAd", [T, 1])
    QFd = dscr("QFd", [D, T]); KFd = dscr("KFd", [D, T]); KTMd = dscr("KTMd", [T, D]); VEd = dscr("VEd", [T, D])
    Y1d = dscr("Y1d", [D, T]); H2d = dscr("H2d", [D, T]); HIDd = dscr("HIDd", [HID0, T])
    SSd = dscr("SSd", [D, T]); X1d = dscr("X1d", [D, T]); X1Nd = dscr("X1Nd", [D, T]); X1NTd = dscr("X1NTd", [T, D])
    YSSMd = dscr("YSSMd", [T, D]); YMOEd = dscr("YMOEd", [D, T]); GATEd = dscr("GATEd", [8, T])

    def sb(name, shape):
        return nc.alloc_sbuf_tensor(name, shape, F32).ap()

    IDN = sb("IDN", [128, 128]); UPI = sb("UPI", [128, 128]); UPN = sb("UPN", [128, 128])
    IOF = sb("IOF", [128, 128]); ONR = sb("ONR", [1, 128]); ONC = sb("ONC", [128, 1])
    NIOP = sb("NIOP", [128, 1]); IOP = sb("IOP", [128, 1])
    W1B = sb("W1B", [128, D]); BWB = sb("BWB", [128, D]); BB128 = sb("BB128", [128, 1])
    PINV_E = sb("PINV_E", [128, D]); PFWD_E = sb("PFWD_E", [128, D])
    OMAR = sb("OMAR", [1, D]); AROWV = sb("AROWV", [1, D])
    OMA127 = AROWV
    GS = sb("GS", [128, 8]); N2W = sb("N2W", [128, 8]); DPP = sb("DPP", [128, 8])
    CB1P = sb("CB1P", [128, 8]); CB2P = sb("CB2P", [128, 4]); CB3P = sb("CB3P", [128, 2])
    GB1P = sb("GB1P", [128, 4]); GB2P = sb("GB2P", [128, 2])
    CB4S = sb("CB4S", [1, 1]); GB3S = sb("GB3S", [1, 1]); AR2 = sb("AR2", [1, 1]); GF07 = sb("GF07", [1, 1])
    LNAB = sb("LNAB", [64, D]); CBND = sb("CBND", [64, D]); CPND = sb("CPND", [64, D])
    SMALL = sb("SMALLb", [128, 2048]); ROWS = sb("ROWSb", [1, 2048]); STAT = sb("STATb", [128, 16])
    PA = sb("PAb", [128, 39100])

    PS = [nc.alloc_psum_tensor(f"pb{i}", [128, 512], F32).ap() for i in range(8)]
    ps_writer = [None] * 8
    ps_reader = [None] * 8

    cur = [0]
    def arena_reset():
        cur[0] = 0
    def alloc(n):
        off = cur[0]
        assert off + n <= 39100, f"arena overflow {off+n}"
        cur[0] += n
        return PA[:, off:off + n]

    tok_last = {e: None for e in ENGINES}
    RELAX = [False]
    fence_base = [()]

    def cfence():
        return tuple(t for t in tok_last.values() if t is not None)

    def soft_fence():
        fence_base[0] = cfence()

    def E(eng, fn, deps=(), dma=False, fence=None):
        if fence is None:
            fence = not RELAX[0]
        if fence:
            d = tuple(deps) + cfence()
        else:
            d = tuple(deps) + fence_base[0]
        t = S.emit(eng, fn, d, dma)
        tok_last[eng] = t
        return t

    def barrier():
        toks = cfence()
        for e in ENGINES:
            tok_last[e] = S.emit(e, None, toks)
        S.marks.append(S.gseq)

    def mm_group(bank, ncols, pairs, deps=(), col0=0, rows=128):
        dl = list(deps)
        if ps_reader[bank] is not None:
            dl.append(ps_reader[bank])
        t = None
        n = len(pairs)
        for i, (lhsT, rhs) in enumerate(pairs):
            def fn(eng, _l=lhsT, _r=rhs, _s=(i == 0), _e=(i == n - 1)):
                return eng.matmul(PS[bank][0:rows, col0:col0 + ncols], _l, _r,
                                  start=_s, stop=_e, skip_group_check=True)
            t = E("tensor", fn, tuple(dl) if i == 0 else (), fence=(None if i == 0 else False))
        ps_writer[bank] = t
        return t

    def mm_acc(bank, ncols, pairs, deps=(), col0=0, rows=128):
        t = None
        for i, (lhsT, rhs) in enumerate(pairs):
            def fn(eng, _l=lhsT, _r=rhs):
                return eng.matmul(PS[bank][0:rows, col0:col0 + ncols], _l, _r,
                                  start=False, stop=True, skip_group_check=True)
            t = E("tensor", fn, tuple(deps) if i == 0 else (), fence=(None if i == 0 else False))
        ps_writer[bank] = t
        return t

    def ps_read(bank, eng, fn, deps=()):
        t = E(eng, fn, tuple(deps) + ((ps_writer[bank],) if ps_writer[bank] else ()))
        ps_reader[bank] = t
        return t

    def transpose(bank, src_ap, dst_ap, p, c, deps=()):
        dl = list(deps)
        if ps_reader[bank] is not None:
            dl.append(ps_reader[bank])
        tt = E("tensor", lambda eng, _s=src_ap, _p=p, _c=c: eng.transpose(
            PS[bank][0:_c, 0:_p], _s, IDN[0:_p, 0:_p]), tuple(dl))
        ps_writer[bank] = tt
        return ps_read(bank, "scalar", lambda eng, _d=dst_ap, _p=p, _c=c: eng.copy(_d, PS[bank][0:_c, 0:_p]))

    def rank1(bank, row_ap, ncols, deps=()):
        dl = list(deps)
        if ps_reader[bank] is not None:
            dl.append(ps_reader[bank])
        t = E("tensor", lambda eng, _r=row_ap, _n=ncols: eng.matmul(
            PS[bank][:, 0:_n], ONR[:], _r, start=True, stop=True, skip_group_check=True), tuple(dl))
        ps_writer[bank] = t
        return t

    def ld(dst, src, deps=(), eng="sync"):
        return E(eng, lambda e, _d=dst, _s=src: e.dma_start(_d, _s), deps, dma=True)

    def st_(dst, src, deps=(), eng="gpsimd"):
        return E(eng, lambda e, _d=dst, _s=src: e.dma_start(_d, _s), deps, dma=True)

    store_toks = []

    ld(IDN[:], id_d[:]); ld(UPI[:], upinc_d[:]); ld(UPN[:], upneg_d[:])
    ld(IOF[:], iotaf_d[:]); ld(ONR[:], onesr_d[:]); ld(ONC[:], onesc_d[:])
    ld(NIOP[:], niop_d[:]); ld(IOP[:], iop_d[:])

    def row_bcast(dst, src_row_ap, width):
        for h in range((width + 511) // 512):
            w = min(512, width - h * 512)
            rank1(0, src_row_ap[:, h * 512:h * 512 + w], w)
            ps_read(0, "scalar", lambda eng, _d=dst, _h=h, _w=w: eng.copy(_d[:, _h * 512:_h * 512 + _w], PS[0][:, 0:_w]))

    t = ld(ROWS[0:1, 0:D], n1w_d[:])
    row_bcast(W1B, ROWS[0:1, 0:D], D)
    ld(ROWS[0:1, 0:D], bw_d[:])
    row_bcast(BWB, ROWS[0:1, 0:D], D)
    ld(ROWS[0:1, 1024:1025], bb_d[:])
    rank1(0, ROWS[0:1, 1024:1025], 1)
    ps_read(0, "scalar", lambda eng: eng.copy(BB128[:], PS[0][:, 0:1]))

    ld(ROWS[0:1, 0:D], alog_d[:])
    E("scalar", lambda eng: eng.activation(AROWV[:], ROWS[0:1, 0:D], AF.Sigmoid))
    E("vector", lambda eng: eng.tensor_scalar(OMAR[:], AROWV[:], -1.0, 1.0, ALU.mult, ALU.add))
    lnoma = ROWS[0:1, 1024:1024 + D]
    E("scalar", lambda eng: eng.activation(lnoma, OMAR[:], AF.Ln))
    A_PP = SMALL[:, 0:8]; LOM_PP = SMALL[:, 8:16]; NLOM_PP = SMALL[:, 16:24]
    STG = SMALL[:, 24:152]
    for dt in range(8):
        transpose(1, AROWV[:, dt * 128:(dt + 1) * 128], A_PP[:, dt:dt + 1], 1, 128)
    for dt in range(8):
        transpose(1, lnoma[:, dt * 128:(dt + 1) * 128], LOM_PP[:, dt:dt + 1], 1, 128)
    E("vector", lambda eng: eng.tensor_scalar_mul(NLOM_PP[:], LOM_PP[:], -1.0))
    E("vector", lambda eng: eng.tensor_scalar_mul(OMA127[:], lnoma, 127.0))
    E("scalar", lambda eng: eng.activation(OMA127[:], OMA127[:], AF.Exp))
    for dt in range(8):
        E("scalar", lambda eng, _dt=dt: eng.activation(STG, IOF[:], AF.Exp, scale=NLOM_PP[:, _dt:_dt + 1]))
        E("vector", lambda eng, _dt=dt: eng.tensor_scalar(STG, STG, A_PP[:, _dt:_dt + 1], None, ALU.mult))
        transpose(1, STG, PINV_E[:, dt * 128:(dt + 1) * 128], 128, 128)
        E("scalar", lambda eng, _dt=dt: eng.activation(STG, IOF[:], AF.Exp, scale=LOM_PP[:, _dt:_dt + 1]))
        transpose(1, STG, PFWD_E[:, dt * 128:(dt + 1) * 128], 128, 128)

    def row_to_pp(dst, src_d, width):
        ld(ROWS[0:1, 0:width], src_d[:])
        for dt in range(width // 128):
            transpose(1, ROWS[0:1, dt * 128:(dt + 1) * 128], dst[:, dt:dt + 1], 1, 128)

    row_to_pp(N2W, n2w_d, D)
    row_to_pp(DPP, Dv_d, D)
    row_to_pp(CB1P, cb1_d, D)
    row_to_pp(CB2P, cb2_d, 512)
    row_to_pp(CB3P, cb3_d, 256)
    row_to_pp(GB1P, gb1_d, 512)
    row_to_pp(GB2P, gb2_d, 256)
    ld(CB4S[:], cb4_d[:]); ld(GB3S[:], gb3_d[:])
    ld(ROWS[0:1, 1024:1025], ar_d[:])
    E("vector", lambda eng: eng.tensor_scalar_mul(AR2[:], ROWS[0:1, 1024:1025], 2.0))

    LA_T = SMALL[:, 152:216]
    for (src_d, dst) in ((logA_d, LNAB), (Bp_d, CBND), (Cp_d, CPND)):
        for dt in range(8):
            ld(LA_T, src_d[dt * 128:(dt + 1) * 128, :])
            transpose(1, LA_T, dst[0:64, dt * 128:(dt + 1) * 128], 128, 64)
    arena_reset()
    TMPA = alloc(1024)[0:64, :]
    RECA = alloc(1024)[0:64, :]
    E("scalar", lambda eng: eng.activation(TMPA, LNAB[0:64, :], AF.Exp))
    E("vector", lambda eng: eng.tensor_scalar_mul(TMPA, TMPA, -1.0))
    E("vector", lambda eng: eng.reciprocal(RECA, TMPA))
    E("vector", lambda eng: eng.tensor_scalar_mul(LNAB[0:64, :], TMPA, 0.1))
    E("scalar", lambda eng: eng.activation(TMPA, LNAB[0:64, :], AF.Exp))
    E("vector", lambda eng: eng.tensor_scalar(TMPA, TMPA, 1.0, None, ALU.subtract))
    E("vector", lambda eng: eng.tensor_mul(TMPA, TMPA, RECA))
    E("vector", lambda eng: eng.tensor_mul(CBND[0:64, :], CBND[0:64, :], TMPA))
    E("vector", lambda eng: eng.tensor_mul(CBND[0:64, :], CBND[0:64, :], CPND[0:64, :]))
    LNABd = dscr("LNABd", [64, D]); CBNDd = dscr("CBNDd", [64, D])
    store_toks.append(st_(LNABd[:], LNAB[0:64, :]))
    store_toks.append(st_(CBNDd[:], CBND[0:64, :]))
    E("vector", lambda eng: eng.memset(GS[:], 0.0))
    barrier()

    arena_reset()
    XT = alloc(1024); XSQ = alloc(1024); XNT = alloc(1024); HT = alloc(1024)
    FMB = alloc(1024); FMB2 = alloc(1024)
    s1 = STAT[:, 0:1]; s2 = STAT[:, 1:2]; s3 = STAT[:, 2:3]; s4 = STAT[:, 3:4]
    for u in range(NSUB):
        ld(XT, x_d[u * 128:(u + 1) * 128, :])
        E("scalar", lambda eng: eng.activation(XSQ, XT, AF.Square))
        E("vector", lambda eng: eng.reduce_sum(s1, XT, AX.X))
        E("vector", lambda eng: eng.reduce_sum(s2, XSQ, AX.X))
        E("vector", lambda eng: eng.tensor_scalar_mul(s1, s1, 1.0 / D))
        E("vector", lambda eng: eng.tensor_mul(s3, s1, s1))
        E("vector", lambda eng: eng.scalar_tensor_tensor(s2, s2, 1.0 / D, s3, ALU.mult, ALU.subtract))
        E("vector", lambda eng: eng.tensor_scalar_add(s2, s2, 1e-5))
        E("scalar", lambda eng: eng.activation(s2, s2, AF.Sqrt))
        E("vector", lambda eng: eng.reciprocal(s2, s2))
        E("vector", lambda eng: eng.tensor_scalar(XNT, XT, s1, s2, ALU.subtract, ALU.mult))
        E("scalar", lambda eng: eng.activation(XSQ, XNT, AF.Square))
        E("vector", lambda eng: eng.reduce_sum(s3, XSQ, AX.X))
        E("vector", lambda eng: eng.tensor_scalar(s3, s3, 1.0 / D, 1e-6, ALU.mult, ALU.add))
        E("scalar", lambda eng: eng.activation(s3, s3, AF.Sqrt))
        E("vector", lambda eng: eng.reciprocal(s3, s3))
        E("vector", lambda eng: eng.tensor_scalar(HT, XNT, s3, None, ALU.mult))
        E("vector", lambda eng: eng.tensor_mul(HT, HT, W1B[:]))
        E("vector", lambda eng: eng.tensor_mul(XSQ, HT, BWB[:]))
        E("vector", lambda eng: eng.reduce_sum(s4, XSQ, AX.X))
        E("scalar", lambda eng: eng.activation(s4, s4, AF.Sigmoid, bias=BB128[:]))
        store_toks.append(st_(BETA[u * 128:(u + 1) * 128, :], s4))
        for dt in range(8):
            transpose(2, XNT[:, dt * 128:(dt + 1) * 128], FMB[:, dt * 128:(dt + 1) * 128], 128, 128)
            transpose(3, HT[:, dt * 128:(dt + 1) * 128], FMB2[:, dt * 128:(dt + 1) * 128], 128, 128)
            store_toks.append(st_(XN[dt * 128:(dt + 1) * 128, u * 128:(u + 1) * 128], FMB[:, dt * 128:(dt + 1) * 128]))
            store_toks.append(st_(HFd[dt * 128:(dt + 1) * 128, u * 128:(u + 1) * 128], FMB2[:, dt * 128:(dt + 1) * 128]))
            E("vector", lambda eng, _dt=dt: eng.reduce_sum(STAT[:, 8 + _dt:9 + _dt], FMB[:, _dt * 128:(_dt + 1) * 128], AX.X))
            E("vector", lambda eng, _dt=dt: eng.tensor_add(GS[:, _dt:_dt + 1], GS[:, _dt:_dt + 1], STAT[:, 8 + _dt:9 + _dt]))
        for dt in range(8):
            transpose(2, XT[:, dt * 128:(dt + 1) * 128], FMB[:, dt * 128:(dt + 1) * 128], 128, 128)
            store_toks.append(st_(XF[dt * 128:(dt + 1) * 128, u * 128:(u + 1) * 128], FMB[:, dt * 128:(dt + 1) * 128]))
    barrier()

    arena_reset()
    WQ = [alloc(1024) for _ in range(8)]
    WK = [alloc(1024) for _ in range(8)]
    WV = [alloc(1024) for _ in range(8)]
    HFB = alloc(4096); KFB = alloc(4096); VFB = alloc(4096)
    KSQ = VFB
    QM = alloc(512); VTM = alloc(1024)
    OMAC = HFB[:, 0:1024]
    KTMS = VTM
    ZB = KFB[:, 0:1024]
    INVB = QM
    CARRY = ROWS[0:1, 0:D]
    KROW = ROWS[0:1, 1024:1536]

    for kp in range(8):
        ld(WQ[kp], wq_d[kp * 128:(kp + 1) * 128, :])
        ld(WK[kp], wk_d[kp * 128:(kp + 1) * 128, :])
        ld(WV[kp], wv_d[kp * 128:(kp + 1) * 128, :])
    E("vector", lambda eng: eng.memset(CARRY, 0.0))
    RELAX[0] = True
    qst_p = None
    for c in range(NCH):
        soft_fence()
        hft = [ld(HFB[:, kp * 512:(kp + 1) * 512], HFd[kp * 128:(kp + 1) * 128, c * 512:(c + 1) * 512]) for kp in range(8)]
        for m in range(8):
            bk = m % 2
            mm_group(bk, 512, [(WQ[kp][:, m * 128:(m + 1) * 128], HFB[:, kp * 512:(kp + 1) * 512]) for kp in range(8)], deps=tuple(hft))
            te = ps_read(bk, "scalar", lambda eng, _b=bk: eng.copy(QM, PS[_b][:, 0:512]), deps=(qst_p,) if qst_p else ())
            qst_p = st_(QFd[m * 128:(m + 1) * 128, c * 512:(c + 1) * 512], QM, deps=(te,))
            store_toks.append(qst_p)
        kev = []
        for m in range(8):
            bk = 2 + m % 2
            mm_group(bk, 512, [(WK[kp][:, m * 128:(m + 1) * 128], HFB[:, kp * 512:(kp + 1) * 512]) for kp in range(8)], deps=tuple(hft))
            kev.append(ps_read(bk, "scalar", lambda eng, _b=bk, _m=m: eng.copy(KFB[:, _m * 512:(_m + 1) * 512], PS[_b][:, 0:512])))
        ksq = [E("scalar", lambda eng, _m=m: eng.activation(KSQ[:, _m * 512:(_m + 1) * 512], KFB[:, _m * 512:(_m + 1) * 512], AF.Square), (kev[m],)) for m in range(8)]
        tss = mm_group(4, 512, [(ONC[:], KSQ[:, m * 512:(m + 1) * 512]) for m in range(8)], rows=1, deps=tuple(ksq))
        tnm = ps_read(4, "scalar", lambda eng: eng.activation(KROW, PS[4][0:1, 0:512], AF.Sqrt))
        tmx = E("vector", lambda eng: eng.tensor_scalar_max(KROW, KROW, 1e-12), (tnm,))
        tin = E("vector", lambda eng: eng.reciprocal(KROW, KROW))
        rank1(4, KROW, 512, deps=(tin,))
        tib = ps_read(4, "scalar", lambda eng: eng.copy(INVB, PS[4][:, 0:512]), deps=(qst_p,))
        kn = [E("vector", lambda eng, _m=m: eng.tensor_mul(KFB[:, _m * 512:(_m + 1) * 512], KFB[:, _m * 512:(_m + 1) * 512], INVB), (tib, kev[m])) for m in range(8)]
        kst = []
        for m in range(8):
            kst.append(st_(KFd[m * 128:(m + 1) * 128, c * 512:(c + 1) * 512], KFB[:, m * 512:(m + 1) * 512], deps=(kn[m],)))
            store_toks.append(kst[-1])
        ktm_st = []
        tst_p = None
        for s in range(4):
            u = c * 4 + s
            trs = []
            for dt in range(8):
                trs.append(transpose(5, KFB[:, dt * 512 + s * 128: dt * 512 + (s + 1) * 128], KTMS[:, dt * 128:(dt + 1) * 128],
                                     128, 128, deps=(kn[dt],) + ((tst_p,) if tst_p else ())))
            tst_p = st_(KTMd[u * 128:(u + 1) * 128, :], KTMS, deps=tuple(trs))
            ktm_st.append(tst_p)
            store_toks.append(tst_p)
        vev = []
        vmm_last = None
        for m in range(8):
            bk = m % 2
            vmm_last = mm_group(bk, 512, [(WV[kp][:, m * 128:(m + 1) * 128], HFB[:, kp * 512:(kp + 1) * 512]) for kp in range(8)], deps=tuple(hft))
            vev.append(ps_read(bk, "scalar", lambda eng, _b=bk, _m=m: eng.copy(VFB[:, _m * 512:(_m + 1) * 512], PS[_b][:, 0:512]), deps=(tss,)))
        vst_p = None
        for s in range(4):
            u = c * 4 + s
            trs = []
            for dt in range(8):
                trs.append(transpose(5, VFB[:, dt * 512 + s * 128: dt * 512 + (s + 1) * 128], VTM[:, dt * 128:(dt + 1) * 128],
                                     128, 128, deps=(vev[dt],) + tuple(ktm_st) + ((vst_p,) if vst_p else ())))
            tz = E("vector", lambda eng: eng.tensor_mul(ZB, PINV_E[:], VTM), tuple(trs) + tuple(ktm_st) + (kst[0], kst[1]))
            tc0 = E("vector", lambda eng: eng.tensor_mul(OMAC[0:1, :], CARRY, OMAR[:]), (vmm_last,))
            tc1 = E("vector", lambda eng: eng.tensor_add(ZB[0:1, :], ZB[0:1, :], OMAC[0:1, :]))
            mm_group(6, 512, [(UPI[:], ZB[:, 0:512])], deps=(tc1,))
            mm_group(7, 512, [(UPI[:], ZB[:, 512:1024])], deps=(tc1,))
            e1 = ps_read(6, "vector", lambda eng: eng.tensor_mul(VTM[:, 0:512], PFWD_E[:, 0:512], PS[6][:, 0:512]))
            e2 = ps_read(7, "vector", lambda eng: eng.tensor_mul(VTM[:, 512:1024], PFWD_E[:, 512:1024], PS[7][:, 0:512]))
            mm_group(2, 512, [(ONC[:], ZB[:, 0:512])], rows=1, deps=(tc1,))
            ps_read(2, "vector", lambda eng: eng.tensor_mul(CARRY[:, 0:512], PS[2][0:1, 0:512], OMA127[:, 0:512]))
            mm_group(3, 512, [(ONC[:], ZB[:, 512:1024])], rows=1, deps=(tc1,))
            ps_read(3, "vector", lambda eng: eng.tensor_mul(CARRY[:, 512:1024], PS[3][0:1, 0:512], OMA127[:, 512:1024]))
            vst_p = st_(VEd[u * 128:(u + 1) * 128, :], VTM, deps=(e1, e2))
            store_toks.append(vst_p)
    RELAX[0] = False
    barrier()

    arena_reset()
    WO = [alloc(1024) for _ in range(8)]
    STT = [alloc(1024) for _ in range(8)]
    QFB = alloc(4096); KFB3 = alloc(4096)
    KTMS2 = alloc(1024); VES = alloc(1024)
    ATN = alloc(1024); AFB = alloc(4096)
    UB = alloc(1024); BB_ = alloc(1024)
    NT = alloc(128); RT_A = alloc(128); RT_B = alloc(128)
    MP0 = alloc(128); MP1 = alloc(128); MTP0 = alloc(128); MTP1 = alloc(128)
    MQK = alloc(128); BBC = alloc(128); BETR = alloc(128)
    BETP = alloc(4)
    Y1M = alloc(512); XNM = alloc(512); H2M = alloc(512); Y1SQ = alloc(512); RSTB = alloc(512)
    Y1ROW = ROWS[0:1, 1024:1536]

    for kp in range(8):
        ld(WO[kp], wo_d[kp * 128:(kp + 1) * 128, :])
        E("gpsimd", lambda eng, _k=kp: eng.memset(STT[_k], 0.0))

    RELAX[0] = True
    for c in range(NCH):
        soft_fence()
        qkt = [ld(QFB[:, kp * 512:(kp + 1) * 512], QFd[kp * 128:(kp + 1) * 128, c * 512:(c + 1) * 512]) for kp in range(8)]
        qkt += [ld(KFB3[:, kp * 512:(kp + 1) * 512], KFd[kp * 128:(kp + 1) * 128, c * 512:(c + 1) * 512]) for kp in range(8)]
        atn_tr = []
        for s in range(4):
            soft_fence()
            u = c * 4 + s
            t_bp = ld(BETP[:, s:s + 1], BETA[u * 128:(u + 1) * 128, :])
            t_kt = ld(KTMS2, KTMd[u * 128:(u + 1) * 128, :])
            t_ve = ld(VES, VEd[u * 128:(u + 1) * 128, :])
            ksl = [KFB3[:, kp * 512 + s * 128: kp * 512 + (s + 1) * 128] for kp in range(8)]
            qsl = [QFB[:, kp * 512 + s * 128: kp * 512 + (s + 1) * 128] for kp in range(8)]
            bpp = BETP[:, s:s + 1]
            tbr = transpose(0, bpp, BETR[0:1, 0:128], 128, 1, deps=(t_bp,))
            rank1(0, BETR[0:1, 0:128], 128, deps=(tbr,))
            tbb = ps_read(0, "scalar", lambda eng: eng.copy(BBC, PS[0][:, 0:128]))
            mm_group(1, 128, [(ksl[kp], ksl[kp]) for kp in range(8)], deps=tuple(qkt))
            tnt1 = ps_read(1, "vector", lambda eng: eng.tensor_mul(NT, PS[1][:, 0:128], UPN[:]))
            t_nt2 = E("vector", lambda eng: eng.tensor_mul(NT, NT, BBC), (tbb,))
            t_rt = E("vector", lambda eng: eng.tensor_add(RT_A, NT, IDN[:]))
            t_mc = transpose(2, NT, MP0, 128, 128, deps=(t_nt2,))
            t_mt = E("scalar", lambda eng: eng.copy(MTP0, NT), (t_nt2,))
            rt_cur, rt_oth = RT_A, RT_B
            m_cur, mt_cur = MP0, MTP0
            m_nxt, mt_nxt = MP1, MTP1
            t_rtc = t_rt
            for lvl in range(6):
                mm_group(3, 128, [(mt_cur, m_cur)], deps=(t_mt, t_mc))
                tcpm = ps_read(3, "scalar", lambda eng, _m=m_nxt: eng.copy(_m, PS[3][:, 0:128]))
                mm_group(2, 128, [(m_nxt, rt_cur)], deps=(tcpm, t_rtc))
                t_rtc = ps_read(2, "vector", lambda eng, _o=rt_oth, _c=rt_cur: eng.tensor_add(_o, _c, PS[2][:, 0:128]))
                rt_cur, rt_oth = rt_oth, rt_cur
                if lvl < 5:
                    t_mc = transpose(3, m_nxt, mt_nxt, 128, 128, deps=(tcpm,))
                    t_mt = tcpm
                    m_cur, mt_cur = m_nxt, mt_nxt
                    m_nxt, mt_nxt = (MP0, MTP0) if m_cur is MP1 else (MP1, MTP1)
            mm_group(4, 512, [(ksl[kp], STT[kp][:, 0:512]) for kp in range(8)], deps=tuple(qkt))
            mm_group(5, 512, [(ksl[kp], STT[kp][:, 512:1024]) for kp in range(8)], deps=tuple(qkt))
            tb1 = ps_read(4, "vector", lambda eng: eng.scalar_tensor_tensor(BB_[:, 0:512], PS[4][:, 0:512], -1.0, VES[:, 0:512], ALU.mult, ALU.add), deps=(t_ve,))
            tb2 = ps_read(5, "vector", lambda eng: eng.scalar_tensor_tensor(BB_[:, 512:1024], PS[5][:, 0:512], -1.0, VES[:, 512:1024], ALU.mult, ALU.add), deps=(t_ve,))
            tb3 = E("vector", lambda eng, _b=bpp: eng.tensor_scalar(BB_, BB_, _b, None, ALU.mult), (t_bp,))
            mm_group(4, 512, [(rt_cur, BB_[:, 0:512])], deps=(t_rtc, tb3))
            tu1 = ps_read(4, "scalar", lambda eng: eng.copy(UB[:, 0:512], PS[4][:, 0:512]))
            mm_group(5, 512, [(rt_cur, BB_[:, 512:1024])], deps=(t_rtc, tb3))
            tu2 = ps_read(5, "scalar", lambda eng: eng.copy(UB[:, 512:1024], PS[5][:, 0:512]))
            mm_group(6, 512, [(qsl[kp], STT[kp][:, 0:512]) for kp in range(8)], deps=tuple(qkt))
            mm_group(7, 512, [(qsl[kp], STT[kp][:, 512:1024]) for kp in range(8)], deps=tuple(qkt))
            mm_group(1, 128, [(ksl[kp], qsl[kp]) for kp in range(8)], deps=(t_nt2,))
            tmq = ps_read(1, "vector", lambda eng: eng.tensor_mul(MQK, PS[1][:, 0:128], UPI[:]))
            mm_acc(6, 512, [(MQK, UB[:, 0:512])], deps=(tmq, tu1))
            mm_acc(7, 512, [(MQK, UB[:, 512:1024])], deps=(tmq, tu2))
            to1 = ps_read(6, "scalar", lambda eng: eng.copy(ATN[:, 0:512], PS[6][:, 0:512]))
            to2 = ps_read(7, "scalar", lambda eng: eng.copy(ATN[:, 512:1024], PS[7][:, 0:512]))
            for m in range(8):
                mm_group(4, 512, [(KTMS2[:, m * 128:(m + 1) * 128], UB[:, 0:512])], deps=(tu1, t_kt))
                ps_read(4, "vector", lambda eng, _m=m: eng.tensor_add(STT[_m][:, 0:512], STT[_m][:, 0:512], PS[4][:, 0:512]))
                mm_group(5, 512, [(KTMS2[:, m * 128:(m + 1) * 128], UB[:, 512:1024])], deps=(tu2, t_kt))
                ps_read(5, "vector", lambda eng, _m=m: eng.tensor_add(STT[_m][:, 512:1024], STT[_m][:, 512:1024], PS[5][:, 0:512]))
            for dt in range(8):
                atn_tr.append(transpose(0, ATN[:, dt * 128:(dt + 1) * 128], AFB[:, dt * 512 + s * 128: dt * 512 + (s + 1) * 128], 128, 128,
                                        deps=(to1, to2)))
        soft_fence()
        ty_p = None; sy_p = None
        mm1_t = []
        for m in range(8):
            bk = m % 2
            mm_group(bk, 512, [(WO[kp][:, m * 128:(m + 1) * 128], AFB[:, kp * 512:(kp + 1) * 512]) for kp in range(8)], deps=tuple(atn_tr))
            txn = ld(XNM, XN[m * 128:(m + 1) * 128, c * 512:(c + 1) * 512], deps=(ty_p,) if ty_p else ())
            ty = ps_read(bk, "vector", lambda eng, _b=bk: eng.tensor_add(Y1M, PS[_b][:, 0:512], XNM), deps=(txn,) + ((sy_p,) if sy_p else ()))
            sy = st_(Y1d[m * 128:(m + 1) * 128, c * 512:(c + 1) * 512], Y1M, deps=(ty,))
            store_toks.append(sy)
            tq = E("scalar", lambda eng: eng.activation(Y1SQ, Y1M, AF.Square), (ty,) + ((mm1_t[-1],) if mm1_t else ()))
            if m == 0:
                mm1_t.append(mm_group(3, 512, [(ONC[:], Y1SQ)], rows=1, deps=(tq,)))
            else:
                mm1_t.append(mm_acc(3, 512, [(ONC[:], Y1SQ)], rows=1, deps=(tq,)))
            ty_p, sy_p = ty, sy
        trow = ps_read(3, "vector", lambda eng: eng.tensor_scalar(Y1ROW, PS[3][0:1, 0:512], 1.0 / D, 1e-6, ALU.mult, ALU.add), deps=(mm1_t[-1],))
        tsq2 = E("scalar", lambda eng: eng.activation(Y1ROW, Y1ROW, AF.Sqrt), (trow,))
        trc2 = E("vector", lambda eng: eng.reciprocal(Y1ROW, Y1ROW), (tsq2,))
        rank1(3, Y1ROW, 512, deps=(trc2,))
        trb = ps_read(3, "scalar", lambda eng: eng.copy(RSTB, PS[3][:, 0:512]))
        tl_p = None; sh_p = None
        for m in range(8):
            tl = ld(Y1M, Y1d[m * 128:(m + 1) * 128, c * 512:(c + 1) * 512], deps=(tl_p,) if tl_p else (sy_p, mm1_t[-1]))
            th = E("vector", lambda eng: eng.tensor_mul(H2M, Y1M, RSTB), (tl, trb) + ((sh_p,) if sh_p else ()))
            th2 = E("vector", lambda eng, _m=m: eng.tensor_scalar(H2M, H2M, N2W[:, _m:_m + 1], None, ALU.mult))
            sh = st_(H2d[m * 128:(m + 1) * 128, c * 512:(c + 1) * 512], H2M, deps=(th2,))
            store_toks.append(sh)
            tl_p = th; sh_p = sh
    RELAX[0] = False
    barrier()

    arena_reset()
    WGB = [alloc(2048) for _ in range(8)]
    WUB = [alloc(2048) for _ in range(8)]
    H2F = alloc(4096); HIDM = alloc(512); HIDT = alloc(512)
    for kp in range(8):
        ld(WGB[kp], wg0_d[kp * 128:(kp + 1) * 128, :])
        ld(WUB[kp], wu0_d[kp * 128:(kp + 1) * 128, :])
    RELAX[0] = True
    thd_p = None
    sst_p = None
    for c in range(NCH):
        soft_fence()
        h2t = [ld(H2F[:, kp * 512:(kp + 1) * 512], H2d[kp * 128:(kp + 1) * 128, c * 512:(c + 1) * 512]) for kp in range(8)]
        for m in range(16):
            b0, b1 = (0, 1) if m % 2 == 0 else (2, 3)
            mm_group(b0, 512, [(WGB[kp][:, m * 128:(m + 1) * 128], H2F[:, kp * 512:(kp + 1) * 512]) for kp in range(8)], deps=tuple(h2t))
            mm_group(b1, 512, [(WUB[kp][:, m * 128:(m + 1) * 128], H2F[:, kp * 512:(kp + 1) * 512]) for kp in range(8)], deps=tuple(h2t))
            tsl = ps_read(b0, "scalar", lambda eng, _b=b0: eng.activation(HIDT, PS[_b][:, 0:512], AF.Silu), deps=(thd_p,) if thd_p else ())
            thd = ps_read(b1, "vector", lambda eng, _b=b1: eng.tensor_mul(HIDM, HIDT, PS[_b][:, 0:512]), deps=(tsl,) + ((sst_p,) if sst_p else ()))
            sst = st_(HIDd[m * 128:(m + 1) * 128, c * 512:(c + 1) * 512], HIDM, deps=(thd,))
            store_toks.append(sst)
            thd_p = thd; sst_p = sst
    RELAX[0] = False
    barrier()

    arena_reset()
    WDB = [alloc(1024) for _ in range(16)]
    HIDF = alloc(8192); Y1M2 = alloc(512); SM_ = alloc(512)
    for kp in range(16):
        ld(WDB[kp], wd0_d[kp * 128:(kp + 1) * 128, :])
    RELAX[0] = True
    ts_p = None
    sst_p = None
    for c in range(NCH):
        soft_fence()
        hidt = [ld(HIDF[:, kp * 512:(kp + 1) * 512], HIDd[kp * 128:(kp + 1) * 128, c * 512:(c + 1) * 512]) for kp in range(16)]
        for m in range(8):
            bk = m % 2
            mm_group(bk, 512, [(WDB[kp][:, m * 128:(m + 1) * 128], HIDF[:, kp * 512:(kp + 1) * 512]) for kp in range(16)], deps=tuple(hidt))
            ty = ld(Y1M2, Y1d[m * 128:(m + 1) * 128, c * 512:(c + 1) * 512], deps=(ts_p,) if ts_p else ())
            ts = ps_read(bk, "vector", lambda eng, _b=bk: eng.tensor_add(SM_, PS[_b][:, 0:512], Y1M2), deps=(ty,) + ((sst_p,) if sst_p else ()))
            sst = st_(SSd[m * 128:(m + 1) * 128, c * 512:(c + 1) * 512], SM_, deps=(ts,))
            store_toks.append(sst)
            ts_p = ts; sst_p = sst
    RELAX[0] = False
    barrier()

    arena_reset()
    CW1 = [alloc(1024) for _ in range(8)]
    CW2 = [alloc(512) for _ in range(8)]
    CW3 = [alloc(256) for _ in range(4)]
    CW4 = [alloc(1) for _ in range(2)]
    GW1B = alloc(4096)
    GW1 = [GW1B[:, kp * 512:(kp + 1) * 512] for kp in range(8)]
    GW2 = [alloc(256) for _ in range(4)]
    GW3 = [alloc(1) for _ in range(2)]
    OUTA = alloc(4096)
    OUTB = GW1B
    C1B = alloc(4096); C2B = alloc(2048); C3B = alloc(1024)
    X1FB = alloc(4096)
    X1NB = OUTA
    XM_ = C2B[:, 0:512]; SM2 = C2B[:, 512:1024]; X1SQ = C2B[:, 1024:1536]
    MB_ = C3B[:, 0:512]; RB_ = C3B[:, 512:1024]
    CPB = alloc(512); TMS_ = alloc(1024)
    GH1 = STAT[:, 8:12]; GH2 = STAT[:, 12:14]
    TFR = ROWS[0:1, 1024:1536]
    MROW = ROWS[0:1, 0:512]; VROW = ROWS[0:1, 512:1024]

    for kp in range(8):
        ld(CW1[kp], cw1_d[kp * 128:(kp + 1) * 128, :])
        ld(CW2[kp], cw2_d[kp * 128:(kp + 1) * 128, :])
        ld(GW1[kp], gw1_d[kp * 128:(kp + 1) * 128, :])
    for kp in range(4):
        ld(CW3[kp], cw3_d[kp * 128:(kp + 1) * 128, :])
        ld(GW2[kp], gw2_d[kp * 128:(kp + 1) * 128, :])
    for kp in range(2):
        ld(CW4[kp], cw4_d[kp * 128:(kp + 1) * 128, :])
        ld(GW3[kp], gw3_d[kp * 128:(kp + 1) * 128, :])

    E("vector", lambda eng: eng.tensor_scalar_mul(GS[:], GS[:], 1.0 / T))
    for mt in range(4):
        mm_group(0, 1, [(GW1[kp][:, mt * 128:(mt + 1) * 128], GS[:, kp:kp + 1]) for kp in range(8)], col0=mt)
    for mt in range(4):
        ps_read(0, "scalar", lambda eng, _m=mt: eng.activation(GH1[:, _m:_m + 1], PS[0][:, _m:_m + 1], AF.Gelu, bias=GB1P[:, _m:_m + 1]))
    for mt in range(2):
        mm_group(1, 1, [(GW2[kp][:, mt * 128:(mt + 1) * 128], GH1[:, kp:kp + 1]) for kp in range(4)], col0=mt)
    for mt in range(2):
        ps_read(1, "scalar", lambda eng, _m=mt: eng.activation(GH2[:, _m:_m + 1], PS[1][:, _m:_m + 1], AF.Gelu, bias=GB2P[:, _m:_m + 1]))
    mm_group(2, 1, [(GW3[kp], GH2[:, kp:kp + 1]) for kp in range(2)], rows=1)
    ps_read(2, "scalar", lambda eng: eng.activation(GF07[:], PS[2][0:1, 0:1], AF.Sigmoid, bias=GB3S[:]))
    E("vector", lambda eng: eng.tensor_scalar_mul(GF07[:], GF07[:], 0.7))

    twist_map = [(4, 1.0), (5, 1.0), (6, -1.0), (7, -1.0), (0, -1.0), (1, -1.0), (2, 1.0), (3, 1.0)]
    RELAX[0] = True
    for c in range(NCH):
        soft_fence()
        xnt_ = [ld(OUTA[:, dt * 512:(dt + 1) * 512], XN[dt * 128:(dt + 1) * 128, c * 512:(c + 1) * 512]) for dt in range(8)]
        cur_out, oth_out = OUTA, OUTB
        for cyc in range(3):
            soft_fence()
            c1t = []
            for m in range(8):
                bk = m % 2
                mm_group(bk, 512, [(CW1[kp][:, m * 128:(m + 1) * 128], cur_out[:, kp * 512:(kp + 1) * 512]) for kp in range(8)],
                         deps=tuple(xnt_))
                c1t.append(ps_read(bk, "scalar", lambda eng, _b=bk, _m=m: eng.activation(C1B[:, _m * 512:(_m + 1) * 512], PS[_b][:, 0:512], AF.Gelu, bias=CB1P[:, _m:_m + 1])))
            c2t = []
            for m in range(4):
                bk = 2 + m % 2
                mm_group(bk, 512, [(CW2[kp][:, m * 128:(m + 1) * 128], C1B[:, kp * 512:(kp + 1) * 512]) for kp in range(8)],
                         deps=tuple(c1t))
                c2t.append(ps_read(bk, "scalar", lambda eng, _b=bk, _m=m: eng.activation(C2B[:, _m * 512:(_m + 1) * 512], PS[_b][:, 0:512], AF.Gelu, bias=CB2P[:, _m:_m + 1])))
            c3t = []
            for m in range(2):
                bk = m % 2
                mm_group(bk, 512, [(CW3[kp][:, m * 128:(m + 1) * 128], C2B[:, kp * 512:(kp + 1) * 512]) for kp in range(4)],
                         deps=tuple(c2t))
                c3t.append(ps_read(bk, "scalar", lambda eng, _b=bk, _m=m: eng.activation(C3B[:, _m * 512:(_m + 1) * 512], PS[_b][:, 0:512], AF.Gelu, bias=CB3P[:, _m:_m + 1])))
            mm_group(2, 512, [(CW4[kp], C3B[:, kp * 512:(kp + 1) * 512]) for kp in range(2)], rows=1, deps=tuple(c3t))
            ttf = ps_read(2, "scalar", lambda eng: eng.activation(TFR, PS[2][0:1, 0:512], AF.Sigmoid, bias=CB4S[:]))
            tcm = E("vector", lambda eng: eng.tensor_scalar(TFR, TFR, 0.3, GF07[:], ALU.mult, ALU.add), (ttf,))
            tcp = E("vector", lambda eng: eng.tensor_scalar(TFR, TFR, 0.5, AR2[:], ALU.subtract, ALU.mult))
            tcp2 = E("vector", lambda eng: eng.tensor_scalar_add(TFR, TFR, 0.1))
            rank1(3, TFR, 512, deps=(tcp2,))
            tcb = ps_read(3, "scalar", lambda eng: eng.copy(CPB, PS[3][:, 0:512]))
            for dt in range(8):
                src2, sign = twist_map[dt]
                E("vector", lambda eng, _s=src2, _g=sign, _c=cur_out: eng.scalar_tensor_tensor(TMS_[:, 0:512], _c[:, _s * 512:(_s + 1) * 512], _g, CPB, ALU.mult, ALU.mult), (tcb,))
                E("vector", lambda eng, _d=dt, _c=cur_out, _o=oth_out: eng.tensor_add(_o[:, _d * 512:(_d + 1) * 512], _c[:, _d * 512:(_d + 1) * 512], TMS_[:, 0:512]))
            cur_out, oth_out = oth_out, cur_out
        soft_fence()
        t1_p = None; t2_p = None
        mm4_t = []; mm5_t = []
        for dt in range(8):
            tg = E("scalar", lambda eng, _d=dt, _c=cur_out: eng.activation(_c[:, _d * 512:(_d + 1) * 512], _c[:, _d * 512:(_d + 1) * 512], AF.Silu))
            tx = ld(XM_, XF[dt * 128:(dt + 1) * 128, c * 512:(c + 1) * 512], deps=(t2_p,) if t2_p else ())
            ts = ld(SM2, SSd[dt * 128:(dt + 1) * 128, c * 512:(c + 1) * 512], deps=(t1_p,) if t1_p else (), eng="gpsimd")
            t1 = E("vector", lambda eng, _d=dt, _c=cur_out: eng.tensor_mul(TMS_[:, 0:512], _c[:, _d * 512:(_d + 1) * 512], SM2), (tg, ts))
            t2 = E("vector", lambda eng, _d=dt: eng.tensor_add(X1FB[:, _d * 512:(_d + 1) * 512], XM_, TMS_[:, 0:512]), (t1, tx))
            store_toks.append(st_(X1d[dt * 128:(dt + 1) * 128, c * 512:(c + 1) * 512], X1FB[:, dt * 512:(dt + 1) * 512], deps=(t2,)))
            if dt == 0:
                mm4_t.append(mm_group(4, 512, [(ONC[:], X1FB[:, dt * 512:(dt + 1) * 512])], rows=1, deps=(t2,)))
            else:
                mm4_t.append(mm_acc(4, 512, [(ONC[:], X1FB[:, dt * 512:(dt + 1) * 512])], rows=1, deps=(t2,)))
            tq = E("scalar", lambda eng, _d=dt: eng.activation(X1SQ, X1FB[:, _d * 512:(_d + 1) * 512], AF.Square), (t2,) + ((mm5_t[-1],) if mm5_t else ()))
            if dt == 0:
                mm5_t.append(mm_group(5, 512, [(ONC[:], X1SQ)], rows=1, deps=(tq,)))
            else:
                mm5_t.append(mm_acc(5, 512, [(ONC[:], X1SQ)], rows=1, deps=(tq,)))
            t1_p, t2_p = t1, t2
        tmean = ps_read(4, "vector", lambda eng: eng.tensor_scalar_mul(MROW, PS[4][0:1, 0:512], 1.0 / D), deps=(mm4_t[-1],))
        tms2 = E("vector", lambda eng: eng.tensor_mul(VROW, MROW, MROW))
        tvar = ps_read(5, "vector", lambda eng: eng.scalar_tensor_tensor(VROW, PS[5][0:1, 0:512], 1.0 / D, VROW, ALU.mult, ALU.subtract), deps=(mm5_t[-1],))
        tve = E("vector", lambda eng: eng.tensor_scalar_add(VROW, VROW, 1e-5))
        tsq = E("scalar", lambda eng: eng.activation(VROW, VROW, AF.Sqrt), (tve,))
        trc = E("vector", lambda eng: eng.reciprocal(VROW, VROW), (tsq,))
        rank1(4, MROW, 512, deps=(tmean, tms2))
        tmb = ps_read(4, "scalar", lambda eng: eng.copy(MB_, PS[4][:, 0:512]))
        rank1(5, VROW, 512, deps=(trc,))
        trb2 = ps_read(5, "scalar", lambda eng: eng.copy(RB_, PS[5][:, 0:512]))
        xn_new = []
        tst_p = None
        for dt in range(8):
            ta = E("vector", lambda eng, _d=dt: eng.tensor_sub(X1NB[:, _d * 512:(_d + 1) * 512], X1FB[:, _d * 512:(_d + 1) * 512], MB_), (tmb,))
            tb = E("vector", lambda eng, _d=dt: eng.tensor_mul(X1NB[:, _d * 512:(_d + 1) * 512], X1NB[:, _d * 512:(_d + 1) * 512], RB_), (trb2,))
            store_toks.append(st_(X1Nd[dt * 128:(dt + 1) * 128, c * 512:(c + 1) * 512], X1NB[:, dt * 512:(dt + 1) * 512], deps=(tb,)))
            xn_new.append(tb)
        for s in range(4):
            trs = []
            for dt in range(8):
                trs.append(transpose(6, X1NB[:, dt * 512 + s * 128: dt * 512 + (s + 1) * 128], TMS_[:, dt * 128:(dt + 1) * 128],
                                     128, 128, deps=tuple(xn_new) + ((tst_p,) if tst_p else ())))
            tst_p = st_(X1NTd[(c * 4 + s) * 128:(c * 4 + s + 1) * 128, :], TMS_[:, 0:1024], deps=tuple(trs))
            store_toks.append(tst_p)
    RELAX[0] = False
    barrier()

    arena_reset()
    XH = alloc(16384); YH = alloc(16384)
    PIN2 = alloc(1024); PPO2 = alloc(1024); ZW = alloc(1024); TW0 = alloc(512); TW1 = alloc(512)
    LNROW = alloc(512); CBROW = alloc(512)
    AB128 = alloc(512)[0:1, :]; CROW = alloc(512)[0:1, :]
    CRY = ROWS[0:1, 0:512]; ABR = ROWS[0:1, 512:1024]; ABC = ROWS[0:1, 1024:1536]; TRW = ROWS[0:1, 1536:2048]
    RELAX[0] = True
    for hf in range(2):
        soft_fence()
        for u in range(NSUB):
            ld(XH[:, u * 512:(u + 1) * 512], X1NTd[u * 128:(u + 1) * 128, hf * 512:(hf + 1) * 512])
        for i in range(4):
            E("gpsimd", lambda eng, _i=i: eng.memset(YH[:, _i * 4096:(_i + 1) * 4096], 0.0))
        soft_fence()
        for n in range(NS):
            soft_fence()
            lnrow = LNROW[0:1, 0:512]
            cbrow = CBROW[0:1, 0:512]
            t_ln = ld(lnrow, LNABd[n:n + 1, hf * 512:(hf + 1) * 512])
            t_cb = ld(cbrow, CBNDd[n:n + 1, hf * 512:(hf + 1) * 512])
            rank1(6, lnrow, 512, deps=(t_ln,))
            tp1 = ps_read(6, "scalar", lambda eng: eng.activation(PIN2[:, 0:512], PS[6][:, 0:512], AF.Exp, scale=NIOP[:]))
            tp1b = E("scalar", lambda eng: eng.copy(PIN2[:, 512:1024], PIN2[:, 0:512]))
            tp2 = ps_read(6, "scalar", lambda eng: eng.activation(PPO2[:, 0:512], PS[6][:, 0:512], AF.Exp, scale=IOP[:]))
            rank1(7, cbrow, 512, deps=(t_cb,))
            tp3 = ps_read(7, "vector", lambda eng: eng.tensor_mul(PPO2[:, 0:512], PPO2[:, 0:512], PS[7][:, 0:512]), deps=(tp2,))
            tp3b = E("vector", lambda eng: eng.tensor_copy(PPO2[:, 512:1024], PPO2[:, 0:512]))
            tab = E("scalar", lambda eng: eng.activation(ABR, lnrow, AF.Exp), (t_ln,))
            tabc1 = E("vector", lambda eng: eng.tensor_scalar_mul(ABC, lnrow, 127.0), (t_ln,))
            tabc = E("scalar", lambda eng: eng.activation(ABC, ABC, AF.Exp), (tabc1,))
            tab128a = E("vector", lambda eng: eng.tensor_scalar_mul(TRW, lnrow, 128.0), (t_ln,))
            tab128 = E("scalar", lambda eng: eng.activation(AB128, TRW, AF.Exp), (tab128a,))
            E("vector", lambda eng: eng.memset(CRY, 0.0))
            tcrow = E("vector", lambda eng: eng.memset(CROW, 0.0))
            zw_guard = ()
            for p in range(16):
                c0, c1 = 2 * p, 2 * p + 1
                tz = E("vector", lambda eng, _p=p: eng.tensor_mul(ZW, PIN2, XH[:, _p * 1024:(_p + 1) * 1024]), zw_guard + (tp1b,))
                b0 = 0 if p % 2 == 0 else 2
                b1 = 1 if p % 2 == 0 else 3
                mm_group(b0, 512, [(UPI[:], ZW[:, 0:512])], deps=(tz,))
                m_4 = mm_group(4, 512, [(ONC[:], ZW[:, 0:512])], rows=1, deps=(tz,))
                inj0 = mm_acc(b0, 512, [(ONR[:], CROW)], deps=(tcrow,))
                t1 = E("vector", lambda eng: eng.tensor_mul(TRW, CRY, AB128), (tab128,))
                t2 = ps_read(4, "vector", lambda eng: eng.tensor_mul(CRY, PS[4][0:1, 0:512], ABC), deps=(tabc,))
                t3 = E("vector", lambda eng: eng.tensor_add(CRY, CRY, TRW))
                tcrow = E("vector", lambda eng: eng.tensor_mul(CROW, CRY, ABR), (tab, inj0))
                mm_group(b1, 512, [(UPI[:], ZW[:, 512:1024])], deps=(tz,))
                m_5 = mm_group(5, 512, [(ONC[:], ZW[:, 512:1024])], rows=1, deps=(tz,))
                inj1 = mm_acc(b1, 512, [(ONR[:], CROW)], deps=(tcrow,))
                t1 = E("vector", lambda eng: eng.tensor_mul(TRW, CRY, AB128))
                t2 = ps_read(5, "vector", lambda eng: eng.tensor_mul(CRY, PS[5][0:1, 0:512], ABC))
                t3 = E("vector", lambda eng: eng.tensor_add(CRY, CRY, TRW))
                tcrow = E("vector", lambda eng: eng.tensor_mul(CROW, CRY, ABR), (inj1,))
                ps_read(b0, "vector", lambda eng, _b=b0: eng.tensor_mul(TW0, PS[_b][:, 0:512], PPO2[:, 0:512]), deps=(tp3,))
                E("vector", lambda eng, _c=c0: eng.tensor_add(YH[:, _c * 512:(_c + 1) * 512], YH[:, _c * 512:(_c + 1) * 512], TW0))
                ps_read(b1, "vector", lambda eng, _b=b1: eng.tensor_mul(TW1, PS[_b][:, 0:512], PPO2[:, 512:1024]), deps=(tp3b,))
                E("vector", lambda eng, _c=c1: eng.tensor_add(YH[:, _c * 512:(_c + 1) * 512], YH[:, _c * 512:(_c + 1) * 512], TW1))
                zw_guard = (m_4, m_5) + (ps_writer[b0], ps_writer[b1])
        soft_fence()
        for u in range(NSUB):
            store_toks.append(st_(YSSMd[u * 128:(u + 1) * 128, hf * 512:(hf + 1) * 512], YH[:, u * 512:(u + 1) * 512]))
    RELAX[0] = False
    barrier()

    arena_reset()
    XSC = alloc(8192); YSC = alloc(8192); HIDH = alloc(4096)
    WDH = [alloc(1024) for _ in range(4)]
    WGT = alloc(1024); WUT = alloc(1024)
    GTS = alloc(8); MK1 = alloc(8); MK2 = alloc(8); SMB = alloc(8)
    GBC = alloc(512); TW0b = alloc(512); TW1b = alloc(512); GROW = alloc(512)
    GATR = SMALL[0:8, 0:1024]; SCF = SMALL[0:8, 1024:1536]
    WGA = [SMALL[:, 1544 + 8 * kp: 1552 + 8 * kp] for kp in range(8)]
    M1 = STAT[:, 4:5]; M2 = STAT[:, 5:6]; P1_ = STAT[:, 6:7]; P2_ = STAT[:, 7:8]
    for kp in range(8):
        ld(WGA[kp], wgate_d[kp * 128:(kp + 1) * 128, :])
    mgroups = [(0, 4), (4, 4), (8, 3)]
    for sc in range(4):
        RELAX[0] = True
        soft_fence()
        for kp in range(8):
            ld(XSC[:, kp * 1024:(kp + 1) * 1024], X1Nd[kp * 128:(kp + 1) * 128, sc * 1024:(sc + 1) * 1024])
        tyz = E("gpsimd", lambda eng: eng.memset(YSC, 0.0))
        RELAX[0] = False
        for tc in range(2):
            mm_group(0, 512, [(WGA[kp], XSC[:, kp * 1024 + tc * 512: kp * 1024 + (tc + 1) * 512]) for kp in range(8)], rows=8)
            ps_read(0, "scalar", lambda eng: eng.copy(SCF, PS[0][0:8, 0:512]))
            for g in range(4):
                transpose(1, SCF[:, g * 128:(g + 1) * 128], GTS, 8, 128)
                E("vector", lambda eng: eng.reduce_max(M1, GTS, AX.X))
                E("vector", lambda eng: eng.tensor_single_scalar(MK1, GTS, M1, ALU.is_equal))
                E("vector", lambda eng: eng.scalar_tensor_tensor(SMB, MK1, -1e9, GTS, ALU.mult, ALU.add))
                E("vector", lambda eng: eng.reduce_max(M2, SMB, AX.X))
                E("vector", lambda eng: eng.tensor_single_scalar(MK2, SMB, M2, ALU.is_equal))
                E("vector", lambda eng: eng.tensor_sub(P1_, M1, M2))
                E("scalar", lambda eng: eng.activation(P1_, P1_, AF.Sigmoid))
                E("vector", lambda eng: eng.tensor_scalar(P2_, P1_, -1.0, 1.0, ALU.mult, ALU.add))
                E("vector", lambda eng: eng.tensor_scalar(MK1, MK1, P1_, None, ALU.mult))
                E("vector", lambda eng: eng.tensor_scalar(MK2, MK2, P2_, None, ALU.mult))
                E("vector", lambda eng: eng.tensor_add(GTS, MK1, MK2))
                transpose(2, GTS, GATR[:, tc * 512 + g * 128: tc * 512 + (g + 1) * 128], 128, 8)
        store_toks.append(st_(GATEd[:, sc * 1024:(sc + 1) * 1024], GATR[:, 0:1024]))
        RELAX[0] = True
        soft_fence()
        thd_prev = None
        tyy_prev = None
        grow_guard = None
        for e in range(ME):
            for (m0, nmt) in mgroups:
                soft_fence()
                hid_done = []
                gu_guard = ()
                for mi in range(nmt):
                    mt = m0 + mi
                    wg_tk = [ld(WGT[:, kp * 128:(kp + 1) * 128], wgE_d[e, mt, kp, :, :], deps=gu_guard) for kp in range(8)]
                    wu_tk = [ld(WUT[:, kp * 128:(kp + 1) * 128], wuE_d[e, mt, kp, :, :], deps=gu_guard, eng="gpsimd") for kp in range(8)]
                    gu_guard = ()
                    for tc in range(2):
                        bg, bu = (3, 4) if tc == 0 else (0, 1)
                        gmm = mm_group(bg, 512, [(WGT[:, kp * 128:(kp + 1) * 128], XSC[:, kp * 1024 + tc * 512: kp * 1024 + (tc + 1) * 512]) for kp in range(8)], deps=tuple(wg_tk))
                        umm = mm_group(bu, 512, [(WUT[:, kp * 128:(kp + 1) * 128], XSC[:, kp * 1024 + tc * 512: kp * 1024 + (tc + 1) * 512]) for kp in range(8)], deps=tuple(wu_tk))
                        tsl = ps_read(bg, "scalar", lambda eng, _b=bg: eng.activation(TW0b, PS[_b][:, 0:512], AF.Silu), deps=(thd_prev,) if thd_prev else ())
                        thd = ps_read(bu, "vector", lambda eng, _b=bu, _mi=mi, _tc=tc: eng.tensor_mul(HIDH[:, _mi * 1024 + _tc * 512: _mi * 1024 + (_tc + 1) * 512], TW0b, PS[_b][:, 0:512]), deps=(tsl,))
                        hid_done.append(thd)
                        thd_prev = thd
                        gu_guard = gu_guard + (gmm, umm)
                wd_tk = [ld(WDH[mi], wdE_d[e, (m0 + mi) * 128:(m0 + mi + 1) * 128, :]) for mi in range(nmt)]
                for tc in range(2):
                    t_gr = ld(GROW[0:1, 0:512], GATEd[e:e + 1, sc * 1024 + tc * 512: sc * 1024 + (tc + 1) * 512],
                              deps=(grow_guard,) if grow_guard else ())
                    grow_guard = rank1(5, GROW[0:1, 0:512], 512, deps=(t_gr,))
                    tgb = ps_read(5, "scalar", lambda eng: eng.copy(GBC, PS[5][:, 0:512]), deps=(tyy_prev,) if tyy_prev else ())
                    for m in range(8):
                        bnk = 6 if m % 2 == 0 else 7
                        mm_group(bnk, 512, [(WDH[mi][:, m * 128:(m + 1) * 128], HIDH[:, mi * 1024 + tc * 512: mi * 1024 + (tc + 1) * 512]) for mi in range(nmt)],
                                 deps=tuple(wd_tk) + tuple(hid_done))
                        tyy = ps_read(bnk, "vector", lambda eng, _b=bnk: eng.tensor_mul(TW1b, PS[_b][:, 0:512], GBC), deps=(tgb,))
                        tyy_prev = tyy
                        E("vector", lambda eng, _m=m, _tc=tc: eng.tensor_add(YSC[:, _m * 1024 + _tc * 512: _m * 1024 + (_tc + 1) * 512],
                                                                             YSC[:, _m * 1024 + _tc * 512: _m * 1024 + (_tc + 1) * 512], TW1b),
                          deps=(tyz,))
        soft_fence()
        for m in range(8):
            store_toks.append(st_(YMOEd[m * 128:(m + 1) * 128, sc * 1024:(sc + 1) * 1024], YSC[:, m * 1024:(m + 1) * 1024]))
        RELAX[0] = False
    barrier()

    arena_reset()
    YST = alloc(1024); YSF = alloc(4096); XOB = alloc(4096); OTM = alloc(1024)
    X1M_ = alloc(512); X1NM_ = alloc(512); YMM = alloc(512); G2M = alloc(512); S2M = alloc(512)
    for c in range(NCH):
        for s in range(4):
            ld(YST, YSSMd[(c * 4 + s) * 128:(c * 4 + s + 1) * 128, :])
            for dt in range(8):
                transpose(0, YST[:, dt * 128:(dt + 1) * 128], YSF[:, dt * 512 + s * 128: dt * 512 + (s + 1) * 128], 128, 128)
        for dt in range(8):
            ld(X1M_, X1d[dt * 128:(dt + 1) * 128, c * 512:(c + 1) * 512])
            ld(X1NM_, X1Nd[dt * 128:(dt + 1) * 128, c * 512:(c + 1) * 512])
            ld(YMM, YMOEd[dt * 128:(dt + 1) * 128, c * 512:(c + 1) * 512], eng="gpsimd")
            E("scalar", lambda eng: eng.activation(G2M, YMM, AF.Silu))
            E("vector", lambda eng, _d=dt: eng.tensor_scalar(S2M, X1NM_, DPP[:, _d:_d + 1], None, ALU.mult))
            E("vector", lambda eng, _d=dt: eng.tensor_add(S2M, S2M, YSF[:, _d * 512:(_d + 1) * 512]))
            E("vector", lambda eng: eng.tensor_mul(S2M, S2M, G2M))
            E("vector", lambda eng, _d=dt: eng.tensor_add(XOB[:, _d * 512:(_d + 1) * 512], X1M_, S2M))
        for s in range(4):
            for dt in range(8):
                transpose(1, XOB[:, dt * 512 + s * 128: dt * 512 + (s + 1) * 128], OTM[:, dt * 128:(dt + 1) * 128], 128, 128)
            store_toks.append(st_(out_d[(c * 4 + s) * 128:(c * 4 + s + 1) * 128, :], OTM))

    if debug:
        dbg = {"XN": XN, "HF": HFd, "QF": QFd, "KF": KFd, "KTM": KTMd, "VE": VEd,
               "Y1": Y1d, "H2": H2d, "SS": SSd, "X1": X1d, "X1N": X1Nd,
               "YSSM": YSSMd, "YMOE": YMOEd, "BETAo": BETA, "GATEo": GATEd}
        for nm, t in dbg.items():
            o = nc.dram_tensor("dbg_" + nm, list(t.tensor.shape), F32, kind="ExternalOutput").ap()
            store_toks.append(st_(o[:], t[:]))
    S.emit("sync", None, tuple(store_toks))
    S.build(nc, maxseq=maxseq)
    return nc, S


_nc_cache = [None]


def kernel(**inputs):
    def f(name):
        return np.ascontiguousarray(np.asarray(inputs[name], dtype=np.float32))

    x_full = f("L0_attn_x")
    wgE = f("L1_ffn_w_g").reshape(ME, 8, 128, MMT, 128).transpose(0, 3, 1, 2, 4).copy()
    wuE = f("L1_ffn_w_u").reshape(ME, 8, 128, MMT, 128).transpose(0, 3, 1, 2, 4).copy()

    iotaf = np.broadcast_to(np.arange(128, dtype=np.float32)[None, :], (128, 128)).copy()
    jj, tt = np.meshgrid(np.arange(128), np.arange(128), indexing="ij")
    upinc = (jj <= tt).astype(np.float32)
    upneg = -(jj < tt).astype(np.float32)

    common = {
        "n1w": f("L0_attn_norm1_w").reshape(1, D),
        "n2w": f("L0_attn_norm2_w").reshape(1, D),
        "wq": f("L0_attn_Wq"), "wk": f("L0_attn_Wk"), "wv": f("L0_attn_Wv"), "wo": f("L0_attn_Wo"),
        "bw": f("L0_attn_beta_w").reshape(1, D),
        "bb": f("L0_attn_beta_b").reshape(1, 1),
        "alog": f("L0_attn_alpha_logit").reshape(1, D),
        "wg0": f("L0_attn_w_gate"), "wu0": f("L0_attn_w_up"), "wd0": f("L0_attn_w_down"),
        "cw1": f("L0_ffn_cn_w1"), "cw2": f("L0_ffn_cn_w2"), "cw3": f("L0_ffn_cn_w3"), "cw4": f("L0_ffn_cn_w4"),
        "cb1": f("L0_ffn_cn_b1").reshape(1, D), "cb2": f("L0_ffn_cn_b2").reshape(1, 512),
        "cb3": f("L0_ffn_cn_b3").reshape(1, 256), "cb4": f("L0_ffn_cn_b4").reshape(1, 1),
        "gw1": f("L0_ffn_gc_w1"), "gw2": f("L0_ffn_gc_w2"), "gw3": f("L0_ffn_gc_w3"),
        "gb1": f("L0_ffn_gc_b1").reshape(1, 512), "gb2": f("L0_ffn_gc_b2").reshape(1, 256),
        "gb3": f("L0_ffn_gc_b3").reshape(1, 1),
        "ar": f("L0_ffn_adaptive_range").reshape(1, 1),
        "logA": f("L1_attn_log_A"), "Bp": f("L1_attn_B_param"), "Cp": f("L1_attn_C"),
        "Dv": f("L1_attn_D").reshape(1, D),
        "wgate": f("L1_ffn_w_gate"),
        "wgE": wgE, "wuE": wuE, "wdE": f("L1_ffn_w_d"),
        "idn": np.eye(128, dtype=np.float32),
        "upinc": upinc, "upneg": upneg, "iotaf": iotaf,
        "onesr": np.ones((1, 128), np.float32), "onesc": np.ones((128, 1), np.float32),
        "niop": -np.arange(128, dtype=np.float32).reshape(128, 1),
        "iop": np.arange(128, dtype=np.float32).reshape(128, 1),
    }
    in_maps = [dict(common, x=np.ascontiguousarray(x_full[b])) for b in range(8)]
    if _nc_cache[0] is None:
        _nc_cache[0] = build_program()[0]
    nc = _nc_cache[0]
    res = run_bass_kernel_spmd(nc, in_maps, core_ids=list(range(8)))
    return np.stack([r["out"] for r in res.results], axis=0)


# version 11 (final)
# speedup vs baseline: 1.5563x; 1.0647x over previous
import numpy as np
import concourse.bass as bass
import concourse.mybir as mybir
from concourse.bass_utils import run_bass_kernel_spmd

F32 = mybir.dt.float32
AF = mybir.ActivationFunctionType
ALU = mybir.AluOpType
AX = mybir.AxisListType

T = 4096
D = 1024
NS = 64
HID0 = 2048
ME = 8
MM = 1408
MMT = 11
NSUB = T // 128
NCH = T // 512

ENGINES = ("sync", "tensor", "vector", "scalar", "gpsimd")
AUTO_FENCE = True


class Sched:
    def __init__(self):
        self.q = {e: [] for e in ENGINES}
        self.gseq = 0
        self.seqs = {e: [] for e in ENGINES}
        self.marks = []

    def emit(self, eng, fn, deps=(), dma=False):
        lst = self.q[eng]
        tok = (eng, len(lst))
        deps = tuple(d for d in deps if d is not None)
        lst.append([fn, deps, dma, False])
        self.seqs[eng].append(self.gseq)
        self.gseq += 1
        return tok

    def build(self, nc, maxseq=None):
        if maxseq is not None:
            for e in ENGINES:
                keep = sum(1 for s in self.seqs[e] if s < maxseq)
                self.q[e] = self.q[e][:keep]
        for e in ENGINES:
            for rec in self.q[e]:
                if rec[2]:
                    rec[3] = True
                for (de, di) in rec[1]:
                    self.q[de][di][3] = True
        vals = {}
        for e in ENGINES:
            c = 0
            for i, rec in enumerate(self.q[e]):
                if rec[3] and rec[0] is not None:
                    c += 16 if rec[2] else 1
                vals[(e, i)] = c
        import contextlib
        stack = contextlib.ExitStack()
        sems = {e: stack.enter_context(nc.semaphore(f"sem_{e}")) for e in ENGINES}
        with nc.Block() as block:
            for e in ENGINES:
                def body(eng, _e=e):
                    hw = {o: 0 for o in ENGINES}
                    for i, (fn, deps, dma, needed) in enumerate(self.q[_e]):
                        for (de, di) in deps:
                            v = vals[(de, di)]
                            if v > hw[de]:
                                eng.wait_ge(sems[de], v)
                                hw[de] = v
                        if fn is not None:
                            inst = fn(eng)
                            if needed:
                                inst.then_inc(sems[_e], 16 if dma else 1)
                getattr(block, e)(body)
        stack.close()


def build_program(maxseq=None, debug=False):
    nc = bass.Bass("TRN2", target_bir_lowering=False)
    S = Sched()

    def din(name, shape):
        return nc.dram_tensor(name, shape, F32, kind="ExternalInput").ap()

    def dscr(name, shape):
        return nc.dram_tensor(name, shape, F32, kind="Internal").ap()

    x_d = din("x", [T, D])
    n1w_d = din("n1w", [1, D]); n2w_d = din("n2w", [1, D])
    wq_d = din("wq", [D, D]); wk_d = din("wk", [D, D]); wv_d = din("wv", [D, D]); wo_d = din("wo", [D, D])
    bw_d = din("bw", [1, D]); bb_d = din("bb", [1, 1]); alog_d = din("alog", [1, D])
    wg0_d = din("wg0", [D, HID0]); wu0_d = din("wu0", [D, HID0]); wd0_d = din("wd0", [HID0, D])
    cw1_d = din("cw1", [D, D]); cw2_d = din("cw2", [D, 512]); cw3_d = din("cw3", [512, 256]); cw4_d = din("cw4", [256, 1])
    cb1_d = din("cb1", [1, D]); cb2_d = din("cb2", [1, 512]); cb3_d = din("cb3", [1, 256]); cb4_d = din("cb4", [1, 1])
    gw1_d = din("gw1", [D, 512]); gw2_d = din("gw2", [512, 256]); gw3_d = din("gw3", [256, 1])
    gb1_d = din("gb1", [1, 512]); gb2_d = din("gb2", [1, 256]); gb3_d = din("gb3", [1, 1])
    ar_d = din("ar", [1, 1])
    logA_d = din("logA", [D, NS]); Bp_d = din("Bp", [D, NS]); Cp_d = din("Cp", [D, NS]); Dv_d = din("Dv", [1, D])
    wgate_d = din("wgate", [D, ME])
    wgE_d = din("wgE", [ME, MMT, 8, 128, 128])
    wuE_d = din("wuE", [ME, MMT, 8, 128, 128])
    wdE_d = din("wdE", [ME, MM, D])
    id_d = din("idn", [128, 128]); upinc_d = din("upinc", [128, 128]); upneg_d = din("upneg", [128, 128])
    iotaf_d = din("iotaf", [128, 128]); onesr_d = din("onesr", [1, 128]); onesc_d = din("onesc", [128, 1])
    niop_d = din("niop", [128, 1]); iop_d = din("iop", [128, 1])

    out_d = nc.dram_tensor("out", [T, D], F32, kind="ExternalOutput").ap()

    XN = dscr("XN", [D, T]); XF = dscr("XF", [D, T]); HFd = dscr("HFd", [D, T])
    BETA = dscr("BETAd", [T, 1])
    QFd = dscr("QFd", [D, T]); KFd = dscr("KFd", [D, T]); KTMd = dscr("KTMd", [T, D]); VEd = dscr("VEd", [T, D])
    Y1d = dscr("Y1d", [D, T]); H2d = dscr("H2d", [D, T]); HIDd = dscr("HIDd", [HID0, T])
    SSd = dscr("SSd", [D, T]); X1d = dscr("X1d", [D, T]); X1Nd = dscr("X1Nd", [D, T]); X1NTd = dscr("X1NTd", [T, D])
    YSSMd = dscr("YSSMd", [T, D]); YMOEd = dscr("YMOEd", [D, T]); GATEd = dscr("GATEd", [8, T])

    def sb(name, shape):
        return nc.alloc_sbuf_tensor(name, shape, F32).ap()

    IDN = sb("IDN", [128, 128]); UPI = sb("UPI", [128, 128]); UPN = sb("UPN", [128, 128])
    IOF = sb("IOF", [128, 128]); ONR = sb("ONR", [1, 128]); ONC = sb("ONC", [128, 1])
    NIOP = sb("NIOP", [128, 1]); IOP = sb("IOP", [128, 1])
    W1B = sb("W1B", [128, D]); BWB = sb("BWB", [128, D]); BB128 = sb("BB128", [128, 1])
    PINV_E = sb("PINV_E", [128, D]); PFWD_E = sb("PFWD_E", [128, D])
    OMAR = sb("OMAR", [1, D]); AROWV = sb("AROWV", [1, D])
    OMA127 = AROWV
    GS = sb("GS", [128, 8]); N2W = sb("N2W", [128, 8]); DPP = sb("DPP", [128, 8])
    CB1P = sb("CB1P", [128, 8]); CB2P = sb("CB2P", [128, 4]); CB3P = sb("CB3P", [128, 2])
    GB1P = sb("GB1P", [128, 4]); GB2P = sb("GB2P", [128, 2])
    CB4S = sb("CB4S", [1, 1]); GB3S = sb("GB3S", [1, 1]); AR2 = sb("AR2", [1, 1]); GF07 = sb("GF07", [1, 1])
    LNAB = sb("LNAB", [64, D]); CBND = sb("CBND", [64, D]); CPND = sb("CPND", [64, D])
    SMALL = sb("SMALLb", [128, 2048]); ROWS = sb("ROWSb", [1, 2048]); STAT = sb("STATb", [128, 16])
    PA = sb("PAb", [128, 39100])

    PS = [nc.alloc_psum_tensor(f"pb{i}", [128, 512], F32).ap() for i in range(8)]
    ps_writer = [None] * 8
    ps_reader = [None] * 8

    cur = [0]
    def arena_reset():
        cur[0] = 0
    def alloc(n):
        off = cur[0]
        assert off + n <= 39100, f"arena overflow {off+n}"
        cur[0] += n
        return PA[:, off:off + n]

    tok_last = {e: None for e in ENGINES}
    RELAX = [False]
    fence_base = [()]

    def cfence():
        return tuple(t for t in tok_last.values() if t is not None)

    def soft_fence():
        fence_base[0] = cfence()

    def E(eng, fn, deps=(), dma=False, fence=None):
        if fence is None:
            fence = not RELAX[0]
        if fence:
            d = tuple(deps) + cfence()
        else:
            d = tuple(deps) + fence_base[0]
        t = S.emit(eng, fn, d, dma)
        tok_last[eng] = t
        return t

    def barrier():
        toks = cfence()
        for e in ENGINES:
            tok_last[e] = S.emit(e, None, toks)
        S.marks.append(S.gseq)

    def mm_group(bank, ncols, pairs, deps=(), col0=0, rows=128):
        dl = list(deps)
        if ps_reader[bank] is not None:
            dl.append(ps_reader[bank])
        t = None
        n = len(pairs)
        for i, (lhsT, rhs) in enumerate(pairs):
            def fn(eng, _l=lhsT, _r=rhs, _s=(i == 0), _e=(i == n - 1)):
                return eng.matmul(PS[bank][0:rows, col0:col0 + ncols], _l, _r,
                                  start=_s, stop=_e, skip_group_check=True)
            t = E("tensor", fn, tuple(dl) if i == 0 else (), fence=(None if i == 0 else False))
        ps_writer[bank] = t
        return t

    def mm_acc(bank, ncols, pairs, deps=(), col0=0, rows=128):
        t = None
        for i, (lhsT, rhs) in enumerate(pairs):
            def fn(eng, _l=lhsT, _r=rhs):
                return eng.matmul(PS[bank][0:rows, col0:col0 + ncols], _l, _r,
                                  start=False, stop=True, skip_group_check=True)
            t = E("tensor", fn, tuple(deps) if i == 0 else (), fence=(None if i == 0 else False))
        ps_writer[bank] = t
        return t

    def ps_read(bank, eng, fn, deps=()):
        t = E(eng, fn, tuple(deps) + ((ps_writer[bank],) if ps_writer[bank] else ()))
        ps_reader[bank] = t
        return t

    def transpose(bank, src_ap, dst_ap, p, c, deps=()):
        dl = list(deps)
        if ps_reader[bank] is not None:
            dl.append(ps_reader[bank])
        tt = E("tensor", lambda eng, _s=src_ap, _p=p, _c=c: eng.transpose(
            PS[bank][0:_c, 0:_p], _s, IDN[0:_p, 0:_p]), tuple(dl))
        ps_writer[bank] = tt
        return ps_read(bank, "scalar", lambda eng, _d=dst_ap, _p=p, _c=c: eng.copy(_d, PS[bank][0:_c, 0:_p]))

    def rank1(bank, row_ap, ncols, deps=()):
        dl = list(deps)
        if ps_reader[bank] is not None:
            dl.append(ps_reader[bank])
        t = E("tensor", lambda eng, _r=row_ap, _n=ncols: eng.matmul(
            PS[bank][:, 0:_n], ONR[:], _r, start=True, stop=True, skip_group_check=True), tuple(dl))
        ps_writer[bank] = t
        return t

    def ld(dst, src, deps=(), eng="sync"):
        return E(eng, lambda e, _d=dst, _s=src: e.dma_start(_d, _s), deps, dma=True)

    def st_(dst, src, deps=(), eng="gpsimd"):
        return E(eng, lambda e, _d=dst, _s=src: e.dma_start(_d, _s), deps, dma=True)

    store_toks = []

    ld(IDN[:], id_d[:]); ld(UPI[:], upinc_d[:]); ld(UPN[:], upneg_d[:])
    ld(IOF[:], iotaf_d[:]); ld(ONR[:], onesr_d[:]); ld(ONC[:], onesc_d[:])
    ld(NIOP[:], niop_d[:]); ld(IOP[:], iop_d[:])

    def row_bcast(dst, src_row_ap, width):
        for h in range((width + 511) // 512):
            w = min(512, width - h * 512)
            rank1(0, src_row_ap[:, h * 512:h * 512 + w], w)
            ps_read(0, "scalar", lambda eng, _d=dst, _h=h, _w=w: eng.copy(_d[:, _h * 512:_h * 512 + _w], PS[0][:, 0:_w]))

    t = ld(ROWS[0:1, 0:D], n1w_d[:])
    row_bcast(W1B, ROWS[0:1, 0:D], D)
    ld(ROWS[0:1, 0:D], bw_d[:])
    row_bcast(BWB, ROWS[0:1, 0:D], D)
    ld(ROWS[0:1, 1024:1025], bb_d[:])
    rank1(0, ROWS[0:1, 1024:1025], 1)
    ps_read(0, "scalar", lambda eng: eng.copy(BB128[:], PS[0][:, 0:1]))

    ld(ROWS[0:1, 0:D], alog_d[:])
    E("scalar", lambda eng: eng.activation(AROWV[:], ROWS[0:1, 0:D], AF.Sigmoid))
    E("vector", lambda eng: eng.tensor_scalar(OMAR[:], AROWV[:], -1.0, 1.0, ALU.mult, ALU.add))
    lnoma = ROWS[0:1, 1024:1024 + D]
    E("scalar", lambda eng: eng.activation(lnoma, OMAR[:], AF.Ln))
    A_PP = SMALL[:, 0:8]; LOM_PP = SMALL[:, 8:16]; NLOM_PP = SMALL[:, 16:24]
    STG = SMALL[:, 24:152]
    for dt in range(8):
        transpose(1, AROWV[:, dt * 128:(dt + 1) * 128], A_PP[:, dt:dt + 1], 1, 128)
    for dt in range(8):
        transpose(1, lnoma[:, dt * 128:(dt + 1) * 128], LOM_PP[:, dt:dt + 1], 1, 128)
    E("vector", lambda eng: eng.tensor_scalar_mul(NLOM_PP[:], LOM_PP[:], -1.0))
    E("vector", lambda eng: eng.tensor_scalar_mul(OMA127[:], lnoma, 127.0))
    E("scalar", lambda eng: eng.activation(OMA127[:], OMA127[:], AF.Exp))
    for dt in range(8):
        E("scalar", lambda eng, _dt=dt: eng.activation(STG, IOF[:], AF.Exp, scale=NLOM_PP[:, _dt:_dt + 1]))
        E("vector", lambda eng, _dt=dt: eng.tensor_scalar(STG, STG, A_PP[:, _dt:_dt + 1], None, ALU.mult))
        transpose(1, STG, PINV_E[:, dt * 128:(dt + 1) * 128], 128, 128)
        E("scalar", lambda eng, _dt=dt: eng.activation(STG, IOF[:], AF.Exp, scale=LOM_PP[:, _dt:_dt + 1]))
        transpose(1, STG, PFWD_E[:, dt * 128:(dt + 1) * 128], 128, 128)

    def row_to_pp(dst, src_d, width):
        ld(ROWS[0:1, 0:width], src_d[:])
        for dt in range(width // 128):
            transpose(1, ROWS[0:1, dt * 128:(dt + 1) * 128], dst[:, dt:dt + 1], 1, 128)

    row_to_pp(N2W, n2w_d, D)
    row_to_pp(DPP, Dv_d, D)
    row_to_pp(CB1P, cb1_d, D)
    row_to_pp(CB2P, cb2_d, 512)
    row_to_pp(CB3P, cb3_d, 256)
    row_to_pp(GB1P, gb1_d, 512)
    row_to_pp(GB2P, gb2_d, 256)
    ld(CB4S[:], cb4_d[:]); ld(GB3S[:], gb3_d[:])
    ld(ROWS[0:1, 1024:1025], ar_d[:])
    E("vector", lambda eng: eng.tensor_scalar_mul(AR2[:], ROWS[0:1, 1024:1025], 2.0))

    LA_T = SMALL[:, 152:216]
    for (src_d, dst) in ((logA_d, LNAB), (Bp_d, CBND), (Cp_d, CPND)):
        for dt in range(8):
            ld(LA_T, src_d[dt * 128:(dt + 1) * 128, :])
            transpose(1, LA_T, dst[0:64, dt * 128:(dt + 1) * 128], 128, 64)
    arena_reset()
    TMPA = alloc(1024)[0:64, :]
    RECA = alloc(1024)[0:64, :]
    E("scalar", lambda eng: eng.activation(TMPA, LNAB[0:64, :], AF.Exp))
    E("vector", lambda eng: eng.tensor_scalar_mul(TMPA, TMPA, -1.0))
    E("vector", lambda eng: eng.reciprocal(RECA, TMPA))
    E("vector", lambda eng: eng.tensor_scalar_mul(LNAB[0:64, :], TMPA, 0.1))
    E("scalar", lambda eng: eng.activation(TMPA, LNAB[0:64, :], AF.Exp))
    E("vector", lambda eng: eng.tensor_scalar(TMPA, TMPA, 1.0, None, ALU.subtract))
    E("vector", lambda eng: eng.tensor_mul(TMPA, TMPA, RECA))
    E("vector", lambda eng: eng.tensor_mul(CBND[0:64, :], CBND[0:64, :], TMPA))
    E("vector", lambda eng: eng.tensor_mul(CBND[0:64, :], CBND[0:64, :], CPND[0:64, :]))
    LNABd = dscr("LNABd", [64, D]); CBNDd = dscr("CBNDd", [64, D])
    store_toks.append(st_(LNABd[:], LNAB[0:64, :]))
    store_toks.append(st_(CBNDd[:], CBND[0:64, :]))
    E("vector", lambda eng: eng.memset(GS[:], 0.0))
    barrier()

    arena_reset()
    XT = alloc(1024); XSQ = alloc(1024); XNT = alloc(1024); HT = alloc(1024)
    FMB = alloc(1024); FMB2 = alloc(1024)
    s1 = STAT[:, 0:1]; s2 = STAT[:, 1:2]; s3 = STAT[:, 2:3]; s4 = STAT[:, 3:4]
    RELAX[0] = True
    soft_fence()
    xt_guard = (); xsq_guard = (); xn_guard = (); ht_guard = ()
    fmb_guard = (); fmb2_guard = (); sb_prev = None
    for u in range(NSUB):
        tx = ld(XT, x_d[u * 128:(u + 1) * 128, :], deps=xt_guard)
        tq = E("scalar", lambda eng: eng.activation(XSQ, XT, AF.Square), (tx,) + xsq_guard)
        tsum = E("vector", lambda eng: eng.reduce_sum(s1, XT, AX.X), (tx,))
        tsq = E("vector", lambda eng: eng.reduce_sum(s2, XSQ, AX.X), (tq,))
        tm = E("vector", lambda eng: eng.tensor_scalar_mul(s1, s1, 1.0 / D), (tsum,))
        tms = E("vector", lambda eng: eng.tensor_mul(s3, s1, s1), (tm,))
        tv = E("vector", lambda eng: eng.scalar_tensor_tensor(s2, s2, 1.0 / D, s3, ALU.mult, ALU.subtract), (tsq, tms))
        tv2 = E("vector", lambda eng: eng.tensor_scalar_add(s2, s2, 1e-5), (tv,))
        tsr = E("scalar", lambda eng: eng.activation(s2, s2, AF.Sqrt), (tv2,))
        trc = E("vector", lambda eng: eng.reciprocal(s2, s2), (tsr,))
        txn = E("vector", lambda eng: eng.tensor_scalar(XNT, XT, s1, s2, ALU.subtract, ALU.mult), (trc, tm) + xn_guard)
        tq2 = E("scalar", lambda eng: eng.activation(XSQ, XNT, AF.Square), (txn, tsq))
        tsq2 = E("vector", lambda eng: eng.reduce_sum(s3, XSQ, AX.X), (tq2, tv))
        tr2a = E("vector", lambda eng: eng.tensor_scalar(s3, s3, 1.0 / D, 1e-6, ALU.mult, ALU.add), (tsq2,))
        tr2b = E("scalar", lambda eng: eng.activation(s3, s3, AF.Sqrt), (tr2a,))
        tr2 = E("vector", lambda eng: eng.reciprocal(s3, s3), (tr2b,))
        th = E("vector", lambda eng: eng.tensor_scalar(HT, XNT, s3, None, ALU.mult), (tr2,) + ht_guard)
        th2 = E("vector", lambda eng: eng.tensor_mul(HT, HT, W1B[:]), (th,))
        tb1 = E("scalar", lambda eng: eng.activation(XSQ, HT, AF.Square), (th2, tsq2)) if False else None
        tb1 = E("vector", lambda eng: eng.tensor_mul(XSQ, HT, BWB[:]), (th2, tsq2))
        tb2 = E("vector", lambda eng: eng.reduce_sum(s4, XSQ, AX.X), (tb1,) + ((sb_prev,) if sb_prev else ()))
        tb3 = E("scalar", lambda eng: eng.activation(s4, s4, AF.Sigmoid, bias=BB128[:]), (tb2,))
        sb_t = st_(BETA[u * 128:(u + 1) * 128, :], s4, deps=(tb3,))
        store_toks.append(sb_t)
        sb_prev = sb_t
        xn_rd = [txn]; ht_rd = [tb1]; xsq_rd = [tsq2, tb2]; xt_rd = [tq, tsum, txn]
        new_fmb_guard = []; new_fmb2_guard = []
        gs_last = None
        for dt in range(8):
            t1 = transpose(2, XNT[:, dt * 128:(dt + 1) * 128], FMB[:, dt * 128:(dt + 1) * 128], 128, 128,
                           deps=(txn,) + fmb_guard)
            t2 = transpose(3, HT[:, dt * 128:(dt + 1) * 128], FMB2[:, dt * 128:(dt + 1) * 128], 128, 128,
                           deps=(th2,) + fmb2_guard)
            xn_rd.append(t1); ht_rd.append(t2)
            sA = st_(XN[dt * 128:(dt + 1) * 128, u * 128:(u + 1) * 128], FMB[:, dt * 128:(dt + 1) * 128], deps=(t1,))
            sB = st_(HFd[dt * 128:(dt + 1) * 128, u * 128:(u + 1) * 128], FMB2[:, dt * 128:(dt + 1) * 128], deps=(t2,))
            store_toks.append(sA); store_toks.append(sB)
            new_fmb2_guard.append(sB)
            tg = E("vector", lambda eng, _dt=dt: eng.reduce_sum(STAT[:, 8 + _dt:9 + _dt], FMB[:, _dt * 128:(_dt + 1) * 128], AX.X), (t1,))
            gs_last = E("vector", lambda eng, _dt=dt: eng.tensor_add(GS[:, _dt:_dt + 1], GS[:, _dt:_dt + 1], STAT[:, 8 + _dt:9 + _dt]), (tg,))
            new_fmb_guard.append(sA); new_fmb_guard.append(tg)
        xfm_t = []
        for dt in range(8):
            t3 = transpose(2, XT[:, dt * 128:(dt + 1) * 128], FMB[:, dt * 128:(dt + 1) * 128], 128, 128,
                           deps=(tx,) + tuple(new_fmb_guard[2 * dt:2 * dt + 2]))
            xt_rd.append(t3)
            sC = st_(XF[dt * 128:(dt + 1) * 128, u * 128:(u + 1) * 128], FMB[:, dt * 128:(dt + 1) * 128], deps=(t3,))
            store_toks.append(sC)
            xfm_t.append(sC)
        xt_guard = tuple(xt_rd)
        xsq_guard = tuple(xsq_rd)
        xn_guard = tuple(xn_rd)
        ht_guard = tuple(ht_rd)
        fmb_guard = tuple(xfm_t)
        fmb2_guard = tuple(new_fmb2_guard)
    RELAX[0] = False
    barrier()

    arena_reset()
    WQ = [alloc(1024) for _ in range(8)]
    WK = [alloc(1024) for _ in range(8)]
    WV = [alloc(1024) for _ in range(8)]
    HFB = alloc(4096); KFB = alloc(4096); VFB = alloc(4096)
    KSQ = VFB
    QM = alloc(512); VTM = alloc(1024)
    OMAC = HFB[:, 0:1024]
    KTMS = VTM
    ZB = KFB[:, 0:1024]
    INVB = QM
    CARRY = ROWS[0:1, 0:D]
    KROW = ROWS[0:1, 1024:1536]

    for kp in range(8):
        ld(WQ[kp], wq_d[kp * 128:(kp + 1) * 128, :])
        ld(WK[kp], wk_d[kp * 128:(kp + 1) * 128, :])
        ld(WV[kp], wv_d[kp * 128:(kp + 1) * 128, :])
    E("vector", lambda eng: eng.memset(CARRY, 0.0))
    RELAX[0] = True
    qst_p = None
    for c in range(NCH):
        soft_fence()
        hft = [ld(HFB[:, kp * 512:(kp + 1) * 512], HFd[kp * 128:(kp + 1) * 128, c * 512:(c + 1) * 512]) for kp in range(8)]
        for m in range(8):
            bk = m % 2
            mm_group(bk, 512, [(WQ[kp][:, m * 128:(m + 1) * 128], HFB[:, kp * 512:(kp + 1) * 512]) for kp in range(8)], deps=tuple(hft))
            te = ps_read(bk, "scalar", lambda eng, _b=bk: eng.copy(QM, PS[_b][:, 0:512]), deps=(qst_p,) if qst_p else ())
            qst_p = st_(QFd[m * 128:(m + 1) * 128, c * 512:(c + 1) * 512], QM, deps=(te,))
            store_toks.append(qst_p)
        kev = []
        for m in range(8):
            bk = 2 + m % 2
            mm_group(bk, 512, [(WK[kp][:, m * 128:(m + 1) * 128], HFB[:, kp * 512:(kp + 1) * 512]) for kp in range(8)], deps=tuple(hft))
            kev.append(ps_read(bk, "scalar", lambda eng, _b=bk, _m=m: eng.copy(KFB[:, _m * 512:(_m + 1) * 512], PS[_b][:, 0:512])))
        ksq = [E("scalar", lambda eng, _m=m: eng.activation(KSQ[:, _m * 512:(_m + 1) * 512], KFB[:, _m * 512:(_m + 1) * 512], AF.Square), (kev[m],)) for m in range(8)]
        tss = mm_group(4, 512, [(ONC[:], KSQ[:, m * 512:(m + 1) * 512]) for m in range(8)], rows=1, deps=tuple(ksq))
        tnm = ps_read(4, "scalar", lambda eng: eng.activation(KROW, PS[4][0:1, 0:512], AF.Sqrt))
        tmx = E("vector", lambda eng: eng.tensor_scalar_max(KROW, KROW, 1e-12), (tnm,))
        tin = E("vector", lambda eng: eng.reciprocal(KROW, KROW))
        rank1(4, KROW, 512, deps=(tin,))
        tib = ps_read(4, "scalar", lambda eng: eng.copy(INVB, PS[4][:, 0:512]), deps=(qst_p,))
        kn = [E("vector", lambda eng, _m=m: eng.tensor_mul(KFB[:, _m * 512:(_m + 1) * 512], KFB[:, _m * 512:(_m + 1) * 512], INVB), (tib, kev[m])) for m in range(8)]
        kst = []
        for m in range(8):
            kst.append(st_(KFd[m * 128:(m + 1) * 128, c * 512:(c + 1) * 512], KFB[:, m * 512:(m + 1) * 512], deps=(kn[m],)))
            store_toks.append(kst[-1])
        ktm_st = []
        tst_p = None
        for s in range(4):
            u = c * 4 + s
            trs = []
            for dt in range(8):
                trs.append(transpose(5, KFB[:, dt * 512 + s * 128: dt * 512 + (s + 1) * 128], KTMS[:, dt * 128:(dt + 1) * 128],
                                     128, 128, deps=(kn[dt],) + ((tst_p,) if tst_p else ())))
            tst_p = st_(KTMd[u * 128:(u + 1) * 128, :], KTMS, deps=tuple(trs))
            ktm_st.append(tst_p)
            store_toks.append(tst_p)
        vev = []
        vmm_last = None
        for m in range(8):
            bk = m % 2
            vmm_last = mm_group(bk, 512, [(WV[kp][:, m * 128:(m + 1) * 128], HFB[:, kp * 512:(kp + 1) * 512]) for kp in range(8)], deps=tuple(hft))
            vev.append(ps_read(bk, "scalar", lambda eng, _b=bk, _m=m: eng.copy(VFB[:, _m * 512:(_m + 1) * 512], PS[_b][:, 0:512]), deps=(tss,)))
        vst_p = None
        for s in range(4):
            u = c * 4 + s
            trs = []
            for dt in range(8):
                trs.append(transpose(5, VFB[:, dt * 512 + s * 128: dt * 512 + (s + 1) * 128], VTM[:, dt * 128:(dt + 1) * 128],
                                     128, 128, deps=(vev[dt],) + tuple(ktm_st) + ((vst_p,) if vst_p else ())))
            tz = E("vector", lambda eng: eng.tensor_mul(ZB, PINV_E[:], VTM), tuple(trs) + tuple(ktm_st) + (kst[0], kst[1]))
            tc0 = E("vector", lambda eng: eng.tensor_mul(OMAC[0:1, :], CARRY, OMAR[:]), (vmm_last,))
            tc1 = E("vector", lambda eng: eng.tensor_add(ZB[0:1, :], ZB[0:1, :], OMAC[0:1, :]))
            mm_group(6, 512, [(UPI[:], ZB[:, 0:512])], deps=(tc1,))
            mm_group(7, 512, [(UPI[:], ZB[:, 512:1024])], deps=(tc1,))
            e1 = ps_read(6, "vector", lambda eng: eng.tensor_mul(VTM[:, 0:512], PFWD_E[:, 0:512], PS[6][:, 0:512]))
            e2 = ps_read(7, "vector", lambda eng: eng.tensor_mul(VTM[:, 512:1024], PFWD_E[:, 512:1024], PS[7][:, 0:512]))
            mm_group(2, 512, [(ONC[:], ZB[:, 0:512])], rows=1, deps=(tc1,))
            ps_read(2, "vector", lambda eng: eng.tensor_mul(CARRY[:, 0:512], PS[2][0:1, 0:512], OMA127[:, 0:512]))
            mm_group(3, 512, [(ONC[:], ZB[:, 512:1024])], rows=1, deps=(tc1,))
            ps_read(3, "vector", lambda eng: eng.tensor_mul(CARRY[:, 512:1024], PS[3][0:1, 0:512], OMA127[:, 512:1024]))
            vst_p = st_(VEd[u * 128:(u + 1) * 128, :], VTM, deps=(e1, e2))
            store_toks.append(vst_p)
    RELAX[0] = False
    barrier()

    arena_reset()
    WO = [alloc(1024) for _ in range(8)]
    STT = [alloc(1024) for _ in range(8)]
    QFB = alloc(4096); KFB3 = alloc(4096)
    KTMS2 = alloc(1024); VES = alloc(1024)
    ATN = alloc(1024); AFB = alloc(4096)
    UB = alloc(1024); BB_ = alloc(1024)
    NT = alloc(128); RT_A = alloc(128); RT_B = alloc(128)
    MP0 = alloc(128); MP1 = alloc(128); MTP0 = alloc(128); MTP1 = alloc(128)
    MQK = alloc(128); BBC = alloc(128); BETR = alloc(128)
    BETP = alloc(4)
    Y1M = alloc(512); XNM = alloc(512); H2M = alloc(512); Y1SQ = alloc(512); RSTB = alloc(512)
    Y1ROW = ROWS[0:1, 1024:1536]

    for kp in range(8):
        ld(WO[kp], wo_d[kp * 128:(kp + 1) * 128, :])
        E("gpsimd", lambda eng, _k=kp: eng.memset(STT[_k], 0.0))

    RELAX[0] = True
    for c in range(NCH):
        soft_fence()
        qkt = [ld(QFB[:, kp * 512:(kp + 1) * 512], QFd[kp * 128:(kp + 1) * 128, c * 512:(c + 1) * 512]) for kp in range(8)]
        qkt += [ld(KFB3[:, kp * 512:(kp + 1) * 512], KFd[kp * 128:(kp + 1) * 128, c * 512:(c + 1) * 512]) for kp in range(8)]
        atn_tr = []
        for s in range(4):
            soft_fence()
            u = c * 4 + s
            t_bp = ld(BETP[:, s:s + 1], BETA[u * 128:(u + 1) * 128, :])
            t_kt = ld(KTMS2, KTMd[u * 128:(u + 1) * 128, :])
            t_ve = ld(VES, VEd[u * 128:(u + 1) * 128, :])
            ksl = [KFB3[:, kp * 512 + s * 128: kp * 512 + (s + 1) * 128] for kp in range(8)]
            qsl = [QFB[:, kp * 512 + s * 128: kp * 512 + (s + 1) * 128] for kp in range(8)]
            bpp = BETP[:, s:s + 1]
            tbr = transpose(0, bpp, BETR[0:1, 0:128], 128, 1, deps=(t_bp,))
            rank1(0, BETR[0:1, 0:128], 128, deps=(tbr,))
            tbb = ps_read(0, "scalar", lambda eng: eng.copy(BBC, PS[0][:, 0:128]))
            mm_group(1, 128, [(ksl[kp], ksl[kp]) for kp in range(8)], deps=tuple(qkt))
            tnt1 = ps_read(1, "vector", lambda eng: eng.tensor_mul(NT, PS[1][:, 0:128], UPN[:]))
            t_nt2 = E("vector", lambda eng: eng.tensor_mul(NT, NT, BBC), (tbb,))
            t_rt = E("vector", lambda eng: eng.tensor_add(RT_A, NT, IDN[:]))
            t_mc = transpose(2, NT, MP0, 128, 128, deps=(t_nt2,))
            t_mt = E("scalar", lambda eng: eng.copy(MTP0, NT), (t_nt2,))
            rt_cur, rt_oth = RT_A, RT_B
            m_cur, mt_cur = MP0, MTP0
            m_nxt, mt_nxt = MP1, MTP1
            t_rtc = t_rt
            for lvl in range(6):
                mm_group(3, 128, [(mt_cur, m_cur)], deps=(t_mt, t_mc))
                tcpm = ps_read(3, "scalar", lambda eng, _m=m_nxt: eng.copy(_m, PS[3][:, 0:128]))
                mm_group(2, 128, [(m_nxt, rt_cur)], deps=(tcpm, t_rtc))
                t_rtc = ps_read(2, "vector", lambda eng, _o=rt_oth, _c=rt_cur: eng.tensor_add(_o, _c, PS[2][:, 0:128]))
                rt_cur, rt_oth = rt_oth, rt_cur
                if lvl < 5:
                    t_mc = transpose(3, m_nxt, mt_nxt, 128, 128, deps=(tcpm,))
                    t_mt = tcpm
                    m_cur, mt_cur = m_nxt, mt_nxt
                    m_nxt, mt_nxt = (MP0, MTP0) if m_cur is MP1 else (MP1, MTP1)
            mm_group(4, 512, [(ksl[kp], STT[kp][:, 0:512]) for kp in range(8)], deps=tuple(qkt))
            mm_group(5, 512, [(ksl[kp], STT[kp][:, 512:1024]) for kp in range(8)], deps=tuple(qkt))
            tb1 = ps_read(4, "vector", lambda eng: eng.scalar_tensor_tensor(BB_[:, 0:512], PS[4][:, 0:512], -1.0, VES[:, 0:512], ALU.mult, ALU.add), deps=(t_ve,))
            tb2 = ps_read(5, "vector", lambda eng: eng.scalar_tensor_tensor(BB_[:, 512:1024], PS[5][:, 0:512], -1.0, VES[:, 512:1024], ALU.mult, ALU.add), deps=(t_ve,))
            tb3 = E("vector", lambda eng, _b=bpp: eng.tensor_scalar(BB_, BB_, _b, None, ALU.mult), (t_bp,))
            mm_group(4, 512, [(rt_cur, BB_[:, 0:512])], deps=(t_rtc, tb3))
            tu1 = ps_read(4, "scalar", lambda eng: eng.copy(UB[:, 0:512], PS[4][:, 0:512]))
            mm_group(5, 512, [(rt_cur, BB_[:, 512:1024])], deps=(t_rtc, tb3))
            tu2 = ps_read(5, "scalar", lambda eng: eng.copy(UB[:, 512:1024], PS[5][:, 0:512]))
            mm_group(6, 512, [(qsl[kp], STT[kp][:, 0:512]) for kp in range(8)], deps=tuple(qkt))
            mm_group(7, 512, [(qsl[kp], STT[kp][:, 512:1024]) for kp in range(8)], deps=tuple(qkt))
            mm_group(1, 128, [(ksl[kp], qsl[kp]) for kp in range(8)], deps=(t_nt2,))
            tmq = ps_read(1, "vector", lambda eng: eng.tensor_mul(MQK, PS[1][:, 0:128], UPI[:]))
            mm_acc(6, 512, [(MQK, UB[:, 0:512])], deps=(tmq, tu1))
            mm_acc(7, 512, [(MQK, UB[:, 512:1024])], deps=(tmq, tu2))
            to1 = ps_read(6, "scalar", lambda eng: eng.copy(ATN[:, 0:512], PS[6][:, 0:512]))
            to2 = ps_read(7, "scalar", lambda eng: eng.copy(ATN[:, 512:1024], PS[7][:, 0:512]))
            for m in range(8):
                mm_group(4, 512, [(KTMS2[:, m * 128:(m + 1) * 128], UB[:, 0:512])], deps=(tu1, t_kt))
                ps_read(4, "vector", lambda eng, _m=m: eng.tensor_add(STT[_m][:, 0:512], STT[_m][:, 0:512], PS[4][:, 0:512]))
                mm_group(5, 512, [(KTMS2[:, m * 128:(m + 1) * 128], UB[:, 512:1024])], deps=(tu2, t_kt))
                ps_read(5, "vector", lambda eng, _m=m: eng.tensor_add(STT[_m][:, 512:1024], STT[_m][:, 512:1024], PS[5][:, 0:512]))
            for dt in range(8):
                atn_tr.append(transpose(0, ATN[:, dt * 128:(dt + 1) * 128], AFB[:, dt * 512 + s * 128: dt * 512 + (s + 1) * 128], 128, 128,
                                        deps=(to1, to2)))
        soft_fence()
        ty_p = None; sy_p = None
        mm1_t = []
        for m in range(8):
            bk = m % 2
            mm_group(bk, 512, [(WO[kp][:, m * 128:(m + 1) * 128], AFB[:, kp * 512:(kp + 1) * 512]) for kp in range(8)], deps=tuple(atn_tr))
            txn = ld(XNM, XN[m * 128:(m + 1) * 128, c * 512:(c + 1) * 512], deps=(ty_p,) if ty_p else ())
            ty = ps_read(bk, "vector", lambda eng, _b=bk: eng.tensor_add(Y1M, PS[_b][:, 0:512], XNM), deps=(txn,) + ((sy_p,) if sy_p else ()))
            sy = st_(Y1d[m * 128:(m + 1) * 128, c * 512:(c + 1) * 512], Y1M, deps=(ty,))
            store_toks.append(sy)
            tq = E("scalar", lambda eng: eng.activation(Y1SQ, Y1M, AF.Square), (ty,) + ((mm1_t[-1],) if mm1_t else ()))
            if m == 0:
                mm1_t.append(mm_group(3, 512, [(ONC[:], Y1SQ)], rows=1, deps=(tq,)))
            else:
                mm1_t.append(mm_acc(3, 512, [(ONC[:], Y1SQ)], rows=1, deps=(tq,)))
            ty_p, sy_p = ty, sy
        trow = ps_read(3, "vector", lambda eng: eng.tensor_scalar(Y1ROW, PS[3][0:1, 0:512], 1.0 / D, 1e-6, ALU.mult, ALU.add), deps=(mm1_t[-1],))
        tsq2 = E("scalar", lambda eng: eng.activation(Y1ROW, Y1ROW, AF.Sqrt), (trow,))
        trc2 = E("vector", lambda eng: eng.reciprocal(Y1ROW, Y1ROW), (tsq2,))
        rank1(3, Y1ROW, 512, deps=(trc2,))
        trb = ps_read(3, "scalar", lambda eng: eng.copy(RSTB, PS[3][:, 0:512]))
        tl_p = None; sh_p = None
        for m in range(8):
            tl = ld(Y1M, Y1d[m * 128:(m + 1) * 128, c * 512:(c + 1) * 512], deps=(tl_p,) if tl_p else (sy_p, mm1_t[-1]))
            th = E("vector", lambda eng: eng.tensor_mul(H2M, Y1M, RSTB), (tl, trb) + ((sh_p,) if sh_p else ()))
            th2 = E("vector", lambda eng, _m=m: eng.tensor_scalar(H2M, H2M, N2W[:, _m:_m + 1], None, ALU.mult))
            sh = st_(H2d[m * 128:(m + 1) * 128, c * 512:(c + 1) * 512], H2M, deps=(th2,))
            store_toks.append(sh)
            tl_p = th; sh_p = sh
    RELAX[0] = False
    barrier()

    arena_reset()
    WGB = [alloc(2048) for _ in range(8)]
    WUB = [alloc(2048) for _ in range(8)]
    H2F = alloc(4096); HIDM = alloc(512); HIDT = alloc(512)
    for kp in range(8):
        ld(WGB[kp], wg0_d[kp * 128:(kp + 1) * 128, :])
        ld(WUB[kp], wu0_d[kp * 128:(kp + 1) * 128, :])
    RELAX[0] = True
    thd_p = None
    sst_p = None
    for c in range(NCH):
        soft_fence()
        h2t = [ld(H2F[:, kp * 512:(kp + 1) * 512], H2d[kp * 128:(kp + 1) * 128, c * 512:(c + 1) * 512]) for kp in range(8)]
        for m in range(16):
            b0, b1 = (0, 1) if m % 2 == 0 else (2, 3)
            mm_group(b0, 512, [(WGB[kp][:, m * 128:(m + 1) * 128], H2F[:, kp * 512:(kp + 1) * 512]) for kp in range(8)], deps=tuple(h2t))
            mm_group(b1, 512, [(WUB[kp][:, m * 128:(m + 1) * 128], H2F[:, kp * 512:(kp + 1) * 512]) for kp in range(8)], deps=tuple(h2t))
            tsl = ps_read(b0, "scalar", lambda eng, _b=b0: eng.activation(HIDT, PS[_b][:, 0:512], AF.Silu), deps=(thd_p,) if thd_p else ())
            thd = ps_read(b1, "vector", lambda eng, _b=b1: eng.tensor_mul(HIDM, HIDT, PS[_b][:, 0:512]), deps=(tsl,) + ((sst_p,) if sst_p else ()))
            sst = st_(HIDd[m * 128:(m + 1) * 128, c * 512:(c + 1) * 512], HIDM, deps=(thd,))
            store_toks.append(sst)
            thd_p = thd; sst_p = sst
    RELAX[0] = False
    barrier()

    arena_reset()
    WDB = [alloc(1024) for _ in range(16)]
    HIDF = alloc(8192); Y1M2 = alloc(512); SM_ = alloc(512)
    for kp in range(16):
        ld(WDB[kp], wd0_d[kp * 128:(kp + 1) * 128, :])
    RELAX[0] = True
    ts_p = None
    sst_p = None
    for c in range(NCH):
        soft_fence()
        hidt = [ld(HIDF[:, kp * 512:(kp + 1) * 512], HIDd[kp * 128:(kp + 1) * 128, c * 512:(c + 1) * 512]) for kp in range(16)]
        for m in range(8):
            bk = m % 2
            mm_group(bk, 512, [(WDB[kp][:, m * 128:(m + 1) * 128], HIDF[:, kp * 512:(kp + 1) * 512]) for kp in range(16)], deps=tuple(hidt))
            ty = ld(Y1M2, Y1d[m * 128:(m + 1) * 128, c * 512:(c + 1) * 512], deps=(ts_p,) if ts_p else ())
            ts = ps_read(bk, "vector", lambda eng, _b=bk: eng.tensor_add(SM_, PS[_b][:, 0:512], Y1M2), deps=(ty,) + ((sst_p,) if sst_p else ()))
            sst = st_(SSd[m * 128:(m + 1) * 128, c * 512:(c + 1) * 512], SM_, deps=(ts,))
            store_toks.append(sst)
            ts_p = ts; sst_p = sst
    RELAX[0] = False
    barrier()

    arena_reset()
    CW1 = [alloc(1024) for _ in range(8)]
    CW2 = [alloc(512) for _ in range(8)]
    CW3 = [alloc(256) for _ in range(4)]
    CW4 = [alloc(1) for _ in range(2)]
    GW1B = alloc(4096)
    GW1 = [GW1B[:, kp * 512:(kp + 1) * 512] for kp in range(8)]
    GW2 = [alloc(256) for _ in range(4)]
    GW3 = [alloc(1) for _ in range(2)]
    OUTA = alloc(4096)
    OUTB = GW1B
    C1B = alloc(4096); C2B = alloc(2048); C3B = alloc(1024)
    X1FB = alloc(4096)
    X1NB = OUTA
    XM_ = C2B[:, 0:512]; SM2 = C2B[:, 512:1024]; X1SQ = C2B[:, 1024:1536]
    MB_ = C3B[:, 0:512]; RB_ = C3B[:, 512:1024]
    CPB = alloc(512); TMS_ = alloc(1024)
    GH1 = STAT[:, 8:12]; GH2 = STAT[:, 12:14]
    TFR = ROWS[0:1, 1024:1536]
    MROW = ROWS[0:1, 0:512]; VROW = ROWS[0:1, 512:1024]

    for kp in range(8):
        ld(CW1[kp], cw1_d[kp * 128:(kp + 1) * 128, :])
        ld(CW2[kp], cw2_d[kp * 128:(kp + 1) * 128, :])
        ld(GW1[kp], gw1_d[kp * 128:(kp + 1) * 128, :])
    for kp in range(4):
        ld(CW3[kp], cw3_d[kp * 128:(kp + 1) * 128, :])
        ld(GW2[kp], gw2_d[kp * 128:(kp + 1) * 128, :])
    for kp in range(2):
        ld(CW4[kp], cw4_d[kp * 128:(kp + 1) * 128, :])
        ld(GW3[kp], gw3_d[kp * 128:(kp + 1) * 128, :])

    E("vector", lambda eng: eng.tensor_scalar_mul(GS[:], GS[:], 1.0 / T))
    for mt in range(4):
        mm_group(0, 1, [(GW1[kp][:, mt * 128:(mt + 1) * 128], GS[:, kp:kp + 1]) for kp in range(8)], col0=mt)
    for mt in range(4):
        ps_read(0, "scalar", lambda eng, _m=mt: eng.activation(GH1[:, _m:_m + 1], PS[0][:, _m:_m + 1], AF.Gelu, bias=GB1P[:, _m:_m + 1]))
    for mt in range(2):
        mm_group(1, 1, [(GW2[kp][:, mt * 128:(mt + 1) * 128], GH1[:, kp:kp + 1]) for kp in range(4)], col0=mt)
    for mt in range(2):
        ps_read(1, "scalar", lambda eng, _m=mt: eng.activation(GH2[:, _m:_m + 1], PS[1][:, _m:_m + 1], AF.Gelu, bias=GB2P[:, _m:_m + 1]))
    mm_group(2, 1, [(GW3[kp], GH2[:, kp:kp + 1]) for kp in range(2)], rows=1)
    ps_read(2, "scalar", lambda eng: eng.activation(GF07[:], PS[2][0:1, 0:1], AF.Sigmoid, bias=GB3S[:]))
    E("vector", lambda eng: eng.tensor_scalar_mul(GF07[:], GF07[:], 0.7))

    twist_map = [(4, 1.0), (5, 1.0), (6, -1.0), (7, -1.0), (0, -1.0), (1, -1.0), (2, 1.0), (3, 1.0)]
    RELAX[0] = True
    for c in range(NCH):
        soft_fence()
        xnt_ = [ld(OUTA[:, dt * 512:(dt + 1) * 512], XN[dt * 128:(dt + 1) * 128, c * 512:(c + 1) * 512]) for dt in range(8)]
        cur_out, oth_out = OUTA, OUTB
        for cyc in range(3):
            soft_fence()
            c1t = []
            for m in range(8):
                bk = m % 2
                mm_group(bk, 512, [(CW1[kp][:, m * 128:(m + 1) * 128], cur_out[:, kp * 512:(kp + 1) * 512]) for kp in range(8)],
                         deps=tuple(xnt_))
                c1t.append(ps_read(bk, "scalar", lambda eng, _b=bk, _m=m: eng.activation(C1B[:, _m * 512:(_m + 1) * 512], PS[_b][:, 0:512], AF.Gelu, bias=CB1P[:, _m:_m + 1])))
            c2t = []
            for m in range(4):
                bk = 2 + m % 2
                mm_group(bk, 512, [(CW2[kp][:, m * 128:(m + 1) * 128], C1B[:, kp * 512:(kp + 1) * 512]) for kp in range(8)],
                         deps=tuple(c1t))
                c2t.append(ps_read(bk, "scalar", lambda eng, _b=bk, _m=m: eng.activation(C2B[:, _m * 512:(_m + 1) * 512], PS[_b][:, 0:512], AF.Gelu, bias=CB2P[:, _m:_m + 1])))
            c3t = []
            for m in range(2):
                bk = m % 2
                mm_group(bk, 512, [(CW3[kp][:, m * 128:(m + 1) * 128], C2B[:, kp * 512:(kp + 1) * 512]) for kp in range(4)],
                         deps=tuple(c2t))
                c3t.append(ps_read(bk, "scalar", lambda eng, _b=bk, _m=m: eng.activation(C3B[:, _m * 512:(_m + 1) * 512], PS[_b][:, 0:512], AF.Gelu, bias=CB3P[:, _m:_m + 1])))
            mm_group(2, 512, [(CW4[kp], C3B[:, kp * 512:(kp + 1) * 512]) for kp in range(2)], rows=1, deps=tuple(c3t))
            ttf = ps_read(2, "scalar", lambda eng: eng.activation(TFR, PS[2][0:1, 0:512], AF.Sigmoid, bias=CB4S[:]))
            tcm = E("vector", lambda eng: eng.tensor_scalar(TFR, TFR, 0.3, GF07[:], ALU.mult, ALU.add), (ttf,))
            tcp = E("vector", lambda eng: eng.tensor_scalar(TFR, TFR, 0.5, AR2[:], ALU.subtract, ALU.mult))
            tcp2 = E("vector", lambda eng: eng.tensor_scalar_add(TFR, TFR, 0.1))
            rank1(3, TFR, 512, deps=(tcp2,))
            tcb = ps_read(3, "scalar", lambda eng: eng.copy(CPB, PS[3][:, 0:512]))
            for dt in range(8):
                src2, sign = twist_map[dt]
                E("vector", lambda eng, _s=src2, _g=sign, _c=cur_out: eng.scalar_tensor_tensor(TMS_[:, 0:512], _c[:, _s * 512:(_s + 1) * 512], _g, CPB, ALU.mult, ALU.mult), (tcb,))
                E("vector", lambda eng, _d=dt, _c=cur_out, _o=oth_out: eng.tensor_add(_o[:, _d * 512:(_d + 1) * 512], _c[:, _d * 512:(_d + 1) * 512], TMS_[:, 0:512]))
            cur_out, oth_out = oth_out, cur_out
        soft_fence()
        t1_p = None; t2_p = None
        mm4_t = []; mm5_t = []
        for dt in range(8):
            tg = E("scalar", lambda eng, _d=dt, _c=cur_out: eng.activation(_c[:, _d * 512:(_d + 1) * 512], _c[:, _d * 512:(_d + 1) * 512], AF.Silu))
            tx = ld(XM_, XF[dt * 128:(dt + 1) * 128, c * 512:(c + 1) * 512], deps=(t2_p,) if t2_p else ())
            ts = ld(SM2, SSd[dt * 128:(dt + 1) * 128, c * 512:(c + 1) * 512], deps=(t1_p,) if t1_p else (), eng="gpsimd")
            t1 = E("vector", lambda eng, _d=dt, _c=cur_out: eng.tensor_mul(TMS_[:, 0:512], _c[:, _d * 512:(_d + 1) * 512], SM2), (tg, ts))
            t2 = E("vector", lambda eng, _d=dt: eng.tensor_add(X1FB[:, _d * 512:(_d + 1) * 512], XM_, TMS_[:, 0:512]), (t1, tx))
            store_toks.append(st_(X1d[dt * 128:(dt + 1) * 128, c * 512:(c + 1) * 512], X1FB[:, dt * 512:(dt + 1) * 512], deps=(t2,)))
            if dt == 0:
                mm4_t.append(mm_group(4, 512, [(ONC[:], X1FB[:, dt * 512:(dt + 1) * 512])], rows=1, deps=(t2,)))
            else:
                mm4_t.append(mm_acc(4, 512, [(ONC[:], X1FB[:, dt * 512:(dt + 1) * 512])], rows=1, deps=(t2,)))
            tq = E("scalar", lambda eng, _d=dt: eng.activation(X1SQ, X1FB[:, _d * 512:(_d + 1) * 512], AF.Square), (t2,) + ((mm5_t[-1],) if mm5_t else ()))
            if dt == 0:
                mm5_t.append(mm_group(5, 512, [(ONC[:], X1SQ)], rows=1, deps=(tq,)))
            else:
                mm5_t.append(mm_acc(5, 512, [(ONC[:], X1SQ)], rows=1, deps=(tq,)))
            t1_p, t2_p = t1, t2
        tmean = ps_read(4, "vector", lambda eng: eng.tensor_scalar_mul(MROW, PS[4][0:1, 0:512], 1.0 / D), deps=(mm4_t[-1],))
        tms2 = E("vector", lambda eng: eng.tensor_mul(VROW, MROW, MROW))
        tvar = ps_read(5, "vector", lambda eng: eng.scalar_tensor_tensor(VROW, PS[5][0:1, 0:512], 1.0 / D, VROW, ALU.mult, ALU.subtract), deps=(mm5_t[-1],))
        tve = E("vector", lambda eng: eng.tensor_scalar_add(VROW, VROW, 1e-5))
        tsq = E("scalar", lambda eng: eng.activation(VROW, VROW, AF.Sqrt), (tve,))
        trc = E("vector", lambda eng: eng.reciprocal(VROW, VROW), (tsq,))
        rank1(4, MROW, 512, deps=(tmean, tms2))
        tmb = ps_read(4, "scalar", lambda eng: eng.copy(MB_, PS[4][:, 0:512]))
        rank1(5, VROW, 512, deps=(trc,))
        trb2 = ps_read(5, "scalar", lambda eng: eng.copy(RB_, PS[5][:, 0:512]))
        xn_new = []
        tst_p = None
        for dt in range(8):
            ta = E("vector", lambda eng, _d=dt: eng.tensor_sub(X1NB[:, _d * 512:(_d + 1) * 512], X1FB[:, _d * 512:(_d + 1) * 512], MB_), (tmb,))
            tb = E("vector", lambda eng, _d=dt: eng.tensor_mul(X1NB[:, _d * 512:(_d + 1) * 512], X1NB[:, _d * 512:(_d + 1) * 512], RB_), (trb2,))
            store_toks.append(st_(X1Nd[dt * 128:(dt + 1) * 128, c * 512:(c + 1) * 512], X1NB[:, dt * 512:(dt + 1) * 512], deps=(tb,)))
            xn_new.append(tb)
        for s in range(4):
            trs = []
            for dt in range(8):
                trs.append(transpose(6, X1NB[:, dt * 512 + s * 128: dt * 512 + (s + 1) * 128], TMS_[:, dt * 128:(dt + 1) * 128],
                                     128, 128, deps=tuple(xn_new) + ((tst_p,) if tst_p else ())))
            tst_p = st_(X1NTd[(c * 4 + s) * 128:(c * 4 + s + 1) * 128, :], TMS_[:, 0:1024], deps=tuple(trs))
            store_toks.append(tst_p)
    RELAX[0] = False
    barrier()

    arena_reset()
    XH = alloc(16384); YH = alloc(16384)
    PIN2 = alloc(1024); PPO2 = alloc(1024); ZW = alloc(1024); TW0 = alloc(512); TW1 = alloc(512)
    LNROW = alloc(512); CBROW = alloc(512)
    AB128 = alloc(512)[0:1, :]; CROW = alloc(512)[0:1, :]
    CRY = ROWS[0:1, 0:512]; ABR = ROWS[0:1, 512:1024]; ABC = ROWS[0:1, 1024:1536]; TRW = ROWS[0:1, 1536:2048]
    RELAX[0] = True
    for hf in range(2):
        soft_fence()
        for u in range(NSUB):
            ld(XH[:, u * 512:(u + 1) * 512], X1NTd[u * 128:(u + 1) * 128, hf * 512:(hf + 1) * 512])
        for i in range(4):
            E("gpsimd", lambda eng, _i=i: eng.memset(YH[:, _i * 4096:(_i + 1) * 4096], 0.0))
        soft_fence()
        for n in range(NS):
            soft_fence()
            lnrow = LNROW[0:1, 0:512]
            cbrow = CBROW[0:1, 0:512]
            t_ln = ld(lnrow, LNABd[n:n + 1, hf * 512:(hf + 1) * 512])
            t_cb = ld(cbrow, CBNDd[n:n + 1, hf * 512:(hf + 1) * 512])
            rank1(6, lnrow, 512, deps=(t_ln,))
            tp1 = ps_read(6, "scalar", lambda eng: eng.activation(PIN2[:, 0:512], PS[6][:, 0:512], AF.Exp, scale=NIOP[:]))
            tp1b = E("scalar", lambda eng: eng.copy(PIN2[:, 512:1024], PIN2[:, 0:512]))
            tp2 = ps_read(6, "scalar", lambda eng: eng.activation(PPO2[:, 0:512], PS[6][:, 0:512], AF.Exp, scale=IOP[:]))
            rank1(7, cbrow, 512, deps=(t_cb,))
            tp3 = ps_read(7, "vector", lambda eng: eng.tensor_mul(PPO2[:, 0:512], PPO2[:, 0:512], PS[7][:, 0:512]), deps=(tp2,))
            tp3b = E("vector", lambda eng: eng.tensor_copy(PPO2[:, 512:1024], PPO2[:, 0:512]))
            tab = E("scalar", lambda eng: eng.activation(ABR, lnrow, AF.Exp), (t_ln,))
            tabc1 = E("vector", lambda eng: eng.tensor_scalar_mul(ABC, lnrow, 127.0), (t_ln,))
            tabc = E("scalar", lambda eng: eng.activation(ABC, ABC, AF.Exp), (tabc1,))
            tab128a = E("vector", lambda eng: eng.tensor_scalar_mul(TRW, lnrow, 128.0), (t_ln,))
            tab128 = E("scalar", lambda eng: eng.activation(AB128, TRW, AF.Exp), (tab128a,))
            E("vector", lambda eng: eng.memset(CRY, 0.0))
            tcrow = E("vector", lambda eng: eng.memset(CROW, 0.0))
            zw_guard = ()
            for p in range(16):
                c0, c1 = 2 * p, 2 * p + 1
                tz = E("vector", lambda eng, _p=p: eng.tensor_mul(ZW, PIN2, XH[:, _p * 1024:(_p + 1) * 1024]), zw_guard + (tp1b,))
                b0 = 0 if p % 2 == 0 else 2
                b1 = 1 if p % 2 == 0 else 3
                mm_group(b0, 512, [(UPI[:], ZW[:, 0:512])], deps=(tz,))
                m_4 = mm_group(4, 512, [(ONC[:], ZW[:, 0:512])], rows=1, deps=(tz,))
                inj0 = mm_acc(b0, 512, [(ONR[:], CROW)], deps=(tcrow,))
                t1 = E("vector", lambda eng: eng.tensor_mul(TRW, CRY, AB128), (tab128,))
                t2 = ps_read(4, "vector", lambda eng: eng.tensor_mul(CRY, PS[4][0:1, 0:512], ABC), deps=(tabc,))
                t3 = E("vector", lambda eng: eng.tensor_add(CRY, CRY, TRW))
                tcrow = E("vector", lambda eng: eng.tensor_mul(CROW, CRY, ABR), (tab, inj0))
                mm_group(b1, 512, [(UPI[:], ZW[:, 512:1024])], deps=(tz,))
                m_5 = mm_group(5, 512, [(ONC[:], ZW[:, 512:1024])], rows=1, deps=(tz,))
                inj1 = mm_acc(b1, 512, [(ONR[:], CROW)], deps=(tcrow,))
                t1 = E("vector", lambda eng: eng.tensor_mul(TRW, CRY, AB128))
                t2 = ps_read(5, "vector", lambda eng: eng.tensor_mul(CRY, PS[5][0:1, 0:512], ABC))
                t3 = E("vector", lambda eng: eng.tensor_add(CRY, CRY, TRW))
                tcrow = E("vector", lambda eng: eng.tensor_mul(CROW, CRY, ABR), (inj1,))
                ps_read(b0, "vector", lambda eng, _b=b0: eng.tensor_mul(TW0, PS[_b][:, 0:512], PPO2[:, 0:512]), deps=(tp3,))
                E("vector", lambda eng, _c=c0: eng.tensor_add(YH[:, _c * 512:(_c + 1) * 512], YH[:, _c * 512:(_c + 1) * 512], TW0))
                ps_read(b1, "vector", lambda eng, _b=b1: eng.tensor_mul(TW1, PS[_b][:, 0:512], PPO2[:, 512:1024]), deps=(tp3b,))
                E("vector", lambda eng, _c=c1: eng.tensor_add(YH[:, _c * 512:(_c + 1) * 512], YH[:, _c * 512:(_c + 1) * 512], TW1))
                zw_guard = (m_4, m_5) + (ps_writer[b0], ps_writer[b1])
        soft_fence()
        for u in range(NSUB):
            store_toks.append(st_(YSSMd[u * 128:(u + 1) * 128, hf * 512:(hf + 1) * 512], YH[:, u * 512:(u + 1) * 512]))
    RELAX[0] = False
    barrier()

    arena_reset()
    XSC = alloc(8192); YSC = alloc(8192); HIDH = alloc(4096)
    WDH = [alloc(1024) for _ in range(4)]
    WGT = alloc(1024); WUT = alloc(1024)
    GTS = alloc(8); MK1 = alloc(8); MK2 = alloc(8); SMB = alloc(8)
    GBC = alloc(512); TW0b = alloc(512); TW1b = alloc(512); GROW = alloc(512)
    GATR = SMALL[0:8, 0:1024]; SCF = SMALL[0:8, 1024:1536]
    WGA = [SMALL[:, 1544 + 8 * kp: 1552 + 8 * kp] for kp in range(8)]
    M1 = STAT[:, 4:5]; M2 = STAT[:, 5:6]; P1_ = STAT[:, 6:7]; P2_ = STAT[:, 7:8]
    for kp in range(8):
        ld(WGA[kp], wgate_d[kp * 128:(kp + 1) * 128, :])
    mgroups = [(0, 4), (4, 4), (8, 3)]
    for sc in range(4):
        RELAX[0] = True
        soft_fence()
        for kp in range(8):
            ld(XSC[:, kp * 1024:(kp + 1) * 1024], X1Nd[kp * 128:(kp + 1) * 128, sc * 1024:(sc + 1) * 1024])
        tyz = E("gpsimd", lambda eng: eng.memset(YSC, 0.0))
        RELAX[0] = False
        for tc in range(2):
            mm_group(0, 512, [(WGA[kp], XSC[:, kp * 1024 + tc * 512: kp * 1024 + (tc + 1) * 512]) for kp in range(8)], rows=8)
            ps_read(0, "scalar", lambda eng: eng.copy(SCF, PS[0][0:8, 0:512]))
            for g in range(4):
                transpose(1, SCF[:, g * 128:(g + 1) * 128], GTS, 8, 128)
                E("vector", lambda eng: eng.reduce_max(M1, GTS, AX.X))
                E("vector", lambda eng: eng.tensor_single_scalar(MK1, GTS, M1, ALU.is_equal))
                E("vector", lambda eng: eng.scalar_tensor_tensor(SMB, MK1, -1e9, GTS, ALU.mult, ALU.add))
                E("vector", lambda eng: eng.reduce_max(M2, SMB, AX.X))
                E("vector", lambda eng: eng.tensor_single_scalar(MK2, SMB, M2, ALU.is_equal))
                E("vector", lambda eng: eng.tensor_sub(P1_, M1, M2))
                E("scalar", lambda eng: eng.activation(P1_, P1_, AF.Sigmoid))
                E("vector", lambda eng: eng.tensor_scalar(P2_, P1_, -1.0, 1.0, ALU.mult, ALU.add))
                E("vector", lambda eng: eng.tensor_scalar(MK1, MK1, P1_, None, ALU.mult))
                E("vector", lambda eng: eng.tensor_scalar(MK2, MK2, P2_, None, ALU.mult))
                E("vector", lambda eng: eng.tensor_add(GTS, MK1, MK2))
                transpose(2, GTS, GATR[:, tc * 512 + g * 128: tc * 512 + (g + 1) * 128], 128, 8)
        store_toks.append(st_(GATEd[:, sc * 1024:(sc + 1) * 1024], GATR[:, 0:1024]))
        RELAX[0] = True
        soft_fence()
        thd_prev = None
        tyy_prev = None
        grow_guard = None
        for e in range(ME):
            for (m0, nmt) in mgroups:
                soft_fence()
                hid_done = []
                gu_guard = ()
                for mi in range(nmt):
                    mt = m0 + mi
                    wg_tk = [ld(WGT[:, kp * 128:(kp + 1) * 128], wgE_d[e, mt, kp, :, :], deps=gu_guard) for kp in range(8)]
                    wu_tk = [ld(WUT[:, kp * 128:(kp + 1) * 128], wuE_d[e, mt, kp, :, :], deps=gu_guard, eng="gpsimd") for kp in range(8)]
                    gu_guard = ()
                    for tc in range(2):
                        bg, bu = (3, 4) if tc == 0 else (0, 1)
                        gmm = mm_group(bg, 512, [(WGT[:, kp * 128:(kp + 1) * 128], XSC[:, kp * 1024 + tc * 512: kp * 1024 + (tc + 1) * 512]) for kp in range(8)], deps=tuple(wg_tk))
                        umm = mm_group(bu, 512, [(WUT[:, kp * 128:(kp + 1) * 128], XSC[:, kp * 1024 + tc * 512: kp * 1024 + (tc + 1) * 512]) for kp in range(8)], deps=tuple(wu_tk))
                        tsl = ps_read(bg, "scalar", lambda eng, _b=bg: eng.activation(TW0b, PS[_b][:, 0:512], AF.Silu), deps=(thd_prev,) if thd_prev else ())
                        thd = ps_read(bu, "vector", lambda eng, _b=bu, _mi=mi, _tc=tc: eng.tensor_mul(HIDH[:, _mi * 1024 + _tc * 512: _mi * 1024 + (_tc + 1) * 512], TW0b, PS[_b][:, 0:512]), deps=(tsl,))
                        hid_done.append(thd)
                        thd_prev = thd
                        gu_guard = gu_guard + (gmm, umm)
                wd_tk = [ld(WDH[mi], wdE_d[e, (m0 + mi) * 128:(m0 + mi + 1) * 128, :]) for mi in range(nmt)]
                for tc in range(2):
                    t_gr = ld(GROW[0:1, 0:512], GATEd[e:e + 1, sc * 1024 + tc * 512: sc * 1024 + (tc + 1) * 512],
                              deps=(grow_guard,) if grow_guard else ())
                    grow_guard = rank1(5, GROW[0:1, 0:512], 512, deps=(t_gr,))
                    tgb = ps_read(5, "scalar", lambda eng: eng.copy(GBC, PS[5][:, 0:512]), deps=(tyy_prev,) if tyy_prev else ())
                    for m in range(8):
                        bnk = 6 if m % 2 == 0 else 7
                        mm_group(bnk, 512, [(WDH[mi][:, m * 128:(m + 1) * 128], HIDH[:, mi * 1024 + tc * 512: mi * 1024 + (tc + 1) * 512]) for mi in range(nmt)],
                                 deps=tuple(wd_tk) + tuple(hid_done))
                        tyy = ps_read(bnk, "vector", lambda eng, _b=bnk: eng.tensor_mul(TW1b, PS[_b][:, 0:512], GBC), deps=(tgb,))
                        tyy_prev = tyy
                        E("vector", lambda eng, _m=m, _tc=tc: eng.tensor_add(YSC[:, _m * 1024 + _tc * 512: _m * 1024 + (_tc + 1) * 512],
                                                                             YSC[:, _m * 1024 + _tc * 512: _m * 1024 + (_tc + 1) * 512], TW1b),
                          deps=(tyz,))
        soft_fence()
        for m in range(8):
            store_toks.append(st_(YMOEd[m * 128:(m + 1) * 128, sc * 1024:(sc + 1) * 1024], YSC[:, m * 1024:(m + 1) * 1024]))
        RELAX[0] = False
    barrier()

    arena_reset()
    YST = alloc(1024); YSF = alloc(4096); XOB = alloc(4096); OTM = alloc(1024)
    X1M_ = alloc(512); X1NM_ = alloc(512); YMM = alloc(512); G2M = alloc(512); S2M = alloc(512)
    for c in range(NCH):
        for s in range(4):
            ld(YST, YSSMd[(c * 4 + s) * 128:(c * 4 + s + 1) * 128, :])
            for dt in range(8):
                transpose(0, YST[:, dt * 128:(dt + 1) * 128], YSF[:, dt * 512 + s * 128: dt * 512 + (s + 1) * 128], 128, 128)
        for dt in range(8):
            ld(X1M_, X1d[dt * 128:(dt + 1) * 128, c * 512:(c + 1) * 512])
            ld(X1NM_, X1Nd[dt * 128:(dt + 1) * 128, c * 512:(c + 1) * 512])
            ld(YMM, YMOEd[dt * 128:(dt + 1) * 128, c * 512:(c + 1) * 512], eng="gpsimd")
            E("scalar", lambda eng: eng.activation(G2M, YMM, AF.Silu))
            E("vector", lambda eng, _d=dt: eng.tensor_scalar(S2M, X1NM_, DPP[:, _d:_d + 1], None, ALU.mult))
            E("vector", lambda eng, _d=dt: eng.tensor_add(S2M, S2M, YSF[:, _d * 512:(_d + 1) * 512]))
            E("vector", lambda eng: eng.tensor_mul(S2M, S2M, G2M))
            E("vector", lambda eng, _d=dt: eng.tensor_add(XOB[:, _d * 512:(_d + 1) * 512], X1M_, S2M))
        for s in range(4):
            for dt in range(8):
                transpose(1, XOB[:, dt * 512 + s * 128: dt * 512 + (s + 1) * 128], OTM[:, dt * 128:(dt + 1) * 128], 128, 128)
            store_toks.append(st_(out_d[(c * 4 + s) * 128:(c * 4 + s + 1) * 128, :], OTM))

    if debug:
        dbg = {"XN": XN, "HF": HFd, "QF": QFd, "KF": KFd, "KTM": KTMd, "VE": VEd,
               "Y1": Y1d, "H2": H2d, "SS": SSd, "X1": X1d, "X1N": X1Nd,
               "YSSM": YSSMd, "YMOE": YMOEd, "BETAo": BETA, "GATEo": GATEd}
        for nm, t in dbg.items():
            o = nc.dram_tensor("dbg_" + nm, list(t.tensor.shape), F32, kind="ExternalOutput").ap()
            store_toks.append(st_(o[:], t[:]))
    S.emit("sync", None, tuple(store_toks))
    S.build(nc, maxseq=maxseq)
    return nc, S


_nc_cache = [None]


def kernel(**inputs):
    def f(name):
        return np.ascontiguousarray(np.asarray(inputs[name], dtype=np.float32))

    x_full = f("L0_attn_x")
    wgE = f("L1_ffn_w_g").reshape(ME, 8, 128, MMT, 128).transpose(0, 3, 1, 2, 4).copy()
    wuE = f("L1_ffn_w_u").reshape(ME, 8, 128, MMT, 128).transpose(0, 3, 1, 2, 4).copy()

    iotaf = np.broadcast_to(np.arange(128, dtype=np.float32)[None, :], (128, 128)).copy()
    jj, tt = np.meshgrid(np.arange(128), np.arange(128), indexing="ij")
    upinc = (jj <= tt).astype(np.float32)
    upneg = -(jj < tt).astype(np.float32)

    common = {
        "n1w": f("L0_attn_norm1_w").reshape(1, D),
        "n2w": f("L0_attn_norm2_w").reshape(1, D),
        "wq": f("L0_attn_Wq"), "wk": f("L0_attn_Wk"), "wv": f("L0_attn_Wv"), "wo": f("L0_attn_Wo"),
        "bw": f("L0_attn_beta_w").reshape(1, D),
        "bb": f("L0_attn_beta_b").reshape(1, 1),
        "alog": f("L0_attn_alpha_logit").reshape(1, D),
        "wg0": f("L0_attn_w_gate"), "wu0": f("L0_attn_w_up"), "wd0": f("L0_attn_w_down"),
        "cw1": f("L0_ffn_cn_w1"), "cw2": f("L0_ffn_cn_w2"), "cw3": f("L0_ffn_cn_w3"), "cw4": f("L0_ffn_cn_w4"),
        "cb1": f("L0_ffn_cn_b1").reshape(1, D), "cb2": f("L0_ffn_cn_b2").reshape(1, 512),
        "cb3": f("L0_ffn_cn_b3").reshape(1, 256), "cb4": f("L0_ffn_cn_b4").reshape(1, 1),
        "gw1": f("L0_ffn_gc_w1"), "gw2": f("L0_ffn_gc_w2"), "gw3": f("L0_ffn_gc_w3"),
        "gb1": f("L0_ffn_gc_b1").reshape(1, 512), "gb2": f("L0_ffn_gc_b2").reshape(1, 256),
        "gb3": f("L0_ffn_gc_b3").reshape(1, 1),
        "ar": f("L0_ffn_adaptive_range").reshape(1, 1),
        "logA": f("L1_attn_log_A"), "Bp": f("L1_attn_B_param"), "Cp": f("L1_attn_C"),
        "Dv": f("L1_attn_D").reshape(1, D),
        "wgate": f("L1_ffn_w_gate"),
        "wgE": wgE, "wuE": wuE, "wdE": f("L1_ffn_w_d"),
        "idn": np.eye(128, dtype=np.float32),
        "upinc": upinc, "upneg": upneg, "iotaf": iotaf,
        "onesr": np.ones((1, 128), np.float32), "onesc": np.ones((128, 1), np.float32),
        "niop": -np.arange(128, dtype=np.float32).reshape(128, 1),
        "iop": np.arange(128, dtype=np.float32).reshape(128, 1),
    }
    in_maps = [dict(common, x=np.ascontiguousarray(x_full[b])) for b in range(8)]
    if _nc_cache[0] is None:
        _nc_cache[0] = build_program()[0]
    nc = _nc_cache[0]
    res = run_bass_kernel_spmd(nc, in_maps, core_ids=list(range(8)))
    return np.stack([r["out"] for r in res.results], axis=0)
